# version 02
# baseline (speedup 1.0000x reference)
import contextlib
import itertools

import numpy as np
import concourse.bass as bass
import concourse.mybir as mybir
from concourse.bass_utils import run_bass_kernel_spmd

F32 = mybir.dt.float32
BF16 = mybir.dt.bfloat16
AF = mybir.ActivationFunctionType
ALU = mybir.AluOpType
AX = mybir.AxisListType

T = 2052
OWN0, OWN1 = 2, 2050
D = 1024
H = 16
DH = 64
DFF = 4096
NF = 8
NHC = 32
RND = float(1.5 * 2.0 ** 23)
TT = [(128 * i, 128) for i in range(16)] + [(2048, 4)]
NC = [(0, 512), (512, 512), (1024, 512), (1536, 512), (2048, 4)]
REPL = [[0, 1], [2, 3], [4, 5], [6, 7]]
_uid = itertools.count()


NSLOT = 12


class Sched:

    def __init__(self, nc, sems):
        self.nc = nc
        self.sems = sems
        self.cnt = {s: 0 for s in sems}
        self.hwm = {}
        self.last_w = {}
        self.readers = {}
        self.eng = {"pe": nc.tensor, "act": nc.scalar, "dve": nc.vector,
                    "pool": nc.gpsimd, "sp": nc.sync}
        self.slot = {"sp": 0, "pool": 0}

    def _wait(self, eng, sem, val):
        if val <= 0 or self.hwm.get((eng, sem), 0) >= val:
            return
        self.hwm[(eng, sem)] = val
        self.eng[eng].wait_ge(self.sems[sem], val)

    def pre(self, eng, reads, writes):
        for b in reads:
            t = self.last_w.get(id(b))
            if t:
                self._wait(eng, t[0], t[1])
        for b in writes:
            t = self.last_w.get(id(b))
            if t:
                self._wait(eng, t[0], t[1])
            for t in self.readers.get(id(b), ()):
                self._wait(eng, t[0], t[1])

    def post(self, sem, inst, reads, writes, amt=1):
        inst.then_inc(self.sems[sem], amt)
        self.cnt[sem] += amt
        tok = (sem, self.cnt[sem])
        for b in writes:
            self.last_w[id(b)] = tok
            self.readers[id(b)] = []
        for b in reads:
            self.readers.setdefault(id(b), []).append(tok)

    def op(self, eng, f, r=(), w=()):
        self.pre(eng, r, w)
        self.post(eng, f(self.eng[eng]), r, w)

    def dma(self, q, f, r=(), w=()):
        s = self.slot[q]
        self.slot[q] = (s + 1) % NSLOT
        sem = f"{q}d{s}"
        self._wait(q, sem, self.cnt[sem])
        self.pre(q, r, w)
        self.post(sem, f(self.eng[q]), r, w, 16)

    def cc(self, f, r=(), w=()):
        self.pre("pool", r, w)
        self._wait("pool", "cc", self.cnt["cc"])
        self.post("cc", f(self.eng["pool"]), r, w)

    def barrier(self):
        for eng in self.eng:
            for sem, v in self.cnt.items():
                self._wait(eng, sem, v)


def build(n_phases=4, dump=False):
    import os
    n_pro = int(os.environ.get('N_PRO', '9'))
    nc = bass.Bass("TRN2", target_bir_lowering=False)

    xin = nc.dram_tensor("xin", [D, T], F32, kind="ExternalInput")
    wqkv_a = nc.dram_tensor("wqkv_a", [D, 3 * D], F32, kind="ExternalInput")
    wout_a = nc.dram_tensor("wout_a", [D, D], F32, kind="ExternalInput")
    bout_a = nc.dram_tensor("bout_a", [D], F32, kind="ExternalInput")
    wqkv_b = nc.dram_tensor("wqkv_b", [3 * D, D], F32, kind="ExternalInput")
    wout_b = nc.dram_tensor("wout_b", [D, D], F32, kind="ExternalInput")
    w1 = nc.dram_tensor("w1", [DFF, D], F32, kind="ExternalInput")
    convw = nc.dram_tensor("convw", [3, DFF], F32, kind="ExternalInput")
    convb = nc.dram_tensor("convb", [DFF], F32, kind="ExternalInput")
    w2 = nc.dram_tensor("w2", [D, DFF], F32, kind="ExternalInput")
    ident_in = nc.dram_tensor("ident", [128, 128], F32, kind="ExternalInput")
    hmask_in = nc.dram_tensor("hmask", [128, 4], F32, kind="ExternalInput")
    sel2_in = nc.dram_tensor("sel2", [128, 2], F32, kind="ExternalInput")
    onescol_in = nc.dram_tensor("onescol_in", [128, 1], F32, kind="ExternalInput")
    onesrow_in = nc.dram_tensor("onesrow_in", [128, 128], F32, kind="ExternalInput")
    xqstage = (nc.dram_tensor("xqstage", [D, T], BF16, kind="ExternalInput")
               if os.environ.get("BYPASS_XQ") else None)
    wqstage = (nc.dram_tensor("wqstage", [D, DFF], BF16, kind="ExternalInput")
               if os.environ.get("BYPASS_WQ") else None)
    yout = nc.dram_tensor("y", [D, 2048], F32, kind="ExternalOutput")

    wa_bf = nc.dram_tensor("wa_bf", [D, 3 * D], BF16)
    woa_bf = nc.dram_tensor("woa_bf", [D, D], BF16)
    wb_nat = nc.dram_tensor("wb_nat", [3 * D, D], BF16)
    wbT = nc.dram_tensor("wbT", [D, 3 * D], BF16)
    wob_nat = nc.dram_tensor("wob_nat", [D, D], BF16)
    wobT = nc.dram_tensor("wobT", [D, D], BF16)
    w1q_nat = nc.dram_tensor("w1q_nat", [DFF, D], BF16)
    wq1T = nc.dram_tensor("wq1T", [D, DFF], BF16)
    w2q_nat = nc.dram_tensor("w2q_nat", [D, DFF], BF16)
    wq2T = nc.dram_tensor("wq2T", [DFF, D], BF16)
    kx = nc.dram_tensor("kx", [D, 2048], BF16)
    vx = nc.dram_tensor("vx", [2048, D], BF16)
    kg = nc.dram_tensor("kg", [2 * D, 2048], BF16)
    vg = nc.dram_tensor("vg", [4096, D], BF16)
    cxl = nc.dram_tensor("cxl", [65, D], F32)
    cxg = nc.dram_tensor("cxg", [65, D], F32)
    hd = nc.dram_tensor("hd", [DFF, T], F32)
    xq2d = nc.dram_tensor("xq2d", [DFF, T], BF16)

    stk = contextlib.ExitStack()
    sb = lambda n, s, d: stk.enter_context(nc.sbuf_tensor(n, s, d))
    sems = {n: stk.enter_context(nc.semaphore(n + "_s"))
            for n in ("pe", "act", "dve", "pool", "sp", "cc")}
    for q in ("sp", "pool"):
        for s in range(NSLOT):
            sems[f"{q}d{s}"] = stk.enter_context(nc.semaphore(f"{q}d{s}_s"))
    S = Sched(nc, sems)

    x8 = [sb(f"x{i}", [128, T], F32) for i in range(NF)]
    s1 = [sb(f"s1_{i}", [128, T], BF16) for i in range(NF)]
    s2 = [sb(f"s2_{i}", [128, T], BF16) for i in range(NF)]
    dscr = sb("dscr", [128, T], F32)
    xbf = [sb("xbf0", [128, 512], BF16)]
    sqbf = [sb("sqbf0", [128, 512], BF16)]
    wbuf = [sb(f"wbuf{i}", [128, 2048], BF16) for i in range(2)]
    ident = sb("identt", [128, 128], F32)
    hmask = sb("hmaskt", [128, 4], F32)
    sel2 = sb("sel2t", [128, 2], BF16)
    onescol = sb("onescolt", [128, 1], BF16)
    onesrow = sb("onesrowt", [128, 128], F32)
    b_out = sb("b_outt", [128, NF], F32)
    convwt = sb("convwt", [128, 3 * NHC], F32)
    convbt = sb("convbt", [128, NHC], F32)
    mwrow = sb("mwrow", [1, 4], F32)
    scal = sb("scal", [1, 16], F32)
    redc = sb("redc", [128, 1], F32)
    bc1 = sb("bc1", [128, 512], F32)
    bc2 = sb("bc2", [128, 512], F32)
    bc3 = sb("bc3", [128, 512], F32)
    rowt = sb("rowt", [65, T], F32)
    bkr = sb("bkr", [1, 1024], F32)
    mwb = sb("mwb", [128, 4], F32)
    _ROWP = {"sum": 0, "sumsq": 32, "mx": 64}

    def R(name, c0=0, c1=T):
        if name == "bk":
            return bkr[0:1, c0:c1], bkr
        p = _ROWP[name]
        return rowt[p:p + 1, c0:c1], rowt

    def RP(name):
        return 0 if name == "bk" else _ROWP[name]

    def mm(e, out, lhsT, rhs, start, stop):
        return e.matmul(out, lhsT, rhs, start=start, stop=stop,
                        skip_group_check=True)

    with stk:
        S.dma("sp", lambda e: e.dma_start(out=ident[:], in_=ident_in.ap()), w=[ident])
        S.dma("sp", lambda e: e.dma_start(out=hmask[:], in_=hmask_in.ap()), w=[hmask])
        S.dma("pool", lambda e: e.dma_start(out=sel2[:], in_=sel2_in.ap()), w=[sel2])
        S.dma("pool", lambda e: e.dma_start(out=onescol[:], in_=onescol_in.ap()), w=[onescol])
        S.dma("sp", lambda e: e.dma_start(out=onesrow[:], in_=onesrow_in.ap()), w=[onesrow])
        for c in range(NF):
            S.dma("sp", lambda e, c=c: e.dma_start(out=x8[c][:], in_=xin.ap()[128 * c:128 * (c + 1), :]), w=[x8[c]])
        S.dma("pool", lambda e: e.dma_start(out=wa_bf.ap(), in_=wqkv_a.ap()), w=[wa_bf])
        S.dma("pool", lambda e: e.dma_start(out=woa_bf.ap(), in_=wout_a.ap()), w=[woa_bf])
        S.dma("pool", lambda e: e.dma_start(out=wb_nat.ap(), in_=wqkv_b.ap()), w=[wb_nat])
        S.dma("pool", lambda e: e.dma_start(out=wob_nat.ap(), in_=wout_b.ap()), w=[wob_nat])

        with nc.psum_tensor("prT", [128, 512], F32) as prT, \
             nc.sbuf_tensor("rload", [1, 512], F32) as rload:
            def load_cols(dram_ap_1d, n, dst, colmap):
                for j in range(n):
                    S.dma("sp", lambda e, j=j: e.dma_start(
                        out=rload[0:1, 0:128],
                        in_=dram_ap_1d[128 * j:128 * (j + 1)].rearrange("(o p) -> o p", o=1)),
                        r=[], w=[rload])
                    S.pre("pe", [rload, ident], [prT])
                    S.post("pe", mm(nc.tensor, prT[0:128, 0:1], rload[0:1, 0:128],
                                    ident[0:1, 0:1], True, True), [rload, ident], [prT])
                    S.op("dve", lambda e, j=j: e.tensor_copy(
                        dst[:, colmap(j):colmap(j) + 1], prT[0:128, 0:1]),
                        r=[prT], w=[dst])
            if n_pro >= 2:
                load_cols(bout_a.ap(), NF, b_out, lambda j: j)
                load_cols(convb.ap(), NHC, convbt, lambda j: j)
                for k in range(3):
                    cwk = convw.ap()[k:k + 1, :].rearrange("one n -> (one n)")
                    load_cols(cwk, NHC, convwt, lambda j, k=k: 3 * j + k)

        tpb = wbuf[0]
        for fc in (range(NF) if n_pro >= 3 else []):
            for oc in range(6):
                S.dma("sp", lambda e, f=fc, o=oc: e.dma_start_transpose(
                    tpb[:, 0:512], wb_nat.ap()[512 * o:512 * (o + 1), 128 * f:128 * (f + 1)]),
                    r=[wb_nat], w=[tpb])
                S.dma("sp", lambda e, f=fc, o=oc: e.dma_start(
                    out=wbT.ap()[128 * f:128 * (f + 1), 512 * o:512 * (o + 1)], in_=tpb[:, 0:512]),
                    r=[tpb], w=[wbT])
            for oc in range(2):
                S.dma("sp", lambda e, f=fc, o=oc: e.dma_start_transpose(
                    tpb[:, 0:512], wob_nat.ap()[512 * o:512 * (o + 1), 128 * f:128 * (f + 1)]),
                    r=[wob_nat], w=[tpb])
                S.dma("sp", lambda e, f=fc, o=oc: e.dma_start(
                    out=wobT.ap()[128 * f:128 * (f + 1), 512 * o:512 * (o + 1)], in_=tpb[:, 0:512]),
                    r=[tpb], w=[wobT])

        with nc.sbuf_tensor("absacc", [128, 48], F32) as absacc, \
             nc.sbuf_tensor("qbf", [128, DFF], BF16) as qbf, \
             nc.psum_tensor("pT0", [128, 512], F32) as pT0:
            w2v = w2.ap().rearrange("a (b c) -> (a b) c", b=2)
            w2qv = w2q_nat.ap().rearrange("a (b c) -> (a b) c", b=2)
            for wi, (wap, nch, wid) in (enumerate(((w1.ap(), NHC, D), (w2v, 16, 2048))) if n_pro >= 4 else []):
                for c in range(nch):
                    S.dma("sp", lambda e, w_=wap, c=c, ww=wid: e.dma_start(
                        out=dscr[:, 0:ww], in_=w_[128 * c:128 * (c + 1), :]), w=[dscr])
                    col = c if wi == 0 else NHC + c
                    S.op("dve", lambda e, col=col, ww=wid: e.reduce_sum(
                        absacc[:, col:col + 1], dscr[:, 0:ww], axis=AX.X,
                        apply_absolute_value=True), r=[dscr], w=[absacc])
            for wi, (lo, n, total) in (enumerate(((0, NHC, DFF * D), (NHC, 16, D * DFF))) if n_pro >= 4 else []):
                S.pre("pe", [absacc, ident], [pT0])
                S.post("pe", mm(nc.tensor, pT0[0:n, 0:128], absacc[:, lo:lo + n],
                                ident[:], True, True), [absacc, ident], [pT0])
                S.op("dve", lambda e, n=n: e.reduce_sum(
                    redc[0:n, 0:1], pT0[0:n, 0:128], axis=AX.X), r=[pT0], w=[redc])
                S.pre("pe", [redc, ident], [pT0])
                S.post("pe", mm(nc.tensor, pT0[0:1, 128:128 + n], redc[0:n, 0:1],
                                ident[0:n, 0:n], True, True), [redc, ident], [pT0])
                S.op("dve", lambda e, wi=wi, n=n: e.reduce_sum(
                    scal[0:1, wi:wi + 1], pT0[0:1, 128:128 + n], axis=AX.X), r=[pT0], w=[scal])
                S.op("dve", lambda e, wi=wi, tot=total: e.tensor_scalar(
                    mwrow[0:1, wi:wi + 1], scal[0:1, wi:wi + 1], 1.0 / tot, 1e-5,
                    ALU.mult, ALU.max), r=[scal], w=[mwrow])
            if n_pro < 4:
                S.op("pool", lambda e: e.memset(mwrow[0:1, :], 1.0), w=[mwrow])
            S.op("dve", lambda e: e.reciprocal(mwrow[0:1, 2:4], mwrow[0:1, 0:2]),
                 r=[mwrow], w=[mwrow])
            S.pre("pe", [onesrow, mwrow], [pT0])
            S.post("pe", mm(nc.tensor, pT0[0:128, 256:260], onesrow[0:1, :],
                            mwrow[0:1, 0:4], True, True), [onesrow, mwrow], [pT0])
            S.op("dve", lambda e: e.tensor_copy(mwb[:, 0:4], pT0[0:128, 256:260]),
                 r=[pT0], w=[mwb])
            for wi, (wap, wq_ap, nch, wid) in (enumerate(
                    ((w1.ap(), w1q_nat.ap(), NHC, D), (w2v, w2qv, 16, 2048))) if n_pro >= 4 else []):
                for c in range(nch):
                    S.dma("sp", lambda e, w_=wap, c=c, ww=wid: e.dma_start(
                        out=dscr[:, 0:ww], in_=w_[128 * c:128 * (c + 1), :]), w=[dscr])
                    S.op("dve", lambda e, ww=wid, wi=wi: e.tensor_scalar(
                        dscr[:, 0:ww], dscr[:, 0:ww], mwb[:, 2 + wi:3 + wi], RND,
                        ALU.mult, ALU.add), r=[dscr, mwb], w=[dscr])
                    S.op("dve", lambda e, ww=wid: e.tensor_scalar(
                        dscr[:, 0:ww], dscr[:, 0:ww], -RND, -1.0, ALU.add, ALU.max),
                        r=[dscr], w=[dscr])
                    S.op("dve", lambda e, ww=wid: e.tensor_scalar(
                        qbf[:, 0:ww], dscr[:, 0:ww], 1.0, 0.0, ALU.min, ALU.add),
                        r=[dscr], w=[qbf])
                    wqt = w1q_nat if wi == 0 else w2q_nat
                    S.dma("sp", lambda e, q_=wq_ap, c=c, ww=wid: e.dma_start(
                        out=q_[128 * c:128 * (c + 1), :], in_=qbf[:, 0:ww]),
                        r=[qbf], w=[wqt])
        for fc in (range(NF) if n_pro >= 5 else []):
            for oc in range(8):
                S.dma("sp", lambda e, f=fc, o=oc: e.dma_start_transpose(
                    tpb[:, 512:1024], w1q_nat.ap()[512 * o:512 * (o + 1), 128 * f:128 * (f + 1)]),
                    r=[w1q_nat], w=[tpb])
                S.dma("sp", lambda e, f=fc, o=oc: e.dma_start(
                    out=wq1T.ap()[128 * f:128 * (f + 1), 512 * o:512 * (o + 1)], in_=tpb[:, 512:1024]),
                    r=[tpb], w=[wq1T])
        for hc in (range(NHC) if n_pro >= 5 else []):
            for oc in range(2):
                S.dma("sp", lambda e, h=hc, o=oc: e.dma_start_transpose(
                    tpb[:, 512:1024], w2q_nat.ap()[512 * o:512 * (o + 1), 128 * h:128 * (h + 1)]),
                    r=[w2q_nat], w=[tpb])
                S.dma("sp", lambda e, h=hc, o=oc: e.dma_start(
                    out=wq2T.ap()[128 * h:128 * (h + 1), 512 * o:512 * (o + 1)], in_=tpb[:, 512:1024]),
                    r=[tpb], w=[wq2T])

        with nc.psum_tensor("bkP", [1, 512], F32) as bkP:
            for half in (range(2) if n_pro >= 6 else []):
                for fc in range(NF):
                    S.dma("sp", lambda e, f=fc, h=half: e.dma_start(
                        out=wbuf[1][:, 0:512],
                        in_=wa_bf.ap()[128 * f:128 * (f + 1), D + 512 * h:D + 512 * (h + 1)]),
                        r=[wa_bf], w=[wbuf[1]])
                    S.op("act", lambda e: e.activation(sqbf[0][:, 0:512], wbuf[1][:, 0:512], AF.Square),
                         r=[wbuf[1]], w=[sqbf[0]])
                    S.pre("pe", [onescol, sqbf[0]], [bkP])
                    S.post("pe", mm(nc.tensor, bkP[0:1, :], onescol[:], sqbf[0][:, 0:512],
                                    fc == 0, fc == NF - 1), [onescol, sqbf[0]], [bkP])
                ap, tt = R("bk", 512 * half, 512 * (half + 1))
                S.op("act", lambda e, ap=ap: e.activation(ap, bkP[0:1, :], AF.Sqrt),
                     r=[bkP], w=[tt])
        ap, tt = R("bk", 0, 1024)
        S.op("dve", lambda e, ap=ap: e.tensor_scalar(
            ap, ap, 32.5, 0.0, ALU.mult, ALU.add), r=[tt], w=[tt])

        def ln_stats(ph):
            statP = [ph.enter_context(nc.psum_tensor(f"stP{next(_uid)}", [1, 512], F32)) for _ in range(2)]
            for off, w in NC:
                for fc in range(NF):
                    xb, sq = xbf[0], sqbf[0]
                    S.op("dve", lambda e, f=fc, o=off, ww=w, xb=xb: e.tensor_copy(
                        xb[:, 0:ww], x8[f][:, o:o + ww]), r=[x8[fc]], w=[xb])
                    S.op("act", lambda e, ww=w, xb=xb, sq=sq: e.activation(
                        sq[:, 0:ww], xb[:, 0:ww], AF.Square), r=[xb], w=[sq])
                    S.pre("pe", [onescol, xb], [statP[0]])
                    S.post("pe", mm(nc.tensor, statP[0][0:1, 0:w], onescol[:], xb[:, 0:w],
                                    fc == 0, fc == NF - 1), [onescol, xb], [statP[0]])
                    S.pre("pe", [onescol, sq], [statP[1]])
                    S.post("pe", mm(nc.tensor, statP[1][0:1, 0:w], onescol[:], sq[:, 0:w],
                                    fc == 0, fc == NF - 1), [onescol, sq], [statP[1]])
                ap, tt = R("sum", off, off + w)
                S.op("dve", lambda e, ap=ap, ww=w: e.tensor_copy(ap, statP[0][0:1, 0:ww]),
                     r=[statP[0]], w=[tt])
                ap, tt = R("sumsq", off, off + w)
                S.op("dve", lambda e, ap=ap, ww=w: e.tensor_copy(ap, statP[1][0:1, 0:ww]),
                     r=[statP[1]], w=[tt])


        def mean_rstd_bc(bcP, off, w, scratch):
            build_bcast(bcP, bc1, "sum", off, w)
            build_bcast(bcP, bc2, "sumsq", off, w)
            S.op("dve", lambda e, ww=w: e.tensor_scalar(
                bc1[:, 0:ww], bc1[:, 0:ww], 1.0 / D, 0.0, ALU.mult, ALU.add),
                r=[bc1], w=[bc1])
            S.op("dve", lambda e, ww=w: e.tensor_scalar(
                bc2[:, 0:ww], bc2[:, 0:ww], 1.0 / D, 1e-5, ALU.mult, ALU.add),
                r=[bc2], w=[bc2])
            S.op("act", lambda e, ww=w: e.activation(
                scratch[:, 0:ww], bc1[:, 0:ww], AF.Square), r=[bc1], w=[scratch])
            S.op("dve", lambda e, ww=w: e.tensor_tensor(
                bc2[:, 0:ww], bc2[:, 0:ww], scratch[:, 0:ww], ALU.subtract),
                r=[bc2, scratch], w=[bc2])
            S.op("act", lambda e, ww=w: e.activation(
                bc2[:, 0:ww], bc2[:, 0:ww], AF.Sqrt), r=[bc2], w=[bc2])
            S.op("dve", lambda e, ww=w: e.reciprocal(bc2[:, 0:ww], bc2[:, 0:ww]),
                 r=[bc2], w=[bc2])

        def build_bcast(bcP, bc, rowname, off, w):
            ap, tt = R(rowname, off, off + w)
            p = RP(rowname)
            S.pre("pe", [onesrow, tt], [bcP])
            S.post("pe", mm(nc.tensor, bcP[0:128, 0:w], onesrow[p:p + 1, :], ap, True, True),
                   [onesrow, tt], [bcP])
            S.op("dve", lambda e, ww=w: e.tensor_copy(bc[:, 0:ww], bcP[0:128, 0:ww]),
                 r=[bcP], w=[bc])

        def ln_to(ph, dest):
            bcP = ph.enter_context(nc.psum_tensor(f"lnb{next(_uid)}", [128, 512], F32))
            for off, w in NC:
                mean_rstd_bc(bcP, off, w, bc3)
                for fc in range(NF):
                    S.op("dve", lambda e, f=fc, o=off, ww=w: e.tensor_tensor(
                        dscr[:, o:o + ww], x8[f][:, o:o + ww], bc1[:, 0:ww], ALU.subtract),
                        r=[x8[fc], bc1], w=[dscr])
                    S.op("dve", lambda e, f=fc, o=off, ww=w: e.tensor_tensor(
                        dest[f][:, o:o + ww], dscr[:, o:o + ww], bc2[:, 0:ww], ALU.mult),
                        r=[dscr, bc2], w=[dest[fc]])

        def rezero_halo():
            for fc in range(NF):
                S.op("dve", lambda e, f=fc: e.tensor_tensor(
                    x8[f][:, 0:2], x8[f][:, 0:2], hmask[:, 0:2], ALU.mult),
                    r=[x8[fc], hmask], w=[x8[fc]])
                S.op("dve", lambda e, f=fc: e.tensor_tensor(
                    x8[f][:, 2050:2052], x8[f][:, 2050:2052], hmask[:, 2:4], ALU.mult),
                    r=[x8[fc], hmask], w=[x8[fc]])

        def tokmax_to_row(ph, src, dstrow, sqrt=False):
            trP = ph.enter_context(nc.psum_tensor(f"tk{next(_uid)}", [128, 512], F32))
            for toff, tw in TT:
                S.pre("pe", [src, ident], [trP])
                S.post("pe", mm(nc.tensor, trP[0:tw, 0:128], src[:, toff:toff + tw],
                                ident[:], True, True), [src, ident], [trP])
                S.op("dve", lambda e, ww=tw: e.reduce_max(
                    redc[0:ww, 0:1], trP[0:ww, 0:128], axis=AX.X), r=[trP], w=[redc])
                S.pre("pe", [redc, ident], [trP])
                S.post("pe", mm(nc.tensor, trP[0:1, 128:128 + tw], redc[0:tw, 0:1],
                                ident[0:tw, 0:tw], True, True), [redc, ident], [trP])
                ap, tt = R(dstrow, toff, toff + tw)
                if sqrt:
                    S.op("act", lambda e, ap=ap, ww=tw: e.activation(
                        ap, trP[0:1, 128:128 + ww], AF.Sqrt), r=[trP], w=[tt])
                else:
                    S.op("dve", lambda e, ap=ap, ww=tw: e.tensor_copy(
                        ap, trP[0:1, 128:128 + ww]), r=[trP], w=[tt])

        if n_phases >= 1 and not os.environ.get("SKIP_P1"):
            with contextlib.ExitStack() as ph:
                ln_stats(ph)
                ln_to(ph, s1)
                S.barrier()
            with contextlib.ExitStack() as ph:
                qsm = ph.enter_context(nc.sbuf_tensor("qsm", [128, H * 65], BF16))
                expk = ph.enter_context(nc.sbuf_tensor("expk", [128, D], BF16))
                qmx = ph.enter_context(nc.sbuf_tensor("qmx", [128, H], F32))
                qsum = ph.enter_context(nc.sbuf_tensor("qsum", [128, H], F32))
                qrec = ph.enter_context(nc.sbuf_tensor("qrec", [128, H], F32))
                dq = ph.enter_context(nc.sbuf_tensor("dq", [128, 1024], F32))
                bkb = ph.enter_context(nc.sbuf_tensor("bkb", [128, 1024], F32))
                ctxs = ph.enter_context(nc.sbuf_tensor("ctxs", [65, 1024], F32))
                vP = [ph.enter_context(nc.psum_tensor(f"vP{i}", [128, 512], F32)) for i in range(2)]
                psA = ph.enter_context(nc.psum_tensor("psA", [128, 2048], F32))
                ctxP = ph.enter_context(nc.psum_tensor("ctxP", [65, 1024], F32))

                wres = [s2[f][:, 0:2048] for f in range(NF)]
                for fc in range(NF):
                    S.dma("sp", lambda e, f=fc: e.dma_start(
                        out=wres[f], in_=wa_bf.ap()[128 * f:128 * (f + 1), 0:2048]),
                        r=[wa_bf], w=[s2[fc]])
                S.op("pool", lambda e: e.memset(
                    qsm[:].rearrange("p (h c) -> p h c", c=65)[:, :, 64:65], 1.0), w=[qsm])
                for half in range(2):
                    ap, tt = R("bk", 512 * half, 512 * (half + 1))
                    S.pre("pe", [onesrow, tt], [vP[0]])
                    S.post("pe", mm(nc.tensor, vP[0][0:128, 0:512], onesrow[0:1, :], ap,
                                    True, True), [onesrow, tt], [vP[0]])
                    S.op("dve", lambda e, h=half: e.tensor_copy(
                        bkb[:, 512 * h:512 * (h + 1)], vP[0][0:128, 0:512]), r=[vP[0]], w=[bkb])

                for ti, (toff, tw) in enumerate(TT):
                    rd = [s2[f] for f in range(NF)] + [s1[f] for f in range(NF)]
                    S.pre("pe", rd, [psA])
                    for fc in range(NF):
                        for oc in range(4):
                            inst = mm(nc.tensor, psA[0:tw, 512 * oc:512 * (oc + 1)],
                                      s1[fc][:, toff:toff + tw],
                                      wres[fc][:, 512 * oc:512 * (oc + 1)],
                                      fc == 0, fc == NF - 1)
                    S.post("pe", inst, rd, [psA])
                    S.op("dve", lambda e, ww=tw: e.reduce_max(
                        qmx[0:ww, :], psA[0:ww, 0:1024].rearrange("p (h c) -> p h c", c=DH),
                        axis=AX.X), r=[psA], w=[qmx])
                    S.op("dve", lambda e, ww=tw: e.tensor_tensor(
                        dq[0:ww, :].rearrange("p (h c) -> p h c", c=DH),
                        psA[0:ww, 0:1024].rearrange("p (h c) -> p h c", c=DH),
                        qmx[0:ww, :].unsqueeze(2).broadcast_to((ww, H, DH)),
                        ALU.subtract), r=[psA, qmx], w=[dq])
                    S.op("act", lambda e, ww=tw: e.activation(
                        dq[0:ww, :], dq[0:ww, :], AF.Exp, scale=0.125), r=[dq], w=[dq])
                    S.op("dve", lambda e, ww=tw: e.reduce_sum(
                        qsum[0:ww, :], dq[0:ww, :].rearrange("p (h c) -> p h c", c=DH),
                        axis=AX.X), r=[dq], w=[qsum])
                    S.op("dve", lambda e, ww=tw: e.reciprocal(
                        qrec[0:ww, :], qsum[0:ww, :]), r=[qsum], w=[qrec])
                    S.op("dve", lambda e, ww=tw: e.tensor_tensor(
                        qsm[0:ww, :].rearrange("p (h c) -> p h c", c=65)[:, :, 0:64],
                        dq[0:ww, :].rearrange("p (h c) -> p h c", c=DH),
                        qrec[0:ww, :].unsqueeze(2).broadcast_to((ww, H, DH)),
                        ALU.mult), r=[dq, qrec], w=[qsm])
                    S.op("dve", lambda e, ww=tw: e.tensor_tensor(
                        dq[0:ww, :], psA[0:ww, 1024:2048], bkb[0:ww, :], ALU.subtract),
                        r=[psA, bkb], w=[dq])
                    S.op("act", lambda e, ww=tw: e.activation(
                        expk[0:ww, :], dq[0:ww, :], AF.Exp), r=[dq], w=[expk])
                    if ti == 0:
                        S.op("pool", lambda e: e.memset(expk[0:2, :], 0.0), w=[expk])
                    kw = tw if ti < 16 else 2
                    S.pre("pe", [qsm, expk], [ctxP])
                    for hh in range(H):
                        inst = mm(nc.tensor, ctxP[0:65, 64 * hh:64 * (hh + 1)],
                                  qsm[0:kw, 65 * hh:65 * (hh + 1)],
                                  expk[0:kw, 64 * hh:64 * (hh + 1)],
                                  ti == 0 and hh in (0, 8), ti == 16)
                    S.post("pe", inst, [qsm, expk], [ctxP])

                S.op("dve", lambda e: e.tensor_copy(ctxs[:], ctxP[0:65, :]), r=[ctxP], w=[ctxs])
                S.dma("sp", lambda e: e.dma_start(out=cxl.ap(), in_=ctxs[:]), r=[ctxs], w=[cxl])
                S.cc(lambda e: e.collective_compute(
                    "AllReduce", ALU.add, replica_groups=REPL,
                    ins=[cxl.ap().opt()], outs=[cxg.ap().opt()]), r=[cxl], w=[cxg])

                for oc in range(NF):
                    wb = wbuf[oc % 2]
                    S.dma("sp", lambda e, o=oc, wb=wb: e.dma_start(
                        out=wb[:, 0:1024],
                        in_=wa_bf.ap().rearrange("(f p) o -> p f o", p=128)[:, :, 2048 + 128 * o:2048 + 128 * (o + 1)]),
                        r=[wa_bf], w=[wb])
                    for ni, (off, w) in enumerate(NC):
                        pb = vP[ni % 2]
                        S.pre("pe", [wb] + [s1[f] for f in range(NF)], [pb])
                        for fc in range(NF):
                            inst = mm(nc.tensor, pb[:, 0:w], wb[:, 128 * fc:128 * (fc + 1)],
                                      s1[fc][:, off:off + w], fc == 0, fc == NF - 1)
                        S.post("pe", inst, [wb] + [s1[f] for f in range(NF)], [pb])
                        S.op("dve", lambda e, o=oc, of=off, ww=w, p=pb: e.tensor_copy(
                            s2[o][:, of:of + ww], p[:, 0:ww]), r=[pb], w=[s2[oc]])
                S.barrier()
            with contextlib.ExitStack() as ph:
                zbs = ph.enter_context(nc.sbuf_tensor("zbs", [64, 1024], F32))
                ctxn = ph.enter_context(nc.sbuf_tensor("ctxn", [128, 512], BF16))
                ctxs2 = ph.enter_context(nc.sbuf_tensor("ctxs2", [65, 1024], F32))
                bcP2 = ph.enter_context(nc.psum_tensor("bcP2", [128, 512], F32))
                oPa = ph.enter_context(nc.psum_tensor("oPa", [64, 512], F32))
                oPb = ph.enter_context(nc.psum_tensor("oPb", [64, 512], F32))
                woP = [ph.enter_context(nc.psum_tensor(f"woP{i}", [128, 512], F32)) for i in range(2)]
                zrowr = ph.enter_context(nc.sbuf_tensor(f"zrw{next(_uid)}", [1, 1024], F32))
                zr, zrt = zrowr[0:1, 0:1024], zrowr
                S.dma("sp", lambda e: e.dma_start(out=ctxs2[:], in_=cxg.ap()), r=[cxg], w=[ctxs2])
                S.op("dve", lambda e: e.reciprocal(zr, ctxs2[64:65, :]), r=[ctxs2], w=[zrt])
                for half in range(2):
                    ap = zrowr[0:1, 512 * half:512 * (half + 1)]
                    S.pre("pe", [onesrow, zrt], [bcP2])
                    S.post("pe", mm(nc.tensor, bcP2[0:64, 0:512], onesrow[0:1, 0:64],
                                    ap, True, True), [onesrow, zrt], [bcP2])
                    S.op("dve", lambda e, h=half: e.tensor_copy(
                        zbs[0:64, 512 * h:512 * (h + 1)], bcP2[0:64, 0:512]), r=[bcP2], w=[zbs])
                for hh in range(H):
                    ro, co = 64 * (hh % 2), 64 * (hh // 2)
                    S.op("dve", lambda e, h=hh, ro=ro, co=co: e.tensor_tensor(
                        ctxn[ro:ro + 64, co:co + 64], ctxs2[0:64, 64 * h:64 * (h + 1)],
                        zbs[0:64, 64 * h:64 * (h + 1)], ALU.mult),
                        r=[ctxs2, zbs], w=[ctxn])
                for ch in range(NF):
                    for ni, (off, w) in enumerate(NC):
                        S.pre("pe", [ctxn, s2[ch]], [oPa])
                        S.post("pe", mm(nc.tensor, oPa[0:64, 0:w],
                                        ctxn[0:64, 64 * ch:64 * (ch + 1)],
                                        s2[ch][0:64, off:off + w], True, True),
                               [ctxn, s2[ch]], [oPa])
                        S.pre("pe", [ctxn, s2[ch]], [oPb])
                        S.post("pe", mm(nc.tensor, oPb[0:64, 0:w],
                                        ctxn[64:128, 64 * ch:64 * (ch + 1)],
                                        s2[ch][64:128, off:off + w], True, True),
                               [ctxn, s2[ch]], [oPb])
                        S.op("dve", lambda e, c=ch, of=off, ww=w: e.tensor_copy(
                            s2[c][0:64, of:of + ww], oPa[0:64, 0:ww]), r=[oPa], w=[s2[ch]])
                        S.op("dve", lambda e, c=ch, of=off, ww=w: e.tensor_copy(
                            s2[c][64:128, of:of + ww], oPb[0:64, 0:ww]), r=[oPb], w=[s2[ch]])
                for oc in range(NF):
                    wb = wbuf[oc % 2]
                    S.dma("sp", lambda e, o=oc, wb=wb: e.dma_start(
                        out=wb[:, 0:1024],
                        in_=woa_bf.ap().rearrange("(f p) o -> p f o", p=128)[:, :, 128 * o:128 * (o + 1)]),
                        r=[woa_bf], w=[wb])
                    for ni, (off, w) in enumerate(NC):
                        pb = woP[ni % 2]
                        S.pre("pe", [wb] + [s2[f] for f in range(NF)], [pb])
                        for fc in range(NF):
                            inst = mm(nc.tensor, pb[:, 0:w], wb[:, 128 * fc:128 * (fc + 1)],
                                      s2[fc][:, off:off + w], fc == 0, fc == NF - 1)
                        S.post("pe", inst, [wb] + [s2[f] for f in range(NF)], [pb])
                        S.op("dve", lambda e, o=oc, of=off, ww=w, p=pb: e.scalar_tensor_tensor(
                            x8[o][:, of:of + ww], p[:, 0:ww], b_out[:, o:o + 1],
                            x8[o][:, of:of + ww], ALU.add, ALU.add),
                            r=[pb, b_out, x8[oc]], w=[x8[oc]])
                S.barrier()
            rezero_halo()

        def ffn_phase(tag):
          def _qi_scope():
            with contextlib.ExitStack() as ph:
                ln_stats(ph)
                bcPf = ph.enter_context(nc.psum_tensor(f"bf{next(_uid)}", [128, 512], F32))
                mx = dscr
                for off, w in NC:
                    build_bcast(bcPf, bc1, "sum", off, w)
                    S.op("dve", lambda e, ww=w: e.tensor_scalar(
                        bc1[:, 0:ww], bc1[:, 0:ww], 1.0 / D, 0.0, ALU.mult, ALU.add),
                        r=[bc1], w=[bc1])
                    for fc in range(NF):
                        S.op("dve", lambda e, f=fc, o=off, ww=w: e.tensor_tensor(
                            bc2[:, 0:ww], x8[f][:, o:o + ww], bc1[:, 0:ww], ALU.subtract),
                            r=[x8[fc], bc1], w=[bc2])
                        S.op("pool", lambda e, ww=w: e.tensor_tensor(
                            bc2[:, 0:ww], bc2[:, 0:ww], bc2[:, 0:ww], ALU.mult),
                            r=[bc2], w=[bc2])
                        if fc == 0:
                            S.op("dve", lambda e, o=off, ww=w: e.tensor_copy(
                                mx[:, o:o + ww], bc2[:, 0:ww]), r=[bc2], w=[mx])
                        else:
                            S.op("dve", lambda e, o=off, ww=w: e.tensor_tensor(
                                mx[:, o:o + ww], bc2[:, 0:ww], mx[:, o:o + ww], ALU.max),
                                r=[bc2, mx], w=[mx])
                tokmax_to_row(ph, mx, "mx", sqrt=True)
                for off, w in NC:
                    mean_rstd_bc(bcPf, off, w, bc3)
                    build_bcast(bcPf, bc3, "mx", off, w)
                    dsl = dscr[:, off:off + w]
                    S.op("dve", lambda e, ww=w: e.tensor_tensor(
                        bc3[:, 0:ww], bc3[:, 0:ww], bc2[:, 0:ww], ALU.mult),
                        r=[bc3, bc2], w=[bc3])
                    S.op("dve", lambda e, ww=w: e.tensor_scalar(
                        bc3[:, 0:ww], bc3[:, 0:ww], 1e-5, 0.0, ALU.max, ALU.add),
                        r=[bc3], w=[bc3])
                    S.op("dve", lambda e, d=dsl: e.reciprocal(d, bc3[:, 0:w]),
                         r=[bc3], w=[dscr])
                    S.op("dve", lambda e, d=dsl, ww=w: e.tensor_tensor(
                        d, d, bc2[:, 0:ww], ALU.mult), r=[dscr, bc2], w=[dscr])
                    S.op("dve", lambda e, d=dsl: e.tensor_scalar(
                        d, d, 127.0, 0.0, ALU.mult, ALU.add), r=[dscr], w=[dscr])
                    S.op("dve", lambda e, d=dsl, ww=w: e.tensor_tensor(
                        bc2[:, 0:ww], bc1[:, 0:ww], d, ALU.mult),
                        r=[bc1, dscr], w=[bc2])
                    S.op("dve", lambda e, ww=w: e.tensor_scalar(
                        bc3[:, 0:ww], bc3[:, 0:ww], mwb[:, 0:1], 1.0 / 127.0,
                        ALU.mult, ALU.mult), r=[bc3, mwb], w=[bc3])
                    S.op("dve", lambda e, d=dsl, ww=w: e.tensor_copy(
                        bc1[:, 0:ww], d), r=[dscr], w=[bc1])
                    for fc in range(NF):
                        S.op("dve", lambda e, f=fc, o=off, ww=w: e.tensor_tensor(
                            dscr[:, o:o + ww], x8[f][:, o:o + ww], bc1[:, 0:ww], ALU.mult),
                            r=[x8[fc], bc1], w=[dscr])
                        S.op("dve", lambda e, o=off, ww=w: e.tensor_tensor(
                            dscr[:, o:o + ww], dscr[:, o:o + ww], bc2[:, 0:ww], ALU.subtract),
                            r=[dscr, bc2], w=[dscr])
                        S.op("dve", lambda e, o=off, ww=w: e.tensor_scalar(
                            dscr[:, o:o + ww], dscr[:, o:o + ww], RND, -RND, ALU.add, ALU.add),
                            r=[dscr], w=[dscr])
                        S.op("dve", lambda e, f=fc, o=off, ww=w: e.tensor_tensor(
                            s1[f][:, o:o + ww], dscr[:, o:o + ww], bc3[:, 0:ww], ALU.mult),
                            r=[dscr, bc3], w=[s1[fc]])
                S.barrier()
          if not os.environ.get("SKIP_QI"):
            _qi_scope()
          if True:
            if xqstage is not None:
                for fc in range(NF):
                    S.dma("sp", lambda e, f=fc: e.dma_start(
                        out=s1[f][:], in_=xqstage.ap()[128 * f:128 * (f + 1), :]),
                        r=[xqstage], w=[s1[fc]])
                S.barrier()
            with contextlib.ExitStack() as ph:
                hs = [ph.enter_context(nc.sbuf_tensor(f"hs{tag}{i}", [128, T], F32)) for i in range(2)]
                ut = ph.enter_context(nc.sbuf_tensor(f"ut{tag}", [128, T], F32))
                s2invb = ph.enter_context(nc.sbuf_tensor(f"s2i{tag}", [128, T], F32))
                hP = [ph.enter_context(nc.psum_tensor(f"hP{tag}{i}", [128, 512], F32)) for i in range(4)]
                hmx = dscr
                wsrc = wqstage if wqstage is not None else wq1T
                for hc in range(NHC):
                    wb = wbuf[hc % 2]
                    hcur = hs[hc % 2]
                    S.dma("sp", lambda e, h=hc, wb=wb, ws=wsrc: e.dma_start(
                        out=wb[:, 0:1024],
                        in_=ws.ap().rearrange("(f p) o -> p f o", p=128)[:, :, 128 * h:128 * (h + 1)]),
                        r=[wsrc], w=[wb])
                    for ni, (off, w) in enumerate(NC):
                        pb = hP[ni % 4]
                        S.pre("pe", [wb] + [s1[f] for f in range(NF)], [pb])
                        for fc in range(NF):
                            inst = mm(nc.tensor, pb[:, 0:w], wb[:, 128 * fc:128 * (fc + 1)],
                                      s1[fc][:, off:off + w], fc == 0, fc == NF - 1)
                        S.post("pe", inst, [wb] + [s1[f] for f in range(NF)], [pb])
                        S.op("act", lambda e, of=off, ww=w, p=pb, hc_=hcur: e.copy(
                            hc_[:, of:of + ww], p[:, 0:ww]), r=[pb], w=[hcur])
                    S.op("dve", lambda e, h=hc, hc_=hcur: e.tensor_scalar(
                        ut[:, :], hc_[:, :], convwt[:, 3 * h + 1:3 * h + 2],
                        convbt[:, h:h + 1], ALU.mult, ALU.add),
                        r=[hcur, convwt, convbt], w=[ut])
                    S.op("dve", lambda e, h=hc, hc_=hcur: e.scalar_tensor_tensor(
                        ut[:, 1:T], hc_[:, 0:T - 1], convwt[:, 3 * h:3 * h + 1],
                        ut[:, 1:T], ALU.mult, ALU.add),
                        r=[hcur, convwt, ut], w=[ut])
                    S.op("dve", lambda e, h=hc, hc_=hcur: e.scalar_tensor_tensor(
                        ut[:, 0:T - 1], hc_[:, 1:T], convwt[:, 3 * h + 2:3 * h + 3],
                        ut[:, 0:T - 1], ALU.mult, ALU.add),
                        r=[hcur, convwt, ut], w=[ut])
                    S.op("act", lambda e: e.activation(ut[:, :], ut[:, :], AF.Silu),
                         r=[ut], w=[ut])
                    if os.environ.get("DUMP_PRECONV"):
                        S.dma("sp", lambda e, h=hc, hc_=hcur: e.dma_start(
                            out=hd.ap()[128 * h:128 * (h + 1), :], in_=hc_[:, :]), r=[hcur], w=[hd])
                    else:
                        S.dma("sp", lambda e, h=hc: e.dma_start(
                            out=hd.ap()[128 * h:128 * (h + 1), :], in_=ut[:, :]), r=[ut], w=[hd])
                    S.op("pool", lambda e, hc_=hcur: e.tensor_tensor(
                        hc_[:, :], ut[:, :], ut[:, :], ALU.mult), r=[ut], w=[hcur])
                    if hc == 0:
                        S.op("dve", lambda e, hc_=hcur: e.tensor_copy(hmx[:, :], hc_[:, :]),
                             r=[hcur], w=[hmx])
                    else:
                        S.op("dve", lambda e, hc_=hcur: e.tensor_tensor(
                            hmx[:, :], hc_[:, :], hmx[:, :], ALU.max), r=[hcur, hmx], w=[hmx])
                tokmax_to_row(ph, hmx, "mx", sqrt=True)
                bcPg = ph.enter_context(nc.psum_tensor(f"bg{next(_uid)}", [128, 512], F32))
                for off, w in NC:
                    build_bcast(bcPg, bc1, "mx", off, w)
                    S.op("dve", lambda e, ww=w: e.tensor_scalar(
                        bc1[:, 0:ww], bc1[:, 0:ww], 1e-5, 0.0, ALU.max, ALU.add),
                        r=[bc1], w=[bc1])
                    S.op("dve", lambda e, o=off, ww=w: e.tensor_scalar(
                        s2invb[:, o:o + ww], bc1[:, 0:ww], mwb[:, 1:2], 1.0 / 127.0,
                        ALU.mult, ALU.mult), r=[bc1, mwb], w=[s2invb])
                    S.op("dve", lambda e, ww=w: e.reciprocal(bc1[:, 0:ww], bc1[:, 0:ww]),
                         r=[bc1], w=[bc1])
                    S.op("dve", lambda e, ww=w: e.tensor_scalar(
                        bc1[:, 0:ww], bc1[:, 0:ww], 127.0, 0.0, ALU.mult, ALU.add),
                        r=[bc1], w=[bc1])
                    for hc in range(NHC):
                        hcur = hs[hc % 2]
                        qr = s2[hc % 2]
                        S.dma("sp", lambda e, h=hc, o=off, ww=w, hc_=hcur: e.dma_start(
                            out=hc_[:, o:o + ww], in_=hd.ap()[128 * h:128 * (h + 1), o:o + ww]),
                            r=[hd], w=[hcur])
                        S.op("dve", lambda e, o=off, ww=w, hc_=hcur: e.tensor_tensor(
                            bc2[:, 0:ww], hc_[:, o:o + ww], bc1[:, 0:ww], ALU.mult),
                            r=[hcur, bc1], w=[bc2])
                        S.op("dve", lambda e, o=off, ww=w, q=qr: e.tensor_scalar(
                            q[:, o:o + ww], bc2[:, 0:ww], RND, -RND, ALU.add, ALU.add),
                            r=[bc2], w=[qr])
                        S.dma("sp", lambda e, h=hc, o=off, ww=w, q=qr: e.dma_start(
                            out=xq2d.ap()[128 * h:128 * (h + 1), o:o + ww], in_=q[:, o:o + ww]),
                            r=[qr], w=[xq2d])
                S.barrier()

            with contextlib.ExitStack() as ph:
                oP = [ph.enter_context(nc.psum_tensor(f"oP{tag}{i}", [128, 512], F32)) for i in range(NF)]
                xq2r = [ph.enter_context(nc.sbuf_tensor(f"xr{tag}{i}", [128, 512], BF16)) for i in range(2)]
                for ni, (off, w) in enumerate(NC):
                    for hc in range(NHC):
                        rb, wb = xq2r[hc % 2], wbuf[hc % 2]
                        S.dma("sp", lambda e, h=hc, o=off, ww=w, r_=rb: e.dma_start(
                            out=r_[:, 0:ww], in_=xq2d.ap()[128 * h:128 * (h + 1), o:o + ww]),
                            r=[xq2d], w=[rb])
                        S.dma("pool", lambda e, h=hc, wb=wb: e.dma_start(
                            out=wb[:, 0:1024], in_=wq2T.ap()[128 * h:128 * (h + 1), :]),
                            r=[wq2T], w=[wb])
                        for oc in range(NF):
                            S.pre("pe", [wb, rb], [oP[oc]])
                            S.post("pe", mm(nc.tensor, oP[oc][:, 0:w],
                                            wb[:, 128 * oc:128 * (oc + 1)], rb[:, 0:w],
                                            hc == 0, hc == NHC - 1), [wb, rb], [oP[oc]])
                    for oc in range(NF):
                        S.op("dve", lambda e, o=oc, of=off, ww=w: e.tensor_tensor(
                            bc2[:, 0:ww], oP[o][:, 0:ww], s2invb[:, of:of + ww], ALU.mult),
                            r=[oP[oc], s2invb], w=[bc2])
                        S.op("dve", lambda e, o=oc, of=off, ww=w: e.tensor_tensor(
                            x8[o][:, of:of + ww], bc2[:, 0:ww], x8[o][:, of:of + ww], ALU.add),
                            r=[bc2, x8[oc]], w=[x8[oc]])
                S.barrier()

        if n_phases >= 2:
            ffn_phase("A")
            rezero_halo()

        if n_phases >= 3:
            with contextlib.ExitStack() as ph:
                ln_stats(ph)
                ln_to(ph, s1)
                S.barrier()
            with contextlib.ExitStack() as ph:
                kvt = ph.enter_context(nc.sbuf_tensor("kvt", [128, T], BF16))
                qmax8 = ph.enter_context(nc.sbuf_tensor("qmax8", [128, NF], F32))
                qmaxr = ph.enter_context(nc.sbuf_tensor("qmaxr", [1, NF], F32))
                vres = [ph.enter_context(nc.sbuf_tensor(f"vres{i}", [128, 1024], BF16)) for i in range(NF)]
                knacc = ph.enter_context(nc.sbuf_tensor("knacc", [2, NF], F32))
                kn2s = ph.enter_context(nc.sbuf_tensor("kn2s", [2, 512], F32))
                ksq = ph.enter_context(nc.sbuf_tensor("ksq", [128, 2048], BF16))
                kchunk = ph.enter_context(nc.sbuf_tensor("kchunk", [128, 2048], BF16))
                qkP = [ph.enter_context(nc.psum_tensor(f"qkP{i}", [128, 512], F32)) for i in range(2)]
                vP3 = ph.enter_context(nc.psum_tensor("vP3", [128, 1024], F32))
                trPq = ph.enter_context(nc.psum_tensor("trPq", [128, 512], F32))
                knP = ph.enter_context(nc.psum_tensor("knP", [2, 512], F32))

                for oc in range(16):
                    wb = wbuf[oc % 2]
                    S.dma("sp", lambda e, o=oc, wb=wb: e.dma_start(
                        out=wb[:, 0:1024],
                        in_=wbT.ap().rearrange("(f p) o -> p f o", p=128)[:, :, 128 * o:128 * (o + 1)]),
                        r=[wbT], w=[wb])
                    dst = s2[oc] if oc < 8 else kvt
                    for ni, (off, w) in enumerate(NC):
                        pb = qkP[ni % 2]
                        S.pre("pe", [wb] + [s1[f] for f in range(NF)], [pb])
                        for fc in range(NF):
                            inst = mm(nc.tensor, pb[:, 0:w], wb[:, 128 * fc:128 * (fc + 1)],
                                      s1[fc][:, off:off + w], fc == 0, fc == NF - 1)
                        S.post("pe", inst, [wb] + [s1[f] for f in range(NF)], [pb])
                        S.op("dve", lambda e, d=dst, of=off, ww=w, p=pb: e.tensor_copy(
                            d[:, of:of + ww], p[:, 0:ww]), r=[pb], w=[dst])
                    if oc < 8:
                        S.op("dve", lambda e, o=oc: e.reduce_max(
                            qmax8[:, o:o + 1], s2[o][:, :], axis=AX.X,
                            apply_absolute_value=True), r=[s2[oc]], w=[qmax8])
                    else:
                        S.dma("sp", lambda e, o=oc: e.dma_start(
                            out=kx.ap()[128 * (o - 8):128 * (o - 7), :],
                            in_=kvt[:, OWN0:OWN1]), r=[kvt], w=[kx])
                S.pre("pe", [qmax8, ident], [trPq])
                S.post("pe", mm(nc.tensor, trPq[0:8, 0:128], qmax8[:, 0:8], ident[:],
                                True, True), [qmax8, ident], [trPq])
                S.op("dve", lambda e: e.reduce_max(redc[0:8, 0:1], trPq[0:8, 0:128], axis=AX.X),
                     r=[trPq], w=[redc])
                S.pre("pe", [redc, ident], [trPq])
                S.post("pe", mm(nc.tensor, trPq[0:1, 128:136], redc[0:8, 0:1],
                                ident[0:8, 0:8], True, True), [redc, ident], [trPq])
                S.op("dve", lambda e: e.tensor_copy(qmaxr[0:1, 0:8], trPq[0:1, 128:136]),
                     r=[trPq], w=[qmaxr])

                for fc in range(NF):
                    S.dma("sp", lambda e, f=fc: e.dma_start(
                        out=vres[f][:], in_=wbT.ap()[128 * f:128 * (f + 1), 2048:3072]),
                        r=[wbT], w=[vres[fc]])
                for ti, (toff, tw) in enumerate(TT):
                    rd = [vres[f] for f in range(NF)] + [s1[f] for f in range(NF)]
                    S.pre("pe", rd, [vP3])
                    for fc in range(NF):
                        for oc in range(2):
                            inst = mm(nc.tensor, vP3[0:tw, 512 * oc:512 * (oc + 1)],
                                      s1[fc][:, toff:toff + tw],
                                      vres[fc][:, 512 * oc:512 * (oc + 1)],
                                      fc == 0, fc == NF - 1)
                    S.post("pe", inst, rd, [vP3])
                    S.op("dve", lambda e, ww=tw: e.tensor_copy(
                        kvt[0:ww, 0:1024], vP3[0:ww, :]), r=[vP3], w=[kvt])
                    lo, hi = max(toff, OWN0), min(toff + tw, OWN1)
                    if hi > lo:
                        S.dma("sp", lambda e, lo=lo, hi=hi, to=toff: e.dma_start(
                            out=vx.ap()[lo - OWN0:hi - OWN0, :],
                            in_=kvt[lo - to:hi - to, 0:1024]), r=[kvt], w=[vx])

                S.cc(lambda e: e.collective_compute(
                    "AllGather", ALU.bypass, replica_groups=REPL,
                    ins=[kx.ap().opt()], outs=[kg.ap().opt()]), r=[kx], w=[kg])
                S.cc(lambda e: e.collective_compute(
                    "AllGather", ALU.bypass, replica_groups=REPL,
                    ins=[vx.ap().opt()], outs=[vg.ap().opt()]), r=[vx], w=[vg])

                for r_ in range(2):
                    for c in range(NF):
                        S.dma("sp", lambda e, r2=r_, c=c: e.dma_start(
                            out=kchunk[:], in_=kg.ap()[1024 * r2 + 128 * c:1024 * r2 + 128 * (c + 1), :]),
                            r=[kg], w=[kchunk])
                        S.op("act", lambda e: e.activation(ksq[:], kchunk[:], AF.Square),
                             r=[kchunk], w=[ksq])
                        for n4 in range(4):
                            S.pre("pe", [sel2, ksq], [knP])
                            S.post("pe", mm(nc.tensor, knP[0:2, :], sel2[:],
                                            ksq[:, 512 * n4:512 * (n4 + 1)], True, True),
                                   [sel2, ksq], [knP])
                            S.op("dve", lambda e: e.tensor_copy(
                                kn2s[0:2, :], knP[0:2, :]), r=[knP], w=[kn2s])
                            S.op("dve", lambda e: e.reduce_max(
                                redc[0:2, 0:1], kn2s[0:2, :], axis=AX.X), r=[kn2s], w=[redc])
                            if r_ == 0 and n4 == 0:
                                S.op("dve", lambda e, c=c: e.tensor_copy(
                                    knacc[0:2, c:c + 1], redc[0:2, 0:1]), r=[redc], w=[knacc])
                            else:
                                S.op("dve", lambda e, c=c: e.tensor_tensor(
                                    knacc[0:2, c:c + 1], redc[0:2, 0:1],
                                    knacc[0:2, c:c + 1], ALU.max), r=[redc, knacc], w=[knacc])
                S.pre("pe", [knacc, ident], [trPq])
                S.post("pe", mm(nc.tensor, trPq[0:8, 136:138], knacc[0:2, 0:8],
                                ident[0:2, 0:2], True, True), [knacc, ident], [trPq])
                S.op("dve", lambda e: e.reduce_max(redc[0:8, 0:1], trPq[0:8, 136:138], axis=AX.X),
                     r=[trPq], w=[redc])
                S.pre("pe", [redc, ident], [trPq])
                S.post("pe", mm(nc.tensor, trPq[0:1, 138:146], redc[0:8, 0:1],
                                ident[0:8, 0:8], True, True), [redc, ident], [trPq])
                S.op("act", lambda e: e.activation(scal[0:1, 8:16], trPq[0:1, 138:146], AF.Sqrt),
                     r=[trPq], w=[scal])
                S.op("dve", lambda e: e.tensor_tensor(scal[0:1, 8:16], scal[0:1, 8:16],
                                                      qmaxr[0:1, 0:8], ALU.mult),
                     r=[scal, qmaxr], w=[scal])
                S.op("dve", lambda e: e.tensor_scalar(scal[0:1, 8:16], scal[0:1, 8:16],
                                                      -0.125, 0.0, ALU.mult, ALU.add),
                     r=[scal], w=[scal])
                S.barrier()
            with contextlib.ExitStack() as ph:
                ktile = [ph.enter_context(nc.sbuf_tensor(f"ktile{i}", [128, 4096], BF16)) for i in range(2)]
                vstat = ph.enter_context(nc.sbuf_tensor("vstat", [128, 2 * 32 * 65], BF16))
                bbt = ph.enter_context(nc.sbuf_tensor("bbt", [128, 8], F32))
                zbs2 = ph.enter_context(nc.sbuf_tensor("zbs2", [64, 1024], F32))
                ptile = [ph.enter_context(nc.sbuf_tensor(f"pt{i}", [128, 1024], BF16)) for i in range(2)]
                zrowr = ph.enter_context(nc.sbuf_tensor(f"zrw{next(_uid)}", [1, 1024], F32))
                zrt = zrowr
                with nc.psum_tensor("bP3", [128, 512], F32) as bP3:
                    S.pre("pe", [onesrow, scal], [bP3])
                    S.post("pe", mm(nc.tensor, bP3[0:128, 0:8], onesrow[0:1, :],
                                    scal[0:1, 8:16], True, True), [onesrow, scal], [bP3])
                    S.op("dve", lambda e: e.tensor_copy(bbt[:, 0:8], bP3[0:128, 0:8]),
                         r=[bP3], w=[bbt])
                    S.barrier()
                STp = [ph.enter_context(nc.psum_tensor(f"STp{i}", [128, 1024], F32)) for i in range(2)]
                pvP = [ph.enter_context(nc.psum_tensor(f"pvP{i}", [65, 512], F32)) for i in range(2)]
                zbP = ph.enter_context(nc.psum_tensor("zbP", [64, 1024], F32))
                S.op("pool", lambda e: e.memset(
                    vstat[:].rearrange("p (k c) -> p k c", c=65)[:, :, 64:65], 1.0), w=[vstat])

                for pr in range(8):
                    kt = ktile[pr % 2]
                    for r_ in range(2):
                        for hl in range(2):
                            S.dma("sp", lambda e, p=pr, r2=r_, hl=hl, kt=kt: e.dma_start(
                                out=kt[64 * hl:64 * (hl + 1), 2048 * r2:2048 * (r2 + 1)],
                                in_=kg.ap()[1024 * r2 + 64 * (2 * p + hl):1024 * r2 + 64 * (2 * p + hl) + 64, :]),
                                r=[kg], w=[kt])
                    for hl in range(2):
                        S.dma("sp", lambda e, p=pr, hl=hl: e.dma_start(
                            out=vstat[:].rearrange("p (k c) -> p k c", c=65)[:, 32 * hl:32 * (hl + 1), 0:64],
                            in_=vg.ap().rearrange("(k p) c -> p k c", p=128)[:, :, 64 * (2 * p + hl):64 * (2 * p + hl) + 64]),
                            r=[vg], w=[vstat])
                    for qi, (qoff, qw) in enumerate(NC):
                        for kb in range(32):
                            st = STp[kb % 2]
                            S.pre("pe", [kt, s2[pr]], [st])
                            mm(nc.tensor, st[:, 0:qw], kt[0:64, 128 * kb:128 * (kb + 1)],
                               s2[pr][0:64, qoff:qoff + qw], True, True)
                            i2 = nc.tensor.matmul(
                                st[:, 512:512 + qw], kt[64:128, 128 * kb:128 * (kb + 1)],
                                s2[pr][64:128, qoff:qoff + qw], start=True, stop=True,
                                tile_position=(64, 0), skip_group_check=True)
                            S.post("pe", i2, [kt, s2[pr]], [st])
                            pt = ptile[kb % 2]
                            if qw == 512:
                                S.op("act", lambda e, st=st, pt=pt, p=pr: e.activation(
                                    pt[:, :], st[:, :], AF.Exp, bias=bbt[:, p:p + 1],
                                    scale=0.125), r=[st, bbt], w=[pt])
                            else:
                                S.op("act", lambda e, st=st, pt=pt, p=pr, qw=qw: e.activation(
                                    pt[:, 0:qw], st[:, 0:qw], AF.Exp, bias=bbt[:, p:p + 1],
                                    scale=0.125), r=[st, bbt], w=[pt])
                                S.op("act", lambda e, st=st, pt=pt, p=pr, qw=qw: e.activation(
                                    pt[:, 512:512 + qw], st[:, 512:512 + qw], AF.Exp,
                                    bias=bbt[:, p:p + 1], scale=0.125), r=[st, bbt], w=[pt])
                            S.pre("pe", [vstat, pt], [pvP[0]])
                            S.post("pe", mm(nc.tensor, pvP[0][0:65, 0:qw],
                                            vstat[:, 65 * kb:65 * (kb + 1)], pt[:, 0:qw],
                                            kb == 0, kb == 31), [vstat, pt], [pvP[0]])
                            S.pre("pe", [vstat, pt], [pvP[1]])
                            S.post("pe", mm(nc.tensor, pvP[1][0:65, 0:qw],
                                            vstat[:, 65 * (32 + kb):65 * (32 + kb) + 65],
                                            pt[:, 512:512 + qw],
                                            kb == 0, kb == 31), [vstat, pt], [pvP[1]])
                        S.op("dve", lambda e, qw=qw: e.reciprocal(
                            zrowr[0:1, 0:qw], pvP[0][64:65, 0:qw]), r=[pvP[0]], w=[zrt])
                        S.op("dve", lambda e, qw=qw: e.reciprocal(
                            zrowr[0:1, 512:512 + qw], pvP[1][64:65, 0:qw]), r=[pvP[1]], w=[zrt])
                        for hl in range(2):
                            S.pre("pe", [onesrow, zrt], [zbP])
                            S.post("pe", mm(nc.tensor, zbP[0:64, 512 * hl:512 * hl + qw],
                                            onesrow[0:1, 0:64],
                                            zrowr[0:1, 512 * hl:512 * hl + qw], True, True),
                                   [onesrow, zrt], [zbP])
                        S.op("act", lambda e, qw=qw: e.copy(
                            zbs2[0:64, 0:qw], zbP[0:64, 0:qw]), r=[zbP], w=[zbs2])
                        S.op("act", lambda e, qw=qw: e.copy(
                            zbs2[0:64, 512:512 + qw], zbP[0:64, 512:512 + qw]), r=[zbP], w=[zbs2])
                        S.op("dve", lambda e, p=pr, qo=qoff, qw=qw: e.tensor_tensor(
                            s2[p][0:64, qo:qo + qw], pvP[0][0:64, 0:qw],
                            zbs2[0:64, 0:qw], ALU.mult), r=[pvP[0], zbs2], w=[s2[pr]])
                        S.op("dve", lambda e, p=pr, qo=qoff, qw=qw: e.tensor_tensor(
                            s2[p][64:128, qo:qo + qw], pvP[1][0:64, 0:qw],
                            zbs2[0:64, 512:512 + qw], ALU.mult), r=[pvP[1], zbs2], w=[s2[pr]])
                S.barrier()
            with contextlib.ExitStack() as ph:
                woP = [ph.enter_context(nc.psum_tensor(f"woB{i}", [128, 512], F32)) for i in range(2)]
                for oc in range(NF):
                    wb = wbuf[oc % 2]
                    S.dma("sp", lambda e, o=oc, wb=wb: e.dma_start(
                        out=wb[:, 0:1024],
                        in_=wobT.ap().rearrange("(f p) o -> p f o", p=128)[:, :, 128 * o:128 * (o + 1)]),
                        r=[wobT], w=[wb])
                    for ni, (off, w) in enumerate(NC):
                        pb = woP[ni % 2]
                        S.pre("pe", [wb] + [s2[f] for f in range(NF)], [pb])
                        for fc in range(NF):
                            inst = mm(nc.tensor, pb[:, 0:w], wb[:, 128 * fc:128 * (fc + 1)],
                                      s2[fc][:, off:off + w], fc == 0, fc == NF - 1)
                        S.post("pe", inst, [wb] + [s2[f] for f in range(NF)], [pb])
                        S.op("dve", lambda e, o=oc, of=off, ww=w, p=pb: e.tensor_tensor(
                            x8[o][:, of:of + ww], p[:, 0:ww], x8[o][:, of:of + ww], ALU.add),
                            r=[pb, x8[oc]], w=[x8[oc]])
                S.barrier()
            rezero_halo()

        if n_phases >= 4:
            ffn_phase("B")

        if dump:
            if os.environ.get("DUMP_XQ1"):
                for fc in range(0, NF):
                    S.dma("sp", lambda e, f=fc: e.dma_start(
                        out=xq2d.ap()[128 * f:128 * (f + 1), :], in_=s1[f][:, :]),
                        r=[s1[fc]], w=[xq2d])
            dhd = nc.dram_tensor("dhd", [DFF, T], F32, kind="ExternalOutput")
            dxq = nc.dram_tensor("dxq", [DFF, T], BF16, kind="ExternalOutput")
            dwq = nc.dram_tensor("dwq", [D, DFF], BF16, kind="ExternalOutput")
            with nc.sbuf_tensor("dmpw", [128, DFF], BF16) as dmpw:
                for f in range(NF):
                    S.dma("sp", lambda e, f=f: e.dma_start(
                        out=dmpw[:], in_=wq1T.ap()[128 * f:128 * (f + 1), :]), r=[wq1T], w=[dmpw])
                    S.dma("sp", lambda e, f=f: e.dma_start(
                        out=dwq.ap()[128 * f:128 * (f + 1), :], in_=dmpw[:]), r=[dmpw], w=[dwq])
            with nc.sbuf_tensor("dmp", [128, T], F32) as dmp, \
                 nc.sbuf_tensor("dmpb", [128, T], BF16) as dmpb:
                for h in range(NHC):
                    S.dma("sp", lambda e, h=h: e.dma_start(
                        out=dmp[:], in_=hd.ap()[128 * h:128 * (h + 1), :]), r=[hd], w=[dmp])
                    S.dma("sp", lambda e, h=h: e.dma_start(
                        out=dhd.ap()[128 * h:128 * (h + 1), :], in_=dmp[:]), r=[dmp], w=[dhd])
                    S.dma("sp", lambda e, h=h: e.dma_start(
                        out=dmpb[:], in_=xq2d.ap()[128 * h:128 * (h + 1), :]), r=[xq2d], w=[dmpb])
                    S.dma("sp", lambda e, h=h: e.dma_start(
                        out=dxq.ap()[128 * h:128 * (h + 1), :], in_=dmpb[:]), r=[dmpb], w=[dxq])
                S.barrier()
        for fc in range(NF):
            S.dma("sp", lambda e, f=fc: e.dma_start(
                out=yout.ap()[128 * f:128 * (f + 1), :], in_=x8[f][:, OWN0:OWN1]),
                r=[x8[fc]], w=[yout])
        S.barrier()

    return nc


_CACHE = {}


def _stage(inputs):
    x = np.asarray(inputs["L0_attn_a_x"], dtype=np.float32)
    base = {
        "wqkv_a": np.ascontiguousarray(inputs["L0_attn_a_w_qkv"], dtype=np.float32),
        "wout_a": np.ascontiguousarray(inputs["L0_attn_a_w_out"], dtype=np.float32),
        "bout_a": np.ascontiguousarray(inputs["L0_attn_a_b_out"], dtype=np.float32),
        "wqkv_b": np.ascontiguousarray(inputs["L0_attn_b_w_qkv"], dtype=np.float32),
        "wout_b": np.ascontiguousarray(inputs["L0_attn_b_w_out"], dtype=np.float32),
        "w1": np.ascontiguousarray(inputs["L0_ffn_w1"], dtype=np.float32),
        "convw": np.ascontiguousarray(
            np.asarray(inputs["L0_ffn_conv_w"], dtype=np.float32).reshape(DFF, 3).T),
        "convb": np.ascontiguousarray(inputs["L0_ffn_conv_b"], dtype=np.float32),
        "w2": np.ascontiguousarray(inputs["L0_ffn_w2"], dtype=np.float32),
        "ident": np.eye(128, dtype=np.float32),
        "sel2": np.concatenate([
            np.repeat(np.float32([[1], [0]]), 64, axis=0),
            np.repeat(np.float32([[0], [1]]), 64, axis=0)], axis=1).astype(np.float32),
        "onescol_in": np.ones((128, 1), dtype=np.float32),
        "onesrow_in": np.ones((128, 128), dtype=np.float32),
    }
    import os as _os
    if _os.environ.get("BYPASS_XQ"):
        import ml_dtypes as _md
        base["xqstage"] = np.load("/root/problem/_xqstage.npy").view(_md.bfloat16)
    if _os.environ.get("BYPASS_WQ"):
        import ml_dtypes as _md
        w1f = np.asarray(inputs.get("L0_ffn_w1", base["w1"]), np.float64)
        tern = np.clip(np.rint(w1f / np.abs(w1f).mean()), -1, 1)
        base["wqstage"] = np.ascontiguousarray(tern.T).astype(_md.bfloat16)
    in_maps = []
    for c in range(8):
        b, hh = c // 2, c % 2
        s, e = 2048 * hh, 2048 * hh + 2048
        pad = np.zeros((T, D), dtype=np.float32)
        lo, hi = s - 2, e + 2
        clo, chi = max(lo, 0), min(hi, 4096)
        pad[clo - lo:chi - lo, :] = x[b, clo:chi, :]
        hm = np.ones((128, 4), dtype=np.float32)
        if hh == 0:
            hm[:, 0:2] = 0.0
        else:
            hm[:, 2:4] = 0.0
        m = dict(base)
        m["xin"] = np.ascontiguousarray(pad.T)
        m["hmask"] = hm
        in_maps.append(m)
    return in_maps


def kernel(**inputs):
    n_phases = int(inputs.pop("_n_phases", 4))
    dump = bool(inputs.pop("_dump", False))
    key = (n_phases, dump)
    if key not in _CACHE:
        _CACHE[key] = build(n_phases, dump)
    nc = _CACHE[key]
    in_maps = _stage(inputs)
    res = run_bass_kernel_spmd(nc, in_maps, core_ids=list(range(8)))
    out = np.empty((4, 4096, 1024), dtype=np.float32)
    for c in range(8):
        b, hh = c // 2, c % 2
        out[b, 2048 * hh:2048 * (hh + 1), :] = res.results[c]["y"].T
    if dump:
        kernel._dbg = res.results
    return out


# version 03
# speedup vs baseline: 1.1416x; 1.1416x over previous
import contextlib
import itertools

import numpy as np
import concourse.bass as bass
import concourse.mybir as mybir
from concourse.bass_utils import run_bass_kernel_spmd

F32 = mybir.dt.float32
BF16 = mybir.dt.bfloat16
AF = mybir.ActivationFunctionType
ALU = mybir.AluOpType
AX = mybir.AxisListType

T = 2052
OWN0, OWN1 = 2, 2050
D = 1024
H = 16
DH = 64
DFF = 4096
NF = 8
NHC = 32
RND = float(1.5 * 2.0 ** 23)
TT = [(128 * i, 128) for i in range(16)] + [(2048, 4)]
NC = [(0, 512), (512, 512), (1024, 512), (1536, 512), (2048, 4)]
REPL = [[0, 1], [2, 3], [4, 5], [6, 7]]
_uid = itertools.count()


NSLOT = 12


class Sched:

    def __init__(self, nc, sems):
        self.nc = nc
        self.sems = sems
        self.cnt = {s: 0 for s in sems}
        self.hwm = {}
        self.writers = {}
        self.readers = {}
        self.eng = {"pe": nc.tensor, "act": nc.scalar, "dve": nc.vector,
                    "pool": nc.gpsimd, "sp": nc.sync}
        self.slot = {"sp": 0, "pool": 0}

    def _wait(self, eng, sem, val):
        if val <= 0 or self.hwm.get((eng, sem), 0) >= val:
            return
        self.hwm[(eng, sem)] = val
        self.eng[eng].wait_ge(self.sems[sem], val)

    def pre(self, eng, reads, writes):
        for b in reads:
            for t in self.writers.get(id(b), ()):
                self._wait(eng, t[0], t[1])
        for b in writes:
            for t in self.readers.get(id(b), ()):
                self._wait(eng, t[0], t[1])

    def post(self, sem, inst, reads, writes, amt=1):
        inst.then_inc(self.sems[sem], amt)
        self.cnt[sem] += amt
        tok = (sem, self.cnt[sem])
        for b in writes:
            self.writers.setdefault(id(b), []).append(tok)
            self.readers[id(b)] = []
        for b in reads:
            self.readers.setdefault(id(b), []).append(tok)

    def op(self, eng, f, r=(), w=()):
        self.pre(eng, r, w)
        self.post(eng, f(self.eng[eng]), r, w)

    def dma(self, q, f, r=(), w=()):
        s = self.slot[q]
        self.slot[q] = (s + 1) % NSLOT
        sem = f"{q}d{s}"
        self._wait(q, sem, self.cnt[sem])
        self.pre(q, r, w)
        self.post(sem, f(self.eng[q]), r, w, 16)

    def cc(self, f, r=(), w=()):
        self.pre("pool", r, w)
        self._wait("pool", "cc", self.cnt["cc"])
        self.post("cc", f(self.eng["pool"]), r, w)

    def barrier(self):
        for eng in self.eng:
            for sem, v in self.cnt.items():
                self._wait(eng, sem, v)
        self.writers.clear()
        self.readers.clear()


def build(n_phases=4, dump=False):
    import os
    n_pro = int(os.environ.get('N_PRO', '9'))
    nc = bass.Bass("TRN2", target_bir_lowering=False)

    xin = nc.dram_tensor("xin", [D, T], F32, kind="ExternalInput")
    wqkv_a = nc.dram_tensor("wqkv_a", [D, 3 * D], F32, kind="ExternalInput")
    wout_a = nc.dram_tensor("wout_a", [D, D], F32, kind="ExternalInput")
    bout_a = nc.dram_tensor("bout_a", [D], F32, kind="ExternalInput")
    wqkv_b = nc.dram_tensor("wqkv_b", [3 * D, D], F32, kind="ExternalInput")
    wout_b = nc.dram_tensor("wout_b", [D, D], F32, kind="ExternalInput")
    w1 = nc.dram_tensor("w1", [DFF, D], F32, kind="ExternalInput")
    convw = nc.dram_tensor("convw", [3, DFF], F32, kind="ExternalInput")
    convb = nc.dram_tensor("convb", [DFF], F32, kind="ExternalInput")
    w2 = nc.dram_tensor("w2", [D, DFF], F32, kind="ExternalInput")
    ident_in = nc.dram_tensor("ident", [128, 128], F32, kind="ExternalInput")
    hmask_in = nc.dram_tensor("hmask", [128, 4], F32, kind="ExternalInput")
    sel2_in = nc.dram_tensor("sel2", [128, 2], F32, kind="ExternalInput")
    onescol_in = nc.dram_tensor("onescol_in", [128, 1], F32, kind="ExternalInput")
    onesrow_in = nc.dram_tensor("onesrow_in", [128, 128], F32, kind="ExternalInput")
    xqstage = (nc.dram_tensor("xqstage", [D, T], BF16, kind="ExternalInput")
               if os.environ.get("BYPASS_XQ") else None)
    wqstage = (nc.dram_tensor("wqstage", [D, DFF], BF16, kind="ExternalInput")
               if os.environ.get("BYPASS_WQ") else None)
    yout = nc.dram_tensor("y", [D, 2048], F32, kind="ExternalOutput")

    wa_bf = nc.dram_tensor("wa_bf", [D, 3 * D], BF16)
    woa_bf = nc.dram_tensor("woa_bf", [D, D], BF16)
    wb_nat = nc.dram_tensor("wb_nat", [3 * D, D], BF16)
    wbT = nc.dram_tensor("wbT", [D, 3 * D], BF16)
    wob_nat = nc.dram_tensor("wob_nat", [D, D], BF16)
    wobT = nc.dram_tensor("wobT", [D, D], BF16)
    w1q_nat = nc.dram_tensor("w1q_nat", [DFF, D], BF16)
    wq1T = nc.dram_tensor("wq1T", [D, DFF], BF16)
    w2q_nat = nc.dram_tensor("w2q_nat", [D, DFF], BF16)
    wq2T = nc.dram_tensor("wq2T", [DFF, D], BF16)
    kx = nc.dram_tensor("kx", [D, 2048], BF16)
    vx = nc.dram_tensor("vx", [2048, D], BF16)
    kg = nc.dram_tensor("kg", [2 * D, 2048], BF16)
    vg = nc.dram_tensor("vg", [4096, D], BF16)
    cxl = nc.dram_tensor("cxl", [65, D], F32)
    cxg = nc.dram_tensor("cxg", [65, D], F32)
    hd = nc.dram_tensor("hd", [DFF, T], F32)
    xq2d = nc.dram_tensor("xq2d", [DFF, T], BF16)

    stk = contextlib.ExitStack()
    sb = lambda n, s, d: stk.enter_context(nc.sbuf_tensor(n, s, d))
    sems = {n: stk.enter_context(nc.semaphore(n + "_s"))
            for n in ("pe", "act", "dve", "pool", "sp", "cc")}
    for q in ("sp", "pool"):
        for s in range(NSLOT):
            sems[f"{q}d{s}"] = stk.enter_context(nc.semaphore(f"{q}d{s}_s"))
    S = Sched(nc, sems)

    x8 = [sb(f"x{i}", [128, T], F32) for i in range(NF)]
    s1 = [sb(f"s1_{i}", [128, T], BF16) for i in range(NF)]
    s2 = [sb(f"s2_{i}", [128, T], BF16) for i in range(NF)]
    dscr = sb("dscr", [128, T], F32)
    xbf = [sb(f"xbf{i}", [128, 512], BF16) for i in range(2)]
    sqbf = [sb(f"sqbf{i}", [128, 512], BF16) for i in range(2)]
    wbuf = [sb(f"wbuf{i}", [128, 2048], BF16) for i in range(2)]
    ident = sb("identt", [128, 128], F32)
    hmask = sb("hmaskt", [128, 4], F32)
    sel2 = sb("sel2t", [128, 2], BF16)
    onescol = sb("onescolt", [128, 1], BF16)
    onesrow = sb("onesrowt", [128, 128], F32)
    b_out = sb("b_outt", [128, NF], F32)
    convwt = sb("convwt", [128, 3 * NHC], F32)
    convbt = sb("convbt", [128, NHC], F32)
    mwrow = sb("mwrow", [1, 4], F32)
    scal = sb("scal", [1, 16], F32)
    redc = sb("redc", [128, 1], F32)
    bc1 = sb("bc1", [128, 512], F32)
    bc2 = sb("bc2", [128, 512], F32)
    bc3 = sb("bc3", [128, 512], F32)
    rowt = sb("rowt", [65, T], F32)
    bkr = sb("bkr", [1, 1024], F32)
    mwb = sb("mwb", [128, 4], F32)
    _ROWP = {"sum": 0, "sumsq": 32, "mx": 64}

    def R(name, c0=0, c1=T):
        if name == "bk":
            return bkr[0:1, c0:c1], bkr
        p = _ROWP[name]
        return rowt[p:p + 1, c0:c1], rowt

    def RP(name):
        return 0 if name == "bk" else _ROWP[name]

    def mm(e, out, lhsT, rhs, start, stop):
        return e.matmul(out, lhsT, rhs, start=start, stop=stop,
                        skip_group_check=True)

    with stk:
        S.dma("sp", lambda e: e.dma_start(out=ident[:], in_=ident_in.ap()), w=[ident])
        S.dma("sp", lambda e: e.dma_start(out=hmask[:], in_=hmask_in.ap()), w=[hmask])
        S.dma("pool", lambda e: e.dma_start(out=sel2[:], in_=sel2_in.ap()), w=[sel2])
        S.dma("pool", lambda e: e.dma_start(out=onescol[:], in_=onescol_in.ap()), w=[onescol])
        S.dma("sp", lambda e: e.dma_start(out=onesrow[:], in_=onesrow_in.ap()), w=[onesrow])
        for c in range(NF):
            S.dma("sp", lambda e, c=c: e.dma_start(out=x8[c][:], in_=xin.ap()[128 * c:128 * (c + 1), :]), w=[x8[c]])
        S.dma("pool", lambda e: e.dma_start(out=wa_bf.ap(), in_=wqkv_a.ap()), w=[wa_bf])
        S.dma("pool", lambda e: e.dma_start(out=woa_bf.ap(), in_=wout_a.ap()), w=[woa_bf])
        S.dma("pool", lambda e: e.dma_start(out=wb_nat.ap(), in_=wqkv_b.ap()), w=[wb_nat])
        S.dma("pool", lambda e: e.dma_start(out=wob_nat.ap(), in_=wout_b.ap()), w=[wob_nat])

        with nc.psum_tensor("prT", [128, 512], F32) as prT, \
             nc.sbuf_tensor("rload", [1, 512], F32) as rload:
            def load_cols(dram_ap_1d, n, dst, colmap):
                for j in range(n):
                    S.dma("sp", lambda e, j=j: e.dma_start(
                        out=rload[0:1, 0:128],
                        in_=dram_ap_1d[128 * j:128 * (j + 1)].rearrange("(o p) -> o p", o=1)),
                        r=[], w=[rload])
                    S.pre("pe", [rload, ident], [prT])
                    S.post("pe", mm(nc.tensor, prT[0:128, 0:1], rload[0:1, 0:128],
                                    ident[0:1, 0:1], True, True), [rload, ident], [prT])
                    S.op("dve", lambda e, j=j: e.tensor_copy(
                        dst[:, colmap(j):colmap(j) + 1], prT[0:128, 0:1]),
                        r=[prT], w=[dst])
            if n_pro >= 2:
                load_cols(bout_a.ap(), NF, b_out, lambda j: j)
                load_cols(convb.ap(), NHC, convbt, lambda j: j)
                for k in range(3):
                    cwk = convw.ap()[k:k + 1, :].rearrange("one n -> (one n)")
                    load_cols(cwk, NHC, convwt, lambda j, k=k: 3 * j + k)

        slabs = [(wbuf[i // 4], 512 * (i % 4)) for i in range(8)]

        def transpose_copy(jobs):
            for b_ in range(0, len(jobs), 8):
                grp = jobs[b_:b_ + 8]
                for i, (sa, da, dt_) in enumerate(grp):
                    t_, o_ = slabs[i]
                    S.dma("sp", lambda e, sa=sa, t_=t_, o_=o_: e.dma_start_transpose(
                        t_[:, o_:o_ + 512], sa), r=[], w=[t_])
                for i, (sa, da, dt_) in enumerate(grp):
                    t_, o_ = slabs[i]
                    S.dma("pool", lambda e, da=da, t_=t_, o_=o_: e.dma_start(
                        out=da, in_=t_[:, o_:o_ + 512]), r=[t_], w=[dt_])
        if n_pro >= 3:
            jobs = []
            for fc in range(NF):
                for oc in range(6):
                    jobs.append((wb_nat.ap()[512 * oc:512 * (oc + 1), 128 * fc:128 * (fc + 1)],
                                 wbT.ap()[128 * fc:128 * (fc + 1), 512 * oc:512 * (oc + 1)], wbT))
                for oc in range(2):
                    jobs.append((wob_nat.ap()[512 * oc:512 * (oc + 1), 128 * fc:128 * (fc + 1)],
                                 wobT.ap()[128 * fc:128 * (fc + 1), 512 * oc:512 * (oc + 1)], wobT))
            S.pre("sp", [wb_nat, wob_nat], [])
            transpose_copy(jobs)

        with nc.sbuf_tensor("absacc", [128, 48], F32) as absacc, \
             nc.sbuf_tensor("qbf", [128, DFF], BF16) as qbf, \
             nc.psum_tensor("pT0", [128, 512], F32) as pT0:
            w2v = w2.ap().rearrange("a (b c) -> (a b) c", b=2)
            w2qv = w2q_nat.ap().rearrange("a (b c) -> (a b) c", b=2)
            for wi, (wap, nch, wid) in (enumerate(((w1.ap(), NHC, D), (w2v, 16, 2048))) if n_pro >= 4 else []):
                for c in range(nch):
                    S.dma("sp", lambda e, w_=wap, c=c, ww=wid: e.dma_start(
                        out=dscr[:, 0:ww], in_=w_[128 * c:128 * (c + 1), :]), w=[dscr])
                    col = c if wi == 0 else NHC + c
                    S.op("dve", lambda e, col=col, ww=wid: e.reduce_sum(
                        absacc[:, col:col + 1], dscr[:, 0:ww], axis=AX.X,
                        apply_absolute_value=True), r=[dscr], w=[absacc])
            for wi, (lo, n, total) in (enumerate(((0, NHC, DFF * D), (NHC, 16, D * DFF))) if n_pro >= 4 else []):
                S.pre("pe", [absacc, ident], [pT0])
                S.post("pe", mm(nc.tensor, pT0[0:n, 0:128], absacc[:, lo:lo + n],
                                ident[:], True, True), [absacc, ident], [pT0])
                S.op("dve", lambda e, n=n: e.reduce_sum(
                    redc[0:n, 0:1], pT0[0:n, 0:128], axis=AX.X), r=[pT0], w=[redc])
                S.pre("pe", [redc, ident], [pT0])
                S.post("pe", mm(nc.tensor, pT0[0:1, 128:128 + n], redc[0:n, 0:1],
                                ident[0:n, 0:n], True, True), [redc, ident], [pT0])
                S.op("dve", lambda e, wi=wi, n=n: e.reduce_sum(
                    scal[0:1, wi:wi + 1], pT0[0:1, 128:128 + n], axis=AX.X), r=[pT0], w=[scal])
                S.op("dve", lambda e, wi=wi, tot=total: e.tensor_scalar(
                    mwrow[0:1, wi:wi + 1], scal[0:1, wi:wi + 1], 1.0 / tot, 1e-5,
                    ALU.mult, ALU.max), r=[scal], w=[mwrow])
            if n_pro < 4:
                S.op("pool", lambda e: e.memset(mwrow[0:1, :], 1.0), w=[mwrow])
            S.op("dve", lambda e: e.reciprocal(mwrow[0:1, 2:4], mwrow[0:1, 0:2]),
                 r=[mwrow], w=[mwrow])
            S.pre("pe", [onesrow, mwrow], [pT0])
            S.post("pe", mm(nc.tensor, pT0[0:128, 256:260], onesrow[0:1, :],
                            mwrow[0:1, 0:4], True, True), [onesrow, mwrow], [pT0])
            S.op("dve", lambda e: e.tensor_copy(mwb[:, 0:4], pT0[0:128, 256:260]),
                 r=[pT0], w=[mwb])
            for wi, (wap, wq_ap, nch, wid) in (enumerate(
                    ((w1.ap(), w1q_nat.ap(), NHC, D), (w2v, w2qv, 16, 2048))) if n_pro >= 4 else []):
                for c in range(nch):
                    S.dma("sp", lambda e, w_=wap, c=c, ww=wid: e.dma_start(
                        out=dscr[:, 0:ww], in_=w_[128 * c:128 * (c + 1), :]), w=[dscr])
                    S.op("dve", lambda e, ww=wid, wi=wi: e.tensor_scalar(
                        dscr[:, 0:ww], dscr[:, 0:ww], mwb[:, 2 + wi:3 + wi], RND,
                        ALU.mult, ALU.add), r=[dscr, mwb], w=[dscr])
                    S.op("dve", lambda e, ww=wid: e.tensor_scalar(
                        dscr[:, 0:ww], dscr[:, 0:ww], -RND, -1.0, ALU.add, ALU.max),
                        r=[dscr], w=[dscr])
                    S.op("dve", lambda e, ww=wid: e.tensor_scalar(
                        qbf[:, 0:ww], dscr[:, 0:ww], 1.0, 0.0, ALU.min, ALU.add),
                        r=[dscr], w=[qbf])
                    wqt = w1q_nat if wi == 0 else w2q_nat
                    S.dma("sp", lambda e, q_=wq_ap, c=c, ww=wid: e.dma_start(
                        out=q_[128 * c:128 * (c + 1), :], in_=qbf[:, 0:ww]),
                        r=[qbf], w=[wqt])
        if n_pro >= 5:
            jobs = []
            for fc in range(NF):
                for oc in range(8):
                    jobs.append((w1q_nat.ap()[512 * oc:512 * (oc + 1), 128 * fc:128 * (fc + 1)],
                                 wq1T.ap()[128 * fc:128 * (fc + 1), 512 * oc:512 * (oc + 1)], wq1T))
            for hc in range(NHC):
                for oc in range(2):
                    jobs.append((w2q_nat.ap()[512 * oc:512 * (oc + 1), 128 * hc:128 * (hc + 1)],
                                 wq2T.ap()[128 * hc:128 * (hc + 1), 512 * oc:512 * (oc + 1)], wq2T))
            S.pre("sp", [w1q_nat, w2q_nat], [])
            transpose_copy(jobs)

        with nc.psum_tensor("bkP", [1, 512], F32) as bkP:
            for half in (range(2) if n_pro >= 6 else []):
                for fc in range(NF):
                    S.dma("sp", lambda e, f=fc, h=half: e.dma_start(
                        out=wbuf[1][:, 0:512],
                        in_=wa_bf.ap()[128 * f:128 * (f + 1), D + 512 * h:D + 512 * (h + 1)]),
                        r=[wa_bf], w=[wbuf[1]])
                    S.op("act", lambda e: e.activation(sqbf[0][:, 0:512], wbuf[1][:, 0:512], AF.Square),
                         r=[wbuf[1]], w=[sqbf[0]])
                    S.pre("pe", [onescol, sqbf[0]], [bkP])
                    S.post("pe", mm(nc.tensor, bkP[0:1, :], onescol[:], sqbf[0][:, 0:512],
                                    fc == 0, fc == NF - 1), [onescol, sqbf[0]], [bkP])
                ap, tt = R("bk", 512 * half, 512 * (half + 1))
                S.op("act", lambda e, ap=ap: e.activation(ap, bkP[0:1, :], AF.Sqrt),
                     r=[bkP], w=[tt])
        ap, tt = R("bk", 0, 1024)
        S.op("dve", lambda e, ap=ap: e.tensor_scalar(
            ap, ap, 32.5, 0.0, ALU.mult, ALU.add), r=[tt], w=[tt])

        def ln_stats(ph):
            statP = [ph.enter_context(nc.psum_tensor(f"stP{next(_uid)}", [1, 512], F32)) for _ in range(2)]
            for off, w in NC:
                for fc in range(NF):
                    xb, sq = xbf[fc % 2], sqbf[fc % 2]
                    S.op("dve", lambda e, f=fc, o=off, ww=w, xb=xb: e.tensor_copy(
                        xb[:, 0:ww], x8[f][:, o:o + ww]), r=[x8[fc]], w=[xb])
                    S.op("act", lambda e, ww=w, xb=xb, sq=sq: e.activation(
                        sq[:, 0:ww], xb[:, 0:ww], AF.Square), r=[xb], w=[sq])
                    S.pre("pe", [onescol, xb], [statP[0]])
                    S.post("pe", mm(nc.tensor, statP[0][0:1, 0:w], onescol[:], xb[:, 0:w],
                                    fc == 0, fc == NF - 1), [onescol, xb], [statP[0]])
                    S.pre("pe", [onescol, sq], [statP[1]])
                    S.post("pe", mm(nc.tensor, statP[1][0:1, 0:w], onescol[:], sq[:, 0:w],
                                    fc == 0, fc == NF - 1), [onescol, sq], [statP[1]])
                ap, tt = R("sum", off, off + w)
                S.op("dve", lambda e, ap=ap, ww=w: e.tensor_copy(ap, statP[0][0:1, 0:ww]),
                     r=[statP[0]], w=[tt])
                ap, tt = R("sumsq", off, off + w)
                S.op("dve", lambda e, ap=ap, ww=w: e.tensor_copy(ap, statP[1][0:1, 0:ww]),
                     r=[statP[1]], w=[tt])


        def mean_rstd_bc(bcP, off, w, scratch):
            build_bcast(bcP, bc1, "sum", off, w)
            build_bcast(bcP, bc2, "sumsq", off, w)
            S.op("dve", lambda e, ww=w: e.tensor_scalar(
                bc1[:, 0:ww], bc1[:, 0:ww], 1.0 / D, 0.0, ALU.mult, ALU.add),
                r=[bc1], w=[bc1])
            S.op("dve", lambda e, ww=w: e.tensor_scalar(
                bc2[:, 0:ww], bc2[:, 0:ww], 1.0 / D, 1e-5, ALU.mult, ALU.add),
                r=[bc2], w=[bc2])
            S.op("act", lambda e, ww=w: e.activation(
                scratch[:, 0:ww], bc1[:, 0:ww], AF.Square), r=[bc1], w=[scratch])
            S.op("dve", lambda e, ww=w: e.tensor_tensor(
                bc2[:, 0:ww], bc2[:, 0:ww], scratch[:, 0:ww], ALU.subtract),
                r=[bc2, scratch], w=[bc2])
            S.op("act", lambda e, ww=w: e.activation(
                bc2[:, 0:ww], bc2[:, 0:ww], AF.Sqrt), r=[bc2], w=[bc2])
            S.op("dve", lambda e, ww=w: e.reciprocal(bc2[:, 0:ww], bc2[:, 0:ww]),
                 r=[bc2], w=[bc2])

        def build_bcast(bcP, bc, rowname, off, w):
            ap, tt = R(rowname, off, off + w)
            p = RP(rowname)
            S.pre("pe", [onesrow, tt], [bcP])
            S.post("pe", mm(nc.tensor, bcP[0:128, 0:w], onesrow[p:p + 1, :], ap, True, True),
                   [onesrow, tt], [bcP])
            S.op("dve", lambda e, ww=w: e.tensor_copy(bc[:, 0:ww], bcP[0:128, 0:ww]),
                 r=[bcP], w=[bc])

        def ln_to(ph, dest):
            bcP = ph.enter_context(nc.psum_tensor(f"lnb{next(_uid)}", [128, 512], F32))
            for off, w in NC:
                mean_rstd_bc(bcP, off, w, bc3)
                for fc in range(NF):
                    S.op("dve", lambda e, f=fc, o=off, ww=w: e.tensor_tensor(
                        dscr[:, o:o + ww], x8[f][:, o:o + ww], bc1[:, 0:ww], ALU.subtract),
                        r=[x8[fc], bc1], w=[dscr])
                    S.op("dve", lambda e, f=fc, o=off, ww=w: e.tensor_tensor(
                        dest[f][:, o:o + ww], dscr[:, o:o + ww], bc2[:, 0:ww], ALU.mult),
                        r=[dscr, bc2], w=[dest[fc]])

        def rezero_halo():
            for fc in range(NF):
                S.op("dve", lambda e, f=fc: e.tensor_tensor(
                    x8[f][:, 0:2], x8[f][:, 0:2], hmask[:, 0:2], ALU.mult),
                    r=[x8[fc], hmask], w=[x8[fc]])
                S.op("dve", lambda e, f=fc: e.tensor_tensor(
                    x8[f][:, 2050:2052], x8[f][:, 2050:2052], hmask[:, 2:4], ALU.mult),
                    r=[x8[fc], hmask], w=[x8[fc]])

        def tokmax_to_row(ph, src, dstrow, sqrt=False):
            trP = ph.enter_context(nc.psum_tensor(f"tk{next(_uid)}", [128, 512], F32))
            for toff, tw in TT:
                S.pre("pe", [src, ident], [trP])
                S.post("pe", mm(nc.tensor, trP[0:tw, 0:128], src[:, toff:toff + tw],
                                ident[:], True, True), [src, ident], [trP])
                S.op("dve", lambda e, ww=tw: e.reduce_max(
                    redc[0:ww, 0:1], trP[0:ww, 0:128], axis=AX.X), r=[trP], w=[redc])
                S.pre("pe", [redc, ident], [trP])
                S.post("pe", mm(nc.tensor, trP[0:1, 128:128 + tw], redc[0:tw, 0:1],
                                ident[0:tw, 0:tw], True, True), [redc, ident], [trP])
                ap, tt = R(dstrow, toff, toff + tw)
                if sqrt:
                    S.op("act", lambda e, ap=ap, ww=tw: e.activation(
                        ap, trP[0:1, 128:128 + ww], AF.Sqrt), r=[trP], w=[tt])
                else:
                    S.op("dve", lambda e, ap=ap, ww=tw: e.tensor_copy(
                        ap, trP[0:1, 128:128 + ww]), r=[trP], w=[tt])

        if n_phases >= 1 and not os.environ.get("SKIP_P1"):
            with contextlib.ExitStack() as ph:
                ln_stats(ph)
                ln_to(ph, s1)
                S.barrier()
            with contextlib.ExitStack() as ph:
                qsm = ph.enter_context(nc.sbuf_tensor("qsm", [128, H * 65], BF16))
                expk = ph.enter_context(nc.sbuf_tensor("expk", [128, D], BF16))
                qmx = ph.enter_context(nc.sbuf_tensor("qmx", [128, H], F32))
                qsum = ph.enter_context(nc.sbuf_tensor("qsum", [128, H], F32))
                qrec = ph.enter_context(nc.sbuf_tensor("qrec", [128, H], F32))
                dq = ph.enter_context(nc.sbuf_tensor("dq", [128, 1024], F32))
                bkb = ph.enter_context(nc.sbuf_tensor("bkb", [128, 1024], F32))
                ctxs = ph.enter_context(nc.sbuf_tensor("ctxs", [65, 1024], F32))
                vP = [ph.enter_context(nc.psum_tensor(f"vP{i}", [128, 512], F32)) for i in range(2)]
                psA = ph.enter_context(nc.psum_tensor("psA", [128, 2048], F32))
                ctxP = ph.enter_context(nc.psum_tensor("ctxP", [65, 1024], F32))

                wres = [s2[f][:, 0:2048] for f in range(NF)]
                for fc in range(NF):
                    S.dma("sp", lambda e, f=fc: e.dma_start(
                        out=wres[f], in_=wa_bf.ap()[128 * f:128 * (f + 1), 0:2048]),
                        r=[wa_bf], w=[s2[fc]])
                S.op("pool", lambda e: e.memset(
                    qsm[:].rearrange("p (h c) -> p h c", c=65)[:, :, 64:65], 1.0), w=[qsm])
                for half in range(2):
                    ap, tt = R("bk", 512 * half, 512 * (half + 1))
                    S.pre("pe", [onesrow, tt], [vP[0]])
                    S.post("pe", mm(nc.tensor, vP[0][0:128, 0:512], onesrow[0:1, :], ap,
                                    True, True), [onesrow, tt], [vP[0]])
                    S.op("dve", lambda e, h=half: e.tensor_copy(
                        bkb[:, 512 * h:512 * (h + 1)], vP[0][0:128, 0:512]), r=[vP[0]], w=[bkb])

                for ti, (toff, tw) in enumerate(TT):
                    rd = [s2[f] for f in range(NF)] + [s1[f] for f in range(NF)]
                    S.pre("pe", rd, [psA])
                    for fc in range(NF):
                        for oc in range(4):
                            inst = mm(nc.tensor, psA[0:tw, 512 * oc:512 * (oc + 1)],
                                      s1[fc][:, toff:toff + tw],
                                      wres[fc][:, 512 * oc:512 * (oc + 1)],
                                      fc == 0, fc == NF - 1)
                    S.post("pe", inst, rd, [psA])
                    S.op("dve", lambda e, ww=tw: e.reduce_max(
                        qmx[0:ww, :], psA[0:ww, 0:1024].rearrange("p (h c) -> p h c", c=DH),
                        axis=AX.X), r=[psA], w=[qmx])
                    S.op("dve", lambda e, ww=tw: e.tensor_tensor(
                        dq[0:ww, :].rearrange("p (h c) -> p h c", c=DH),
                        psA[0:ww, 0:1024].rearrange("p (h c) -> p h c", c=DH),
                        qmx[0:ww, :].unsqueeze(2).broadcast_to((ww, H, DH)),
                        ALU.subtract), r=[psA, qmx], w=[dq])
                    S.op("act", lambda e, ww=tw: e.activation(
                        dq[0:ww, :], dq[0:ww, :], AF.Exp, scale=0.125), r=[dq], w=[dq])
                    S.op("dve", lambda e, ww=tw: e.reduce_sum(
                        qsum[0:ww, :], dq[0:ww, :].rearrange("p (h c) -> p h c", c=DH),
                        axis=AX.X), r=[dq], w=[qsum])
                    S.op("dve", lambda e, ww=tw: e.reciprocal(
                        qrec[0:ww, :], qsum[0:ww, :]), r=[qsum], w=[qrec])
                    S.op("dve", lambda e, ww=tw: e.tensor_tensor(
                        qsm[0:ww, :].rearrange("p (h c) -> p h c", c=65)[:, :, 0:64],
                        dq[0:ww, :].rearrange("p (h c) -> p h c", c=DH),
                        qrec[0:ww, :].unsqueeze(2).broadcast_to((ww, H, DH)),
                        ALU.mult), r=[dq, qrec], w=[qsm])
                    S.op("dve", lambda e, ww=tw: e.tensor_tensor(
                        dq[0:ww, :], psA[0:ww, 1024:2048], bkb[0:ww, :], ALU.subtract),
                        r=[psA, bkb], w=[dq])
                    S.op("act", lambda e, ww=tw: e.activation(
                        expk[0:ww, :], dq[0:ww, :], AF.Exp), r=[dq], w=[expk])
                    if ti == 0:
                        S.op("pool", lambda e: e.memset(expk[0:2, :], 0.0), w=[expk])
                    kw = tw if ti < 16 else 2
                    S.pre("pe", [qsm, expk], [ctxP])
                    for hh in range(H):
                        inst = mm(nc.tensor, ctxP[0:65, 64 * hh:64 * (hh + 1)],
                                  qsm[0:kw, 65 * hh:65 * (hh + 1)],
                                  expk[0:kw, 64 * hh:64 * (hh + 1)],
                                  ti == 0 and hh in (0, 8), ti == 16)
                    S.post("pe", inst, [qsm, expk], [ctxP])

                S.op("dve", lambda e: e.tensor_copy(ctxs[:], ctxP[0:65, :]), r=[ctxP], w=[ctxs])
                S.dma("sp", lambda e: e.dma_start(out=cxl.ap(), in_=ctxs[:]), r=[ctxs], w=[cxl])
                S.cc(lambda e: e.collective_compute(
                    "AllReduce", ALU.add, replica_groups=REPL,
                    ins=[cxl.ap().opt()], outs=[cxg.ap().opt()]), r=[cxl], w=[cxg])

                for oc in range(NF):
                    wb = wbuf[oc % 2]
                    S.dma("sp", lambda e, o=oc, wb=wb: e.dma_start(
                        out=wb[:, 0:1024],
                        in_=wa_bf.ap().rearrange("(f p) o -> p f o", p=128)[:, :, 2048 + 128 * o:2048 + 128 * (o + 1)]),
                        r=[wa_bf], w=[wb])
                    for ni, (off, w) in enumerate(NC):
                        pb = vP[ni % 2]
                        S.pre("pe", [wb] + [s1[f] for f in range(NF)], [pb])
                        for fc in range(NF):
                            inst = mm(nc.tensor, pb[:, 0:w], wb[:, 128 * fc:128 * (fc + 1)],
                                      s1[fc][:, off:off + w], fc == 0, fc == NF - 1)
                        S.post("pe", inst, [wb] + [s1[f] for f in range(NF)], [pb])
                        S.op("dve", lambda e, o=oc, of=off, ww=w, p=pb: e.tensor_copy(
                            s2[o][:, of:of + ww], p[:, 0:ww]), r=[pb], w=[s2[oc]])
                S.barrier()
            with contextlib.ExitStack() as ph:
                zbs = ph.enter_context(nc.sbuf_tensor("zbs", [64, 1024], F32))
                ctxn = ph.enter_context(nc.sbuf_tensor("ctxn", [128, 512], BF16))
                ctxs2 = ph.enter_context(nc.sbuf_tensor("ctxs2", [65, 1024], F32))
                bcP2 = ph.enter_context(nc.psum_tensor("bcP2", [128, 512], F32))
                oPa = ph.enter_context(nc.psum_tensor("oPa", [64, 512], F32))
                oPb = ph.enter_context(nc.psum_tensor("oPb", [64, 512], F32))
                woP = [ph.enter_context(nc.psum_tensor(f"woP{i}", [128, 512], F32)) for i in range(2)]
                zrowr = ph.enter_context(nc.sbuf_tensor(f"zrw{next(_uid)}", [1, 1024], F32))
                zr, zrt = zrowr[0:1, 0:1024], zrowr
                S.dma("sp", lambda e: e.dma_start(out=ctxs2[:], in_=cxg.ap()), r=[cxg], w=[ctxs2])
                S.op("dve", lambda e: e.reciprocal(zr, ctxs2[64:65, :]), r=[ctxs2], w=[zrt])
                for half in range(2):
                    ap = zrowr[0:1, 512 * half:512 * (half + 1)]
                    S.pre("pe", [onesrow, zrt], [bcP2])
                    S.post("pe", mm(nc.tensor, bcP2[0:64, 0:512], onesrow[0:1, 0:64],
                                    ap, True, True), [onesrow, zrt], [bcP2])
                    S.op("dve", lambda e, h=half: e.tensor_copy(
                        zbs[0:64, 512 * h:512 * (h + 1)], bcP2[0:64, 0:512]), r=[bcP2], w=[zbs])
                for hh in range(H):
                    ro, co = 64 * (hh % 2), 64 * (hh // 2)
                    S.op("dve", lambda e, h=hh, ro=ro, co=co: e.tensor_tensor(
                        ctxn[ro:ro + 64, co:co + 64], ctxs2[0:64, 64 * h:64 * (h + 1)],
                        zbs[0:64, 64 * h:64 * (h + 1)], ALU.mult),
                        r=[ctxs2, zbs], w=[ctxn])
                for ch in range(NF):
                    for ni, (off, w) in enumerate(NC):
                        S.pre("pe", [ctxn, s2[ch]], [oPa])
                        S.post("pe", mm(nc.tensor, oPa[0:64, 0:w],
                                        ctxn[0:64, 64 * ch:64 * (ch + 1)],
                                        s2[ch][0:64, off:off + w], True, True),
                               [ctxn, s2[ch]], [oPa])
                        S.pre("pe", [ctxn, s2[ch]], [oPb])
                        S.post("pe", mm(nc.tensor, oPb[0:64, 0:w],
                                        ctxn[64:128, 64 * ch:64 * (ch + 1)],
                                        s2[ch][64:128, off:off + w], True, True),
                               [ctxn, s2[ch]], [oPb])
                        S.op("dve", lambda e, c=ch, of=off, ww=w: e.tensor_copy(
                            s2[c][0:64, of:of + ww], oPa[0:64, 0:ww]), r=[oPa], w=[s2[ch]])
                        S.op("dve", lambda e, c=ch, of=off, ww=w: e.tensor_copy(
                            s2[c][64:128, of:of + ww], oPb[0:64, 0:ww]), r=[oPb], w=[s2[ch]])
                for oc in range(NF):
                    wb = wbuf[oc % 2]
                    S.dma("sp", lambda e, o=oc, wb=wb: e.dma_start(
                        out=wb[:, 0:1024],
                        in_=woa_bf.ap().rearrange("(f p) o -> p f o", p=128)[:, :, 128 * o:128 * (o + 1)]),
                        r=[woa_bf], w=[wb])
                    for ni, (off, w) in enumerate(NC):
                        pb = woP[ni % 2]
                        S.pre("pe", [wb] + [s2[f] for f in range(NF)], [pb])
                        for fc in range(NF):
                            inst = mm(nc.tensor, pb[:, 0:w], wb[:, 128 * fc:128 * (fc + 1)],
                                      s2[fc][:, off:off + w], fc == 0, fc == NF - 1)
                        S.post("pe", inst, [wb] + [s2[f] for f in range(NF)], [pb])
                        S.op("dve", lambda e, o=oc, of=off, ww=w, p=pb: e.scalar_tensor_tensor(
                            x8[o][:, of:of + ww], p[:, 0:ww], b_out[:, o:o + 1],
                            x8[o][:, of:of + ww], ALU.add, ALU.add),
                            r=[pb, b_out, x8[oc]], w=[x8[oc]])
                S.barrier()
            rezero_halo()

        def ffn_phase(tag):
          def _qi_scope():
            with contextlib.ExitStack() as ph:
                ln_stats(ph)
                bcPf = ph.enter_context(nc.psum_tensor(f"bf{next(_uid)}", [128, 512], F32))
                mx = dscr
                for off, w in NC:
                    build_bcast(bcPf, bc1, "sum", off, w)
                    S.op("dve", lambda e, ww=w: e.tensor_scalar(
                        bc1[:, 0:ww], bc1[:, 0:ww], 1.0 / D, 0.0, ALU.mult, ALU.add),
                        r=[bc1], w=[bc1])
                    for fc in range(NF):
                        S.op("dve", lambda e, f=fc, o=off, ww=w: e.tensor_tensor(
                            bc2[:, 0:ww], x8[f][:, o:o + ww], bc1[:, 0:ww], ALU.subtract),
                            r=[x8[fc], bc1], w=[bc2])
                        S.op("pool", lambda e, ww=w: e.tensor_tensor(
                            bc2[:, 0:ww], bc2[:, 0:ww], bc2[:, 0:ww], ALU.mult),
                            r=[bc2], w=[bc2])
                        if fc == 0:
                            S.op("dve", lambda e, o=off, ww=w: e.tensor_copy(
                                mx[:, o:o + ww], bc2[:, 0:ww]), r=[bc2], w=[mx])
                        else:
                            S.op("dve", lambda e, o=off, ww=w: e.tensor_tensor(
                                mx[:, o:o + ww], bc2[:, 0:ww], mx[:, o:o + ww], ALU.max),
                                r=[bc2, mx], w=[mx])
                tokmax_to_row(ph, mx, "mx", sqrt=True)
                for off, w in NC:
                    mean_rstd_bc(bcPf, off, w, bc3)
                    build_bcast(bcPf, bc3, "mx", off, w)
                    dsl = dscr[:, off:off + w]
                    S.op("dve", lambda e, ww=w: e.tensor_tensor(
                        bc3[:, 0:ww], bc3[:, 0:ww], bc2[:, 0:ww], ALU.mult),
                        r=[bc3, bc2], w=[bc3])
                    S.op("dve", lambda e, ww=w: e.tensor_scalar(
                        bc3[:, 0:ww], bc3[:, 0:ww], 1e-5, 0.0, ALU.max, ALU.add),
                        r=[bc3], w=[bc3])
                    S.op("dve", lambda e, d=dsl: e.reciprocal(d, bc3[:, 0:w]),
                         r=[bc3], w=[dscr])
                    S.op("dve", lambda e, d=dsl, ww=w: e.tensor_tensor(
                        d, d, bc2[:, 0:ww], ALU.mult), r=[dscr, bc2], w=[dscr])
                    S.op("dve", lambda e, d=dsl: e.tensor_scalar(
                        d, d, 127.0, 0.0, ALU.mult, ALU.add), r=[dscr], w=[dscr])
                    S.op("dve", lambda e, d=dsl, ww=w: e.tensor_tensor(
                        bc2[:, 0:ww], bc1[:, 0:ww], d, ALU.mult),
                        r=[bc1, dscr], w=[bc2])
                    S.op("dve", lambda e, ww=w: e.tensor_scalar(
                        bc3[:, 0:ww], bc3[:, 0:ww], mwb[:, 0:1], 1.0 / 127.0,
                        ALU.mult, ALU.mult), r=[bc3, mwb], w=[bc3])
                    S.op("dve", lambda e, d=dsl, ww=w: e.tensor_copy(
                        bc1[:, 0:ww], d), r=[dscr], w=[bc1])
                    for fc in range(NF):
                        S.op("dve", lambda e, f=fc, o=off, ww=w: e.tensor_tensor(
                            dscr[:, o:o + ww], x8[f][:, o:o + ww], bc1[:, 0:ww], ALU.mult),
                            r=[x8[fc], bc1], w=[dscr])
                        S.op("dve", lambda e, o=off, ww=w: e.tensor_tensor(
                            dscr[:, o:o + ww], dscr[:, o:o + ww], bc2[:, 0:ww], ALU.subtract),
                            r=[dscr, bc2], w=[dscr])
                        S.op("dve", lambda e, o=off, ww=w: e.tensor_scalar(
                            dscr[:, o:o + ww], dscr[:, o:o + ww], RND, -RND, ALU.add, ALU.add),
                            r=[dscr], w=[dscr])
                        S.op("dve", lambda e, f=fc, o=off, ww=w: e.tensor_tensor(
                            s1[f][:, o:o + ww], dscr[:, o:o + ww], bc3[:, 0:ww], ALU.mult),
                            r=[dscr, bc3], w=[s1[fc]])
                S.barrier()
          if not os.environ.get("SKIP_QI"):
            _qi_scope()
          if True:
            if xqstage is not None:
                for fc in range(NF):
                    S.dma("sp", lambda e, f=fc: e.dma_start(
                        out=s1[f][:], in_=xqstage.ap()[128 * f:128 * (f + 1), :]),
                        r=[xqstage], w=[s1[fc]])
                S.barrier()
            with contextlib.ExitStack() as ph:
                hs = [ph.enter_context(nc.sbuf_tensor(f"hs{tag}{i}", [128, T], F32)) for i in range(2)]
                ut = ph.enter_context(nc.sbuf_tensor(f"ut{tag}", [128, T], F32))
                s2invb = ph.enter_context(nc.sbuf_tensor(f"s2i{tag}", [128, T], F32))
                hP = [ph.enter_context(nc.psum_tensor(f"hP{tag}{i}", [128, 512], F32)) for i in range(4)]
                hmx = dscr
                wsrc = wqstage if wqstage is not None else wq1T
                for hc in range(NHC):
                    wb = wbuf[hc % 2]
                    hcur = hs[hc % 2]
                    S.dma("sp", lambda e, h=hc, wb=wb, ws=wsrc: e.dma_start(
                        out=wb[:, 0:1024],
                        in_=ws.ap().rearrange("(f p) o -> p f o", p=128)[:, :, 128 * h:128 * (h + 1)]),
                        r=[wsrc], w=[wb])
                    for ni, (off, w) in enumerate(NC):
                        pb = hP[ni % 4]
                        S.pre("pe", [wb] + [s1[f] for f in range(NF)], [pb])
                        for fc in range(NF):
                            inst = mm(nc.tensor, pb[:, 0:w], wb[:, 128 * fc:128 * (fc + 1)],
                                      s1[fc][:, off:off + w], fc == 0, fc == NF - 1)
                        S.post("pe", inst, [wb] + [s1[f] for f in range(NF)], [pb])
                        S.op("act", lambda e, of=off, ww=w, p=pb, hc_=hcur: e.copy(
                            hc_[:, of:of + ww], p[:, 0:ww]), r=[pb], w=[hcur])
                    S.op("dve", lambda e, h=hc, hc_=hcur: e.tensor_scalar(
                        ut[:, :], hc_[:, :], convwt[:, 3 * h + 1:3 * h + 2],
                        convbt[:, h:h + 1], ALU.mult, ALU.add),
                        r=[hcur, convwt, convbt], w=[ut])
                    S.op("dve", lambda e, h=hc, hc_=hcur: e.scalar_tensor_tensor(
                        ut[:, 1:T], hc_[:, 0:T - 1], convwt[:, 3 * h:3 * h + 1],
                        ut[:, 1:T], ALU.mult, ALU.add),
                        r=[hcur, convwt, ut], w=[ut])
                    S.op("dve", lambda e, h=hc, hc_=hcur: e.scalar_tensor_tensor(
                        ut[:, 0:T - 1], hc_[:, 1:T], convwt[:, 3 * h + 2:3 * h + 3],
                        ut[:, 0:T - 1], ALU.mult, ALU.add),
                        r=[hcur, convwt, ut], w=[ut])
                    S.op("act", lambda e: e.activation(ut[:, :], ut[:, :], AF.Silu),
                         r=[ut], w=[ut])
                    if os.environ.get("DUMP_PRECONV"):
                        S.dma("sp", lambda e, h=hc, hc_=hcur: e.dma_start(
                            out=hd.ap()[128 * h:128 * (h + 1), :], in_=hc_[:, :]), r=[hcur], w=[hd])
                    else:
                        S.dma("sp", lambda e, h=hc: e.dma_start(
                            out=hd.ap()[128 * h:128 * (h + 1), :], in_=ut[:, :]), r=[ut], w=[hd])
                    S.op("pool", lambda e, hc_=hcur: e.tensor_tensor(
                        hc_[:, :], ut[:, :], ut[:, :], ALU.mult), r=[ut], w=[hcur])
                    if hc == 0:
                        S.op("dve", lambda e, hc_=hcur: e.tensor_copy(hmx[:, :], hc_[:, :]),
                             r=[hcur], w=[hmx])
                    else:
                        S.op("dve", lambda e, hc_=hcur: e.tensor_tensor(
                            hmx[:, :], hc_[:, :], hmx[:, :], ALU.max), r=[hcur, hmx], w=[hmx])
                tokmax_to_row(ph, hmx, "mx", sqrt=True)
                bcPg = ph.enter_context(nc.psum_tensor(f"bg{next(_uid)}", [128, 512], F32))
                for off, w in NC:
                    build_bcast(bcPg, bc1, "mx", off, w)
                    S.op("dve", lambda e, ww=w: e.tensor_scalar(
                        bc1[:, 0:ww], bc1[:, 0:ww], 1e-5, 0.0, ALU.max, ALU.add),
                        r=[bc1], w=[bc1])
                    S.op("dve", lambda e, o=off, ww=w: e.tensor_scalar(
                        s2invb[:, o:o + ww], bc1[:, 0:ww], mwb[:, 1:2], 1.0 / 127.0,
                        ALU.mult, ALU.mult), r=[bc1, mwb], w=[s2invb])
                    S.op("dve", lambda e, ww=w: e.reciprocal(bc1[:, 0:ww], bc1[:, 0:ww]),
                         r=[bc1], w=[bc1])
                    S.op("dve", lambda e, ww=w: e.tensor_scalar(
                        bc1[:, 0:ww], bc1[:, 0:ww], 127.0, 0.0, ALU.mult, ALU.add),
                        r=[bc1], w=[bc1])
                    for hc in range(NHC):
                        hcur = hs[hc % 2]
                        qr = s2[hc % 2]
                        S.dma("sp", lambda e, h=hc, o=off, ww=w, hc_=hcur: e.dma_start(
                            out=hc_[:, o:o + ww], in_=hd.ap()[128 * h:128 * (h + 1), o:o + ww]),
                            r=[hd], w=[hcur])
                        S.op("dve", lambda e, o=off, ww=w, hc_=hcur: e.tensor_tensor(
                            bc2[:, 0:ww], hc_[:, o:o + ww], bc1[:, 0:ww], ALU.mult),
                            r=[hcur, bc1], w=[bc2])
                        S.op("dve", lambda e, o=off, ww=w, q=qr: e.tensor_scalar(
                            q[:, o:o + ww], bc2[:, 0:ww], RND, -RND, ALU.add, ALU.add),
                            r=[bc2], w=[qr])
                        S.dma("sp", lambda e, h=hc, o=off, ww=w, q=qr: e.dma_start(
                            out=xq2d.ap()[128 * h:128 * (h + 1), o:o + ww], in_=q[:, o:o + ww]),
                            r=[qr], w=[xq2d])
                S.barrier()

            with contextlib.ExitStack() as ph:
                oP = [ph.enter_context(nc.psum_tensor(f"oP{tag}{i}", [128, 512], F32)) for i in range(NF)]
                xq2r = [ph.enter_context(nc.sbuf_tensor(f"xr{tag}{i}", [128, 512], BF16)) for i in range(2)]
                for ni, (off, w) in enumerate(NC):
                    for hc in range(NHC):
                        rb, wb = xq2r[hc % 2], wbuf[hc % 2]
                        S.dma("sp", lambda e, h=hc, o=off, ww=w, r_=rb: e.dma_start(
                            out=r_[:, 0:ww], in_=xq2d.ap()[128 * h:128 * (h + 1), o:o + ww]),
                            r=[xq2d], w=[rb])
                        S.dma("pool", lambda e, h=hc, wb=wb: e.dma_start(
                            out=wb[:, 0:1024], in_=wq2T.ap()[128 * h:128 * (h + 1), :]),
                            r=[wq2T], w=[wb])
                        for oc in range(NF):
                            S.pre("pe", [wb, rb], [oP[oc]])
                            S.post("pe", mm(nc.tensor, oP[oc][:, 0:w],
                                            wb[:, 128 * oc:128 * (oc + 1)], rb[:, 0:w],
                                            hc == 0, hc == NHC - 1), [wb, rb], [oP[oc]])
                    for oc in range(NF):
                        S.op("dve", lambda e, o=oc, of=off, ww=w: e.tensor_tensor(
                            bc2[:, 0:ww], oP[o][:, 0:ww], s2invb[:, of:of + ww], ALU.mult),
                            r=[oP[oc], s2invb], w=[bc2])
                        S.op("dve", lambda e, o=oc, of=off, ww=w: e.tensor_tensor(
                            x8[o][:, of:of + ww], bc2[:, 0:ww], x8[o][:, of:of + ww], ALU.add),
                            r=[bc2, x8[oc]], w=[x8[oc]])
                S.barrier()

        if n_phases >= 2:
            ffn_phase("A")
            rezero_halo()

        if n_phases >= 3:
            with contextlib.ExitStack() as ph:
                ln_stats(ph)
                ln_to(ph, s1)
                S.barrier()
            with contextlib.ExitStack() as ph:
                kvt = ph.enter_context(nc.sbuf_tensor("kvt", [128, T], BF16))
                qmax8 = ph.enter_context(nc.sbuf_tensor("qmax8", [128, NF], F32))
                qmaxr = ph.enter_context(nc.sbuf_tensor("qmaxr", [1, NF], F32))
                vres = [ph.enter_context(nc.sbuf_tensor(f"vres{i}", [128, 1024], BF16)) for i in range(NF)]
                knacc = ph.enter_context(nc.sbuf_tensor("knacc", [2, NF], F32))
                kn2s = ph.enter_context(nc.sbuf_tensor("kn2s", [2, 512], F32))
                ksq = ph.enter_context(nc.sbuf_tensor("ksq", [128, 2048], BF16))
                kchunk = ph.enter_context(nc.sbuf_tensor("kchunk", [128, 2048], BF16))
                qkP = [ph.enter_context(nc.psum_tensor(f"qkP{i}", [128, 512], F32)) for i in range(2)]
                vP3 = ph.enter_context(nc.psum_tensor("vP3", [128, 1024], F32))
                trPq = ph.enter_context(nc.psum_tensor("trPq", [128, 512], F32))
                knP = ph.enter_context(nc.psum_tensor("knP", [2, 512], F32))

                for oc in range(16):
                    wb = wbuf[oc % 2]
                    S.dma("sp", lambda e, o=oc, wb=wb: e.dma_start(
                        out=wb[:, 0:1024],
                        in_=wbT.ap().rearrange("(f p) o -> p f o", p=128)[:, :, 128 * o:128 * (o + 1)]),
                        r=[wbT], w=[wb])
                    dst = s2[oc] if oc < 8 else kvt
                    for ni, (off, w) in enumerate(NC):
                        pb = qkP[ni % 2]
                        S.pre("pe", [wb] + [s1[f] for f in range(NF)], [pb])
                        for fc in range(NF):
                            inst = mm(nc.tensor, pb[:, 0:w], wb[:, 128 * fc:128 * (fc + 1)],
                                      s1[fc][:, off:off + w], fc == 0, fc == NF - 1)
                        S.post("pe", inst, [wb] + [s1[f] for f in range(NF)], [pb])
                        S.op("dve", lambda e, d=dst, of=off, ww=w, p=pb: e.tensor_copy(
                            d[:, of:of + ww], p[:, 0:ww]), r=[pb], w=[dst])
                    if oc < 8:
                        S.op("dve", lambda e, o=oc: e.reduce_max(
                            qmax8[:, o:o + 1], s2[o][:, :], axis=AX.X,
                            apply_absolute_value=True), r=[s2[oc]], w=[qmax8])
                    else:
                        S.dma("sp", lambda e, o=oc: e.dma_start(
                            out=kx.ap()[128 * (o - 8):128 * (o - 7), :],
                            in_=kvt[:, OWN0:OWN1]), r=[kvt], w=[kx])
                S.pre("pe", [qmax8, ident], [trPq])
                S.post("pe", mm(nc.tensor, trPq[0:8, 0:128], qmax8[:, 0:8], ident[:],
                                True, True), [qmax8, ident], [trPq])
                S.op("dve", lambda e: e.reduce_max(redc[0:8, 0:1], trPq[0:8, 0:128], axis=AX.X),
                     r=[trPq], w=[redc])
                S.pre("pe", [redc, ident], [trPq])
                S.post("pe", mm(nc.tensor, trPq[0:1, 128:136], redc[0:8, 0:1],
                                ident[0:8, 0:8], True, True), [redc, ident], [trPq])
                S.op("dve", lambda e: e.tensor_copy(qmaxr[0:1, 0:8], trPq[0:1, 128:136]),
                     r=[trPq], w=[qmaxr])

                for fc in range(NF):
                    S.dma("sp", lambda e, f=fc: e.dma_start(
                        out=vres[f][:], in_=wbT.ap()[128 * f:128 * (f + 1), 2048:3072]),
                        r=[wbT], w=[vres[fc]])
                for ti, (toff, tw) in enumerate(TT):
                    rd = [vres[f] for f in range(NF)] + [s1[f] for f in range(NF)]
                    S.pre("pe", rd, [vP3])
                    for fc in range(NF):
                        for oc in range(2):
                            inst = mm(nc.tensor, vP3[0:tw, 512 * oc:512 * (oc + 1)],
                                      s1[fc][:, toff:toff + tw],
                                      vres[fc][:, 512 * oc:512 * (oc + 1)],
                                      fc == 0, fc == NF - 1)
                    S.post("pe", inst, rd, [vP3])
                    S.op("dve", lambda e, ww=tw: e.tensor_copy(
                        kvt[0:ww, 0:1024], vP3[0:ww, :]), r=[vP3], w=[kvt])
                    lo, hi = max(toff, OWN0), min(toff + tw, OWN1)
                    if hi > lo:
                        S.dma("sp", lambda e, lo=lo, hi=hi, to=toff: e.dma_start(
                            out=vx.ap()[lo - OWN0:hi - OWN0, :],
                            in_=kvt[lo - to:hi - to, 0:1024]), r=[kvt], w=[vx])

                S.cc(lambda e: e.collective_compute(
                    "AllGather", ALU.bypass, replica_groups=REPL,
                    ins=[kx.ap().opt()], outs=[kg.ap().opt()]), r=[kx], w=[kg])
                S.cc(lambda e: e.collective_compute(
                    "AllGather", ALU.bypass, replica_groups=REPL,
                    ins=[vx.ap().opt()], outs=[vg.ap().opt()]), r=[vx], w=[vg])

                for r_ in range(2):
                    for c in range(NF):
                        S.dma("sp", lambda e, r2=r_, c=c: e.dma_start(
                            out=kchunk[:], in_=kg.ap()[1024 * r2 + 128 * c:1024 * r2 + 128 * (c + 1), :]),
                            r=[kg], w=[kchunk])
                        S.op("act", lambda e: e.activation(ksq[:], kchunk[:], AF.Square),
                             r=[kchunk], w=[ksq])
                        for n4 in range(4):
                            S.pre("pe", [sel2, ksq], [knP])
                            S.post("pe", mm(nc.tensor, knP[0:2, :], sel2[:],
                                            ksq[:, 512 * n4:512 * (n4 + 1)], True, True),
                                   [sel2, ksq], [knP])
                            S.op("dve", lambda e: e.tensor_copy(
                                kn2s[0:2, :], knP[0:2, :]), r=[knP], w=[kn2s])
                            S.op("dve", lambda e: e.reduce_max(
                                redc[0:2, 0:1], kn2s[0:2, :], axis=AX.X), r=[kn2s], w=[redc])
                            if r_ == 0 and n4 == 0:
                                S.op("dve", lambda e, c=c: e.tensor_copy(
                                    knacc[0:2, c:c + 1], redc[0:2, 0:1]), r=[redc], w=[knacc])
                            else:
                                S.op("dve", lambda e, c=c: e.tensor_tensor(
                                    knacc[0:2, c:c + 1], redc[0:2, 0:1],
                                    knacc[0:2, c:c + 1], ALU.max), r=[redc, knacc], w=[knacc])
                S.pre("pe", [knacc, ident], [trPq])
                S.post("pe", mm(nc.tensor, trPq[0:8, 136:138], knacc[0:2, 0:8],
                                ident[0:2, 0:2], True, True), [knacc, ident], [trPq])
                S.op("dve", lambda e: e.reduce_max(redc[0:8, 0:1], trPq[0:8, 136:138], axis=AX.X),
                     r=[trPq], w=[redc])
                S.pre("pe", [redc, ident], [trPq])
                S.post("pe", mm(nc.tensor, trPq[0:1, 138:146], redc[0:8, 0:1],
                                ident[0:8, 0:8], True, True), [redc, ident], [trPq])
                S.op("act", lambda e: e.activation(scal[0:1, 8:16], trPq[0:1, 138:146], AF.Sqrt),
                     r=[trPq], w=[scal])
                S.op("dve", lambda e: e.tensor_tensor(scal[0:1, 8:16], scal[0:1, 8:16],
                                                      qmaxr[0:1, 0:8], ALU.mult),
                     r=[scal, qmaxr], w=[scal])
                S.op("dve", lambda e: e.tensor_scalar(scal[0:1, 8:16], scal[0:1, 8:16],
                                                      -0.125, 0.0, ALU.mult, ALU.add),
                     r=[scal], w=[scal])
                S.barrier()
            with contextlib.ExitStack() as ph:
                ktile = [ph.enter_context(nc.sbuf_tensor("ktile0", [128, 4096], BF16))] * 2
                vstat = ph.enter_context(nc.sbuf_tensor("vstat", [128, 2 * 32 * 65], BF16))
                bbt = ph.enter_context(nc.sbuf_tensor("bbt", [128, 8], F32))
                zbs2 = ph.enter_context(nc.sbuf_tensor("zbs2", [64, 1024], F32))
                ptile = [ph.enter_context(nc.sbuf_tensor(f"pt{i}", [128, 2048], BF16)) for i in range(2)]
                zrowr = ph.enter_context(nc.sbuf_tensor(f"zrw{next(_uid)}", [1, 1024], F32))
                zrt = zrowr
                with nc.psum_tensor("bP3", [128, 512], F32) as bP3:
                    S.pre("pe", [onesrow, scal], [bP3])
                    S.post("pe", mm(nc.tensor, bP3[0:128, 0:8], onesrow[0:1, :],
                                    scal[0:1, 8:16], True, True), [onesrow, scal], [bP3])
                    S.op("dve", lambda e: e.tensor_copy(bbt[:, 0:8], bP3[0:128, 0:8]),
                         r=[bP3], w=[bbt])
                    S.barrier()
                STp = ph.enter_context(nc.psum_tensor("STp", [128, 2048], F32))
                pvP = [ph.enter_context(nc.psum_tensor(f"pvP{i}", [65, 512], F32)) for i in range(2)]
                zbP = ph.enter_context(nc.psum_tensor("zbP", [64, 1024], F32))
                S.op("pool", lambda e: e.memset(
                    vstat[:].rearrange("p (k c) -> p k c", c=65)[:, :, 64:65], 1.0), w=[vstat])

                for pr in range(8):
                    kt = ktile[pr % 2]
                    for r_ in range(2):
                        for hl in range(2):
                            S.dma("sp", lambda e, p=pr, r2=r_, hl=hl, kt=kt: e.dma_start(
                                out=kt[64 * hl:64 * (hl + 1), 2048 * r2:2048 * (r2 + 1)],
                                in_=kg.ap()[1024 * r2 + 64 * (2 * p + hl):1024 * r2 + 64 * (2 * p + hl) + 64, :]),
                                r=[kg], w=[kt])
                    for hl in range(2):
                        S.dma("sp", lambda e, p=pr, hl=hl: e.dma_start(
                            out=vstat[:].rearrange("p (k c) -> p k c", c=65)[:, 32 * hl:32 * (hl + 1), 0:64],
                            in_=vg.ap().rearrange("(k p) c -> p k c", p=128)[:, :, 64 * (2 * p + hl):64 * (2 * p + hl) + 64]),
                            r=[vg], w=[vstat])
                    for qi, (qoff, qw) in enumerate(NC):
                        def pv_emit(kb2, pt):
                            kb = 2 * kb2
                            S.pre("pe", [vstat, pt], [pvP[0]])
                            mm(nc.tensor, pvP[0][0:65, 0:qw],
                               vstat[:, 65 * kb:65 * (kb + 1)], pt[:, 0:qw],
                               kb == 0, False)
                            ia = mm(nc.tensor, pvP[0][0:65, 0:qw],
                                    vstat[:, 65 * (kb + 1):65 * (kb + 2)],
                                    pt[:, 1024:1024 + qw], False, kb == 30)
                            S.post("pe", ia, [vstat, pt], [pvP[0]])
                            S.pre("pe", [vstat, pt], [pvP[1]])
                            mm(nc.tensor, pvP[1][0:65, 0:qw],
                               vstat[:, 65 * (32 + kb):65 * (32 + kb) + 65],
                               pt[:, 512:512 + qw], kb == 0, False)
                            ib = mm(nc.tensor, pvP[1][0:65, 0:qw],
                                    vstat[:, 65 * (33 + kb):65 * (33 + kb) + 65],
                                    pt[:, 1536:1536 + qw], False, kb == 30)
                            S.post("pe", ib, [vstat, pt], [pvP[1]])
                        for kb2 in range(16):
                            kb = 2 * kb2
                            st = STp
                            S.pre("pe", [kt, s2[pr]], [st])
                            mm(nc.tensor, st[:, 0:qw], kt[0:64, 128 * kb:128 * (kb + 1)],
                               s2[pr][0:64, qoff:qoff + qw], True, True)
                            nc.tensor.matmul(
                                st[:, 512:512 + qw], kt[64:128, 128 * kb:128 * (kb + 1)],
                                s2[pr][64:128, qoff:qoff + qw], start=True, stop=True,
                                tile_position=(64, 0), skip_group_check=True)
                            mm(nc.tensor, st[:, 1024:1024 + qw],
                               kt[0:64, 128 * (kb + 1):128 * (kb + 2)],
                               s2[pr][0:64, qoff:qoff + qw], True, True)
                            i2 = nc.tensor.matmul(
                                st[:, 1536:1536 + qw], kt[64:128, 128 * (kb + 1):128 * (kb + 2)],
                                s2[pr][64:128, qoff:qoff + qw], start=True, stop=True,
                                tile_position=(64, 0), skip_group_check=True)
                            S.post("pe", i2, [kt, s2[pr]], [st])
                            pt = ptile[kb2 % 2]
                            if qw == 512:
                                S.op("act", lambda e, st=st, pt=pt, p=pr: e.activation(
                                    pt[:, :], st[:, :], AF.Exp, bias=bbt[:, p:p + 1],
                                    scale=0.125), r=[st, bbt], w=[pt])
                            else:
                                S.op("act", lambda e, st=st, pt=pt, p=pr, qw=qw: e.activation(
                                    pt[:, 0:2048].rearrange("p (s c) -> p s c", c=512)[:, :, 0:qw],
                                    st[:, 0:2048].rearrange("p (s c) -> p s c", c=512)[:, :, 0:qw],
                                    AF.Exp, bias=bbt[:, p:p + 1], scale=0.125),
                                    r=[st, bbt], w=[pt])
                            pv_emit(kb2, pt)
                        S.op("dve", lambda e, qw=qw: e.reciprocal(
                            zrowr[0:1, 0:qw], pvP[0][64:65, 0:qw]), r=[pvP[0]], w=[zrt])
                        S.op("dve", lambda e, qw=qw: e.reciprocal(
                            zrowr[0:1, 512:512 + qw], pvP[1][64:65, 0:qw]), r=[pvP[1]], w=[zrt])
                        for hl in range(2):
                            S.pre("pe", [onesrow, zrt], [zbP])
                            S.post("pe", mm(nc.tensor, zbP[0:64, 512 * hl:512 * hl + qw],
                                            onesrow[0:1, 0:64],
                                            zrowr[0:1, 512 * hl:512 * hl + qw], True, True),
                                   [onesrow, zrt], [zbP])
                        S.op("act", lambda e, qw=qw: e.copy(
                            zbs2[0:64, 0:qw], zbP[0:64, 0:qw]), r=[zbP], w=[zbs2])
                        S.op("act", lambda e, qw=qw: e.copy(
                            zbs2[0:64, 512:512 + qw], zbP[0:64, 512:512 + qw]), r=[zbP], w=[zbs2])
                        S.op("dve", lambda e, p=pr, qo=qoff, qw=qw: e.tensor_tensor(
                            s2[p][0:64, qo:qo + qw], pvP[0][0:64, 0:qw],
                            zbs2[0:64, 0:qw], ALU.mult), r=[pvP[0], zbs2], w=[s2[pr]])
                        S.op("dve", lambda e, p=pr, qo=qoff, qw=qw: e.tensor_tensor(
                            s2[p][64:128, qo:qo + qw], pvP[1][0:64, 0:qw],
                            zbs2[0:64, 512:512 + qw], ALU.mult), r=[pvP[1], zbs2], w=[s2[pr]])
                S.barrier()
            with contextlib.ExitStack() as ph:
                woP = [ph.enter_context(nc.psum_tensor(f"woB{i}", [128, 512], F32)) for i in range(2)]
                for oc in range(NF):
                    wb = wbuf[oc % 2]
                    S.dma("sp", lambda e, o=oc, wb=wb: e.dma_start(
                        out=wb[:, 0:1024],
                        in_=wobT.ap().rearrange("(f p) o -> p f o", p=128)[:, :, 128 * o:128 * (o + 1)]),
                        r=[wobT], w=[wb])
                    for ni, (off, w) in enumerate(NC):
                        pb = woP[ni % 2]
                        S.pre("pe", [wb] + [s2[f] for f in range(NF)], [pb])
                        for fc in range(NF):
                            inst = mm(nc.tensor, pb[:, 0:w], wb[:, 128 * fc:128 * (fc + 1)],
                                      s2[fc][:, off:off + w], fc == 0, fc == NF - 1)
                        S.post("pe", inst, [wb] + [s2[f] for f in range(NF)], [pb])
                        S.op("dve", lambda e, o=oc, of=off, ww=w, p=pb: e.tensor_tensor(
                            x8[o][:, of:of + ww], p[:, 0:ww], x8[o][:, of:of + ww], ALU.add),
                            r=[pb, x8[oc]], w=[x8[oc]])
                S.barrier()
            rezero_halo()

        if n_phases >= 4:
            ffn_phase("B")

        if dump:
            if os.environ.get("DUMP_XQ1"):
                for fc in range(0, NF):
                    S.dma("sp", lambda e, f=fc: e.dma_start(
                        out=xq2d.ap()[128 * f:128 * (f + 1), :], in_=s1[f][:, :]),
                        r=[s1[fc]], w=[xq2d])
            dhd = nc.dram_tensor("dhd", [DFF, T], F32, kind="ExternalOutput")
            dxq = nc.dram_tensor("dxq", [DFF, T], BF16, kind="ExternalOutput")
            dwq = nc.dram_tensor("dwq", [D, DFF], BF16, kind="ExternalOutput")
            with nc.sbuf_tensor("dmpw", [128, DFF], BF16) as dmpw:
                for f in range(NF):
                    S.dma("sp", lambda e, f=f: e.dma_start(
                        out=dmpw[:], in_=wq1T.ap()[128 * f:128 * (f + 1), :]), r=[wq1T], w=[dmpw])
                    S.dma("sp", lambda e, f=f: e.dma_start(
                        out=dwq.ap()[128 * f:128 * (f + 1), :], in_=dmpw[:]), r=[dmpw], w=[dwq])
            with nc.sbuf_tensor("dmp", [128, T], F32) as dmp, \
                 nc.sbuf_tensor("dmpb", [128, T], BF16) as dmpb:
                for h in range(NHC):
                    S.dma("sp", lambda e, h=h: e.dma_start(
                        out=dmp[:], in_=hd.ap()[128 * h:128 * (h + 1), :]), r=[hd], w=[dmp])
                    S.dma("sp", lambda e, h=h: e.dma_start(
                        out=dhd.ap()[128 * h:128 * (h + 1), :], in_=dmp[:]), r=[dmp], w=[dhd])
                    S.dma("sp", lambda e, h=h: e.dma_start(
                        out=dmpb[:], in_=xq2d.ap()[128 * h:128 * (h + 1), :]), r=[xq2d], w=[dmpb])
                    S.dma("sp", lambda e, h=h: e.dma_start(
                        out=dxq.ap()[128 * h:128 * (h + 1), :], in_=dmpb[:]), r=[dmpb], w=[dxq])
                S.barrier()
        for fc in range(NF):
            S.dma("sp", lambda e, f=fc: e.dma_start(
                out=yout.ap()[128 * f:128 * (f + 1), :], in_=x8[f][:, OWN0:OWN1]),
                r=[x8[fc]], w=[yout])
        S.barrier()

    return nc


_CACHE = {}


def _stage(inputs):
    x = np.asarray(inputs["L0_attn_a_x"], dtype=np.float32)
    base = {
        "wqkv_a": np.ascontiguousarray(inputs["L0_attn_a_w_qkv"], dtype=np.float32),
        "wout_a": np.ascontiguousarray(inputs["L0_attn_a_w_out"], dtype=np.float32),
        "bout_a": np.ascontiguousarray(inputs["L0_attn_a_b_out"], dtype=np.float32),
        "wqkv_b": np.ascontiguousarray(inputs["L0_attn_b_w_qkv"], dtype=np.float32),
        "wout_b": np.ascontiguousarray(inputs["L0_attn_b_w_out"], dtype=np.float32),
        "w1": np.ascontiguousarray(inputs["L0_ffn_w1"], dtype=np.float32),
        "convw": np.ascontiguousarray(
            np.asarray(inputs["L0_ffn_conv_w"], dtype=np.float32).reshape(DFF, 3).T),
        "convb": np.ascontiguousarray(inputs["L0_ffn_conv_b"], dtype=np.float32),
        "w2": np.ascontiguousarray(inputs["L0_ffn_w2"], dtype=np.float32),
        "ident": np.eye(128, dtype=np.float32),
        "sel2": np.concatenate([
            np.repeat(np.float32([[1], [0]]), 64, axis=0),
            np.repeat(np.float32([[0], [1]]), 64, axis=0)], axis=1).astype(np.float32),
        "onescol_in": np.ones((128, 1), dtype=np.float32),
        "onesrow_in": np.ones((128, 128), dtype=np.float32),
    }
    import os as _os
    if _os.environ.get("BYPASS_XQ"):
        import ml_dtypes as _md
        base["xqstage"] = np.load("/root/problem/_xqstage.npy").view(_md.bfloat16)
    if _os.environ.get("BYPASS_WQ"):
        import ml_dtypes as _md
        w1f = np.asarray(inputs.get("L0_ffn_w1", base["w1"]), np.float64)
        tern = np.clip(np.rint(w1f / np.abs(w1f).mean()), -1, 1)
        base["wqstage"] = np.ascontiguousarray(tern.T).astype(_md.bfloat16)
    in_maps = []
    for c in range(8):
        b, hh = c // 2, c % 2
        s, e = 2048 * hh, 2048 * hh + 2048
        pad = np.zeros((T, D), dtype=np.float32)
        lo, hi = s - 2, e + 2
        clo, chi = max(lo, 0), min(hi, 4096)
        pad[clo - lo:chi - lo, :] = x[b, clo:chi, :]
        hm = np.ones((128, 4), dtype=np.float32)
        if hh == 0:
            hm[:, 0:2] = 0.0
        else:
            hm[:, 2:4] = 0.0
        m = dict(base)
        m["xin"] = np.ascontiguousarray(pad.T)
        m["hmask"] = hm
        in_maps.append(m)
    return in_maps


def kernel(**inputs):
    n_phases = int(inputs.pop("_n_phases", 4))
    dump = bool(inputs.pop("_dump", False))
    key = (n_phases, dump)
    if key not in _CACHE:
        _CACHE[key] = build(n_phases, dump)
    nc = _CACHE[key]
    in_maps = _stage(inputs)
    res = run_bass_kernel_spmd(nc, in_maps, core_ids=list(range(8)))
    out = np.empty((4, 4096, 1024), dtype=np.float32)
    for c in range(8):
        b, hh = c // 2, c % 2
        out[b, 2048 * hh:2048 * (hh + 1), :] = res.results[c]["y"].T
    if dump:
        kernel._dbg = res.results
    return out


# version 04
# speedup vs baseline: 1.1619x; 1.0177x over previous
import contextlib
import itertools

import numpy as np
import concourse.bass as bass
import concourse.mybir as mybir
from concourse.bass_utils import run_bass_kernel_spmd

F32 = mybir.dt.float32
BF16 = mybir.dt.bfloat16
AF = mybir.ActivationFunctionType
ALU = mybir.AluOpType
AX = mybir.AxisListType

T = 2052
OWN0, OWN1 = 2, 2050
D = 1024
H = 16
DH = 64
DFF = 4096
NF = 8
NHC = 32
RND = float(1.5 * 2.0 ** 23)
TT = [(128 * i, 128) for i in range(16)] + [(2048, 4)]
NC = [(0, 512), (512, 512), (1024, 512), (1536, 512), (2048, 4)]
REPL = [[0, 1], [2, 3], [4, 5], [6, 7]]
_uid = itertools.count()


NSLOT = 12


class Sched:

    def __init__(self, nc, sems):
        self.nc = nc
        self.sems = sems
        self.cnt = {s: 0 for s in sems}
        self.hwm = {}
        self.writers = {}
        self.readers = {}
        self.eng = {"pe": nc.tensor, "act": nc.scalar, "dve": nc.vector,
                    "pool": nc.gpsimd, "sp": nc.sync}
        self.slot = {"sp": 0, "pool": 0}

    def _wait(self, eng, sem, val):
        if val <= 0 or self.hwm.get((eng, sem), 0) >= val:
            return
        self.hwm[(eng, sem)] = val
        self.eng[eng].wait_ge(self.sems[sem], val)

    def pre(self, eng, reads, writes):
        for b in reads:
            for t in self.writers.get(id(b), ()):
                self._wait(eng, t[0], t[1])
        for b in writes:
            for t in self.readers.get(id(b), ()):
                self._wait(eng, t[0], t[1])

    def post(self, sem, inst, reads, writes, amt=1):
        inst.then_inc(self.sems[sem], amt)
        self.cnt[sem] += amt
        tok = (sem, self.cnt[sem])
        for b in writes:
            self.writers.setdefault(id(b), []).append(tok)
            self.readers[id(b)] = []
        for b in reads:
            self.readers.setdefault(id(b), []).append(tok)

    def op(self, eng, f, r=(), w=()):
        self.pre(eng, r, w)
        self.post(eng, f(self.eng[eng]), r, w)

    def dma(self, q, f, r=(), w=()):
        s = self.slot[q]
        self.slot[q] = (s + 1) % NSLOT
        sem = f"{q}d{s}"
        self._wait(q, sem, self.cnt[sem])
        self.pre(q, r, w)
        self.post(sem, f(self.eng[q]), r, w, 16)

    def cc(self, f, r=(), w=()):
        self.pre("pool", r, w)
        self._wait("pool", "cc", self.cnt["cc"])
        self.post("cc", f(self.eng["pool"]), r, w)

    def barrier(self):
        for eng in self.eng:
            for sem, v in self.cnt.items():
                self._wait(eng, sem, v)
        self.writers.clear()
        self.readers.clear()


def build(n_phases=4, dump=False):
    import os
    n_pro = int(os.environ.get('N_PRO', '9'))
    nc = bass.Bass("TRN2", target_bir_lowering=False)

    xin = nc.dram_tensor("xin", [D, T], F32, kind="ExternalInput")
    wqkv_a = nc.dram_tensor("wqkv_a", [D, 3 * D], F32, kind="ExternalInput")
    wout_a = nc.dram_tensor("wout_a", [D, D], F32, kind="ExternalInput")
    bout_a = nc.dram_tensor("bout_a", [D], F32, kind="ExternalInput")
    wqkv_b = nc.dram_tensor("wqkv_b", [3 * D, D], F32, kind="ExternalInput")
    wout_b = nc.dram_tensor("wout_b", [D, D], F32, kind="ExternalInput")
    w1 = nc.dram_tensor("w1", [DFF, D], F32, kind="ExternalInput")
    convw = nc.dram_tensor("convw", [3, DFF], F32, kind="ExternalInput")
    convb = nc.dram_tensor("convb", [DFF], F32, kind="ExternalInput")
    w2 = nc.dram_tensor("w2", [D, DFF], F32, kind="ExternalInput")
    ident_in = nc.dram_tensor("ident", [128, 128], F32, kind="ExternalInput")
    hmask_in = nc.dram_tensor("hmask", [128, 4], F32, kind="ExternalInput")
    sel2_in = nc.dram_tensor("sel2", [128, 2], F32, kind="ExternalInput")
    onescol_in = nc.dram_tensor("onescol_in", [128, 1], F32, kind="ExternalInput")
    onesrow_in = nc.dram_tensor("onesrow_in", [128, 128], F32, kind="ExternalInput")
    xqstage = (nc.dram_tensor("xqstage", [D, T], BF16, kind="ExternalInput")
               if os.environ.get("BYPASS_XQ") else None)
    wqstage = (nc.dram_tensor("wqstage", [D, DFF], BF16, kind="ExternalInput")
               if os.environ.get("BYPASS_WQ") else None)
    yout = nc.dram_tensor("y", [D, 2048], F32, kind="ExternalOutput")

    wa_bf = nc.dram_tensor("wa_bf", [D, 3 * D], BF16)
    woa_bf = nc.dram_tensor("woa_bf", [D, D], BF16)
    wb_nat = nc.dram_tensor("wb_nat", [3 * D, D], BF16)
    wbT = nc.dram_tensor("wbT", [D, 3 * D], BF16)
    wob_nat = nc.dram_tensor("wob_nat", [D, D], BF16)
    wobT = nc.dram_tensor("wobT", [D, D], BF16)
    w1q_nat = nc.dram_tensor("w1q_nat", [DFF, D], BF16)
    wq1T = nc.dram_tensor("wq1T", [D, DFF], BF16)
    w2q_nat = nc.dram_tensor("w2q_nat", [D, DFF], BF16)
    wq2T = nc.dram_tensor("wq2T", [DFF, D], BF16)
    kx = nc.dram_tensor("kx", [D, 2048], BF16)
    vx = nc.dram_tensor("vx", [2048, D], BF16)
    kg = nc.dram_tensor("kg", [2 * D, 2048], BF16)
    vg = nc.dram_tensor("vg", [4096, D], BF16)
    cxl = nc.dram_tensor("cxl", [65, D], F32)
    cxg = nc.dram_tensor("cxg", [65, D], F32)
    hd = nc.dram_tensor("hd", [DFF, T], F32)
    xq2d = nc.dram_tensor("xq2d", [DFF, T], BF16)

    stk = contextlib.ExitStack()
    sb = lambda n, s, d: stk.enter_context(nc.sbuf_tensor(n, s, d))
    sems = {n: stk.enter_context(nc.semaphore(n + "_s"))
            for n in ("pe", "act", "dve", "pool", "sp", "cc")}
    for q in ("sp", "pool"):
        for s in range(NSLOT):
            sems[f"{q}d{s}"] = stk.enter_context(nc.semaphore(f"{q}d{s}_s"))
    S = Sched(nc, sems)

    x8 = [sb(f"x{i}", [128, T], F32) for i in range(NF)]
    s1 = [sb(f"s1_{i}", [128, T], BF16) for i in range(NF)]
    s2 = [sb(f"s2_{i}", [128, T], BF16) for i in range(NF)]
    dscr = sb("dscr", [128, T], F32)
    xbf = [sb(f"xbf{i}", [128, 512], BF16) for i in range(2)]
    sqbf = [sb(f"sqbf{i}", [128, 512], BF16) for i in range(2)]
    wbuf = [sb(f"wbuf{i}", [128, 2048], BF16) for i in range(2)]
    ident = sb("identt", [128, 128], F32)
    hmask = sb("hmaskt", [128, 4], F32)
    sel2 = sb("sel2t", [128, 2], BF16)
    onescol = sb("onescolt", [128, 1], BF16)
    onesrow = sb("onesrowt", [128, 128], F32)
    b_out = sb("b_outt", [128, NF], F32)
    convwt = sb("convwt", [128, 3 * NHC], F32)
    convbt = sb("convbt", [128, NHC], F32)
    mwrow = sb("mwrow", [1, 4], F32)
    scal = sb("scal", [1, 16], F32)
    redc = sb("redc", [128, 1], F32)
    bc1 = sb("bc1", [128, 512], F32)
    bc2 = sb("bc2", [128, 512], F32)
    bc3 = sb("bc3", [128, 512], F32)
    rowt = sb("rowt", [65, T], F32)
    mwb = sb("mwb", [128, 4], F32)
    _ROWP = {"sum": 0, "sumsq": 32, "mx": 64, "bk": 64}

    def R(name, c0=0, c1=T):
        p = _ROWP[name]
        return rowt[p:p + 1, c0:c1], rowt

    def RP(name):
        return _ROWP[name]

    def mm(e, out, lhsT, rhs, start, stop):
        return e.matmul(out, lhsT, rhs, start=start, stop=stop,
                        skip_group_check=True)

    with stk:
        S.dma("sp", lambda e: e.dma_start(out=ident[:], in_=ident_in.ap()), w=[ident])
        S.dma("sp", lambda e: e.dma_start(out=hmask[:], in_=hmask_in.ap()), w=[hmask])
        S.dma("pool", lambda e: e.dma_start(out=sel2[:], in_=sel2_in.ap()), w=[sel2])
        S.dma("pool", lambda e: e.dma_start(out=onescol[:], in_=onescol_in.ap()), w=[onescol])
        S.dma("sp", lambda e: e.dma_start(out=onesrow[:], in_=onesrow_in.ap()), w=[onesrow])
        for c in range(NF):
            S.dma("sp", lambda e, c=c: e.dma_start(out=x8[c][:], in_=xin.ap()[128 * c:128 * (c + 1), :]), w=[x8[c]])
        S.dma("pool", lambda e: e.dma_start(out=wa_bf.ap(), in_=wqkv_a.ap()), w=[wa_bf])
        S.dma("pool", lambda e: e.dma_start(out=woa_bf.ap(), in_=wout_a.ap()), w=[woa_bf])
        S.dma("pool", lambda e: e.dma_start(out=wb_nat.ap(), in_=wqkv_b.ap()), w=[wb_nat])
        S.dma("pool", lambda e: e.dma_start(out=wob_nat.ap(), in_=wout_b.ap()), w=[wob_nat])

        with nc.psum_tensor("prT", [128, 512], F32) as prT, \
             nc.sbuf_tensor("rload", [1, 512], F32) as rload:
            def load_cols(dram_ap_1d, n, dst, colmap):
                for j in range(n):
                    S.dma("sp", lambda e, j=j: e.dma_start(
                        out=rload[0:1, 0:128],
                        in_=dram_ap_1d[128 * j:128 * (j + 1)].rearrange("(o p) -> o p", o=1)),
                        r=[], w=[rload])
                    S.pre("pe", [rload, ident], [prT])
                    S.post("pe", mm(nc.tensor, prT[0:128, 0:1], rload[0:1, 0:128],
                                    ident[0:1, 0:1], True, True), [rload, ident], [prT])
                    S.op("dve", lambda e, j=j: e.tensor_copy(
                        dst[:, colmap(j):colmap(j) + 1], prT[0:128, 0:1]),
                        r=[prT], w=[dst])
            if n_pro >= 2:
                load_cols(bout_a.ap(), NF, b_out, lambda j: j)
                load_cols(convb.ap(), NHC, convbt, lambda j: j)
                for k in range(3):
                    cwk = convw.ap()[k:k + 1, :].rearrange("one n -> (one n)")
                    load_cols(cwk, NHC, convwt, lambda j, k=k: 3 * j + k)

        slabs = [(wbuf[i // 4], 512 * (i % 4)) for i in range(8)]

        def transpose_copy(jobs):
            for b_ in range(0, len(jobs), 8):
                grp = jobs[b_:b_ + 8]
                for i, (sa, da, dt_) in enumerate(grp):
                    t_, o_ = slabs[i]
                    S.dma("sp", lambda e, sa=sa, t_=t_, o_=o_: e.dma_start_transpose(
                        t_[:, o_:o_ + 512], sa), r=[], w=[t_])
                for i, (sa, da, dt_) in enumerate(grp):
                    t_, o_ = slabs[i]
                    S.dma("pool", lambda e, da=da, t_=t_, o_=o_: e.dma_start(
                        out=da, in_=t_[:, o_:o_ + 512]), r=[t_], w=[dt_])
        if n_pro >= 3:
            jobs = []
            for fc in range(NF):
                for oc in range(6):
                    jobs.append((wb_nat.ap()[512 * oc:512 * (oc + 1), 128 * fc:128 * (fc + 1)],
                                 wbT.ap()[128 * fc:128 * (fc + 1), 512 * oc:512 * (oc + 1)], wbT))
                for oc in range(2):
                    jobs.append((wob_nat.ap()[512 * oc:512 * (oc + 1), 128 * fc:128 * (fc + 1)],
                                 wobT.ap()[128 * fc:128 * (fc + 1), 512 * oc:512 * (oc + 1)], wobT))
            S.pre("sp", [wb_nat, wob_nat], [])
            transpose_copy(jobs)

        with nc.sbuf_tensor("absacc", [128, 48], F32) as absacc, \
             nc.sbuf_tensor("qbf", [128, DFF], BF16) as qbf, \
             nc.psum_tensor("pT0", [128, 512], F32) as pT0:
            w2v = w2.ap().rearrange("a (b c) -> (a b) c", b=2)
            w2qv = w2q_nat.ap().rearrange("a (b c) -> (a b) c", b=2)
            for wi, (wap, nch, wid) in (enumerate(((w1.ap(), NHC, D), (w2v, 16, 2048))) if n_pro >= 4 else []):
                for c in range(nch):
                    S.dma("sp", lambda e, w_=wap, c=c, ww=wid: e.dma_start(
                        out=dscr[:, 0:ww], in_=w_[128 * c:128 * (c + 1), :]), w=[dscr])
                    col = c if wi == 0 else NHC + c
                    S.op("dve", lambda e, col=col, ww=wid: e.reduce_sum(
                        absacc[:, col:col + 1], dscr[:, 0:ww], axis=AX.X,
                        apply_absolute_value=True), r=[dscr], w=[absacc])
            for wi, (lo, n, total) in (enumerate(((0, NHC, DFF * D), (NHC, 16, D * DFF))) if n_pro >= 4 else []):
                S.pre("pe", [absacc, ident], [pT0])
                S.post("pe", mm(nc.tensor, pT0[0:n, 0:128], absacc[:, lo:lo + n],
                                ident[:], True, True), [absacc, ident], [pT0])
                S.op("dve", lambda e, n=n: e.reduce_sum(
                    redc[0:n, 0:1], pT0[0:n, 0:128], axis=AX.X), r=[pT0], w=[redc])
                S.pre("pe", [redc, ident], [pT0])
                S.post("pe", mm(nc.tensor, pT0[0:1, 128:128 + n], redc[0:n, 0:1],
                                ident[0:n, 0:n], True, True), [redc, ident], [pT0])
                S.op("dve", lambda e, wi=wi, n=n: e.reduce_sum(
                    scal[0:1, wi:wi + 1], pT0[0:1, 128:128 + n], axis=AX.X), r=[pT0], w=[scal])
                S.op("dve", lambda e, wi=wi, tot=total: e.tensor_scalar(
                    mwrow[0:1, wi:wi + 1], scal[0:1, wi:wi + 1], 1.0 / tot, 1e-5,
                    ALU.mult, ALU.max), r=[scal], w=[mwrow])
            if n_pro < 4:
                S.op("pool", lambda e: e.memset(mwrow[0:1, :], 1.0), w=[mwrow])
            S.op("dve", lambda e: e.reciprocal(mwrow[0:1, 2:4], mwrow[0:1, 0:2]),
                 r=[mwrow], w=[mwrow])
            S.pre("pe", [onesrow, mwrow], [pT0])
            S.post("pe", mm(nc.tensor, pT0[0:128, 256:260], onesrow[0:1, :],
                            mwrow[0:1, 0:4], True, True), [onesrow, mwrow], [pT0])
            S.op("dve", lambda e: e.tensor_copy(mwb[:, 0:4], pT0[0:128, 256:260]),
                 r=[pT0], w=[mwb])
            for wi, (wap, wq_ap, nch, wid) in (enumerate(
                    ((w1.ap(), w1q_nat.ap(), NHC, D), (w2v, w2qv, 16, 2048))) if n_pro >= 4 else []):
                for c in range(nch):
                    S.dma("sp", lambda e, w_=wap, c=c, ww=wid: e.dma_start(
                        out=dscr[:, 0:ww], in_=w_[128 * c:128 * (c + 1), :]), w=[dscr])
                    S.op("dve", lambda e, ww=wid, wi=wi: e.tensor_scalar(
                        dscr[:, 0:ww], dscr[:, 0:ww], mwb[:, 2 + wi:3 + wi], RND,
                        ALU.mult, ALU.add), r=[dscr, mwb], w=[dscr])
                    S.op("dve", lambda e, ww=wid: e.tensor_scalar(
                        dscr[:, 0:ww], dscr[:, 0:ww], -RND, -1.0, ALU.add, ALU.max),
                        r=[dscr], w=[dscr])
                    S.op("dve", lambda e, ww=wid: e.tensor_scalar(
                        qbf[:, 0:ww], dscr[:, 0:ww], 1.0, 0.0, ALU.min, ALU.add),
                        r=[dscr], w=[qbf])
                    wqt = w1q_nat if wi == 0 else w2q_nat
                    S.dma("sp", lambda e, q_=wq_ap, c=c, ww=wid: e.dma_start(
                        out=q_[128 * c:128 * (c + 1), :], in_=qbf[:, 0:ww]),
                        r=[qbf], w=[wqt])
        if n_pro >= 5:
            jobs = []
            for fc in range(NF):
                for oc in range(8):
                    jobs.append((w1q_nat.ap()[512 * oc:512 * (oc + 1), 128 * fc:128 * (fc + 1)],
                                 wq1T.ap()[128 * fc:128 * (fc + 1), 512 * oc:512 * (oc + 1)], wq1T))
            for hc in range(NHC):
                for oc in range(2):
                    jobs.append((w2q_nat.ap()[512 * oc:512 * (oc + 1), 128 * hc:128 * (hc + 1)],
                                 wq2T.ap()[128 * hc:128 * (hc + 1), 512 * oc:512 * (oc + 1)], wq2T))
            S.pre("sp", [w1q_nat, w2q_nat], [])
            transpose_copy(jobs)

        with nc.psum_tensor("bkP", [1, 512], F32) as bkP:
            for half in (range(2) if n_pro >= 6 else []):
                for fc in range(NF):
                    S.dma("sp", lambda e, f=fc, h=half: e.dma_start(
                        out=wbuf[1][:, 0:512],
                        in_=wa_bf.ap()[128 * f:128 * (f + 1), D + 512 * h:D + 512 * (h + 1)]),
                        r=[wa_bf], w=[wbuf[1]])
                    S.op("act", lambda e: e.activation(sqbf[0][:, 0:512], wbuf[1][:, 0:512], AF.Square),
                         r=[wbuf[1]], w=[sqbf[0]])
                    S.pre("pe", [onescol, sqbf[0]], [bkP])
                    S.post("pe", mm(nc.tensor, bkP[0:1, :], onescol[:], sqbf[0][:, 0:512],
                                    fc == 0, fc == NF - 1), [onescol, sqbf[0]], [bkP])
                ap, tt = R("bk", 512 * half, 512 * (half + 1))
                S.op("act", lambda e, ap=ap: e.activation(ap, bkP[0:1, :], AF.Sqrt),
                     r=[bkP], w=[tt])
        ap, tt = R("bk", 0, 1024)
        S.op("dve", lambda e, ap=ap: e.tensor_scalar(
            ap, ap, 32.5, 0.0, ALU.mult, ALU.add), r=[tt], w=[tt])

        def ln_stats(ph):
            statP = [ph.enter_context(nc.psum_tensor(f"stP{next(_uid)}", [1, 512], F32)) for _ in range(2)]
            for off, w in NC:
                for fc in range(NF):
                    xb, sq = xbf[fc % 2], sqbf[fc % 2]
                    S.op("dve", lambda e, f=fc, o=off, ww=w, xb=xb: e.tensor_copy(
                        xb[:, 0:ww], x8[f][:, o:o + ww]), r=[x8[fc]], w=[xb])
                    S.op("act", lambda e, ww=w, xb=xb, sq=sq: e.activation(
                        sq[:, 0:ww], xb[:, 0:ww], AF.Square), r=[xb], w=[sq])
                    S.pre("pe", [onescol, xb], [statP[0]])
                    S.post("pe", mm(nc.tensor, statP[0][0:1, 0:w], onescol[:], xb[:, 0:w],
                                    fc == 0, fc == NF - 1), [onescol, xb], [statP[0]])
                    S.pre("pe", [onescol, sq], [statP[1]])
                    S.post("pe", mm(nc.tensor, statP[1][0:1, 0:w], onescol[:], sq[:, 0:w],
                                    fc == 0, fc == NF - 1), [onescol, sq], [statP[1]])
                ap, tt = R("sum", off, off + w)
                S.op("dve", lambda e, ap=ap, ww=w: e.tensor_copy(ap, statP[0][0:1, 0:ww]),
                     r=[statP[0]], w=[tt])
                ap, tt = R("sumsq", off, off + w)
                S.op("dve", lambda e, ap=ap, ww=w: e.tensor_copy(ap, statP[1][0:1, 0:ww]),
                     r=[statP[1]], w=[tt])


        def mean_rstd_bc(bcP, off, w, scratch):
            build_bcast(bcP, bc1, "sum", off, w)
            build_bcast(bcP, bc2, "sumsq", off, w)
            S.op("dve", lambda e, ww=w: e.tensor_scalar(
                bc1[:, 0:ww], bc1[:, 0:ww], 1.0 / D, 0.0, ALU.mult, ALU.add),
                r=[bc1], w=[bc1])
            S.op("dve", lambda e, ww=w: e.tensor_scalar(
                bc2[:, 0:ww], bc2[:, 0:ww], 1.0 / D, 1e-5, ALU.mult, ALU.add),
                r=[bc2], w=[bc2])
            S.op("act", lambda e, ww=w: e.activation(
                scratch[:, 0:ww], bc1[:, 0:ww], AF.Square), r=[bc1], w=[scratch])
            S.op("dve", lambda e, ww=w: e.tensor_tensor(
                bc2[:, 0:ww], bc2[:, 0:ww], scratch[:, 0:ww], ALU.subtract),
                r=[bc2, scratch], w=[bc2])
            S.op("act", lambda e, ww=w: e.activation(
                bc2[:, 0:ww], bc2[:, 0:ww], AF.Sqrt), r=[bc2], w=[bc2])
            S.op("dve", lambda e, ww=w: e.reciprocal(bc2[:, 0:ww], bc2[:, 0:ww]),
                 r=[bc2], w=[bc2])

        def build_bcast(bcP, bc, rowname, off, w):
            ap, tt = R(rowname, off, off + w)
            p = RP(rowname)
            S.pre("pe", [onesrow, tt], [bcP])
            S.post("pe", mm(nc.tensor, bcP[0:128, 0:w], onesrow[p:p + 1, :], ap, True, True),
                   [onesrow, tt], [bcP])
            S.op("dve", lambda e, ww=w: e.tensor_copy(bc[:, 0:ww], bcP[0:128, 0:ww]),
                 r=[bcP], w=[bc])

        def ln_to(ph, dest):
            bcP = ph.enter_context(nc.psum_tensor(f"lnb{next(_uid)}", [128, 512], F32))
            for off, w in NC:
                mean_rstd_bc(bcP, off, w, bc3)
                for fc in range(NF):
                    S.op("dve", lambda e, f=fc, o=off, ww=w: e.tensor_tensor(
                        dscr[:, o:o + ww], x8[f][:, o:o + ww], bc1[:, 0:ww], ALU.subtract),
                        r=[x8[fc], bc1], w=[dscr])
                    S.op("dve", lambda e, f=fc, o=off, ww=w: e.tensor_tensor(
                        dest[f][:, o:o + ww], dscr[:, o:o + ww], bc2[:, 0:ww], ALU.mult),
                        r=[dscr, bc2], w=[dest[fc]])

        def rezero_halo():
            for fc in range(NF):
                S.op("dve", lambda e, f=fc: e.tensor_tensor(
                    x8[f][:, 0:2], x8[f][:, 0:2], hmask[:, 0:2], ALU.mult),
                    r=[x8[fc], hmask], w=[x8[fc]])
                S.op("dve", lambda e, f=fc: e.tensor_tensor(
                    x8[f][:, 2050:2052], x8[f][:, 2050:2052], hmask[:, 2:4], ALU.mult),
                    r=[x8[fc], hmask], w=[x8[fc]])

        def tokmax_to_row(ph, src, dstrow, sqrt=False):
            trP = ph.enter_context(nc.psum_tensor(f"tk{next(_uid)}", [128, 512], F32))
            for toff, tw in TT:
                S.pre("pe", [src, ident], [trP])
                S.post("pe", mm(nc.tensor, trP[0:tw, 0:128], src[:, toff:toff + tw],
                                ident[:], True, True), [src, ident], [trP])
                S.op("dve", lambda e, ww=tw: e.reduce_max(
                    redc[0:ww, 0:1], trP[0:ww, 0:128], axis=AX.X), r=[trP], w=[redc])
                S.pre("pe", [redc, ident], [trP])
                S.post("pe", mm(nc.tensor, trP[0:1, 128:128 + tw], redc[0:tw, 0:1],
                                ident[0:tw, 0:tw], True, True), [redc, ident], [trP])
                ap, tt = R(dstrow, toff, toff + tw)
                if sqrt:
                    S.op("act", lambda e, ap=ap, ww=tw: e.activation(
                        ap, trP[0:1, 128:128 + ww], AF.Sqrt), r=[trP], w=[tt])
                else:
                    S.op("dve", lambda e, ap=ap, ww=tw: e.tensor_copy(
                        ap, trP[0:1, 128:128 + ww]), r=[trP], w=[tt])

        if n_phases >= 1 and not os.environ.get("SKIP_P1"):
            with contextlib.ExitStack() as ph:
                ln_stats(ph)
                ln_to(ph, s1)
                S.barrier()
            with contextlib.ExitStack() as ph:
                qsm = ph.enter_context(nc.sbuf_tensor("qsm", [128, H * 65], BF16))
                expk = ph.enter_context(nc.sbuf_tensor("expk", [128, D], BF16))
                qmx = ph.enter_context(nc.sbuf_tensor("qmx", [128, H], F32))
                qsum = ph.enter_context(nc.sbuf_tensor("qsum", [128, H], F32))
                qrec = ph.enter_context(nc.sbuf_tensor("qrec", [128, H], F32))
                dq = ph.enter_context(nc.sbuf_tensor("dq", [128, 1024], F32))
                dqk = ph.enter_context(nc.sbuf_tensor("dqk", [128, 1024], F32))
                bkb = ph.enter_context(nc.sbuf_tensor("bkb", [128, 1024], F32))
                ctxs = ph.enter_context(nc.sbuf_tensor("ctxs", [65, 1024], F32))
                vP = [ph.enter_context(nc.psum_tensor(f"vP{i}", [128, 512], F32)) for i in range(2)]
                psA = ph.enter_context(nc.psum_tensor("psA", [128, 2048], F32))
                ctxP = ph.enter_context(nc.psum_tensor("ctxP", [65, 1024], F32))

                wres = [s2[f][:, 0:2048] for f in range(NF)]
                for fc in range(NF):
                    S.dma("sp", lambda e, f=fc: e.dma_start(
                        out=wres[f], in_=wa_bf.ap()[128 * f:128 * (f + 1), 0:2048]),
                        r=[wa_bf], w=[s2[fc]])
                S.op("pool", lambda e: e.memset(
                    qsm[:].rearrange("p (h c) -> p h c", c=65)[:, :, 64:65], 1.0), w=[qsm])
                for half in range(2):
                    ap, tt = R("bk", 512 * half, 512 * (half + 1))
                    S.pre("pe", [onesrow, tt], [vP[0]])
                    S.post("pe", mm(nc.tensor, vP[0][0:128, 0:512], onesrow[64:65, :], ap,
                                    True, True), [onesrow, tt], [vP[0]])
                    S.op("dve", lambda e, h=half: e.tensor_copy(
                        bkb[:, 512 * h:512 * (h + 1)], vP[0][0:128, 0:512]), r=[vP[0]], w=[bkb])

                for ti, (toff, tw) in enumerate(TT):
                    rd = [s2[f] for f in range(NF)] + [s1[f] for f in range(NF)]
                    S.pre("pe", rd, [psA])
                    for fc in range(NF):
                        for oc in range(4):
                            inst = mm(nc.tensor, psA[0:tw, 512 * oc:512 * (oc + 1)],
                                      s1[fc][:, toff:toff + tw],
                                      wres[fc][:, 512 * oc:512 * (oc + 1)],
                                      fc == 0, fc == NF - 1)
                    S.post("pe", inst, rd, [psA])
                    S.op("dve", lambda e, ww=tw: e.reduce_max(
                        qmx[0:ww, :], psA[0:ww, 0:1024].rearrange("p (h c) -> p h c", c=DH),
                        axis=AX.X), r=[psA], w=[qmx])
                    S.op("dve", lambda e, ww=tw: e.tensor_tensor(
                        dq[0:ww, :].rearrange("p (h c) -> p h c", c=DH),
                        psA[0:ww, 0:1024].rearrange("p (h c) -> p h c", c=DH),
                        qmx[0:ww, :].unsqueeze(2).broadcast_to((ww, H, DH)),
                        ALU.subtract), r=[psA, qmx], w=[dq])
                    S.op("dve", lambda e, ww=tw: e.tensor_tensor(
                        dqk[0:ww, :], psA[0:ww, 1024:2048], bkb[0:ww, :], ALU.subtract),
                        r=[psA, bkb], w=[dqk])
                    S.op("act", lambda e, ww=tw: e.activation(
                        dq[0:ww, :], dq[0:ww, :], AF.Exp, scale=0.125), r=[dq], w=[dq])
                    S.op("act", lambda e, ww=tw: e.activation(
                        expk[0:ww, :], dqk[0:ww, :], AF.Exp), r=[dqk], w=[expk])
                    S.op("dve", lambda e, ww=tw: e.reduce_sum(
                        qsum[0:ww, :], dq[0:ww, :].rearrange("p (h c) -> p h c", c=DH),
                        axis=AX.X), r=[dq], w=[qsum])
                    S.op("dve", lambda e, ww=tw: e.reciprocal(
                        qrec[0:ww, :], qsum[0:ww, :]), r=[qsum], w=[qrec])
                    S.op("dve", lambda e, ww=tw: e.tensor_tensor(
                        qsm[0:ww, :].rearrange("p (h c) -> p h c", c=65)[:, :, 0:64],
                        dq[0:ww, :].rearrange("p (h c) -> p h c", c=DH),
                        qrec[0:ww, :].unsqueeze(2).broadcast_to((ww, H, DH)),
                        ALU.mult), r=[dq, qrec], w=[qsm])
                    if ti == 0:
                        S.op("pool", lambda e: e.memset(expk[0:2, :], 0.0), w=[expk])
                    kw = tw if ti < 16 else 2
                    S.pre("pe", [qsm, expk], [ctxP])
                    for hh in range(H):
                        inst = mm(nc.tensor, ctxP[0:65, 64 * hh:64 * (hh + 1)],
                                  qsm[0:kw, 65 * hh:65 * (hh + 1)],
                                  expk[0:kw, 64 * hh:64 * (hh + 1)],
                                  ti == 0 and hh in (0, 8), ti == 16)
                    S.post("pe", inst, [qsm, expk], [ctxP])

                S.op("dve", lambda e: e.tensor_copy(ctxs[:], ctxP[0:65, :]), r=[ctxP], w=[ctxs])
                S.dma("sp", lambda e: e.dma_start(out=cxl.ap(), in_=ctxs[:]), r=[ctxs], w=[cxl])
                S.cc(lambda e: e.collective_compute(
                    "AllReduce", ALU.add, replica_groups=REPL,
                    ins=[cxl.ap().opt()], outs=[cxg.ap().opt()]), r=[cxl], w=[cxg])

                for oc in range(NF):
                    wb = wbuf[oc % 2]
                    S.dma("sp", lambda e, o=oc, wb=wb: e.dma_start(
                        out=wb[:, 0:1024],
                        in_=wa_bf.ap().rearrange("(f p) o -> p f o", p=128)[:, :, 2048 + 128 * o:2048 + 128 * (o + 1)]),
                        r=[wa_bf], w=[wb])
                    for ni, (off, w) in enumerate(NC):
                        pb = vP[ni % 2]
                        S.pre("pe", [wb] + [s1[f] for f in range(NF)], [pb])
                        for fc in range(NF):
                            inst = mm(nc.tensor, pb[:, 0:w], wb[:, 128 * fc:128 * (fc + 1)],
                                      s1[fc][:, off:off + w], fc == 0, fc == NF - 1)
                        S.post("pe", inst, [wb] + [s1[f] for f in range(NF)], [pb])
                        S.op("dve", lambda e, o=oc, of=off, ww=w, p=pb: e.tensor_copy(
                            s2[o][:, of:of + ww], p[:, 0:ww]), r=[pb], w=[s2[oc]])
                S.barrier()
            with contextlib.ExitStack() as ph:
                zbs = ph.enter_context(nc.sbuf_tensor("zbs", [64, 1024], F32))
                ctxn = ph.enter_context(nc.sbuf_tensor("ctxn", [128, 512], BF16))
                ctxs2 = ph.enter_context(nc.sbuf_tensor("ctxs2", [65, 1024], F32))
                bcP2 = ph.enter_context(nc.psum_tensor("bcP2", [128, 512], F32))
                oPa = ph.enter_context(nc.psum_tensor("oPa", [64, 512], F32))
                oPb = ph.enter_context(nc.psum_tensor("oPb", [64, 512], F32))
                woP = [ph.enter_context(nc.psum_tensor(f"woP{i}", [128, 512], F32)) for i in range(2)]
                zrowr = ph.enter_context(nc.sbuf_tensor(f"zrw{next(_uid)}", [1, 1024], F32))
                zr, zrt = zrowr[0:1, 0:1024], zrowr
                S.dma("sp", lambda e: e.dma_start(out=ctxs2[:], in_=cxg.ap()), r=[cxg], w=[ctxs2])
                S.op("dve", lambda e: e.reciprocal(zr, ctxs2[64:65, :]), r=[ctxs2], w=[zrt])
                for half in range(2):
                    ap = zrowr[0:1, 512 * half:512 * (half + 1)]
                    S.pre("pe", [onesrow, zrt], [bcP2])
                    S.post("pe", mm(nc.tensor, bcP2[0:64, 0:512], onesrow[0:1, 0:64],
                                    ap, True, True), [onesrow, zrt], [bcP2])
                    S.op("dve", lambda e, h=half: e.tensor_copy(
                        zbs[0:64, 512 * h:512 * (h + 1)], bcP2[0:64, 0:512]), r=[bcP2], w=[zbs])
                for hh in range(H):
                    ro, co = 64 * (hh % 2), 64 * (hh // 2)
                    S.op("dve", lambda e, h=hh, ro=ro, co=co: e.tensor_tensor(
                        ctxn[ro:ro + 64, co:co + 64], ctxs2[0:64, 64 * h:64 * (h + 1)],
                        zbs[0:64, 64 * h:64 * (h + 1)], ALU.mult),
                        r=[ctxs2, zbs], w=[ctxn])
                for ch in range(NF):
                    for ni, (off, w) in enumerate(NC):
                        S.pre("pe", [ctxn, s2[ch]], [oPa])
                        S.post("pe", mm(nc.tensor, oPa[0:64, 0:w],
                                        ctxn[0:64, 64 * ch:64 * (ch + 1)],
                                        s2[ch][0:64, off:off + w], True, True),
                               [ctxn, s2[ch]], [oPa])
                        S.pre("pe", [ctxn, s2[ch]], [oPb])
                        S.post("pe", mm(nc.tensor, oPb[0:64, 0:w],
                                        ctxn[64:128, 64 * ch:64 * (ch + 1)],
                                        s2[ch][64:128, off:off + w], True, True),
                               [ctxn, s2[ch]], [oPb])
                        S.op("dve", lambda e, c=ch, of=off, ww=w: e.tensor_copy(
                            s2[c][0:64, of:of + ww], oPa[0:64, 0:ww]), r=[oPa], w=[s2[ch]])
                        S.op("dve", lambda e, c=ch, of=off, ww=w: e.tensor_copy(
                            s2[c][64:128, of:of + ww], oPb[0:64, 0:ww]), r=[oPb], w=[s2[ch]])
                for oc in range(NF):
                    wb = wbuf[oc % 2]
                    S.dma("sp", lambda e, o=oc, wb=wb: e.dma_start(
                        out=wb[:, 0:1024],
                        in_=woa_bf.ap().rearrange("(f p) o -> p f o", p=128)[:, :, 128 * o:128 * (o + 1)]),
                        r=[woa_bf], w=[wb])
                    for ni, (off, w) in enumerate(NC):
                        pb = woP[ni % 2]
                        S.pre("pe", [wb] + [s2[f] for f in range(NF)], [pb])
                        for fc in range(NF):
                            inst = mm(nc.tensor, pb[:, 0:w], wb[:, 128 * fc:128 * (fc + 1)],
                                      s2[fc][:, off:off + w], fc == 0, fc == NF - 1)
                        S.post("pe", inst, [wb] + [s2[f] for f in range(NF)], [pb])
                        S.op("dve", lambda e, o=oc, of=off, ww=w, p=pb: e.scalar_tensor_tensor(
                            x8[o][:, of:of + ww], p[:, 0:ww], b_out[:, o:o + 1],
                            x8[o][:, of:of + ww], ALU.add, ALU.add),
                            r=[pb, b_out, x8[oc]], w=[x8[oc]])
                S.barrier()
            rezero_halo()

        def ffn_phase(tag):
          def _qi_scope():
            with contextlib.ExitStack() as ph:
                ln_stats(ph)
                bcPf = ph.enter_context(nc.psum_tensor(f"bf{next(_uid)}", [128, 512], F32))
                mx = dscr
                for off, w in NC:
                    build_bcast(bcPf, bc1, "sum", off, w)
                    S.op("dve", lambda e, ww=w: e.tensor_scalar(
                        bc1[:, 0:ww], bc1[:, 0:ww], 1.0 / D, 0.0, ALU.mult, ALU.add),
                        r=[bc1], w=[bc1])
                    for fc in range(NF):
                        S.op("dve", lambda e, f=fc, o=off, ww=w: e.tensor_tensor(
                            bc2[:, 0:ww], x8[f][:, o:o + ww], bc1[:, 0:ww], ALU.subtract),
                            r=[x8[fc], bc1], w=[bc2])
                        S.op("pool", lambda e, ww=w: e.tensor_tensor(
                            bc2[:, 0:ww], bc2[:, 0:ww], bc2[:, 0:ww], ALU.mult),
                            r=[bc2], w=[bc2])
                        if fc == 0:
                            S.op("dve", lambda e, o=off, ww=w: e.tensor_copy(
                                mx[:, o:o + ww], bc2[:, 0:ww]), r=[bc2], w=[mx])
                        else:
                            S.op("dve", lambda e, o=off, ww=w: e.tensor_tensor(
                                mx[:, o:o + ww], bc2[:, 0:ww], mx[:, o:o + ww], ALU.max),
                                r=[bc2, mx], w=[mx])
                tokmax_to_row(ph, mx, "mx", sqrt=True)
                for off, w in NC:
                    mean_rstd_bc(bcPf, off, w, bc3)
                    build_bcast(bcPf, bc3, "mx", off, w)
                    dsl = dscr[:, off:off + w]
                    S.op("dve", lambda e, ww=w: e.tensor_tensor(
                        bc3[:, 0:ww], bc3[:, 0:ww], bc2[:, 0:ww], ALU.mult),
                        r=[bc3, bc2], w=[bc3])
                    S.op("dve", lambda e, ww=w: e.tensor_scalar(
                        bc3[:, 0:ww], bc3[:, 0:ww], 1e-5, 0.0, ALU.max, ALU.add),
                        r=[bc3], w=[bc3])
                    S.op("dve", lambda e, d=dsl: e.reciprocal(d, bc3[:, 0:w]),
                         r=[bc3], w=[dscr])
                    S.op("dve", lambda e, d=dsl, ww=w: e.tensor_tensor(
                        d, d, bc2[:, 0:ww], ALU.mult), r=[dscr, bc2], w=[dscr])
                    S.op("dve", lambda e, d=dsl: e.tensor_scalar(
                        d, d, 127.0, 0.0, ALU.mult, ALU.add), r=[dscr], w=[dscr])
                    S.op("dve", lambda e, d=dsl, ww=w: e.tensor_tensor(
                        bc2[:, 0:ww], bc1[:, 0:ww], d, ALU.mult),
                        r=[bc1, dscr], w=[bc2])
                    S.op("dve", lambda e, ww=w: e.tensor_scalar(
                        bc3[:, 0:ww], bc3[:, 0:ww], mwb[:, 0:1], 1.0 / 127.0,
                        ALU.mult, ALU.mult), r=[bc3, mwb], w=[bc3])
                    S.op("dve", lambda e, d=dsl, ww=w: e.tensor_copy(
                        bc1[:, 0:ww], d), r=[dscr], w=[bc1])
                    for fc in range(NF):
                        S.op("dve", lambda e, f=fc, o=off, ww=w: e.tensor_tensor(
                            dscr[:, o:o + ww], x8[f][:, o:o + ww], bc1[:, 0:ww], ALU.mult),
                            r=[x8[fc], bc1], w=[dscr])
                        S.op("dve", lambda e, o=off, ww=w: e.tensor_tensor(
                            dscr[:, o:o + ww], dscr[:, o:o + ww], bc2[:, 0:ww], ALU.subtract),
                            r=[dscr, bc2], w=[dscr])
                        S.op("dve", lambda e, o=off, ww=w: e.tensor_scalar(
                            dscr[:, o:o + ww], dscr[:, o:o + ww], RND, -RND, ALU.add, ALU.add),
                            r=[dscr], w=[dscr])
                        S.op("dve", lambda e, f=fc, o=off, ww=w: e.tensor_tensor(
                            s1[f][:, o:o + ww], dscr[:, o:o + ww], bc3[:, 0:ww], ALU.mult),
                            r=[dscr, bc3], w=[s1[fc]])
                S.barrier()
          if not os.environ.get("SKIP_QI"):
            _qi_scope()
          if True:
            if xqstage is not None:
                for fc in range(NF):
                    S.dma("sp", lambda e, f=fc: e.dma_start(
                        out=s1[f][:], in_=xqstage.ap()[128 * f:128 * (f + 1), :]),
                        r=[xqstage], w=[s1[fc]])
                S.barrier()
            with contextlib.ExitStack() as ph:
                hs = [ph.enter_context(nc.sbuf_tensor(f"hs{tag}{i}", [128, T], F32)) for i in range(2)]
                utb = [ph.enter_context(nc.sbuf_tensor(f"ut{tag}{i}", [128, T], F32)) for i in range(2)]
                s2invb = ph.enter_context(nc.sbuf_tensor(f"s2i{tag}", [128, T], F32))
                hP = [ph.enter_context(nc.psum_tensor(f"hP{tag}{i}", [128, 512], F32)) for i in range(4)]
                hmx = dscr
                wsrc = wqstage if wqstage is not None else wq1T
                for hc in range(NHC):
                    wb = wbuf[hc % 2]
                    hcur = hs[hc % 2]
                    ut = utb[hc % 2]
                    S.dma("sp", lambda e, h=hc, wb=wb, ws=wsrc: e.dma_start(
                        out=wb[:, 0:1024],
                        in_=ws.ap().rearrange("(f p) o -> p f o", p=128)[:, :, 128 * h:128 * (h + 1)]),
                        r=[wsrc], w=[wb])
                    for ni, (off, w) in enumerate(NC):
                        pb = hP[ni % 4]
                        S.pre("pe", [wb] + [s1[f] for f in range(NF)], [pb])
                        for fc in range(NF):
                            inst = mm(nc.tensor, pb[:, 0:w], wb[:, 128 * fc:128 * (fc + 1)],
                                      s1[fc][:, off:off + w], fc == 0, fc == NF - 1)
                        S.post("pe", inst, [wb] + [s1[f] for f in range(NF)], [pb])
                        S.op("act", lambda e, of=off, ww=w, p=pb, hc_=hcur: e.copy(
                            hc_[:, of:of + ww], p[:, 0:ww]), r=[pb], w=[hcur])
                    S.op("dve", lambda e, h=hc, hc_=hcur: e.tensor_scalar(
                        ut[:, :], hc_[:, :], convwt[:, 3 * h + 1:3 * h + 2],
                        convbt[:, h:h + 1], ALU.mult, ALU.add),
                        r=[hcur, convwt, convbt], w=[ut])
                    S.op("dve", lambda e, h=hc, hc_=hcur: e.scalar_tensor_tensor(
                        ut[:, 1:T], hc_[:, 0:T - 1], convwt[:, 3 * h:3 * h + 1],
                        ut[:, 1:T], ALU.mult, ALU.add),
                        r=[hcur, convwt, ut], w=[ut])
                    S.op("dve", lambda e, h=hc, hc_=hcur: e.scalar_tensor_tensor(
                        ut[:, 0:T - 1], hc_[:, 1:T], convwt[:, 3 * h + 2:3 * h + 3],
                        ut[:, 0:T - 1], ALU.mult, ALU.add),
                        r=[hcur, convwt, ut], w=[ut])
                    S.op("act", lambda e: e.activation(ut[:, :], ut[:, :], AF.Silu),
                         r=[ut], w=[ut])
                    if os.environ.get("DUMP_PRECONV"):
                        S.dma("sp", lambda e, h=hc, hc_=hcur: e.dma_start(
                            out=hd.ap()[128 * h:128 * (h + 1), :], in_=hc_[:, :]), r=[hcur], w=[hd])
                    else:
                        S.dma("sp", lambda e, h=hc: e.dma_start(
                            out=hd.ap()[128 * h:128 * (h + 1), :], in_=ut[:, :]), r=[ut], w=[hd])
                    S.op("pool", lambda e, hc_=hcur: e.tensor_tensor(
                        hc_[:, :], ut[:, :], ut[:, :], ALU.mult), r=[ut], w=[hcur])
                    if hc == 0:
                        S.op("dve", lambda e, hc_=hcur: e.tensor_copy(hmx[:, :], hc_[:, :]),
                             r=[hcur], w=[hmx])
                    else:
                        S.op("dve", lambda e, hc_=hcur: e.tensor_tensor(
                            hmx[:, :], hc_[:, :], hmx[:, :], ALU.max), r=[hcur, hmx], w=[hmx])
                tokmax_to_row(ph, hmx, "mx", sqrt=True)
                bcPg = ph.enter_context(nc.psum_tensor(f"bg{next(_uid)}", [128, 512], F32))
                for off, w in NC:
                    build_bcast(bcPg, bc1, "mx", off, w)
                    S.op("dve", lambda e, ww=w: e.tensor_scalar(
                        bc1[:, 0:ww], bc1[:, 0:ww], 1e-5, 0.0, ALU.max, ALU.add),
                        r=[bc1], w=[bc1])
                    S.op("dve", lambda e, o=off, ww=w: e.tensor_scalar(
                        s2invb[:, o:o + ww], bc1[:, 0:ww], mwb[:, 1:2], 1.0 / 127.0,
                        ALU.mult, ALU.mult), r=[bc1, mwb], w=[s2invb])
                    S.op("dve", lambda e, ww=w: e.reciprocal(bc1[:, 0:ww], bc1[:, 0:ww]),
                         r=[bc1], w=[bc1])
                    S.op("dve", lambda e, ww=w: e.tensor_scalar(
                        bc1[:, 0:ww], bc1[:, 0:ww], 127.0, 0.0, ALU.mult, ALU.add),
                        r=[bc1], w=[bc1])
                    for hc in range(NHC):
                        hcur = hs[hc % 2]
                        qr = s2[hc % 2]
                        S.dma("sp", lambda e, h=hc, o=off, ww=w, hc_=hcur: e.dma_start(
                            out=hc_[:, o:o + ww], in_=hd.ap()[128 * h:128 * (h + 1), o:o + ww]),
                            r=[hd], w=[hcur])
                        S.op("dve", lambda e, o=off, ww=w, hc_=hcur: e.tensor_tensor(
                            bc2[:, 0:ww], hc_[:, o:o + ww], bc1[:, 0:ww], ALU.mult),
                            r=[hcur, bc1], w=[bc2])
                        S.op("dve", lambda e, o=off, ww=w, q=qr: e.tensor_scalar(
                            q[:, o:o + ww], bc2[:, 0:ww], RND, -RND, ALU.add, ALU.add),
                            r=[bc2], w=[qr])
                        S.dma("sp", lambda e, h=hc, o=off, ww=w, q=qr: e.dma_start(
                            out=xq2d.ap()[128 * h:128 * (h + 1), o:o + ww], in_=q[:, o:o + ww]),
                            r=[qr], w=[xq2d])
                S.barrier()

            with contextlib.ExitStack() as ph:
                oP = [ph.enter_context(nc.psum_tensor(f"oP{tag}{i}", [128, 512], F32)) for i in range(NF)]
                xq2r = [ph.enter_context(nc.sbuf_tensor(f"xr{tag}{i}", [128, 512], BF16)) for i in range(2)]
                for ni, (off, w) in enumerate(NC):
                    for hc in range(NHC):
                        rb, wb = xq2r[hc % 2], wbuf[hc % 2]
                        S.dma("sp", lambda e, h=hc, o=off, ww=w, r_=rb: e.dma_start(
                            out=r_[:, 0:ww], in_=xq2d.ap()[128 * h:128 * (h + 1), o:o + ww]),
                            r=[xq2d], w=[rb])
                        S.dma("pool", lambda e, h=hc, wb=wb: e.dma_start(
                            out=wb[:, 0:1024], in_=wq2T.ap()[128 * h:128 * (h + 1), :]),
                            r=[wq2T], w=[wb])
                        for oc in range(NF):
                            S.pre("pe", [wb, rb], [oP[oc]])
                            S.post("pe", mm(nc.tensor, oP[oc][:, 0:w],
                                            wb[:, 128 * oc:128 * (oc + 1)], rb[:, 0:w],
                                            hc == 0, hc == NHC - 1), [wb, rb], [oP[oc]])
                    for oc in range(NF):
                        S.op("dve", lambda e, o=oc, of=off, ww=w: e.tensor_tensor(
                            bc2[:, 0:ww], oP[o][:, 0:ww], s2invb[:, of:of + ww], ALU.mult),
                            r=[oP[oc], s2invb], w=[bc2])
                        S.op("dve", lambda e, o=oc, of=off, ww=w: e.tensor_tensor(
                            x8[o][:, of:of + ww], bc2[:, 0:ww], x8[o][:, of:of + ww], ALU.add),
                            r=[bc2, x8[oc]], w=[x8[oc]])
                S.barrier()

        if n_phases >= 2:
            ffn_phase("A")
            rezero_halo()

        if n_phases >= 3:
            with contextlib.ExitStack() as ph:
                ln_stats(ph)
                ln_to(ph, s1)
                S.barrier()
            with contextlib.ExitStack() as ph:
                kvt = ph.enter_context(nc.sbuf_tensor("kvt", [128, T], BF16))
                qmax8 = ph.enter_context(nc.sbuf_tensor("qmax8", [128, NF], F32))
                qmaxr = ph.enter_context(nc.sbuf_tensor("qmaxr", [1, NF], F32))
                vres = [ph.enter_context(nc.sbuf_tensor(f"vres{i}", [128, 1024], BF16)) for i in range(NF)]
                knacc = ph.enter_context(nc.sbuf_tensor("knacc", [2, NF], F32))
                kn2s = ph.enter_context(nc.sbuf_tensor("kn2s", [2, 512], F32))
                ksq = ph.enter_context(nc.sbuf_tensor("ksq", [128, 2048], BF16))
                kchunk = ph.enter_context(nc.sbuf_tensor("kchunk", [128, 2048], BF16))
                qkP = [ph.enter_context(nc.psum_tensor(f"qkP{i}", [128, 512], F32)) for i in range(2)]
                vP3 = ph.enter_context(nc.psum_tensor("vP3", [128, 1024], F32))
                trPq = ph.enter_context(nc.psum_tensor("trPq", [128, 512], F32))
                knP = ph.enter_context(nc.psum_tensor("knP", [2, 512], F32))

                for oc in range(16):
                    wb = wbuf[oc % 2]
                    S.dma("sp", lambda e, o=oc, wb=wb: e.dma_start(
                        out=wb[:, 0:1024],
                        in_=wbT.ap().rearrange("(f p) o -> p f o", p=128)[:, :, 128 * o:128 * (o + 1)]),
                        r=[wbT], w=[wb])
                    dst = s2[oc] if oc < 8 else kvt
                    for ni, (off, w) in enumerate(NC):
                        pb = qkP[ni % 2]
                        S.pre("pe", [wb] + [s1[f] for f in range(NF)], [pb])
                        for fc in range(NF):
                            inst = mm(nc.tensor, pb[:, 0:w], wb[:, 128 * fc:128 * (fc + 1)],
                                      s1[fc][:, off:off + w], fc == 0, fc == NF - 1)
                        S.post("pe", inst, [wb] + [s1[f] for f in range(NF)], [pb])
                        S.op("dve", lambda e, d=dst, of=off, ww=w, p=pb: e.tensor_copy(
                            d[:, of:of + ww], p[:, 0:ww]), r=[pb], w=[dst])
                    if oc < 8:
                        S.op("dve", lambda e, o=oc: e.reduce_max(
                            qmax8[:, o:o + 1], s2[o][:, :], axis=AX.X,
                            apply_absolute_value=True), r=[s2[oc]], w=[qmax8])
                    else:
                        S.dma("sp", lambda e, o=oc: e.dma_start(
                            out=kx.ap()[128 * (o - 8):128 * (o - 7), :],
                            in_=kvt[:, OWN0:OWN1]), r=[kvt], w=[kx])
                S.pre("pe", [qmax8, ident], [trPq])
                S.post("pe", mm(nc.tensor, trPq[0:8, 0:128], qmax8[:, 0:8], ident[:],
                                True, True), [qmax8, ident], [trPq])
                S.op("dve", lambda e: e.reduce_max(redc[0:8, 0:1], trPq[0:8, 0:128], axis=AX.X),
                     r=[trPq], w=[redc])
                S.pre("pe", [redc, ident], [trPq])
                S.post("pe", mm(nc.tensor, trPq[0:1, 128:136], redc[0:8, 0:1],
                                ident[0:8, 0:8], True, True), [redc, ident], [trPq])
                S.op("dve", lambda e: e.tensor_copy(qmaxr[0:1, 0:8], trPq[0:1, 128:136]),
                     r=[trPq], w=[qmaxr])

                for fc in range(NF):
                    S.dma("sp", lambda e, f=fc: e.dma_start(
                        out=vres[f][:], in_=wbT.ap()[128 * f:128 * (f + 1), 2048:3072]),
                        r=[wbT], w=[vres[fc]])
                for ti, (toff, tw) in enumerate(TT):
                    rd = [vres[f] for f in range(NF)] + [s1[f] for f in range(NF)]
                    S.pre("pe", rd, [vP3])
                    for fc in range(NF):
                        for oc in range(2):
                            inst = mm(nc.tensor, vP3[0:tw, 512 * oc:512 * (oc + 1)],
                                      s1[fc][:, toff:toff + tw],
                                      vres[fc][:, 512 * oc:512 * (oc + 1)],
                                      fc == 0, fc == NF - 1)
                    S.post("pe", inst, rd, [vP3])
                    S.op("dve", lambda e, ww=tw: e.tensor_copy(
                        kvt[0:ww, 0:1024], vP3[0:ww, :]), r=[vP3], w=[kvt])
                    lo, hi = max(toff, OWN0), min(toff + tw, OWN1)
                    if hi > lo:
                        S.dma("sp", lambda e, lo=lo, hi=hi, to=toff: e.dma_start(
                            out=vx.ap()[lo - OWN0:hi - OWN0, :],
                            in_=kvt[lo - to:hi - to, 0:1024]), r=[kvt], w=[vx])

                S.cc(lambda e: e.collective_compute(
                    "AllGather", ALU.bypass, replica_groups=REPL,
                    ins=[kx.ap().opt()], outs=[kg.ap().opt()]), r=[kx], w=[kg])
                S.cc(lambda e: e.collective_compute(
                    "AllGather", ALU.bypass, replica_groups=REPL,
                    ins=[vx.ap().opt()], outs=[vg.ap().opt()]), r=[vx], w=[vg])

                for r_ in range(2):
                    for c in range(NF):
                        S.dma("sp", lambda e, r2=r_, c=c: e.dma_start(
                            out=kchunk[:], in_=kg.ap()[1024 * r2 + 128 * c:1024 * r2 + 128 * (c + 1), :]),
                            r=[kg], w=[kchunk])
                        S.op("act", lambda e: e.activation(ksq[:], kchunk[:], AF.Square),
                             r=[kchunk], w=[ksq])
                        for n4 in range(4):
                            S.pre("pe", [sel2, ksq], [knP])
                            S.post("pe", mm(nc.tensor, knP[0:2, :], sel2[:],
                                            ksq[:, 512 * n4:512 * (n4 + 1)], True, True),
                                   [sel2, ksq], [knP])
                            S.op("dve", lambda e: e.tensor_copy(
                                kn2s[0:2, :], knP[0:2, :]), r=[knP], w=[kn2s])
                            S.op("dve", lambda e: e.reduce_max(
                                redc[0:2, 0:1], kn2s[0:2, :], axis=AX.X), r=[kn2s], w=[redc])
                            if r_ == 0 and n4 == 0:
                                S.op("dve", lambda e, c=c: e.tensor_copy(
                                    knacc[0:2, c:c + 1], redc[0:2, 0:1]), r=[redc], w=[knacc])
                            else:
                                S.op("dve", lambda e, c=c: e.tensor_tensor(
                                    knacc[0:2, c:c + 1], redc[0:2, 0:1],
                                    knacc[0:2, c:c + 1], ALU.max), r=[redc, knacc], w=[knacc])
                S.pre("pe", [knacc, ident], [trPq])
                S.post("pe", mm(nc.tensor, trPq[0:8, 136:138], knacc[0:2, 0:8],
                                ident[0:2, 0:2], True, True), [knacc, ident], [trPq])
                S.op("dve", lambda e: e.reduce_max(redc[0:8, 0:1], trPq[0:8, 136:138], axis=AX.X),
                     r=[trPq], w=[redc])
                S.pre("pe", [redc, ident], [trPq])
                S.post("pe", mm(nc.tensor, trPq[0:1, 138:146], redc[0:8, 0:1],
                                ident[0:8, 0:8], True, True), [redc, ident], [trPq])
                S.op("act", lambda e: e.activation(scal[0:1, 8:16], trPq[0:1, 138:146], AF.Sqrt),
                     r=[trPq], w=[scal])
                S.op("dve", lambda e: e.tensor_tensor(scal[0:1, 8:16], scal[0:1, 8:16],
                                                      qmaxr[0:1, 0:8], ALU.mult),
                     r=[scal, qmaxr], w=[scal])
                S.op("dve", lambda e: e.tensor_scalar(scal[0:1, 8:16], scal[0:1, 8:16],
                                                      -0.125, 0.0, ALU.mult, ALU.add),
                     r=[scal], w=[scal])
                S.barrier()
            with contextlib.ExitStack() as ph:
                ktile = [ph.enter_context(nc.sbuf_tensor("ktile0", [128, 4096], BF16))] * 2
                vstat = ph.enter_context(nc.sbuf_tensor("vstat", [128, 2 * 32 * 65], BF16))
                bbt = ph.enter_context(nc.sbuf_tensor("bbt", [128, 8], F32))
                zbs2 = ph.enter_context(nc.sbuf_tensor("zbs2", [64, 1024], F32))
                ptile = [ph.enter_context(nc.sbuf_tensor(f"pt{i}", [128, 2048], BF16)) for i in range(2)]
                zrowr = ph.enter_context(nc.sbuf_tensor(f"zrw{next(_uid)}", [1, 1024], F32))
                zrt = zrowr
                with nc.psum_tensor("bP3", [128, 512], F32) as bP3:
                    S.pre("pe", [onesrow, scal], [bP3])
                    S.post("pe", mm(nc.tensor, bP3[0:128, 0:8], onesrow[0:1, :],
                                    scal[0:1, 8:16], True, True), [onesrow, scal], [bP3])
                    S.op("dve", lambda e: e.tensor_copy(bbt[:, 0:8], bP3[0:128, 0:8]),
                         r=[bP3], w=[bbt])
                    S.barrier()
                STp = ph.enter_context(nc.psum_tensor("STp", [128, 2048], F32))
                pvP = [ph.enter_context(nc.psum_tensor(f"pvP{i}", [65, 512], F32)) for i in range(2)]
                zbP = ph.enter_context(nc.psum_tensor("zbP", [64, 1024], F32))
                S.op("pool", lambda e: e.memset(
                    vstat[:].rearrange("p (k c) -> p k c", c=65)[:, :, 64:65], 1.0), w=[vstat])

                for pr in range(8):
                    kt = ktile[pr % 2]
                    for r_ in range(2):
                        for hl in range(2):
                            S.dma("sp", lambda e, p=pr, r2=r_, hl=hl, kt=kt: e.dma_start(
                                out=kt[64 * hl:64 * (hl + 1), 2048 * r2:2048 * (r2 + 1)],
                                in_=kg.ap()[1024 * r2 + 64 * (2 * p + hl):1024 * r2 + 64 * (2 * p + hl) + 64, :]),
                                r=[kg], w=[kt])
                    for hl in range(2):
                        S.dma("sp", lambda e, p=pr, hl=hl: e.dma_start(
                            out=vstat[:].rearrange("p (k c) -> p k c", c=65)[:, 32 * hl:32 * (hl + 1), 0:64],
                            in_=vg.ap().rearrange("(k p) c -> p k c", p=128)[:, :, 64 * (2 * p + hl):64 * (2 * p + hl) + 64]),
                            r=[vg], w=[vstat])
                    for qi, (qoff, qw) in enumerate(NC):
                        def pv_emit(kb2, pt):
                            kb = 2 * kb2
                            S.pre("pe", [vstat, pt], [pvP[0]])
                            mm(nc.tensor, pvP[0][0:65, 0:qw],
                               vstat[:, 65 * kb:65 * (kb + 1)], pt[:, 0:qw],
                               kb == 0, False)
                            ia = mm(nc.tensor, pvP[0][0:65, 0:qw],
                                    vstat[:, 65 * (kb + 1):65 * (kb + 2)],
                                    pt[:, 1024:1024 + qw], False, kb == 30)
                            S.post("pe", ia, [vstat, pt], [pvP[0]])
                            S.pre("pe", [vstat, pt], [pvP[1]])
                            mm(nc.tensor, pvP[1][0:65, 0:qw],
                               vstat[:, 65 * (32 + kb):65 * (32 + kb) + 65],
                               pt[:, 512:512 + qw], kb == 0, False)
                            ib = mm(nc.tensor, pvP[1][0:65, 0:qw],
                                    vstat[:, 65 * (33 + kb):65 * (33 + kb) + 65],
                                    pt[:, 1536:1536 + qw], False, kb == 30)
                            S.post("pe", ib, [vstat, pt], [pvP[1]])
                        for kb2 in range(16):
                            kb = 2 * kb2
                            st = STp
                            S.pre("pe", [kt, s2[pr]], [st])
                            mm(nc.tensor, st[:, 0:qw], kt[0:64, 128 * kb:128 * (kb + 1)],
                               s2[pr][0:64, qoff:qoff + qw], True, True)
                            nc.tensor.matmul(
                                st[:, 512:512 + qw], kt[64:128, 128 * kb:128 * (kb + 1)],
                                s2[pr][64:128, qoff:qoff + qw], start=True, stop=True,
                                tile_position=(64, 0), skip_group_check=True)
                            mm(nc.tensor, st[:, 1024:1024 + qw],
                               kt[0:64, 128 * (kb + 1):128 * (kb + 2)],
                               s2[pr][0:64, qoff:qoff + qw], True, True)
                            i2 = nc.tensor.matmul(
                                st[:, 1536:1536 + qw], kt[64:128, 128 * (kb + 1):128 * (kb + 2)],
                                s2[pr][64:128, qoff:qoff + qw], start=True, stop=True,
                                tile_position=(64, 0), skip_group_check=True)
                            S.post("pe", i2, [kt, s2[pr]], [st])
                            pt = ptile[kb2 % 2]
                            if qw == 512:
                                S.op("act", lambda e, st=st, pt=pt, p=pr: e.activation(
                                    pt[:, :], st[:, :], AF.Exp, bias=bbt[:, p:p + 1],
                                    scale=0.125), r=[st, bbt], w=[pt])
                            else:
                                S.op("act", lambda e, st=st, pt=pt, p=pr, qw=qw: e.activation(
                                    pt[:, 0:2048].rearrange("p (s c) -> p s c", c=512)[:, :, 0:qw],
                                    st[:, 0:2048].rearrange("p (s c) -> p s c", c=512)[:, :, 0:qw],
                                    AF.Exp, bias=bbt[:, p:p + 1], scale=0.125),
                                    r=[st, bbt], w=[pt])
                            pv_emit(kb2, pt)
                        S.op("dve", lambda e, qw=qw: e.reciprocal(
                            zrowr[0:1, 0:qw], pvP[0][64:65, 0:qw]), r=[pvP[0]], w=[zrt])
                        S.op("dve", lambda e, qw=qw: e.reciprocal(
                            zrowr[0:1, 512:512 + qw], pvP[1][64:65, 0:qw]), r=[pvP[1]], w=[zrt])
                        for hl in range(2):
                            S.pre("pe", [onesrow, zrt], [zbP])
                            S.post("pe", mm(nc.tensor, zbP[0:64, 512 * hl:512 * hl + qw],
                                            onesrow[0:1, 0:64],
                                            zrowr[0:1, 512 * hl:512 * hl + qw], True, True),
                                   [onesrow, zrt], [zbP])
                        S.op("act", lambda e, qw=qw: e.copy(
                            zbs2[0:64, 0:qw], zbP[0:64, 0:qw]), r=[zbP], w=[zbs2])
                        S.op("act", lambda e, qw=qw: e.copy(
                            zbs2[0:64, 512:512 + qw], zbP[0:64, 512:512 + qw]), r=[zbP], w=[zbs2])
                        S.op("dve", lambda e, p=pr, qo=qoff, qw=qw: e.tensor_tensor(
                            s2[p][0:64, qo:qo + qw], pvP[0][0:64, 0:qw],
                            zbs2[0:64, 0:qw], ALU.mult), r=[pvP[0], zbs2], w=[s2[pr]])
                        S.op("dve", lambda e, p=pr, qo=qoff, qw=qw: e.tensor_tensor(
                            s2[p][64:128, qo:qo + qw], pvP[1][0:64, 0:qw],
                            zbs2[0:64, 512:512 + qw], ALU.mult), r=[pvP[1], zbs2], w=[s2[pr]])
                S.barrier()
            with contextlib.ExitStack() as ph:
                woP = [ph.enter_context(nc.psum_tensor(f"woB{i}", [128, 512], F32)) for i in range(2)]
                for oc in range(NF):
                    wb = wbuf[oc % 2]
                    S.dma("sp", lambda e, o=oc, wb=wb: e.dma_start(
                        out=wb[:, 0:1024],
                        in_=wobT.ap().rearrange("(f p) o -> p f o", p=128)[:, :, 128 * o:128 * (o + 1)]),
                        r=[wobT], w=[wb])
                    for ni, (off, w) in enumerate(NC):
                        pb = woP[ni % 2]
                        S.pre("pe", [wb] + [s2[f] for f in range(NF)], [pb])
                        for fc in range(NF):
                            inst = mm(nc.tensor, pb[:, 0:w], wb[:, 128 * fc:128 * (fc + 1)],
                                      s2[fc][:, off:off + w], fc == 0, fc == NF - 1)
                        S.post("pe", inst, [wb] + [s2[f] for f in range(NF)], [pb])
                        S.op("dve", lambda e, o=oc, of=off, ww=w, p=pb: e.tensor_tensor(
                            x8[o][:, of:of + ww], p[:, 0:ww], x8[o][:, of:of + ww], ALU.add),
                            r=[pb, x8[oc]], w=[x8[oc]])
                S.barrier()
            rezero_halo()

        if n_phases >= 4:
            ffn_phase("B")

        if dump:
            if os.environ.get("DUMP_XQ1"):
                for fc in range(0, NF):
                    S.dma("sp", lambda e, f=fc: e.dma_start(
                        out=xq2d.ap()[128 * f:128 * (f + 1), :], in_=s1[f][:, :]),
                        r=[s1[fc]], w=[xq2d])
            dhd = nc.dram_tensor("dhd", [DFF, T], F32, kind="ExternalOutput")
            dxq = nc.dram_tensor("dxq", [DFF, T], BF16, kind="ExternalOutput")
            dwq = nc.dram_tensor("dwq", [D, DFF], BF16, kind="ExternalOutput")
            with nc.sbuf_tensor("dmpw", [128, DFF], BF16) as dmpw:
                for f in range(NF):
                    S.dma("sp", lambda e, f=f: e.dma_start(
                        out=dmpw[:], in_=wq1T.ap()[128 * f:128 * (f + 1), :]), r=[wq1T], w=[dmpw])
                    S.dma("sp", lambda e, f=f: e.dma_start(
                        out=dwq.ap()[128 * f:128 * (f + 1), :], in_=dmpw[:]), r=[dmpw], w=[dwq])
            with nc.sbuf_tensor("dmp", [128, T], F32) as dmp, \
                 nc.sbuf_tensor("dmpb", [128, T], BF16) as dmpb:
                for h in range(NHC):
                    S.dma("sp", lambda e, h=h: e.dma_start(
                        out=dmp[:], in_=hd.ap()[128 * h:128 * (h + 1), :]), r=[hd], w=[dmp])
                    S.dma("sp", lambda e, h=h: e.dma_start(
                        out=dhd.ap()[128 * h:128 * (h + 1), :], in_=dmp[:]), r=[dmp], w=[dhd])
                    S.dma("sp", lambda e, h=h: e.dma_start(
                        out=dmpb[:], in_=xq2d.ap()[128 * h:128 * (h + 1), :]), r=[xq2d], w=[dmpb])
                    S.dma("sp", lambda e, h=h: e.dma_start(
                        out=dxq.ap()[128 * h:128 * (h + 1), :], in_=dmpb[:]), r=[dmpb], w=[dxq])
                S.barrier()
        for fc in range(NF):
            S.dma("sp", lambda e, f=fc: e.dma_start(
                out=yout.ap()[128 * f:128 * (f + 1), :], in_=x8[f][:, OWN0:OWN1]),
                r=[x8[fc]], w=[yout])
        S.barrier()

    return nc


_CACHE = {}


def _stage(inputs):
    x = np.asarray(inputs["L0_attn_a_x"], dtype=np.float32)
    base = {
        "wqkv_a": np.ascontiguousarray(inputs["L0_attn_a_w_qkv"], dtype=np.float32),
        "wout_a": np.ascontiguousarray(inputs["L0_attn_a_w_out"], dtype=np.float32),
        "bout_a": np.ascontiguousarray(inputs["L0_attn_a_b_out"], dtype=np.float32),
        "wqkv_b": np.ascontiguousarray(inputs["L0_attn_b_w_qkv"], dtype=np.float32),
        "wout_b": np.ascontiguousarray(inputs["L0_attn_b_w_out"], dtype=np.float32),
        "w1": np.ascontiguousarray(inputs["L0_ffn_w1"], dtype=np.float32),
        "convw": np.ascontiguousarray(
            np.asarray(inputs["L0_ffn_conv_w"], dtype=np.float32).reshape(DFF, 3).T),
        "convb": np.ascontiguousarray(inputs["L0_ffn_conv_b"], dtype=np.float32),
        "w2": np.ascontiguousarray(inputs["L0_ffn_w2"], dtype=np.float32),
        "ident": np.eye(128, dtype=np.float32),
        "sel2": np.concatenate([
            np.repeat(np.float32([[1], [0]]), 64, axis=0),
            np.repeat(np.float32([[0], [1]]), 64, axis=0)], axis=1).astype(np.float32),
        "onescol_in": np.ones((128, 1), dtype=np.float32),
        "onesrow_in": np.ones((128, 128), dtype=np.float32),
    }
    import os as _os
    if _os.environ.get("BYPASS_XQ"):
        import ml_dtypes as _md
        base["xqstage"] = np.load("/root/problem/_xqstage.npy").view(_md.bfloat16)
    if _os.environ.get("BYPASS_WQ"):
        import ml_dtypes as _md
        w1f = np.asarray(inputs.get("L0_ffn_w1", base["w1"]), np.float64)
        tern = np.clip(np.rint(w1f / np.abs(w1f).mean()), -1, 1)
        base["wqstage"] = np.ascontiguousarray(tern.T).astype(_md.bfloat16)
    in_maps = []
    for c in range(8):
        b, hh = c // 2, c % 2
        s, e = 2048 * hh, 2048 * hh + 2048
        pad = np.zeros((T, D), dtype=np.float32)
        lo, hi = s - 2, e + 2
        clo, chi = max(lo, 0), min(hi, 4096)
        pad[clo - lo:chi - lo, :] = x[b, clo:chi, :]
        hm = np.ones((128, 4), dtype=np.float32)
        if hh == 0:
            hm[:, 0:2] = 0.0
        else:
            hm[:, 2:4] = 0.0
        m = dict(base)
        m["xin"] = np.ascontiguousarray(pad.T)
        m["hmask"] = hm
        in_maps.append(m)
    return in_maps


def kernel(**inputs):
    n_phases = int(inputs.pop("_n_phases", 4))
    dump = bool(inputs.pop("_dump", False))
    key = (n_phases, dump)
    if key not in _CACHE:
        _CACHE[key] = build(n_phases, dump)
    nc = _CACHE[key]
    in_maps = _stage(inputs)
    res = run_bass_kernel_spmd(nc, in_maps, core_ids=list(range(8)))
    out = np.empty((4, 4096, 1024), dtype=np.float32)
    for c in range(8):
        b, hh = c // 2, c % 2
        out[b, 2048 * hh:2048 * (hh + 1), :] = res.results[c]["y"].T
    if dump:
        kernel._dbg = res.results
    return out


# version 05
# speedup vs baseline: 1.1687x; 1.0059x over previous
import contextlib
import itertools

import numpy as np
import concourse.bass as bass
import concourse.mybir as mybir
from concourse.bass_utils import run_bass_kernel_spmd

F32 = mybir.dt.float32
BF16 = mybir.dt.bfloat16
AF = mybir.ActivationFunctionType
ALU = mybir.AluOpType
AX = mybir.AxisListType

T = 2052
OWN0, OWN1 = 2, 2050
D = 1024
H = 16
DH = 64
DFF = 4096
NF = 8
NHC = 32
RND = float(1.5 * 2.0 ** 23)
TT = [(128 * i, 128) for i in range(16)] + [(2048, 4)]
NC = [(0, 512), (512, 512), (1024, 512), (1536, 512), (2048, 4)]
REPL = [[0, 1], [2, 3], [4, 5], [6, 7]]
_uid = itertools.count()


NSLOT = 12


class Sched:

    def __init__(self, nc, sems):
        self.nc = nc
        self.sems = sems
        self.cnt = {s: 0 for s in sems}
        self.hwm = {}
        self.writers = {}
        self.readers = {}
        self.eng = {"pe": nc.tensor, "act": nc.scalar, "dve": nc.vector,
                    "pool": nc.gpsimd, "sp": nc.sync}
        self.slot = {"sp": 0, "pool": 0}

    def _wait(self, eng, sem, val):
        if val <= 0 or self.hwm.get((eng, sem), 0) >= val:
            return
        self.hwm[(eng, sem)] = val
        self.eng[eng].wait_ge(self.sems[sem], val)

    def pre(self, eng, reads, writes):
        for b in reads:
            for t in self.writers.get(id(b), ()):
                self._wait(eng, t[0], t[1])
        for b in writes:
            for t in self.readers.get(id(b), ()):
                self._wait(eng, t[0], t[1])

    def post(self, sem, inst, reads, writes, amt=1):
        inst.then_inc(self.sems[sem], amt)
        self.cnt[sem] += amt
        tok = (sem, self.cnt[sem])
        for b in writes:
            self.writers.setdefault(id(b), []).append(tok)
            self.readers[id(b)] = []
        for b in reads:
            self.readers.setdefault(id(b), []).append(tok)

    def op(self, eng, f, r=(), w=()):
        self.pre(eng, r, w)
        self.post(eng, f(self.eng[eng]), r, w)

    def dma(self, q, f, r=(), w=()):
        s = self.slot[q]
        self.slot[q] = (s + 1) % NSLOT
        sem = f"{q}d{s}"
        self._wait(q, sem, self.cnt[sem])
        self.pre(q, r, w)
        self.post(sem, f(self.eng[q]), r, w, 16)

    def cc(self, f, r=(), w=()):
        self.pre("pool", r, w)
        self._wait("pool", "cc", self.cnt["cc"])
        self.post("cc", f(self.eng["pool"]), r, w)

    def barrier(self):
        for eng in self.eng:
            for sem, v in self.cnt.items():
                self._wait(eng, sem, v)
        self.writers.clear()
        self.readers.clear()


def build(n_phases=4, dump=False):
    import os
    n_pro = int(os.environ.get('N_PRO', '9'))
    nc = bass.Bass("TRN2", target_bir_lowering=False)

    xin = nc.dram_tensor("xin", [D, T], F32, kind="ExternalInput")
    wqkv_a = nc.dram_tensor("wqkv_a", [D, 3 * D], F32, kind="ExternalInput")
    wout_a = nc.dram_tensor("wout_a", [D, D], F32, kind="ExternalInput")
    bout_a = nc.dram_tensor("bout_a", [D], F32, kind="ExternalInput")
    wqkv_b = nc.dram_tensor("wqkv_b", [3 * D, D], F32, kind="ExternalInput")
    wout_b = nc.dram_tensor("wout_b", [D, D], F32, kind="ExternalInput")
    w1 = nc.dram_tensor("w1", [DFF, D], F32, kind="ExternalInput")
    convw = nc.dram_tensor("convw", [3, DFF], F32, kind="ExternalInput")
    convb = nc.dram_tensor("convb", [DFF], F32, kind="ExternalInput")
    w2 = nc.dram_tensor("w2", [D, DFF], F32, kind="ExternalInput")
    ident_in = nc.dram_tensor("ident", [128, 128], F32, kind="ExternalInput")
    hmask_in = nc.dram_tensor("hmask", [128, 4], F32, kind="ExternalInput")
    sel2_in = nc.dram_tensor("sel2", [128, 2], F32, kind="ExternalInput")
    onescol_in = nc.dram_tensor("onescol_in", [128, 1], F32, kind="ExternalInput")
    onesrow_in = nc.dram_tensor("onesrow_in", [128, 128], F32, kind="ExternalInput")
    xqstage = (nc.dram_tensor("xqstage", [D, T], BF16, kind="ExternalInput")
               if os.environ.get("BYPASS_XQ") else None)
    wqstage = (nc.dram_tensor("wqstage", [D, DFF], BF16, kind="ExternalInput")
               if os.environ.get("BYPASS_WQ") else None)
    yout = nc.dram_tensor("y", [D, 2048], F32, kind="ExternalOutput")

    wa_bf = nc.dram_tensor("wa_bf", [D, 3 * D], BF16)
    woa_bf = nc.dram_tensor("woa_bf", [D, D], BF16)
    wb_nat = nc.dram_tensor("wb_nat", [3 * D, D], BF16)
    wbT = nc.dram_tensor("wbT", [D, 3 * D], BF16)
    wob_nat = nc.dram_tensor("wob_nat", [D, D], BF16)
    wobT = nc.dram_tensor("wobT", [D, D], BF16)
    w1q_nat = nc.dram_tensor("w1q_nat", [DFF, D], BF16)
    wq1T = nc.dram_tensor("wq1T", [D, DFF], BF16)
    w2q_nat = nc.dram_tensor("w2q_nat", [D, DFF], BF16)
    wq2T = nc.dram_tensor("wq2T", [DFF, D], BF16)
    kx = nc.dram_tensor("kx", [D, 2048], BF16)
    vx = nc.dram_tensor("vx", [2048, D], BF16)
    kg = nc.dram_tensor("kg", [2 * D, 2048], BF16)
    vg = nc.dram_tensor("vg", [4096, D], BF16)
    cxl = nc.dram_tensor("cxl", [65, D], F32)
    cxg = nc.dram_tensor("cxg", [65, D], F32)
    hd = nc.dram_tensor("hd", [DFF, T], F32)
    xq2d = nc.dram_tensor("xq2d", [DFF, T], BF16)

    stk = contextlib.ExitStack()
    sb = lambda n, s, d: stk.enter_context(nc.sbuf_tensor(n, s, d))
    sems = {n: stk.enter_context(nc.semaphore(n + "_s"))
            for n in ("pe", "act", "dve", "pool", "sp", "cc")}
    for q in ("sp", "pool"):
        for s in range(NSLOT):
            sems[f"{q}d{s}"] = stk.enter_context(nc.semaphore(f"{q}d{s}_s"))
    S = Sched(nc, sems)

    x8 = [sb(f"x{i}", [128, T], F32) for i in range(NF)]
    s1 = [sb(f"s1_{i}", [128, T], BF16) for i in range(NF)]
    s2 = [sb(f"s2_{i}", [128, T], BF16) for i in range(NF)]
    dscr = sb("dscr", [128, T], F32)
    xbf = [sb(f"xbf{i}", [128, 512], BF16) for i in range(2)]
    sqbf = [sb(f"sqbf{i}", [128, 512], BF16) for i in range(2)]
    wbuf = [sb(f"wbuf{i}", [128, 2048], BF16) for i in range(2)]
    ident = sb("identt", [128, 128], F32)
    hmask = sb("hmaskt", [128, 4], F32)
    sel2 = sb("sel2t", [128, 2], BF16)
    onescol = sb("onescolt", [128, 1], BF16)
    onesrow = sb("onesrowt", [128, 128], F32)
    b_out = sb("b_outt", [128, NF], F32)
    convwt = sb("convwt", [128, 3 * NHC], F32)
    convbt = sb("convbt", [128, NHC], F32)
    mwrow = sb("mwrow", [1, 4], F32)
    scal = sb("scal", [1, 16], F32)
    redc = sb("redc", [128, 1], F32)
    bc1 = sb("bc1", [128, 512], F32)
    bc2 = sb("bc2", [128, 512], F32)
    bc3 = sb("bc3", [128, 512], F32)
    rowt = sb("rowt", [65, T], F32)
    mwb = sb("mwb", [128, 4], F32)
    _ROWP = {"sum": 0, "sumsq": 32, "mx": 64, "bk": 64}

    def R(name, c0=0, c1=T):
        p = _ROWP[name]
        return rowt[p:p + 1, c0:c1], rowt

    def RP(name):
        return _ROWP[name]

    def mm(e, out, lhsT, rhs, start, stop):
        return e.matmul(out, lhsT, rhs, start=start, stop=stop,
                        skip_group_check=True)

    with stk:
        S.dma("sp", lambda e: e.dma_start(out=ident[:], in_=ident_in.ap()), w=[ident])
        S.dma("sp", lambda e: e.dma_start(out=hmask[:], in_=hmask_in.ap()), w=[hmask])
        S.dma("pool", lambda e: e.dma_start(out=sel2[:], in_=sel2_in.ap()), w=[sel2])
        S.dma("pool", lambda e: e.dma_start(out=onescol[:], in_=onescol_in.ap()), w=[onescol])
        S.dma("sp", lambda e: e.dma_start(out=onesrow[:], in_=onesrow_in.ap()), w=[onesrow])
        for c in range(NF):
            S.dma("sp", lambda e, c=c: e.dma_start(out=x8[c][:], in_=xin.ap()[128 * c:128 * (c + 1), :]), w=[x8[c]])
        S.dma("pool", lambda e: e.dma_start(out=wa_bf.ap(), in_=wqkv_a.ap()), w=[wa_bf])
        S.dma("pool", lambda e: e.dma_start(out=woa_bf.ap(), in_=wout_a.ap()), w=[woa_bf])
        S.dma("pool", lambda e: e.dma_start(out=wb_nat.ap(), in_=wqkv_b.ap()), w=[wb_nat])
        S.dma("pool", lambda e: e.dma_start(out=wob_nat.ap(), in_=wout_b.ap()), w=[wob_nat])

        with nc.psum_tensor("prT", [128, 512], F32) as prT, \
             nc.sbuf_tensor("rload", [1, 512], F32) as rload:
            def load_cols(dram_ap_1d, n, dst, colmap):
                for j in range(n):
                    S.dma("sp", lambda e, j=j: e.dma_start(
                        out=rload[0:1, 0:128],
                        in_=dram_ap_1d[128 * j:128 * (j + 1)].rearrange("(o p) -> o p", o=1)),
                        r=[], w=[rload])
                    S.pre("pe", [rload, ident], [prT])
                    S.post("pe", mm(nc.tensor, prT[0:128, 0:1], rload[0:1, 0:128],
                                    ident[0:1, 0:1], True, True), [rload, ident], [prT])
                    S.op("dve", lambda e, j=j: e.tensor_copy(
                        dst[:, colmap(j):colmap(j) + 1], prT[0:128, 0:1]),
                        r=[prT], w=[dst])
            if n_pro >= 2:
                load_cols(bout_a.ap(), NF, b_out, lambda j: j)
                load_cols(convb.ap(), NHC, convbt, lambda j: j)
                for k in range(3):
                    cwk = convw.ap()[k:k + 1, :].rearrange("one n -> (one n)")
                    load_cols(cwk, NHC, convwt, lambda j, k=k: 3 * j + k)

        slabs = [(wbuf[i // 4], 512 * (i % 4)) for i in range(8)]

        def transpose_copy(jobs):
            for b_ in range(0, len(jobs), 8):
                grp = jobs[b_:b_ + 8]
                for i, (sa, da, dt_) in enumerate(grp):
                    t_, o_ = slabs[i]
                    S.dma("sp", lambda e, sa=sa, t_=t_, o_=o_: e.dma_start_transpose(
                        t_[:, o_:o_ + 512], sa), r=[], w=[t_])
                for i, (sa, da, dt_) in enumerate(grp):
                    t_, o_ = slabs[i]
                    S.dma("pool", lambda e, da=da, t_=t_, o_=o_: e.dma_start(
                        out=da, in_=t_[:, o_:o_ + 512]), r=[t_], w=[dt_])
        if n_pro >= 3:
            jobs = []
            for fc in range(NF):
                for oc in range(6):
                    jobs.append((wb_nat.ap()[512 * oc:512 * (oc + 1), 128 * fc:128 * (fc + 1)],
                                 wbT.ap()[128 * fc:128 * (fc + 1), 512 * oc:512 * (oc + 1)], wbT))
                for oc in range(2):
                    jobs.append((wob_nat.ap()[512 * oc:512 * (oc + 1), 128 * fc:128 * (fc + 1)],
                                 wobT.ap()[128 * fc:128 * (fc + 1), 512 * oc:512 * (oc + 1)], wobT))
            S.pre("sp", [wb_nat, wob_nat], [])
            transpose_copy(jobs)

        with nc.sbuf_tensor("absacc", [128, 48], F32) as absacc, \
             nc.sbuf_tensor("qbf0", [128, 2048], BF16) as qbf0, \
             nc.sbuf_tensor("qbf1", [128, 2048], BF16) as qbf1, \
             nc.sbuf_tensor("wfa", [128, 2048], F32) as wfa, \
             nc.sbuf_tensor("wfb", [128, 2048], F32) as wfb, \
             nc.psum_tensor("pT0", [128, 512], F32) as pT0:
            wfs = [wfa, wfb]
            qbfs = [qbf0, qbf1]
            w2v = w2.ap().rearrange("a (b c) -> (a b) c", b=2)
            w2qv = w2q_nat.ap().rearrange("a (b c) -> (a b) c", b=2)
            for wi, (wap, nch, wid) in (enumerate(((w1.ap(), NHC, D), (w2v, 16, 2048))) if n_pro >= 4 else []):
                for c in range(nch):
                    wf = wfs[c % 2]
                    S.dma("sp", lambda e, w_=wap, c=c, ww=wid, wf=wf: e.dma_start(
                        out=wf[:, 0:ww], in_=w_[128 * c:128 * (c + 1), :]), w=[wf])
                    col = c if wi == 0 else NHC + c
                    S.op("dve", lambda e, col=col, ww=wid, wf=wf: e.reduce_sum(
                        absacc[:, col:col + 1], wf[:, 0:ww], axis=AX.X,
                        apply_absolute_value=True), r=[wf], w=[absacc])
            for wi, (lo, n, total) in (enumerate(((0, NHC, DFF * D), (NHC, 16, D * DFF))) if n_pro >= 4 else []):
                S.pre("pe", [absacc, ident], [pT0])
                S.post("pe", mm(nc.tensor, pT0[0:n, 0:128], absacc[:, lo:lo + n],
                                ident[:], True, True), [absacc, ident], [pT0])
                S.op("dve", lambda e, n=n: e.reduce_sum(
                    redc[0:n, 0:1], pT0[0:n, 0:128], axis=AX.X), r=[pT0], w=[redc])
                S.pre("pe", [redc, ident], [pT0])
                S.post("pe", mm(nc.tensor, pT0[0:1, 128:128 + n], redc[0:n, 0:1],
                                ident[0:n, 0:n], True, True), [redc, ident], [pT0])
                S.op("dve", lambda e, wi=wi, n=n: e.reduce_sum(
                    scal[0:1, wi:wi + 1], pT0[0:1, 128:128 + n], axis=AX.X), r=[pT0], w=[scal])
                S.op("dve", lambda e, wi=wi, tot=total: e.tensor_scalar(
                    mwrow[0:1, wi:wi + 1], scal[0:1, wi:wi + 1], 1.0 / tot, 1e-5,
                    ALU.mult, ALU.max), r=[scal], w=[mwrow])
            if n_pro < 4:
                S.op("pool", lambda e: e.memset(mwrow[0:1, :], 1.0), w=[mwrow])
            S.op("dve", lambda e: e.reciprocal(mwrow[0:1, 2:4], mwrow[0:1, 0:2]),
                 r=[mwrow], w=[mwrow])
            S.pre("pe", [onesrow, mwrow], [pT0])
            S.post("pe", mm(nc.tensor, pT0[0:128, 256:260], onesrow[0:1, :],
                            mwrow[0:1, 0:4], True, True), [onesrow, mwrow], [pT0])
            S.op("dve", lambda e: e.tensor_copy(mwb[:, 0:4], pT0[0:128, 256:260]),
                 r=[pT0], w=[mwb])
            for wi, (wap, wq_ap, nch, wid) in (enumerate(
                    ((w1.ap(), w1q_nat.ap(), NHC, D), (w2v, w2qv, 16, 2048))) if n_pro >= 4 else []):
                for c in range(nch):
                    wf = wfs[c % 2]
                    qb = qbfs[c % 2]
                    S.dma("sp", lambda e, w_=wap, c=c, ww=wid, wf=wf: e.dma_start(
                        out=wf[:, 0:ww], in_=w_[128 * c:128 * (c + 1), :]), w=[wf])
                    S.op("dve", lambda e, ww=wid, wi=wi, wf=wf: e.tensor_scalar(
                        wf[:, 0:ww], wf[:, 0:ww], mwb[:, 2 + wi:3 + wi], RND,
                        ALU.mult, ALU.add), r=[wf, mwb], w=[wf])
                    S.op("dve", lambda e, ww=wid, wf=wf: e.tensor_scalar(
                        wf[:, 0:ww], wf[:, 0:ww], -RND, -1.0, ALU.add, ALU.max),
                        r=[wf], w=[wf])
                    S.op("dve", lambda e, ww=wid, wf=wf, qb=qb: e.tensor_scalar(
                        qb[:, 0:ww], wf[:, 0:ww], 1.0, 0.0, ALU.min, ALU.add),
                        r=[wf], w=[qb])
                    wqt = w1q_nat if wi == 0 else w2q_nat
                    S.dma("sp", lambda e, q_=wq_ap, c=c, ww=wid, qb=qb: e.dma_start(
                        out=q_[128 * c:128 * (c + 1), :], in_=qb[:, 0:ww]),
                        r=[qb], w=[wqt])
        if n_pro >= 5:
            jobs = []
            for fc in range(NF):
                for oc in range(8):
                    jobs.append((w1q_nat.ap()[512 * oc:512 * (oc + 1), 128 * fc:128 * (fc + 1)],
                                 wq1T.ap()[128 * fc:128 * (fc + 1), 512 * oc:512 * (oc + 1)], wq1T))
            for hc in range(NHC):
                for oc in range(2):
                    jobs.append((w2q_nat.ap()[512 * oc:512 * (oc + 1), 128 * hc:128 * (hc + 1)],
                                 wq2T.ap()[128 * hc:128 * (hc + 1), 512 * oc:512 * (oc + 1)], wq2T))
            S.pre("sp", [w1q_nat, w2q_nat], [])
            transpose_copy(jobs)

        with nc.psum_tensor("bkP", [1, 512], F32) as bkP:
            for half in (range(2) if n_pro >= 6 else []):
                for fc in range(NF):
                    S.dma("sp", lambda e, f=fc, h=half: e.dma_start(
                        out=wbuf[1][:, 0:512],
                        in_=wa_bf.ap()[128 * f:128 * (f + 1), D + 512 * h:D + 512 * (h + 1)]),
                        r=[wa_bf], w=[wbuf[1]])
                    S.op("act", lambda e: e.activation(sqbf[0][:, 0:512], wbuf[1][:, 0:512], AF.Square),
                         r=[wbuf[1]], w=[sqbf[0]])
                    S.pre("pe", [onescol, sqbf[0]], [bkP])
                    S.post("pe", mm(nc.tensor, bkP[0:1, :], onescol[:], sqbf[0][:, 0:512],
                                    fc == 0, fc == NF - 1), [onescol, sqbf[0]], [bkP])
                ap, tt = R("bk", 512 * half, 512 * (half + 1))
                S.op("act", lambda e, ap=ap: e.activation(ap, bkP[0:1, :], AF.Sqrt),
                     r=[bkP], w=[tt])
        ap, tt = R("bk", 0, 1024)
        S.op("dve", lambda e, ap=ap: e.tensor_scalar(
            ap, ap, 32.5, 0.0, ALU.mult, ALU.add), r=[tt], w=[tt])

        def ln_stats(ph):
            statP = [ph.enter_context(nc.psum_tensor(f"stP{next(_uid)}", [1, 512], F32)) for _ in range(2)]
            for off, w in NC:
                for fc in range(NF):
                    xb, sq = xbf[fc % 2], sqbf[fc % 2]
                    S.op("dve", lambda e, f=fc, o=off, ww=w, xb=xb: e.tensor_copy(
                        xb[:, 0:ww], x8[f][:, o:o + ww]), r=[x8[fc]], w=[xb])
                    S.op("act", lambda e, ww=w, xb=xb, sq=sq: e.activation(
                        sq[:, 0:ww], xb[:, 0:ww], AF.Square), r=[xb], w=[sq])
                    S.pre("pe", [onescol, xb], [statP[0]])
                    S.post("pe", mm(nc.tensor, statP[0][0:1, 0:w], onescol[:], xb[:, 0:w],
                                    fc == 0, fc == NF - 1), [onescol, xb], [statP[0]])
                    S.pre("pe", [onescol, sq], [statP[1]])
                    S.post("pe", mm(nc.tensor, statP[1][0:1, 0:w], onescol[:], sq[:, 0:w],
                                    fc == 0, fc == NF - 1), [onescol, sq], [statP[1]])
                ap, tt = R("sum", off, off + w)
                S.op("dve", lambda e, ap=ap, ww=w: e.tensor_copy(ap, statP[0][0:1, 0:ww]),
                     r=[statP[0]], w=[tt])
                ap, tt = R("sumsq", off, off + w)
                S.op("dve", lambda e, ap=ap, ww=w: e.tensor_copy(ap, statP[1][0:1, 0:ww]),
                     r=[statP[1]], w=[tt])


        def mean_rstd_bc(bcP, off, w, scratch):
            build_bcast(bcP, bc1, "sum", off, w)
            build_bcast(bcP, bc2, "sumsq", off, w)
            S.op("dve", lambda e, ww=w: e.tensor_scalar(
                bc1[:, 0:ww], bc1[:, 0:ww], 1.0 / D, 0.0, ALU.mult, ALU.add),
                r=[bc1], w=[bc1])
            S.op("dve", lambda e, ww=w: e.tensor_scalar(
                bc2[:, 0:ww], bc2[:, 0:ww], 1.0 / D, 1e-5, ALU.mult, ALU.add),
                r=[bc2], w=[bc2])
            S.op("act", lambda e, ww=w: e.activation(
                scratch[:, 0:ww], bc1[:, 0:ww], AF.Square), r=[bc1], w=[scratch])
            S.op("dve", lambda e, ww=w: e.tensor_tensor(
                bc2[:, 0:ww], bc2[:, 0:ww], scratch[:, 0:ww], ALU.subtract),
                r=[bc2, scratch], w=[bc2])
            S.op("act", lambda e, ww=w: e.activation(
                bc2[:, 0:ww], bc2[:, 0:ww], AF.Sqrt), r=[bc2], w=[bc2])
            S.op("dve", lambda e, ww=w: e.reciprocal(bc2[:, 0:ww], bc2[:, 0:ww]),
                 r=[bc2], w=[bc2])

        def build_bcast(bcP, bc, rowname, off, w):
            ap, tt = R(rowname, off, off + w)
            p = RP(rowname)
            S.pre("pe", [onesrow, tt], [bcP])
            S.post("pe", mm(nc.tensor, bcP[0:128, 0:w], onesrow[p:p + 1, :], ap, True, True),
                   [onesrow, tt], [bcP])
            S.op("dve", lambda e, ww=w: e.tensor_copy(bc[:, 0:ww], bcP[0:128, 0:ww]),
                 r=[bcP], w=[bc])

        def ln_to(ph, dest):
            bcP = ph.enter_context(nc.psum_tensor(f"lnb{next(_uid)}", [128, 512], F32))
            for off, w in NC:
                mean_rstd_bc(bcP, off, w, bc3)
                for fc in range(NF):
                    S.op("dve", lambda e, f=fc, o=off, ww=w: e.tensor_tensor(
                        dscr[:, o:o + ww], x8[f][:, o:o + ww], bc1[:, 0:ww], ALU.subtract),
                        r=[x8[fc], bc1], w=[dscr])
                    S.op("dve", lambda e, f=fc, o=off, ww=w: e.tensor_tensor(
                        dest[f][:, o:o + ww], dscr[:, o:o + ww], bc2[:, 0:ww], ALU.mult),
                        r=[dscr, bc2], w=[dest[fc]])

        def rezero_halo():
            for fc in range(NF):
                S.op("dve", lambda e, f=fc: e.tensor_tensor(
                    x8[f][:, 0:2], x8[f][:, 0:2], hmask[:, 0:2], ALU.mult),
                    r=[x8[fc], hmask], w=[x8[fc]])
                S.op("dve", lambda e, f=fc: e.tensor_tensor(
                    x8[f][:, 2050:2052], x8[f][:, 2050:2052], hmask[:, 2:4], ALU.mult),
                    r=[x8[fc], hmask], w=[x8[fc]])

        def tokmax_to_row(ph, src, dstrow, sqrt=False):
            trP = ph.enter_context(nc.psum_tensor(f"tk{next(_uid)}", [128, 512], F32))
            for toff, tw in TT:
                S.pre("pe", [src, ident], [trP])
                S.post("pe", mm(nc.tensor, trP[0:tw, 0:128], src[:, toff:toff + tw],
                                ident[:], True, True), [src, ident], [trP])
                S.op("dve", lambda e, ww=tw: e.reduce_max(
                    redc[0:ww, 0:1], trP[0:ww, 0:128], axis=AX.X), r=[trP], w=[redc])
                S.pre("pe", [redc, ident], [trP])
                S.post("pe", mm(nc.tensor, trP[0:1, 128:128 + tw], redc[0:tw, 0:1],
                                ident[0:tw, 0:tw], True, True), [redc, ident], [trP])
                ap, tt = R(dstrow, toff, toff + tw)
                if sqrt:
                    S.op("act", lambda e, ap=ap, ww=tw: e.activation(
                        ap, trP[0:1, 128:128 + ww], AF.Sqrt), r=[trP], w=[tt])
                else:
                    S.op("dve", lambda e, ap=ap, ww=tw: e.tensor_copy(
                        ap, trP[0:1, 128:128 + ww]), r=[trP], w=[tt])

        if n_phases >= 1 and not os.environ.get("SKIP_P1"):
            with contextlib.ExitStack() as ph:
                ln_stats(ph)
                ln_to(ph, s1)
                S.barrier()
            with contextlib.ExitStack() as ph:
                qsm = ph.enter_context(nc.sbuf_tensor("qsm", [128, H * 65], BF16))
                expk = ph.enter_context(nc.sbuf_tensor("expk", [128, D], BF16))
                qmx = ph.enter_context(nc.sbuf_tensor("qmx", [128, H], F32))
                qsum = ph.enter_context(nc.sbuf_tensor("qsum", [128, H], F32))
                qrec = ph.enter_context(nc.sbuf_tensor("qrec", [128, H], F32))
                dq = ph.enter_context(nc.sbuf_tensor("dq", [128, 1024], F32))
                dqk = ph.enter_context(nc.sbuf_tensor("dqk", [128, 1024], F32))
                bkb = ph.enter_context(nc.sbuf_tensor("bkb", [128, 1024], F32))
                ctxs = ph.enter_context(nc.sbuf_tensor("ctxs", [65, 1024], F32))
                vP = [ph.enter_context(nc.psum_tensor(f"vP{i}", [128, 512], F32)) for i in range(2)]
                psA = ph.enter_context(nc.psum_tensor("psA", [128, 2048], F32))
                ctxP = ph.enter_context(nc.psum_tensor("ctxP", [65, 1024], F32))

                wres = [s2[f][:, 0:2048] for f in range(NF)]
                for fc in range(NF):
                    S.dma("sp", lambda e, f=fc: e.dma_start(
                        out=wres[f], in_=wa_bf.ap()[128 * f:128 * (f + 1), 0:2048]),
                        r=[wa_bf], w=[s2[fc]])
                S.op("pool", lambda e: e.memset(
                    qsm[:].rearrange("p (h c) -> p h c", c=65)[:, :, 64:65], 1.0), w=[qsm])
                for half in range(2):
                    ap, tt = R("bk", 512 * half, 512 * (half + 1))
                    S.pre("pe", [onesrow, tt], [vP[0]])
                    S.post("pe", mm(nc.tensor, vP[0][0:128, 0:512], onesrow[64:65, :], ap,
                                    True, True), [onesrow, tt], [vP[0]])
                    S.op("dve", lambda e, h=half: e.tensor_copy(
                        bkb[:, 512 * h:512 * (h + 1)], vP[0][0:128, 0:512]), r=[vP[0]], w=[bkb])

                for ti, (toff, tw) in enumerate(TT):
                    rd = [s2[f] for f in range(NF)] + [s1[f] for f in range(NF)]
                    S.pre("pe", rd, [psA])
                    for fc in range(NF):
                        for oc in range(4):
                            inst = mm(nc.tensor, psA[0:tw, 512 * oc:512 * (oc + 1)],
                                      s1[fc][:, toff:toff + tw],
                                      wres[fc][:, 512 * oc:512 * (oc + 1)],
                                      fc == 0, fc == NF - 1)
                    S.post("pe", inst, rd, [psA])
                    S.op("dve", lambda e, ww=tw: e.reduce_max(
                        qmx[0:ww, :], psA[0:ww, 0:1024].rearrange("p (h c) -> p h c", c=DH),
                        axis=AX.X), r=[psA], w=[qmx])
                    S.op("dve", lambda e, ww=tw: e.tensor_tensor(
                        dq[0:ww, :].rearrange("p (h c) -> p h c", c=DH),
                        psA[0:ww, 0:1024].rearrange("p (h c) -> p h c", c=DH),
                        qmx[0:ww, :].unsqueeze(2).broadcast_to((ww, H, DH)),
                        ALU.subtract), r=[psA, qmx], w=[dq])
                    S.op("dve", lambda e, ww=tw: e.tensor_tensor(
                        dqk[0:ww, :], psA[0:ww, 1024:2048], bkb[0:ww, :], ALU.subtract),
                        r=[psA, bkb], w=[dqk])
                    S.op("act", lambda e, ww=tw: e.activation(
                        dq[0:ww, :], dq[0:ww, :], AF.Exp, scale=0.125), r=[dq], w=[dq])
                    S.op("act", lambda e, ww=tw: e.activation(
                        expk[0:ww, :], dqk[0:ww, :], AF.Exp), r=[dqk], w=[expk])
                    S.op("dve", lambda e, ww=tw: e.reduce_sum(
                        qsum[0:ww, :], dq[0:ww, :].rearrange("p (h c) -> p h c", c=DH),
                        axis=AX.X), r=[dq], w=[qsum])
                    S.op("dve", lambda e, ww=tw: e.reciprocal(
                        qrec[0:ww, :], qsum[0:ww, :]), r=[qsum], w=[qrec])
                    S.op("dve", lambda e, ww=tw: e.tensor_tensor(
                        qsm[0:ww, :].rearrange("p (h c) -> p h c", c=65)[:, :, 0:64],
                        dq[0:ww, :].rearrange("p (h c) -> p h c", c=DH),
                        qrec[0:ww, :].unsqueeze(2).broadcast_to((ww, H, DH)),
                        ALU.mult), r=[dq, qrec], w=[qsm])
                    if ti == 0:
                        S.op("pool", lambda e: e.memset(expk[0:2, :], 0.0), w=[expk])
                    kw = tw if ti < 16 else 2
                    S.pre("pe", [qsm, expk], [ctxP])
                    for hh in range(H):
                        inst = mm(nc.tensor, ctxP[0:65, 64 * hh:64 * (hh + 1)],
                                  qsm[0:kw, 65 * hh:65 * (hh + 1)],
                                  expk[0:kw, 64 * hh:64 * (hh + 1)],
                                  ti == 0 and hh in (0, 8), ti == 16)
                    S.post("pe", inst, [qsm, expk], [ctxP])

                S.op("dve", lambda e: e.tensor_copy(ctxs[:], ctxP[0:65, :]), r=[ctxP], w=[ctxs])
                S.dma("sp", lambda e: e.dma_start(out=cxl.ap(), in_=ctxs[:]), r=[ctxs], w=[cxl])
                S.cc(lambda e: e.collective_compute(
                    "AllReduce", ALU.add, replica_groups=REPL,
                    ins=[cxl.ap().opt()], outs=[cxg.ap().opt()]), r=[cxl], w=[cxg])

                for oc in range(NF):
                    wb = wbuf[oc % 2]
                    S.dma("sp", lambda e, o=oc, wb=wb: e.dma_start(
                        out=wb[:, 0:1024],
                        in_=wa_bf.ap().rearrange("(f p) o -> p f o", p=128)[:, :, 2048 + 128 * o:2048 + 128 * (o + 1)]),
                        r=[wa_bf], w=[wb])
                    for ni, (off, w) in enumerate(NC):
                        pb = vP[ni % 2]
                        S.pre("pe", [wb] + [s1[f] for f in range(NF)], [pb])
                        for fc in range(NF):
                            inst = mm(nc.tensor, pb[:, 0:w], wb[:, 128 * fc:128 * (fc + 1)],
                                      s1[fc][:, off:off + w], fc == 0, fc == NF - 1)
                        S.post("pe", inst, [wb] + [s1[f] for f in range(NF)], [pb])
                        S.op("dve", lambda e, o=oc, of=off, ww=w, p=pb: e.tensor_copy(
                            s2[o][:, of:of + ww], p[:, 0:ww]), r=[pb], w=[s2[oc]])
                S.barrier()
            with contextlib.ExitStack() as ph:
                zbs = ph.enter_context(nc.sbuf_tensor("zbs", [64, 1024], F32))
                ctxn = ph.enter_context(nc.sbuf_tensor("ctxn", [128, 512], BF16))
                ctxs2 = ph.enter_context(nc.sbuf_tensor("ctxs2", [65, 1024], F32))
                bcP2 = ph.enter_context(nc.psum_tensor("bcP2", [128, 512], F32))
                oPa = ph.enter_context(nc.psum_tensor("oPa", [64, 512], F32))
                oPb = ph.enter_context(nc.psum_tensor("oPb", [64, 512], F32))
                woP = [ph.enter_context(nc.psum_tensor(f"woP{i}", [128, 512], F32)) for i in range(2)]
                zrowr = ph.enter_context(nc.sbuf_tensor(f"zrw{next(_uid)}", [1, 1024], F32))
                zr, zrt = zrowr[0:1, 0:1024], zrowr
                S.dma("sp", lambda e: e.dma_start(out=ctxs2[:], in_=cxg.ap()), r=[cxg], w=[ctxs2])
                S.op("dve", lambda e: e.reciprocal(zr, ctxs2[64:65, :]), r=[ctxs2], w=[zrt])
                for half in range(2):
                    ap = zrowr[0:1, 512 * half:512 * (half + 1)]
                    S.pre("pe", [onesrow, zrt], [bcP2])
                    S.post("pe", mm(nc.tensor, bcP2[0:64, 0:512], onesrow[0:1, 0:64],
                                    ap, True, True), [onesrow, zrt], [bcP2])
                    S.op("dve", lambda e, h=half: e.tensor_copy(
                        zbs[0:64, 512 * h:512 * (h + 1)], bcP2[0:64, 0:512]), r=[bcP2], w=[zbs])
                for hh in range(H):
                    ro, co = 64 * (hh % 2), 64 * (hh // 2)
                    S.op("dve", lambda e, h=hh, ro=ro, co=co: e.tensor_tensor(
                        ctxn[ro:ro + 64, co:co + 64], ctxs2[0:64, 64 * h:64 * (h + 1)],
                        zbs[0:64, 64 * h:64 * (h + 1)], ALU.mult),
                        r=[ctxs2, zbs], w=[ctxn])
                for ch in range(NF):
                    for ni, (off, w) in enumerate(NC):
                        S.pre("pe", [ctxn, s2[ch]], [oPa])
                        S.post("pe", mm(nc.tensor, oPa[0:64, 0:w],
                                        ctxn[0:64, 64 * ch:64 * (ch + 1)],
                                        s2[ch][0:64, off:off + w], True, True),
                               [ctxn, s2[ch]], [oPa])
                        S.pre("pe", [ctxn, s2[ch]], [oPb])
                        S.post("pe", mm(nc.tensor, oPb[0:64, 0:w],
                                        ctxn[64:128, 64 * ch:64 * (ch + 1)],
                                        s2[ch][64:128, off:off + w], True, True),
                               [ctxn, s2[ch]], [oPb])
                        S.op("dve", lambda e, c=ch, of=off, ww=w: e.tensor_copy(
                            s2[c][0:64, of:of + ww], oPa[0:64, 0:ww]), r=[oPa], w=[s2[ch]])
                        S.op("dve", lambda e, c=ch, of=off, ww=w: e.tensor_copy(
                            s2[c][64:128, of:of + ww], oPb[0:64, 0:ww]), r=[oPb], w=[s2[ch]])
                for oc in range(NF):
                    wb = wbuf[oc % 2]
                    S.dma("sp", lambda e, o=oc, wb=wb: e.dma_start(
                        out=wb[:, 0:1024],
                        in_=woa_bf.ap().rearrange("(f p) o -> p f o", p=128)[:, :, 128 * o:128 * (o + 1)]),
                        r=[woa_bf], w=[wb])
                    for ni, (off, w) in enumerate(NC):
                        pb = woP[ni % 2]
                        S.pre("pe", [wb] + [s2[f] for f in range(NF)], [pb])
                        for fc in range(NF):
                            inst = mm(nc.tensor, pb[:, 0:w], wb[:, 128 * fc:128 * (fc + 1)],
                                      s2[fc][:, off:off + w], fc == 0, fc == NF - 1)
                        S.post("pe", inst, [wb] + [s2[f] for f in range(NF)], [pb])
                        S.op("dve", lambda e, o=oc, of=off, ww=w, p=pb: e.scalar_tensor_tensor(
                            x8[o][:, of:of + ww], p[:, 0:ww], b_out[:, o:o + 1],
                            x8[o][:, of:of + ww], ALU.add, ALU.add),
                            r=[pb, b_out, x8[oc]], w=[x8[oc]])
                S.barrier()
            rezero_halo()

        def ffn_phase(tag):
          def _qi_scope():
            with contextlib.ExitStack() as ph:
                ln_stats(ph)
                bcPf = ph.enter_context(nc.psum_tensor(f"bf{next(_uid)}", [128, 512], F32))
                mx = dscr
                for off, w in NC:
                    build_bcast(bcPf, bc1, "sum", off, w)
                    S.op("dve", lambda e, ww=w: e.tensor_scalar(
                        bc1[:, 0:ww], bc1[:, 0:ww], 1.0 / D, 0.0, ALU.mult, ALU.add),
                        r=[bc1], w=[bc1])
                    for fc in range(NF):
                        S.op("dve", lambda e, f=fc, o=off, ww=w: e.tensor_tensor(
                            bc2[:, 0:ww], x8[f][:, o:o + ww], bc1[:, 0:ww], ALU.subtract),
                            r=[x8[fc], bc1], w=[bc2])
                        S.op("pool", lambda e, ww=w: e.tensor_tensor(
                            bc2[:, 0:ww], bc2[:, 0:ww], bc2[:, 0:ww], ALU.mult),
                            r=[bc2], w=[bc2])
                        if fc == 0:
                            S.op("dve", lambda e, o=off, ww=w: e.tensor_copy(
                                mx[:, o:o + ww], bc2[:, 0:ww]), r=[bc2], w=[mx])
                        else:
                            S.op("dve", lambda e, o=off, ww=w: e.tensor_tensor(
                                mx[:, o:o + ww], bc2[:, 0:ww], mx[:, o:o + ww], ALU.max),
                                r=[bc2, mx], w=[mx])
                tokmax_to_row(ph, mx, "mx", sqrt=True)
                for off, w in NC:
                    mean_rstd_bc(bcPf, off, w, bc3)
                    build_bcast(bcPf, bc3, "mx", off, w)
                    dsl = dscr[:, off:off + w]
                    S.op("dve", lambda e, ww=w: e.tensor_tensor(
                        bc3[:, 0:ww], bc3[:, 0:ww], bc2[:, 0:ww], ALU.mult),
                        r=[bc3, bc2], w=[bc3])
                    S.op("dve", lambda e, ww=w: e.tensor_scalar(
                        bc3[:, 0:ww], bc3[:, 0:ww], 1e-5, 0.0, ALU.max, ALU.add),
                        r=[bc3], w=[bc3])
                    S.op("dve", lambda e, d=dsl: e.reciprocal(d, bc3[:, 0:w]),
                         r=[bc3], w=[dscr])
                    S.op("dve", lambda e, d=dsl, ww=w: e.tensor_tensor(
                        d, d, bc2[:, 0:ww], ALU.mult), r=[dscr, bc2], w=[dscr])
                    S.op("dve", lambda e, d=dsl: e.tensor_scalar(
                        d, d, 127.0, 0.0, ALU.mult, ALU.add), r=[dscr], w=[dscr])
                    S.op("dve", lambda e, d=dsl, ww=w: e.tensor_tensor(
                        bc2[:, 0:ww], bc1[:, 0:ww], d, ALU.mult),
                        r=[bc1, dscr], w=[bc2])
                    S.op("dve", lambda e, ww=w: e.tensor_scalar(
                        bc3[:, 0:ww], bc3[:, 0:ww], mwb[:, 0:1], 1.0 / 127.0,
                        ALU.mult, ALU.mult), r=[bc3, mwb], w=[bc3])
                    S.op("dve", lambda e, d=dsl, ww=w: e.tensor_copy(
                        bc1[:, 0:ww], d), r=[dscr], w=[bc1])
                    for fc in range(NF):
                        S.op("dve", lambda e, f=fc, o=off, ww=w: e.tensor_tensor(
                            dscr[:, o:o + ww], x8[f][:, o:o + ww], bc1[:, 0:ww], ALU.mult),
                            r=[x8[fc], bc1], w=[dscr])
                        S.op("dve", lambda e, o=off, ww=w: e.tensor_tensor(
                            dscr[:, o:o + ww], dscr[:, o:o + ww], bc2[:, 0:ww], ALU.subtract),
                            r=[dscr, bc2], w=[dscr])
                        S.op("dve", lambda e, o=off, ww=w: e.tensor_scalar(
                            dscr[:, o:o + ww], dscr[:, o:o + ww], RND, -RND, ALU.add, ALU.add),
                            r=[dscr], w=[dscr])
                        S.op("dve", lambda e, f=fc, o=off, ww=w: e.tensor_tensor(
                            s1[f][:, o:o + ww], dscr[:, o:o + ww], bc3[:, 0:ww], ALU.mult),
                            r=[dscr, bc3], w=[s1[fc]])
                S.barrier()
          if not os.environ.get("SKIP_QI"):
            _qi_scope()
          if True:
            if xqstage is not None:
                for fc in range(NF):
                    S.dma("sp", lambda e, f=fc: e.dma_start(
                        out=s1[f][:], in_=xqstage.ap()[128 * f:128 * (f + 1), :]),
                        r=[xqstage], w=[s1[fc]])
                S.barrier()
            with contextlib.ExitStack() as ph:
                hs = [ph.enter_context(nc.sbuf_tensor(f"hs{tag}{i}", [128, T], F32)) for i in range(2)]
                utb = [ph.enter_context(nc.sbuf_tensor(f"ut{tag}{i}", [128, T], F32)) for i in range(2)]
                s2invb = ph.enter_context(nc.sbuf_tensor(f"s2i{tag}", [128, T], F32))
                hP = [ph.enter_context(nc.psum_tensor(f"hP{tag}{i}", [128, 512], F32)) for i in range(4)]
                hmx = dscr
                wsrc = wqstage if wqstage is not None else wq1T
                for hc in range(NHC):
                    wb = wbuf[hc % 2]
                    hcur = hs[hc % 2]
                    ut = utb[hc % 2]
                    S.dma("sp", lambda e, h=hc, wb=wb, ws=wsrc: e.dma_start(
                        out=wb[:, 0:1024],
                        in_=ws.ap().rearrange("(f p) o -> p f o", p=128)[:, :, 128 * h:128 * (h + 1)]),
                        r=[wsrc], w=[wb])
                    for ni, (off, w) in enumerate(NC):
                        pb = hP[ni % 4]
                        S.pre("pe", [wb] + [s1[f] for f in range(NF)], [pb])
                        for fc in range(NF):
                            inst = mm(nc.tensor, pb[:, 0:w], wb[:, 128 * fc:128 * (fc + 1)],
                                      s1[fc][:, off:off + w], fc == 0, fc == NF - 1)
                        S.post("pe", inst, [wb] + [s1[f] for f in range(NF)], [pb])
                        S.op("act", lambda e, of=off, ww=w, p=pb, hc_=hcur: e.copy(
                            hc_[:, of:of + ww], p[:, 0:ww]), r=[pb], w=[hcur])
                    S.op("dve", lambda e, h=hc, hc_=hcur: e.tensor_scalar(
                        ut[:, :], hc_[:, :], convwt[:, 3 * h + 1:3 * h + 2],
                        convbt[:, h:h + 1], ALU.mult, ALU.add),
                        r=[hcur, convwt, convbt], w=[ut])
                    S.op("dve", lambda e, h=hc, hc_=hcur: e.scalar_tensor_tensor(
                        ut[:, 1:T], hc_[:, 0:T - 1], convwt[:, 3 * h:3 * h + 1],
                        ut[:, 1:T], ALU.mult, ALU.add),
                        r=[hcur, convwt, ut], w=[ut])
                    S.op("dve", lambda e, h=hc, hc_=hcur: e.scalar_tensor_tensor(
                        ut[:, 0:T - 1], hc_[:, 1:T], convwt[:, 3 * h + 2:3 * h + 3],
                        ut[:, 0:T - 1], ALU.mult, ALU.add),
                        r=[hcur, convwt, ut], w=[ut])
                    S.op("act", lambda e: e.activation(ut[:, :], ut[:, :], AF.Silu),
                         r=[ut], w=[ut])
                    if os.environ.get("DUMP_PRECONV"):
                        S.dma("sp", lambda e, h=hc, hc_=hcur: e.dma_start(
                            out=hd.ap()[128 * h:128 * (h + 1), :], in_=hc_[:, :]), r=[hcur], w=[hd])
                    else:
                        S.dma("sp", lambda e, h=hc: e.dma_start(
                            out=hd.ap()[128 * h:128 * (h + 1), :], in_=ut[:, :]), r=[ut], w=[hd])
                    S.op("pool", lambda e, hc_=hcur: e.tensor_tensor(
                        hc_[:, :], ut[:, :], ut[:, :], ALU.mult), r=[ut], w=[hcur])
                    if hc == 0:
                        S.op("dve", lambda e, hc_=hcur: e.tensor_copy(hmx[:, :], hc_[:, :]),
                             r=[hcur], w=[hmx])
                    else:
                        S.op("dve", lambda e, hc_=hcur: e.tensor_tensor(
                            hmx[:, :], hc_[:, :], hmx[:, :], ALU.max), r=[hcur, hmx], w=[hmx])
                tokmax_to_row(ph, hmx, "mx", sqrt=True)
                bcPg = ph.enter_context(nc.psum_tensor(f"bg{next(_uid)}", [128, 512], F32))
                for off, w in NC:
                    build_bcast(bcPg, bc1, "mx", off, w)
                    S.op("dve", lambda e, ww=w: e.tensor_scalar(
                        bc1[:, 0:ww], bc1[:, 0:ww], 1e-5, 0.0, ALU.max, ALU.add),
                        r=[bc1], w=[bc1])
                    S.op("dve", lambda e, o=off, ww=w: e.tensor_scalar(
                        s2invb[:, o:o + ww], bc1[:, 0:ww], mwb[:, 1:2], 1.0 / 127.0,
                        ALU.mult, ALU.mult), r=[bc1, mwb], w=[s2invb])
                    S.op("dve", lambda e, ww=w: e.reciprocal(bc1[:, 0:ww], bc1[:, 0:ww]),
                         r=[bc1], w=[bc1])
                    S.op("dve", lambda e, ww=w: e.tensor_scalar(
                        bc1[:, 0:ww], bc1[:, 0:ww], 127.0, 0.0, ALU.mult, ALU.add),
                        r=[bc1], w=[bc1])
                    for hc in range(NHC):
                        hcur = hs[hc % 2]
                        qr = s2[hc % 2]
                        S.dma("sp", lambda e, h=hc, o=off, ww=w, hc_=hcur: e.dma_start(
                            out=hc_[:, o:o + ww], in_=hd.ap()[128 * h:128 * (h + 1), o:o + ww]),
                            r=[hd], w=[hcur])
                        S.op("dve", lambda e, o=off, ww=w, hc_=hcur: e.tensor_tensor(
                            bc2[:, 0:ww], hc_[:, o:o + ww], bc1[:, 0:ww], ALU.mult),
                            r=[hcur, bc1], w=[bc2])
                        S.op("dve", lambda e, o=off, ww=w, q=qr: e.tensor_scalar(
                            q[:, o:o + ww], bc2[:, 0:ww], RND, -RND, ALU.add, ALU.add),
                            r=[bc2], w=[qr])
                        S.dma("sp", lambda e, h=hc, o=off, ww=w, q=qr: e.dma_start(
                            out=xq2d.ap()[128 * h:128 * (h + 1), o:o + ww], in_=q[:, o:o + ww]),
                            r=[qr], w=[xq2d])
                S.barrier()

            with contextlib.ExitStack() as ph:
                oP = [ph.enter_context(nc.psum_tensor(f"oP{tag}{i}", [128, 512], F32)) for i in range(NF)]
                xq2r = [ph.enter_context(nc.sbuf_tensor(f"xr{tag}{i}", [128, 512], BF16)) for i in range(2)]
                for ni, (off, w) in enumerate(NC):
                    for hc in range(NHC):
                        rb, wb = xq2r[hc % 2], wbuf[hc % 2]
                        S.dma("sp", lambda e, h=hc, o=off, ww=w, r_=rb: e.dma_start(
                            out=r_[:, 0:ww], in_=xq2d.ap()[128 * h:128 * (h + 1), o:o + ww]),
                            r=[xq2d], w=[rb])
                        S.dma("pool", lambda e, h=hc, wb=wb: e.dma_start(
                            out=wb[:, 0:1024], in_=wq2T.ap()[128 * h:128 * (h + 1), :]),
                            r=[wq2T], w=[wb])
                        for oc in range(NF):
                            S.pre("pe", [wb, rb], [oP[oc]])
                            S.post("pe", mm(nc.tensor, oP[oc][:, 0:w],
                                            wb[:, 128 * oc:128 * (oc + 1)], rb[:, 0:w],
                                            hc == 0, hc == NHC - 1), [wb, rb], [oP[oc]])
                    for oc in range(NF):
                        S.op("dve", lambda e, o=oc, of=off, ww=w: e.tensor_tensor(
                            bc2[:, 0:ww], oP[o][:, 0:ww], s2invb[:, of:of + ww], ALU.mult),
                            r=[oP[oc], s2invb], w=[bc2])
                        S.op("dve", lambda e, o=oc, of=off, ww=w: e.tensor_tensor(
                            x8[o][:, of:of + ww], bc2[:, 0:ww], x8[o][:, of:of + ww], ALU.add),
                            r=[bc2, x8[oc]], w=[x8[oc]])
                S.barrier()

        if n_phases >= 2:
            ffn_phase("A")
            rezero_halo()

        if n_phases >= 3:
            with contextlib.ExitStack() as ph:
                ln_stats(ph)
                ln_to(ph, s1)
                S.barrier()
            with contextlib.ExitStack() as ph:
                kvt = ph.enter_context(nc.sbuf_tensor("kvt", [128, T], BF16))
                qmax8 = ph.enter_context(nc.sbuf_tensor("qmax8", [128, NF], F32))
                qmaxr = ph.enter_context(nc.sbuf_tensor("qmaxr", [1, NF], F32))
                vres = [ph.enter_context(nc.sbuf_tensor(f"vres{i}", [128, 1024], BF16)) for i in range(NF)]
                knacc = ph.enter_context(nc.sbuf_tensor("knacc", [2, NF], F32))
                kn2s = ph.enter_context(nc.sbuf_tensor("kn2s", [2, 512], F32))
                ksq = ph.enter_context(nc.sbuf_tensor("ksq", [128, 2048], BF16))
                kchunk = ph.enter_context(nc.sbuf_tensor("kchunk", [128, 2048], BF16))
                qkP = [ph.enter_context(nc.psum_tensor(f"qkP{i}", [128, 512], F32)) for i in range(2)]
                vP3 = ph.enter_context(nc.psum_tensor("vP3", [128, 1024], F32))
                trPq = ph.enter_context(nc.psum_tensor("trPq", [128, 512], F32))
                knP = ph.enter_context(nc.psum_tensor("knP", [2, 512], F32))

                for oc in range(16):
                    wb = wbuf[oc % 2]
                    S.dma("sp", lambda e, o=oc, wb=wb: e.dma_start(
                        out=wb[:, 0:1024],
                        in_=wbT.ap().rearrange("(f p) o -> p f o", p=128)[:, :, 128 * o:128 * (o + 1)]),
                        r=[wbT], w=[wb])
                    dst = s2[oc] if oc < 8 else kvt
                    for ni, (off, w) in enumerate(NC):
                        pb = qkP[ni % 2]
                        S.pre("pe", [wb] + [s1[f] for f in range(NF)], [pb])
                        for fc in range(NF):
                            inst = mm(nc.tensor, pb[:, 0:w], wb[:, 128 * fc:128 * (fc + 1)],
                                      s1[fc][:, off:off + w], fc == 0, fc == NF - 1)
                        S.post("pe", inst, [wb] + [s1[f] for f in range(NF)], [pb])
                        S.op("dve", lambda e, d=dst, of=off, ww=w, p=pb: e.tensor_copy(
                            d[:, of:of + ww], p[:, 0:ww]), r=[pb], w=[dst])
                    if oc < 8:
                        S.op("dve", lambda e, o=oc: e.reduce_max(
                            qmax8[:, o:o + 1], s2[o][:, :], axis=AX.X,
                            apply_absolute_value=True), r=[s2[oc]], w=[qmax8])
                    else:
                        S.dma("sp", lambda e, o=oc: e.dma_start(
                            out=kx.ap()[128 * (o - 8):128 * (o - 7), :],
                            in_=kvt[:, OWN0:OWN1]), r=[kvt], w=[kx])
                S.pre("pe", [qmax8, ident], [trPq])
                S.post("pe", mm(nc.tensor, trPq[0:8, 0:128], qmax8[:, 0:8], ident[:],
                                True, True), [qmax8, ident], [trPq])
                S.op("dve", lambda e: e.reduce_max(redc[0:8, 0:1], trPq[0:8, 0:128], axis=AX.X),
                     r=[trPq], w=[redc])
                S.pre("pe", [redc, ident], [trPq])
                S.post("pe", mm(nc.tensor, trPq[0:1, 128:136], redc[0:8, 0:1],
                                ident[0:8, 0:8], True, True), [redc, ident], [trPq])
                S.op("dve", lambda e: e.tensor_copy(qmaxr[0:1, 0:8], trPq[0:1, 128:136]),
                     r=[trPq], w=[qmaxr])

                for fc in range(NF):
                    S.dma("sp", lambda e, f=fc: e.dma_start(
                        out=vres[f][:], in_=wbT.ap()[128 * f:128 * (f + 1), 2048:3072]),
                        r=[wbT], w=[vres[fc]])
                for ti, (toff, tw) in enumerate(TT):
                    rd = [vres[f] for f in range(NF)] + [s1[f] for f in range(NF)]
                    S.pre("pe", rd, [vP3])
                    for fc in range(NF):
                        for oc in range(2):
                            inst = mm(nc.tensor, vP3[0:tw, 512 * oc:512 * (oc + 1)],
                                      s1[fc][:, toff:toff + tw],
                                      vres[fc][:, 512 * oc:512 * (oc + 1)],
                                      fc == 0, fc == NF - 1)
                    S.post("pe", inst, rd, [vP3])
                    S.op("dve", lambda e, ww=tw: e.tensor_copy(
                        kvt[0:ww, 0:1024], vP3[0:ww, :]), r=[vP3], w=[kvt])
                    lo, hi = max(toff, OWN0), min(toff + tw, OWN1)
                    if hi > lo:
                        S.dma("sp", lambda e, lo=lo, hi=hi, to=toff: e.dma_start(
                            out=vx.ap()[lo - OWN0:hi - OWN0, :],
                            in_=kvt[lo - to:hi - to, 0:1024]), r=[kvt], w=[vx])

                S.cc(lambda e: e.collective_compute(
                    "AllGather", ALU.bypass, replica_groups=REPL,
                    ins=[kx.ap().opt()], outs=[kg.ap().opt()]), r=[kx], w=[kg])
                S.cc(lambda e: e.collective_compute(
                    "AllGather", ALU.bypass, replica_groups=REPL,
                    ins=[vx.ap().opt()], outs=[vg.ap().opt()]), r=[vx], w=[vg])

                for r_ in range(2):
                    for c in range(NF):
                        S.dma("sp", lambda e, r2=r_, c=c: e.dma_start(
                            out=kchunk[:], in_=kg.ap()[1024 * r2 + 128 * c:1024 * r2 + 128 * (c + 1), :]),
                            r=[kg], w=[kchunk])
                        S.op("act", lambda e: e.activation(ksq[:], kchunk[:], AF.Square),
                             r=[kchunk], w=[ksq])
                        for n4 in range(4):
                            S.pre("pe", [sel2, ksq], [knP])
                            S.post("pe", mm(nc.tensor, knP[0:2, :], sel2[:],
                                            ksq[:, 512 * n4:512 * (n4 + 1)], True, True),
                                   [sel2, ksq], [knP])
                            S.op("dve", lambda e: e.tensor_copy(
                                kn2s[0:2, :], knP[0:2, :]), r=[knP], w=[kn2s])
                            S.op("dve", lambda e: e.reduce_max(
                                redc[0:2, 0:1], kn2s[0:2, :], axis=AX.X), r=[kn2s], w=[redc])
                            if r_ == 0 and n4 == 0:
                                S.op("dve", lambda e, c=c: e.tensor_copy(
                                    knacc[0:2, c:c + 1], redc[0:2, 0:1]), r=[redc], w=[knacc])
                            else:
                                S.op("dve", lambda e, c=c: e.tensor_tensor(
                                    knacc[0:2, c:c + 1], redc[0:2, 0:1],
                                    knacc[0:2, c:c + 1], ALU.max), r=[redc, knacc], w=[knacc])
                S.pre("pe", [knacc, ident], [trPq])
                S.post("pe", mm(nc.tensor, trPq[0:8, 136:138], knacc[0:2, 0:8],
                                ident[0:2, 0:2], True, True), [knacc, ident], [trPq])
                S.op("dve", lambda e: e.reduce_max(redc[0:8, 0:1], trPq[0:8, 136:138], axis=AX.X),
                     r=[trPq], w=[redc])
                S.pre("pe", [redc, ident], [trPq])
                S.post("pe", mm(nc.tensor, trPq[0:1, 138:146], redc[0:8, 0:1],
                                ident[0:8, 0:8], True, True), [redc, ident], [trPq])
                S.op("act", lambda e: e.activation(scal[0:1, 8:16], trPq[0:1, 138:146], AF.Sqrt),
                     r=[trPq], w=[scal])
                S.op("dve", lambda e: e.tensor_tensor(scal[0:1, 8:16], scal[0:1, 8:16],
                                                      qmaxr[0:1, 0:8], ALU.mult),
                     r=[scal, qmaxr], w=[scal])
                S.op("dve", lambda e: e.tensor_scalar(scal[0:1, 8:16], scal[0:1, 8:16],
                                                      -0.125, 0.0, ALU.mult, ALU.add),
                     r=[scal], w=[scal])
                S.barrier()
            with contextlib.ExitStack() as ph:
                ktile = [ph.enter_context(nc.sbuf_tensor("ktile0", [128, 4096], BF16))] * 2
                vstat = ph.enter_context(nc.sbuf_tensor("vstat", [128, 2 * 32 * 65], BF16))
                bbt = ph.enter_context(nc.sbuf_tensor("bbt", [128, 8], F32))
                zbs2 = ph.enter_context(nc.sbuf_tensor("zbs2", [64, 1024], F32))
                ptile = [ph.enter_context(nc.sbuf_tensor(f"pt{i}", [128, 2048], BF16)) for i in range(2)]
                zrowr = ph.enter_context(nc.sbuf_tensor(f"zrw{next(_uid)}", [1, 1024], F32))
                zrt = zrowr
                with nc.psum_tensor("bP3", [128, 512], F32) as bP3:
                    S.pre("pe", [onesrow, scal], [bP3])
                    S.post("pe", mm(nc.tensor, bP3[0:128, 0:8], onesrow[0:1, :],
                                    scal[0:1, 8:16], True, True), [onesrow, scal], [bP3])
                    S.op("dve", lambda e: e.tensor_copy(bbt[:, 0:8], bP3[0:128, 0:8]),
                         r=[bP3], w=[bbt])
                    S.barrier()
                STp = ph.enter_context(nc.psum_tensor("STp", [128, 2048], F32))
                pvP = [ph.enter_context(nc.psum_tensor(f"pvP{i}", [65, 512], F32)) for i in range(2)]
                zbP = ph.enter_context(nc.psum_tensor("zbP", [64, 1024], F32))
                S.op("pool", lambda e: e.memset(
                    vstat[:].rearrange("p (k c) -> p k c", c=65)[:, :, 64:65], 1.0), w=[vstat])

                for pr in range(8):
                    kt = ktile[pr % 2]
                    for r_ in range(2):
                        for hl in range(2):
                            S.dma("sp", lambda e, p=pr, r2=r_, hl=hl, kt=kt: e.dma_start(
                                out=kt[64 * hl:64 * (hl + 1), 2048 * r2:2048 * (r2 + 1)],
                                in_=kg.ap()[1024 * r2 + 64 * (2 * p + hl):1024 * r2 + 64 * (2 * p + hl) + 64, :]),
                                r=[kg], w=[kt])
                    for hl in range(2):
                        S.dma("sp", lambda e, p=pr, hl=hl: e.dma_start(
                            out=vstat[:].rearrange("p (k c) -> p k c", c=65)[:, 32 * hl:32 * (hl + 1), 0:64],
                            in_=vg.ap().rearrange("(k p) c -> p k c", p=128)[:, :, 64 * (2 * p + hl):64 * (2 * p + hl) + 64]),
                            r=[vg], w=[vstat])
                    for qi, (qoff, qw) in enumerate(NC):
                        def pv_emit(kb2, pt):
                            kb = 2 * kb2
                            S.pre("pe", [vstat, pt], [pvP[0]])
                            mm(nc.tensor, pvP[0][0:65, 0:qw],
                               vstat[:, 65 * kb:65 * (kb + 1)], pt[:, 0:qw],
                               kb == 0, False)
                            ia = mm(nc.tensor, pvP[0][0:65, 0:qw],
                                    vstat[:, 65 * (kb + 1):65 * (kb + 2)],
                                    pt[:, 1024:1024 + qw], False, kb == 30)
                            S.post("pe", ia, [vstat, pt], [pvP[0]])
                            S.pre("pe", [vstat, pt], [pvP[1]])
                            mm(nc.tensor, pvP[1][0:65, 0:qw],
                               vstat[:, 65 * (32 + kb):65 * (32 + kb) + 65],
                               pt[:, 512:512 + qw], kb == 0, False)
                            ib = mm(nc.tensor, pvP[1][0:65, 0:qw],
                                    vstat[:, 65 * (33 + kb):65 * (33 + kb) + 65],
                                    pt[:, 1536:1536 + qw], False, kb == 30)
                            S.post("pe", ib, [vstat, pt], [pvP[1]])
                        for kb2 in range(16):
                            kb = 2 * kb2
                            st = STp
                            S.pre("pe", [kt, s2[pr]], [st])
                            mm(nc.tensor, st[:, 0:qw], kt[0:64, 128 * kb:128 * (kb + 1)],
                               s2[pr][0:64, qoff:qoff + qw], True, True)
                            nc.tensor.matmul(
                                st[:, 512:512 + qw], kt[64:128, 128 * kb:128 * (kb + 1)],
                                s2[pr][64:128, qoff:qoff + qw], start=True, stop=True,
                                tile_position=(64, 0), skip_group_check=True)
                            mm(nc.tensor, st[:, 1024:1024 + qw],
                               kt[0:64, 128 * (kb + 1):128 * (kb + 2)],
                               s2[pr][0:64, qoff:qoff + qw], True, True)
                            i2 = nc.tensor.matmul(
                                st[:, 1536:1536 + qw], kt[64:128, 128 * (kb + 1):128 * (kb + 2)],
                                s2[pr][64:128, qoff:qoff + qw], start=True, stop=True,
                                tile_position=(64, 0), skip_group_check=True)
                            S.post("pe", i2, [kt, s2[pr]], [st])
                            pt = ptile[kb2 % 2]
                            if qw == 512:
                                S.op("act", lambda e, st=st, pt=pt, p=pr: e.activation(
                                    pt[:, :], st[:, :], AF.Exp, bias=bbt[:, p:p + 1],
                                    scale=0.125), r=[st, bbt], w=[pt])
                            else:
                                S.op("act", lambda e, st=st, pt=pt, p=pr, qw=qw: e.activation(
                                    pt[:, 0:2048].rearrange("p (s c) -> p s c", c=512)[:, :, 0:qw],
                                    st[:, 0:2048].rearrange("p (s c) -> p s c", c=512)[:, :, 0:qw],
                                    AF.Exp, bias=bbt[:, p:p + 1], scale=0.125),
                                    r=[st, bbt], w=[pt])
                            pv_emit(kb2, pt)
                        S.op("dve", lambda e, qw=qw: e.reciprocal(
                            zrowr[0:1, 0:qw], pvP[0][64:65, 0:qw]), r=[pvP[0]], w=[zrt])
                        S.op("dve", lambda e, qw=qw: e.reciprocal(
                            zrowr[0:1, 512:512 + qw], pvP[1][64:65, 0:qw]), r=[pvP[1]], w=[zrt])
                        for hl in range(2):
                            S.pre("pe", [onesrow, zrt], [zbP])
                            S.post("pe", mm(nc.tensor, zbP[0:64, 512 * hl:512 * hl + qw],
                                            onesrow[0:1, 0:64],
                                            zrowr[0:1, 512 * hl:512 * hl + qw], True, True),
                                   [onesrow, zrt], [zbP])
                        S.op("act", lambda e, qw=qw: e.copy(
                            zbs2[0:64, 0:qw], zbP[0:64, 0:qw]), r=[zbP], w=[zbs2])
                        S.op("act", lambda e, qw=qw: e.copy(
                            zbs2[0:64, 512:512 + qw], zbP[0:64, 512:512 + qw]), r=[zbP], w=[zbs2])
                        S.op("dve", lambda e, p=pr, qo=qoff, qw=qw: e.tensor_tensor(
                            s2[p][0:64, qo:qo + qw], pvP[0][0:64, 0:qw],
                            zbs2[0:64, 0:qw], ALU.mult), r=[pvP[0], zbs2], w=[s2[pr]])
                        S.op("dve", lambda e, p=pr, qo=qoff, qw=qw: e.tensor_tensor(
                            s2[p][64:128, qo:qo + qw], pvP[1][0:64, 0:qw],
                            zbs2[0:64, 512:512 + qw], ALU.mult), r=[pvP[1], zbs2], w=[s2[pr]])
                S.barrier()
            with contextlib.ExitStack() as ph:
                woP = [ph.enter_context(nc.psum_tensor(f"woB{i}", [128, 512], F32)) for i in range(2)]
                for oc in range(NF):
                    wb = wbuf[oc % 2]
                    S.dma("sp", lambda e, o=oc, wb=wb: e.dma_start(
                        out=wb[:, 0:1024],
                        in_=wobT.ap().rearrange("(f p) o -> p f o", p=128)[:, :, 128 * o:128 * (o + 1)]),
                        r=[wobT], w=[wb])
                    for ni, (off, w) in enumerate(NC):
                        pb = woP[ni % 2]
                        S.pre("pe", [wb] + [s2[f] for f in range(NF)], [pb])
                        for fc in range(NF):
                            inst = mm(nc.tensor, pb[:, 0:w], wb[:, 128 * fc:128 * (fc + 1)],
                                      s2[fc][:, off:off + w], fc == 0, fc == NF - 1)
                        S.post("pe", inst, [wb] + [s2[f] for f in range(NF)], [pb])
                        S.op("dve", lambda e, o=oc, of=off, ww=w, p=pb: e.tensor_tensor(
                            x8[o][:, of:of + ww], p[:, 0:ww], x8[o][:, of:of + ww], ALU.add),
                            r=[pb, x8[oc]], w=[x8[oc]])
                S.barrier()
            rezero_halo()

        if n_phases >= 4:
            ffn_phase("B")

        if dump:
            if os.environ.get("DUMP_XQ1"):
                for fc in range(0, NF):
                    S.dma("sp", lambda e, f=fc: e.dma_start(
                        out=xq2d.ap()[128 * f:128 * (f + 1), :], in_=s1[f][:, :]),
                        r=[s1[fc]], w=[xq2d])
            dhd = nc.dram_tensor("dhd", [DFF, T], F32, kind="ExternalOutput")
            dxq = nc.dram_tensor("dxq", [DFF, T], BF16, kind="ExternalOutput")
            dwq = nc.dram_tensor("dwq", [D, DFF], BF16, kind="ExternalOutput")
            with nc.sbuf_tensor("dmpw", [128, DFF], BF16) as dmpw:
                for f in range(NF):
                    S.dma("sp", lambda e, f=f: e.dma_start(
                        out=dmpw[:], in_=wq1T.ap()[128 * f:128 * (f + 1), :]), r=[wq1T], w=[dmpw])
                    S.dma("sp", lambda e, f=f: e.dma_start(
                        out=dwq.ap()[128 * f:128 * (f + 1), :], in_=dmpw[:]), r=[dmpw], w=[dwq])
            with nc.sbuf_tensor("dmp", [128, T], F32) as dmp, \
                 nc.sbuf_tensor("dmpb", [128, T], BF16) as dmpb:
                for h in range(NHC):
                    S.dma("sp", lambda e, h=h: e.dma_start(
                        out=dmp[:], in_=hd.ap()[128 * h:128 * (h + 1), :]), r=[hd], w=[dmp])
                    S.dma("sp", lambda e, h=h: e.dma_start(
                        out=dhd.ap()[128 * h:128 * (h + 1), :], in_=dmp[:]), r=[dmp], w=[dhd])
                    S.dma("sp", lambda e, h=h: e.dma_start(
                        out=dmpb[:], in_=xq2d.ap()[128 * h:128 * (h + 1), :]), r=[xq2d], w=[dmpb])
                    S.dma("sp", lambda e, h=h: e.dma_start(
                        out=dxq.ap()[128 * h:128 * (h + 1), :], in_=dmpb[:]), r=[dmpb], w=[dxq])
                S.barrier()
        for fc in range(NF):
            S.dma("sp", lambda e, f=fc: e.dma_start(
                out=yout.ap()[128 * f:128 * (f + 1), :], in_=x8[f][:, OWN0:OWN1]),
                r=[x8[fc]], w=[yout])
        S.barrier()

    return nc


_CACHE = {}


def _stage(inputs):
    x = np.asarray(inputs["L0_attn_a_x"], dtype=np.float32)
    base = {
        "wqkv_a": np.ascontiguousarray(inputs["L0_attn_a_w_qkv"], dtype=np.float32),
        "wout_a": np.ascontiguousarray(inputs["L0_attn_a_w_out"], dtype=np.float32),
        "bout_a": np.ascontiguousarray(inputs["L0_attn_a_b_out"], dtype=np.float32),
        "wqkv_b": np.ascontiguousarray(inputs["L0_attn_b_w_qkv"], dtype=np.float32),
        "wout_b": np.ascontiguousarray(inputs["L0_attn_b_w_out"], dtype=np.float32),
        "w1": np.ascontiguousarray(inputs["L0_ffn_w1"], dtype=np.float32),
        "convw": np.ascontiguousarray(
            np.asarray(inputs["L0_ffn_conv_w"], dtype=np.float32).reshape(DFF, 3).T),
        "convb": np.ascontiguousarray(inputs["L0_ffn_conv_b"], dtype=np.float32),
        "w2": np.ascontiguousarray(inputs["L0_ffn_w2"], dtype=np.float32),
        "ident": np.eye(128, dtype=np.float32),
        "sel2": np.concatenate([
            np.repeat(np.float32([[1], [0]]), 64, axis=0),
            np.repeat(np.float32([[0], [1]]), 64, axis=0)], axis=1).astype(np.float32),
        "onescol_in": np.ones((128, 1), dtype=np.float32),
        "onesrow_in": np.ones((128, 128), dtype=np.float32),
    }
    import os as _os
    if _os.environ.get("BYPASS_XQ"):
        import ml_dtypes as _md
        base["xqstage"] = np.load("/root/problem/_xqstage.npy").view(_md.bfloat16)
    if _os.environ.get("BYPASS_WQ"):
        import ml_dtypes as _md
        w1f = np.asarray(inputs.get("L0_ffn_w1", base["w1"]), np.float64)
        tern = np.clip(np.rint(w1f / np.abs(w1f).mean()), -1, 1)
        base["wqstage"] = np.ascontiguousarray(tern.T).astype(_md.bfloat16)
    in_maps = []
    for c in range(8):
        b, hh = c // 2, c % 2
        s, e = 2048 * hh, 2048 * hh + 2048
        pad = np.zeros((T, D), dtype=np.float32)
        lo, hi = s - 2, e + 2
        clo, chi = max(lo, 0), min(hi, 4096)
        pad[clo - lo:chi - lo, :] = x[b, clo:chi, :]
        hm = np.ones((128, 4), dtype=np.float32)
        if hh == 0:
            hm[:, 0:2] = 0.0
        else:
            hm[:, 2:4] = 0.0
        m = dict(base)
        m["xin"] = np.ascontiguousarray(pad.T)
        m["hmask"] = hm
        in_maps.append(m)
    return in_maps


def kernel(**inputs):
    n_phases = int(inputs.pop("_n_phases", 4))
    dump = bool(inputs.pop("_dump", False))
    key = (n_phases, dump)
    if key not in _CACHE:
        _CACHE[key] = build(n_phases, dump)
    nc = _CACHE[key]
    in_maps = _stage(inputs)
    res = run_bass_kernel_spmd(nc, in_maps, core_ids=list(range(8)))
    out = np.empty((4, 4096, 1024), dtype=np.float32)
    for c in range(8):
        b, hh = c // 2, c % 2
        out[b, 2048 * hh:2048 * (hh + 1), :] = res.results[c]["y"].T
    if dump:
        kernel._dbg = res.results
    return out


# version 06 (final)
# speedup vs baseline: 1.2686x; 1.0855x over previous
import contextlib
import itertools

import numpy as np
import concourse.bass as bass
import concourse.mybir as mybir
from concourse.bass_utils import run_bass_kernel_spmd

F32 = mybir.dt.float32
BF16 = mybir.dt.bfloat16
AF = mybir.ActivationFunctionType
ALU = mybir.AluOpType
AX = mybir.AxisListType

T = 2052
OWN0, OWN1 = 2, 2050
D = 1024
H = 16
DH = 64
DFF = 4096
NF = 8
NHC = 32
RND = float(1.5 * 2.0 ** 23)
TT = [(128 * i, 128) for i in range(16)] + [(2048, 4)]
NC = [(0, 512), (512, 512), (1024, 512), (1536, 512), (2048, 4)]
REPL = [[0, 1], [2, 3], [4, 5], [6, 7]]
_uid = itertools.count()


NSLOT = 12


class Sched:

    def __init__(self, nc, sems):
        self.nc = nc
        self.sems = sems
        self.cnt = {s: 0 for s in sems}
        self.hwm = {}
        self.writers = {}
        self.readers = {}
        self.eng = {"pe": nc.tensor, "act": nc.scalar, "dve": nc.vector,
                    "pool": nc.gpsimd, "sp": nc.sync}
        self.slot = {"sp": 0, "pool": 0}

    def _wait(self, eng, sem, val):
        if val <= 0 or self.hwm.get((eng, sem), 0) >= val:
            return
        self.hwm[(eng, sem)] = val
        self.eng[eng].wait_ge(self.sems[sem], val)

    def pre(self, eng, reads, writes):
        for b in reads:
            for t in self.writers.get(id(b), ()):
                self._wait(eng, t[0], t[1])
        for b in writes:
            for t in self.readers.get(id(b), ()):
                self._wait(eng, t[0], t[1])

    def post(self, sem, inst, reads, writes, amt=1):
        inst.then_inc(self.sems[sem], amt)
        self.cnt[sem] += amt
        tok = (sem, self.cnt[sem])
        for b in writes:
            self.writers.setdefault(id(b), []).append(tok)
            self.readers[id(b)] = []
        for b in reads:
            self.readers.setdefault(id(b), []).append(tok)

    def op(self, eng, f, r=(), w=()):
        self.pre(eng, r, w)
        self.post(eng, f(self.eng[eng]), r, w)

    def dma(self, q, f, r=(), w=()):
        s = self.slot[q]
        self.slot[q] = (s + 1) % NSLOT
        sem = f"{q}d{s}"
        self._wait(q, sem, self.cnt[sem])
        self.pre(q, r, w)
        self.post(sem, f(self.eng[q]), r, w, 16)

    def cc(self, f, r=(), w=()):
        self.pre("pool", r, w)
        self._wait("pool", "cc", self.cnt["cc"])
        self.post("cc", f(self.eng["pool"]), r, w)

    def barrier(self):
        for eng in self.eng:
            for sem, v in self.cnt.items():
                self._wait(eng, sem, v)
        self.writers.clear()
        self.readers.clear()


def build(n_phases=4, dump=False):
    import os
    n_pro = int(os.environ.get('N_PRO', '9'))
    nc = bass.Bass("TRN2", target_bir_lowering=False)

    xin = nc.dram_tensor("xin", [D, T], F32, kind="ExternalInput")
    wqkv_a = nc.dram_tensor("wqkv_a", [D, 3 * D], F32, kind="ExternalInput")
    wout_a = nc.dram_tensor("wout_a", [D, D], F32, kind="ExternalInput")
    bout_a = nc.dram_tensor("bout_a", [D], F32, kind="ExternalInput")
    wqkv_b = nc.dram_tensor("wqkv_b", [3 * D, D], F32, kind="ExternalInput")
    wout_b = nc.dram_tensor("wout_b", [D, D], F32, kind="ExternalInput")
    w1 = nc.dram_tensor("w1", [DFF, D], F32, kind="ExternalInput")
    convw = nc.dram_tensor("convw", [3, DFF], F32, kind="ExternalInput")
    convb = nc.dram_tensor("convb", [DFF], F32, kind="ExternalInput")
    w2 = nc.dram_tensor("w2", [D, DFF], F32, kind="ExternalInput")
    ident_in = nc.dram_tensor("ident", [128, 128], F32, kind="ExternalInput")
    hmask_in = nc.dram_tensor("hmask", [128, 4], F32, kind="ExternalInput")
    sel2_in = nc.dram_tensor("sel2", [128, 2], F32, kind="ExternalInput")
    onescol_in = nc.dram_tensor("onescol_in", [128, 1], F32, kind="ExternalInput")
    onesrow_in = nc.dram_tensor("onesrow_in", [128, 128], F32, kind="ExternalInput")
    xqstage = (nc.dram_tensor("xqstage", [D, T], BF16, kind="ExternalInput")
               if os.environ.get("BYPASS_XQ") else None)
    wqstage = (nc.dram_tensor("wqstage", [D, DFF], BF16, kind="ExternalInput")
               if os.environ.get("BYPASS_WQ") else None)
    yout = nc.dram_tensor("y", [D, 2048], F32, kind="ExternalOutput")

    wa_bf = nc.dram_tensor("wa_bf", [D, 3 * D], BF16)
    woa_bf = nc.dram_tensor("woa_bf", [D, D], BF16)
    wb_nat = nc.dram_tensor("wb_nat", [3 * D, D], BF16)
    wbT = nc.dram_tensor("wbT", [D, 3 * D], BF16)
    wob_nat = nc.dram_tensor("wob_nat", [D, D], BF16)
    wobT = nc.dram_tensor("wobT", [D, D], BF16)
    w1q_nat = nc.dram_tensor("w1q_nat", [DFF, D], BF16)
    wq1T = nc.dram_tensor("wq1T", [D, DFF], BF16)
    w2q_nat = nc.dram_tensor("w2q_nat", [D, DFF], BF16)
    wq2T = nc.dram_tensor("wq2T", [DFF, D], BF16)
    kx = nc.dram_tensor("kx", [D, 2048], BF16)
    vx = nc.dram_tensor("vx", [2048, D], BF16)
    kg = nc.dram_tensor("kg", [2 * D, 2048], BF16)
    vg = nc.dram_tensor("vg", [4096, D], BF16)
    cxl = nc.dram_tensor("cxl", [65, D], F32)
    cxg = nc.dram_tensor("cxg", [65, D], F32)
    hd = nc.dram_tensor("hd", [DFF, T], F32)
    xq2d = nc.dram_tensor("xq2d", [DFF, T], BF16)

    stk = contextlib.ExitStack()
    sb = lambda n, s, d: stk.enter_context(nc.sbuf_tensor(n, s, d))
    sems = {n: stk.enter_context(nc.semaphore(n + "_s"))
            for n in ("pe", "act", "dve", "pool", "sp", "cc")}
    for q in ("sp", "pool"):
        for s in range(NSLOT):
            sems[f"{q}d{s}"] = stk.enter_context(nc.semaphore(f"{q}d{s}_s"))
    S = Sched(nc, sems)

    x8 = [sb(f"x{i}", [128, T], F32) for i in range(NF)]
    s1 = [sb(f"s1_{i}", [128, T], BF16) for i in range(NF)]
    s2 = [sb(f"s2_{i}", [128, T], BF16) for i in range(NF)]
    dscr = sb("dscr", [128, T], F32)
    xbf = [sb(f"xbf{i}", [128, 512], BF16) for i in range(2)]
    sqbf = [sb(f"sqbf{i}", [128, 512], BF16) for i in range(2)]
    wbuf = [sb(f"wbuf{i}", [128, 2048], BF16) for i in range(2)]
    ident = sb("identt", [128, 128], F32)
    hmask = sb("hmaskt", [128, 4], F32)
    sel2 = sb("sel2t", [128, 2], BF16)
    onescol = sb("onescolt", [128, 1], BF16)
    onesrow = sb("onesrowt", [128, 128], F32)
    b_out = sb("b_outt", [128, NF], F32)
    convwt = sb("convwt", [128, 3 * NHC], F32)
    convbt = sb("convbt", [128, NHC], F32)
    mwrow = sb("mwrow", [1, 4], F32)
    scal = sb("scal", [1, 16], F32)
    redc = sb("redc", [128, 1], F32)
    bc1 = sb("bc1", [128, 512], F32)
    bc2 = sb("bc2", [128, 512], F32)
    bc3 = sb("bc3", [128, 512], F32)
    rowt = sb("rowt", [65, T], F32)
    mwb = sb("mwb", [128, 4], F32)
    _ROWP = {"sum": 0, "sumsq": 32, "mx": 64, "bk": 64}

    def R(name, c0=0, c1=T):
        p = _ROWP[name]
        return rowt[p:p + 1, c0:c1], rowt

    def RP(name):
        return _ROWP[name]

    def mm(e, out, lhsT, rhs, start, stop):
        return e.matmul(out, lhsT, rhs, start=start, stop=stop,
                        skip_group_check=True)

    with stk:
        S.dma("sp", lambda e: e.dma_start(out=ident[:], in_=ident_in.ap()), w=[ident])
        S.dma("sp", lambda e: e.dma_start(out=hmask[:], in_=hmask_in.ap()), w=[hmask])
        S.dma("pool", lambda e: e.dma_start(out=sel2[:], in_=sel2_in.ap()), w=[sel2])
        S.dma("pool", lambda e: e.dma_start(out=onescol[:], in_=onescol_in.ap()), w=[onescol])
        S.dma("sp", lambda e: e.dma_start(out=onesrow[:], in_=onesrow_in.ap()), w=[onesrow])
        for c in range(NF):
            S.dma("sp", lambda e, c=c: e.dma_start(out=x8[c][:], in_=xin.ap()[128 * c:128 * (c + 1), :]), w=[x8[c]])
        S.dma("pool", lambda e: e.dma_start(out=wa_bf.ap(), in_=wqkv_a.ap()), w=[wa_bf])
        S.dma("pool", lambda e: e.dma_start(out=woa_bf.ap(), in_=wout_a.ap()), w=[woa_bf])
        S.dma("pool", lambda e: e.dma_start(out=wb_nat.ap(), in_=wqkv_b.ap()), w=[wb_nat])
        S.dma("pool", lambda e: e.dma_start(out=wob_nat.ap(), in_=wout_b.ap()), w=[wob_nat])

        with nc.psum_tensor("prT", [128, 512], F32) as prT, \
             nc.sbuf_tensor("rload", [1, 512], F32) as rload:
            def load_cols(dram_ap_1d, n, dst, colmap):
                for j in range(n):
                    S.dma("sp", lambda e, j=j: e.dma_start(
                        out=rload[0:1, 0:128],
                        in_=dram_ap_1d[128 * j:128 * (j + 1)].rearrange("(o p) -> o p", o=1)),
                        r=[], w=[rload])
                    S.pre("pe", [rload, ident], [prT])
                    S.post("pe", mm(nc.tensor, prT[0:128, 0:1], rload[0:1, 0:128],
                                    ident[0:1, 0:1], True, True), [rload, ident], [prT])
                    S.op("dve", lambda e, j=j: e.tensor_copy(
                        dst[:, colmap(j):colmap(j) + 1], prT[0:128, 0:1]),
                        r=[prT], w=[dst])
            if n_pro >= 2:
                load_cols(bout_a.ap(), NF, b_out, lambda j: j)
                load_cols(convb.ap(), NHC, convbt, lambda j: j)
                for k in range(3):
                    cwk = convw.ap()[k:k + 1, :].rearrange("one n -> (one n)")
                    load_cols(cwk, NHC, convwt, lambda j, k=k: 3 * j + k)

        slabs = [(wbuf[i // 4], 512 * (i % 4)) for i in range(8)]

        def transpose_copy(jobs):
            for b_ in range(0, len(jobs), 8):
                grp = jobs[b_:b_ + 8]
                for i, (sa, da, dt_) in enumerate(grp):
                    t_, o_ = slabs[i]
                    S.dma("sp", lambda e, sa=sa, t_=t_, o_=o_: e.dma_start_transpose(
                        t_[:, o_:o_ + 512], sa), r=[], w=[t_])
                for i, (sa, da, dt_) in enumerate(grp):
                    t_, o_ = slabs[i]
                    S.dma("pool", lambda e, da=da, t_=t_, o_=o_: e.dma_start(
                        out=da, in_=t_[:, o_:o_ + 512]), r=[t_], w=[dt_])
        if n_pro >= 3:
            jobs = []
            for fc in range(NF):
                for oc in range(6):
                    jobs.append((wb_nat.ap()[512 * oc:512 * (oc + 1), 128 * fc:128 * (fc + 1)],
                                 wbT.ap()[128 * fc:128 * (fc + 1), 512 * oc:512 * (oc + 1)], wbT))
                for oc in range(2):
                    jobs.append((wob_nat.ap()[512 * oc:512 * (oc + 1), 128 * fc:128 * (fc + 1)],
                                 wobT.ap()[128 * fc:128 * (fc + 1), 512 * oc:512 * (oc + 1)], wobT))
            S.pre("sp", [wb_nat, wob_nat], [])
            transpose_copy(jobs)

        with nc.sbuf_tensor("absacc", [128, 48], F32) as absacc, \
             nc.sbuf_tensor("qbf0", [128, 2048], BF16) as qbf0, \
             nc.sbuf_tensor("qbf1", [128, 2048], BF16) as qbf1, \
             nc.sbuf_tensor("wfa", [128, 2048], F32) as wfa, \
             nc.sbuf_tensor("wfb", [128, 2048], F32) as wfb, \
             nc.psum_tensor("pT0", [128, 512], F32) as pT0:
            wfs = [wfa, wfb]
            qbfs = [qbf0, qbf1]
            w2v = w2.ap().rearrange("a (b c) -> (a b) c", b=2)
            w2qv = w2q_nat.ap().rearrange("a (b c) -> (a b) c", b=2)
            for wi, (wap, nch, wid) in (enumerate(((w1.ap(), NHC, D), (w2v, 16, 2048))) if n_pro >= 4 else []):
                for c in range(nch):
                    wf = wfs[c % 2]
                    S.dma("sp", lambda e, w_=wap, c=c, ww=wid, wf=wf: e.dma_start(
                        out=wf[:, 0:ww], in_=w_[128 * c:128 * (c + 1), :]), w=[wf])
                    col = c if wi == 0 else NHC + c
                    S.op("dve", lambda e, col=col, ww=wid, wf=wf: e.reduce_sum(
                        absacc[:, col:col + 1], wf[:, 0:ww], axis=AX.X,
                        apply_absolute_value=True), r=[wf], w=[absacc])
            for wi, (lo, n, total) in (enumerate(((0, NHC, DFF * D), (NHC, 16, D * DFF))) if n_pro >= 4 else []):
                S.pre("pe", [absacc, ident], [pT0])
                S.post("pe", mm(nc.tensor, pT0[0:n, 0:128], absacc[:, lo:lo + n],
                                ident[:], True, True), [absacc, ident], [pT0])
                S.op("dve", lambda e, n=n: e.reduce_sum(
                    redc[0:n, 0:1], pT0[0:n, 0:128], axis=AX.X), r=[pT0], w=[redc])
                S.pre("pe", [redc, ident], [pT0])
                S.post("pe", mm(nc.tensor, pT0[0:1, 128:128 + n], redc[0:n, 0:1],
                                ident[0:n, 0:n], True, True), [redc, ident], [pT0])
                S.op("dve", lambda e, wi=wi, n=n: e.reduce_sum(
                    scal[0:1, wi:wi + 1], pT0[0:1, 128:128 + n], axis=AX.X), r=[pT0], w=[scal])
                S.op("dve", lambda e, wi=wi, tot=total: e.tensor_scalar(
                    mwrow[0:1, wi:wi + 1], scal[0:1, wi:wi + 1], 1.0 / tot, 1e-5,
                    ALU.mult, ALU.max), r=[scal], w=[mwrow])
            if n_pro < 4:
                S.op("pool", lambda e: e.memset(mwrow[0:1, :], 1.0), w=[mwrow])
            S.op("dve", lambda e: e.reciprocal(mwrow[0:1, 2:4], mwrow[0:1, 0:2]),
                 r=[mwrow], w=[mwrow])
            S.pre("pe", [onesrow, mwrow], [pT0])
            S.post("pe", mm(nc.tensor, pT0[0:128, 256:260], onesrow[0:1, :],
                            mwrow[0:1, 0:4], True, True), [onesrow, mwrow], [pT0])
            S.op("dve", lambda e: e.tensor_copy(mwb[:, 0:4], pT0[0:128, 256:260]),
                 r=[pT0], w=[mwb])
            for wi, (wap, wq_ap, nch, wid) in (enumerate(
                    ((w1.ap(), w1q_nat.ap(), NHC, D), (w2v, w2qv, 16, 2048))) if n_pro >= 4 else []):
                for c in range(nch):
                    wf = wfs[c % 2]
                    qb = qbfs[c % 2]
                    S.dma("sp", lambda e, w_=wap, c=c, ww=wid, wf=wf: e.dma_start(
                        out=wf[:, 0:ww], in_=w_[128 * c:128 * (c + 1), :]), w=[wf])
                    S.op("dve", lambda e, ww=wid, wi=wi, wf=wf: e.tensor_scalar(
                        wf[:, 0:ww], wf[:, 0:ww], mwb[:, 2 + wi:3 + wi], RND,
                        ALU.mult, ALU.add), r=[wf, mwb], w=[wf])
                    S.op("dve", lambda e, ww=wid, wf=wf: e.tensor_scalar(
                        wf[:, 0:ww], wf[:, 0:ww], -RND, -1.0, ALU.add, ALU.max),
                        r=[wf], w=[wf])
                    S.op("dve", lambda e, ww=wid, wf=wf, qb=qb: e.tensor_scalar(
                        qb[:, 0:ww], wf[:, 0:ww], 1.0, 0.0, ALU.min, ALU.add),
                        r=[wf], w=[qb])
                    wqt = w1q_nat if wi == 0 else w2q_nat
                    S.dma("sp", lambda e, q_=wq_ap, c=c, ww=wid, qb=qb: e.dma_start(
                        out=q_[128 * c:128 * (c + 1), :], in_=qb[:, 0:ww]),
                        r=[qb], w=[wqt])
        if n_pro >= 5:
            jobs = []
            for fc in range(NF):
                for oc in range(8):
                    jobs.append((w1q_nat.ap()[512 * oc:512 * (oc + 1), 128 * fc:128 * (fc + 1)],
                                 wq1T.ap()[128 * fc:128 * (fc + 1), 512 * oc:512 * (oc + 1)], wq1T))
            for hc in range(NHC):
                for oc in range(2):
                    jobs.append((w2q_nat.ap()[512 * oc:512 * (oc + 1), 128 * hc:128 * (hc + 1)],
                                 wq2T.ap()[128 * hc:128 * (hc + 1), 512 * oc:512 * (oc + 1)], wq2T))
            S.pre("sp", [w1q_nat, w2q_nat], [])
            transpose_copy(jobs)

        with nc.psum_tensor("bkP", [1, 512], F32) as bkP:
            for half in (range(2) if n_pro >= 6 else []):
                for fc in range(NF):
                    S.dma("sp", lambda e, f=fc, h=half: e.dma_start(
                        out=wbuf[1][:, 0:512],
                        in_=wa_bf.ap()[128 * f:128 * (f + 1), D + 512 * h:D + 512 * (h + 1)]),
                        r=[wa_bf], w=[wbuf[1]])
                    S.op("act", lambda e: e.activation(sqbf[0][:, 0:512], wbuf[1][:, 0:512], AF.Square),
                         r=[wbuf[1]], w=[sqbf[0]])
                    S.pre("pe", [onescol, sqbf[0]], [bkP])
                    S.post("pe", mm(nc.tensor, bkP[0:1, :], onescol[:], sqbf[0][:, 0:512],
                                    fc == 0, fc == NF - 1), [onescol, sqbf[0]], [bkP])
                ap, tt = R("bk", 512 * half, 512 * (half + 1))
                S.op("act", lambda e, ap=ap: e.activation(ap, bkP[0:1, :], AF.Sqrt),
                     r=[bkP], w=[tt])
        ap, tt = R("bk", 0, 1024)
        S.op("dve", lambda e, ap=ap: e.tensor_scalar(
            ap, ap, 32.5, 0.0, ALU.mult, ALU.add), r=[tt], w=[tt])

        def ln_stats(ph):
            statP = [ph.enter_context(nc.psum_tensor(f"stP{next(_uid)}", [1, 512], F32)) for _ in range(2)]
            for off, w in NC:
                for fc in range(NF):
                    xb, sq = xbf[fc % 2], sqbf[fc % 2]
                    S.op("dve", lambda e, f=fc, o=off, ww=w, xb=xb: e.tensor_copy(
                        xb[:, 0:ww], x8[f][:, o:o + ww]), r=[x8[fc]], w=[xb])
                    S.op("act", lambda e, ww=w, xb=xb, sq=sq: e.activation(
                        sq[:, 0:ww], xb[:, 0:ww], AF.Square), r=[xb], w=[sq])
                    S.pre("pe", [onescol, xb], [statP[0]])
                    S.post("pe", mm(nc.tensor, statP[0][0:1, 0:w], onescol[:], xb[:, 0:w],
                                    fc == 0, fc == NF - 1), [onescol, xb], [statP[0]])
                    S.pre("pe", [onescol, sq], [statP[1]])
                    S.post("pe", mm(nc.tensor, statP[1][0:1, 0:w], onescol[:], sq[:, 0:w],
                                    fc == 0, fc == NF - 1), [onescol, sq], [statP[1]])
                ap, tt = R("sum", off, off + w)
                S.op("dve", lambda e, ap=ap, ww=w: e.tensor_copy(ap, statP[0][0:1, 0:ww]),
                     r=[statP[0]], w=[tt])
                ap, tt = R("sumsq", off, off + w)
                S.op("dve", lambda e, ap=ap, ww=w: e.tensor_copy(ap, statP[1][0:1, 0:ww]),
                     r=[statP[1]], w=[tt])


        def mean_rstd_bc(bcP, off, w, scratch):
            build_bcast(bcP, bc1, "sum", off, w)
            build_bcast(bcP, bc2, "sumsq", off, w)
            S.op("dve", lambda e, ww=w: e.tensor_scalar(
                bc1[:, 0:ww], bc1[:, 0:ww], 1.0 / D, 0.0, ALU.mult, ALU.add),
                r=[bc1], w=[bc1])
            S.op("dve", lambda e, ww=w: e.tensor_scalar(
                bc2[:, 0:ww], bc2[:, 0:ww], 1.0 / D, 1e-5, ALU.mult, ALU.add),
                r=[bc2], w=[bc2])
            S.op("act", lambda e, ww=w: e.activation(
                scratch[:, 0:ww], bc1[:, 0:ww], AF.Square), r=[bc1], w=[scratch])
            S.op("dve", lambda e, ww=w: e.tensor_tensor(
                bc2[:, 0:ww], bc2[:, 0:ww], scratch[:, 0:ww], ALU.subtract),
                r=[bc2, scratch], w=[bc2])
            S.op("act", lambda e, ww=w: e.activation(
                bc2[:, 0:ww], bc2[:, 0:ww], AF.Sqrt), r=[bc2], w=[bc2])
            S.op("dve", lambda e, ww=w: e.reciprocal(bc2[:, 0:ww], bc2[:, 0:ww]),
                 r=[bc2], w=[bc2])

        def build_bcast(bcP, bc, rowname, off, w):
            ap, tt = R(rowname, off, off + w)
            p = RP(rowname)
            S.pre("pe", [onesrow, tt], [bcP])
            S.post("pe", mm(nc.tensor, bcP[0:128, 0:w], onesrow[p:p + 1, :], ap, True, True),
                   [onesrow, tt], [bcP])
            S.op("dve", lambda e, ww=w: e.tensor_copy(bc[:, 0:ww], bcP[0:128, 0:ww]),
                 r=[bcP], w=[bc])

        def ln_to(ph, dest):
            bcP = ph.enter_context(nc.psum_tensor(f"lnb{next(_uid)}", [128, 512], F32))
            for off, w in NC:
                mean_rstd_bc(bcP, off, w, bc3)
                for fc in range(NF):
                    S.op("dve", lambda e, f=fc, o=off, ww=w: e.tensor_tensor(
                        dscr[:, o:o + ww], x8[f][:, o:o + ww], bc1[:, 0:ww], ALU.subtract),
                        r=[x8[fc], bc1], w=[dscr])
                    S.op("dve", lambda e, f=fc, o=off, ww=w: e.tensor_tensor(
                        dest[f][:, o:o + ww], dscr[:, o:o + ww], bc2[:, 0:ww], ALU.mult),
                        r=[dscr, bc2], w=[dest[fc]])

        def rezero_halo():
            for fc in range(NF):
                S.op("dve", lambda e, f=fc: e.tensor_tensor(
                    x8[f][:, 0:2], x8[f][:, 0:2], hmask[:, 0:2], ALU.mult),
                    r=[x8[fc], hmask], w=[x8[fc]])
                S.op("dve", lambda e, f=fc: e.tensor_tensor(
                    x8[f][:, 2050:2052], x8[f][:, 2050:2052], hmask[:, 2:4], ALU.mult),
                    r=[x8[fc], hmask], w=[x8[fc]])

        def tokmax_to_row(ph, src, dstrow, sqrt=False):
            trP = ph.enter_context(nc.psum_tensor(f"tk{next(_uid)}", [128, 512], F32))
            for toff, tw in TT:
                S.pre("pe", [src, ident], [trP])
                S.post("pe", mm(nc.tensor, trP[0:tw, 0:128], src[:, toff:toff + tw],
                                ident[:], True, True), [src, ident], [trP])
                S.op("dve", lambda e, ww=tw: e.reduce_max(
                    redc[0:ww, 0:1], trP[0:ww, 0:128], axis=AX.X), r=[trP], w=[redc])
                S.pre("pe", [redc, ident], [trP])
                S.post("pe", mm(nc.tensor, trP[0:1, 128:128 + tw], redc[0:tw, 0:1],
                                ident[0:tw, 0:tw], True, True), [redc, ident], [trP])
                ap, tt = R(dstrow, toff, toff + tw)
                if sqrt:
                    S.op("act", lambda e, ap=ap, ww=tw: e.activation(
                        ap, trP[0:1, 128:128 + ww], AF.Sqrt), r=[trP], w=[tt])
                else:
                    S.op("dve", lambda e, ap=ap, ww=tw: e.tensor_copy(
                        ap, trP[0:1, 128:128 + ww]), r=[trP], w=[tt])

        if n_phases >= 1 and not os.environ.get("SKIP_P1"):
            with contextlib.ExitStack() as ph:
                ln_stats(ph)
                ln_to(ph, s1)
                S.barrier()
            with contextlib.ExitStack() as ph:
                qsm = ph.enter_context(nc.sbuf_tensor("qsm", [128, H * 65], BF16))
                expk = ph.enter_context(nc.sbuf_tensor("expk", [128, D], BF16))
                qmx = ph.enter_context(nc.sbuf_tensor("qmx", [128, H], F32))
                qsum = ph.enter_context(nc.sbuf_tensor("qsum", [128, H], F32))
                qrec = ph.enter_context(nc.sbuf_tensor("qrec", [128, H], F32))
                dq = ph.enter_context(nc.sbuf_tensor("dq", [128, 1024], F32))
                dqk = ph.enter_context(nc.sbuf_tensor("dqk", [128, 1024], F32))
                bkb = ph.enter_context(nc.sbuf_tensor("bkb", [128, 1024], F32))
                ctxs = ph.enter_context(nc.sbuf_tensor("ctxs", [65, 1024], F32))
                vP = [ph.enter_context(nc.psum_tensor(f"vP{i}", [128, 512], F32)) for i in range(2)]
                psA = ph.enter_context(nc.psum_tensor("psA", [128, 2048], F32))
                ctxP = ph.enter_context(nc.psum_tensor("ctxP", [65, 1024], F32))

                wres = [s2[f][:, 0:2048] for f in range(NF)]
                for fc in range(NF):
                    S.dma("sp", lambda e, f=fc: e.dma_start(
                        out=wres[f], in_=wa_bf.ap()[128 * f:128 * (f + 1), 0:2048]),
                        r=[wa_bf], w=[s2[fc]])
                S.op("pool", lambda e: e.memset(
                    qsm[:].rearrange("p (h c) -> p h c", c=65)[:, :, 64:65], 1.0), w=[qsm])
                for half in range(2):
                    ap, tt = R("bk", 512 * half, 512 * (half + 1))
                    S.pre("pe", [onesrow, tt], [vP[0]])
                    S.post("pe", mm(nc.tensor, vP[0][0:128, 0:512], onesrow[64:65, :], ap,
                                    True, True), [onesrow, tt], [vP[0]])
                    S.op("dve", lambda e, h=half: e.tensor_copy(
                        bkb[:, 512 * h:512 * (h + 1)], vP[0][0:128, 0:512]), r=[vP[0]], w=[bkb])

                for ti, (toff, tw) in enumerate(TT):
                    rd = [s2[f] for f in range(NF)] + [s1[f] for f in range(NF)]
                    S.pre("pe", rd, [psA])
                    for fc in range(NF):
                        for oc in range(4):
                            inst = mm(nc.tensor, psA[0:tw, 512 * oc:512 * (oc + 1)],
                                      s1[fc][:, toff:toff + tw],
                                      wres[fc][:, 512 * oc:512 * (oc + 1)],
                                      fc == 0, fc == NF - 1)
                    S.post("pe", inst, rd, [psA])
                    S.op("dve", lambda e, ww=tw: e.reduce_max(
                        qmx[0:ww, :], psA[0:ww, 0:1024].rearrange("p (h c) -> p h c", c=DH),
                        axis=AX.X), r=[psA], w=[qmx])
                    S.op("dve", lambda e, ww=tw: e.tensor_tensor(
                        dq[0:ww, :].rearrange("p (h c) -> p h c", c=DH),
                        psA[0:ww, 0:1024].rearrange("p (h c) -> p h c", c=DH),
                        qmx[0:ww, :].unsqueeze(2).broadcast_to((ww, H, DH)),
                        ALU.subtract), r=[psA, qmx], w=[dq])
                    S.op("dve", lambda e, ww=tw: e.tensor_tensor(
                        dqk[0:ww, :], psA[0:ww, 1024:2048], bkb[0:ww, :], ALU.subtract),
                        r=[psA, bkb], w=[dqk])
                    S.op("act", lambda e, ww=tw: e.activation(
                        dq[0:ww, :], dq[0:ww, :], AF.Exp, scale=0.125), r=[dq], w=[dq])
                    S.op("act", lambda e, ww=tw: e.activation(
                        expk[0:ww, :], dqk[0:ww, :], AF.Exp), r=[dqk], w=[expk])
                    S.op("dve", lambda e, ww=tw: e.reduce_sum(
                        qsum[0:ww, :], dq[0:ww, :].rearrange("p (h c) -> p h c", c=DH),
                        axis=AX.X), r=[dq], w=[qsum])
                    S.op("dve", lambda e, ww=tw: e.reciprocal(
                        qrec[0:ww, :], qsum[0:ww, :]), r=[qsum], w=[qrec])
                    S.op("dve", lambda e, ww=tw: e.tensor_tensor(
                        qsm[0:ww, :].rearrange("p (h c) -> p h c", c=65)[:, :, 0:64],
                        dq[0:ww, :].rearrange("p (h c) -> p h c", c=DH),
                        qrec[0:ww, :].unsqueeze(2).broadcast_to((ww, H, DH)),
                        ALU.mult), r=[dq, qrec], w=[qsm])
                    if ti == 0:
                        S.op("pool", lambda e: e.memset(expk[0:2, :], 0.0), w=[expk])
                    kw = tw if ti < 16 else 2
                    S.pre("pe", [qsm, expk], [ctxP])
                    for hh in range(H):
                        inst = mm(nc.tensor, ctxP[0:65, 64 * hh:64 * (hh + 1)],
                                  qsm[0:kw, 65 * hh:65 * (hh + 1)],
                                  expk[0:kw, 64 * hh:64 * (hh + 1)],
                                  ti == 0 and hh in (0, 8), ti == 16)
                    S.post("pe", inst, [qsm, expk], [ctxP])

                S.op("dve", lambda e: e.tensor_copy(ctxs[:], ctxP[0:65, :]), r=[ctxP], w=[ctxs])
                S.dma("sp", lambda e: e.dma_start(out=cxl.ap(), in_=ctxs[:]), r=[ctxs], w=[cxl])
                S.cc(lambda e: e.collective_compute(
                    "AllReduce", ALU.add, replica_groups=REPL,
                    ins=[cxl.ap().opt()], outs=[cxg.ap().opt()]), r=[cxl], w=[cxg])

                for oc in range(NF):
                    wb = wbuf[oc % 2]
                    S.dma("sp", lambda e, o=oc, wb=wb: e.dma_start(
                        out=wb[:, 0:1024],
                        in_=wa_bf.ap().rearrange("(f p) o -> p f o", p=128)[:, :, 2048 + 128 * o:2048 + 128 * (o + 1)]),
                        r=[wa_bf], w=[wb])
                    for ni, (off, w) in enumerate(NC):
                        pb = vP[ni % 2]
                        S.pre("pe", [wb] + [s1[f] for f in range(NF)], [pb])
                        for fc in range(NF):
                            inst = mm(nc.tensor, pb[:, 0:w], wb[:, 128 * fc:128 * (fc + 1)],
                                      s1[fc][:, off:off + w], fc == 0, fc == NF - 1)
                        S.post("pe", inst, [wb] + [s1[f] for f in range(NF)], [pb])
                        S.op("dve", lambda e, o=oc, of=off, ww=w, p=pb: e.tensor_copy(
                            s2[o][:, of:of + ww], p[:, 0:ww]), r=[pb], w=[s2[oc]])
                S.barrier()
            with contextlib.ExitStack() as ph:
                zbs = ph.enter_context(nc.sbuf_tensor("zbs", [64, 1024], F32))
                ctxn = ph.enter_context(nc.sbuf_tensor("ctxn", [128, 512], BF16))
                ctxs2 = ph.enter_context(nc.sbuf_tensor("ctxs2", [65, 1024], F32))
                bcP2 = ph.enter_context(nc.psum_tensor("bcP2", [128, 512], F32))
                oPa = ph.enter_context(nc.psum_tensor("oPa", [64, 512], F32))
                oPb = ph.enter_context(nc.psum_tensor("oPb", [64, 512], F32))
                woP = [ph.enter_context(nc.psum_tensor(f"woP{i}", [128, 512], F32)) for i in range(2)]
                zrowr = ph.enter_context(nc.sbuf_tensor(f"zrw{next(_uid)}", [1, 1024], F32))
                zr, zrt = zrowr[0:1, 0:1024], zrowr
                S.dma("sp", lambda e: e.dma_start(out=ctxs2[:], in_=cxg.ap()), r=[cxg], w=[ctxs2])
                S.op("dve", lambda e: e.reciprocal(zr, ctxs2[64:65, :]), r=[ctxs2], w=[zrt])
                for half in range(2):
                    ap = zrowr[0:1, 512 * half:512 * (half + 1)]
                    S.pre("pe", [onesrow, zrt], [bcP2])
                    S.post("pe", mm(nc.tensor, bcP2[0:64, 0:512], onesrow[0:1, 0:64],
                                    ap, True, True), [onesrow, zrt], [bcP2])
                    S.op("dve", lambda e, h=half: e.tensor_copy(
                        zbs[0:64, 512 * h:512 * (h + 1)], bcP2[0:64, 0:512]), r=[bcP2], w=[zbs])
                for hh in range(H):
                    ro, co = 64 * (hh % 2), 64 * (hh // 2)
                    S.op("dve", lambda e, h=hh, ro=ro, co=co: e.tensor_tensor(
                        ctxn[ro:ro + 64, co:co + 64], ctxs2[0:64, 64 * h:64 * (h + 1)],
                        zbs[0:64, 64 * h:64 * (h + 1)], ALU.mult),
                        r=[ctxs2, zbs], w=[ctxn])
                for ch in range(NF):
                    for ni, (off, w) in enumerate(NC):
                        S.pre("pe", [ctxn, s2[ch]], [oPa])
                        S.post("pe", mm(nc.tensor, oPa[0:64, 0:w],
                                        ctxn[0:64, 64 * ch:64 * (ch + 1)],
                                        s2[ch][0:64, off:off + w], True, True),
                               [ctxn, s2[ch]], [oPa])
                        S.pre("pe", [ctxn, s2[ch]], [oPb])
                        S.post("pe", mm(nc.tensor, oPb[0:64, 0:w],
                                        ctxn[64:128, 64 * ch:64 * (ch + 1)],
                                        s2[ch][64:128, off:off + w], True, True),
                               [ctxn, s2[ch]], [oPb])
                        S.op("dve", lambda e, c=ch, of=off, ww=w: e.tensor_copy(
                            s2[c][0:64, of:of + ww], oPa[0:64, 0:ww]), r=[oPa], w=[s2[ch]])
                        S.op("dve", lambda e, c=ch, of=off, ww=w: e.tensor_copy(
                            s2[c][64:128, of:of + ww], oPb[0:64, 0:ww]), r=[oPb], w=[s2[ch]])
                for oc in range(NF):
                    wb = wbuf[oc % 2]
                    S.dma("sp", lambda e, o=oc, wb=wb: e.dma_start(
                        out=wb[:, 0:1024],
                        in_=woa_bf.ap().rearrange("(f p) o -> p f o", p=128)[:, :, 128 * o:128 * (o + 1)]),
                        r=[woa_bf], w=[wb])
                    for ni, (off, w) in enumerate(NC):
                        pb = woP[ni % 2]
                        S.pre("pe", [wb] + [s2[f] for f in range(NF)], [pb])
                        for fc in range(NF):
                            inst = mm(nc.tensor, pb[:, 0:w], wb[:, 128 * fc:128 * (fc + 1)],
                                      s2[fc][:, off:off + w], fc == 0, fc == NF - 1)
                        S.post("pe", inst, [wb] + [s2[f] for f in range(NF)], [pb])
                        S.op("dve", lambda e, o=oc, of=off, ww=w, p=pb: e.scalar_tensor_tensor(
                            x8[o][:, of:of + ww], p[:, 0:ww], b_out[:, o:o + 1],
                            x8[o][:, of:of + ww], ALU.add, ALU.add),
                            r=[pb, b_out, x8[oc]], w=[x8[oc]])
                S.barrier()
            rezero_halo()

        def ffn_phase(tag):
          def _qi_scope():
            with contextlib.ExitStack() as ph:
                ln_stats(ph)
                bcPf = ph.enter_context(nc.psum_tensor(f"bf{next(_uid)}", [128, 512], F32))
                mx = dscr
                for off, w in NC:
                    build_bcast(bcPf, bc1, "sum", off, w)
                    S.op("dve", lambda e, ww=w: e.tensor_scalar(
                        bc1[:, 0:ww], bc1[:, 0:ww], 1.0 / D, 0.0, ALU.mult, ALU.add),
                        r=[bc1], w=[bc1])
                    for fc in range(NF):
                        S.op("dve", lambda e, f=fc, o=off, ww=w: e.tensor_tensor(
                            bc2[:, 0:ww], x8[f][:, o:o + ww], bc1[:, 0:ww], ALU.subtract),
                            r=[x8[fc], bc1], w=[bc2])
                        S.op("pool", lambda e, ww=w: e.tensor_tensor(
                            bc2[:, 0:ww], bc2[:, 0:ww], bc2[:, 0:ww], ALU.mult),
                            r=[bc2], w=[bc2])
                        if fc == 0:
                            S.op("dve", lambda e, o=off, ww=w: e.tensor_copy(
                                mx[:, o:o + ww], bc2[:, 0:ww]), r=[bc2], w=[mx])
                        else:
                            S.op("dve", lambda e, o=off, ww=w: e.tensor_tensor(
                                mx[:, o:o + ww], bc2[:, 0:ww], mx[:, o:o + ww], ALU.max),
                                r=[bc2, mx], w=[mx])
                tokmax_to_row(ph, mx, "mx", sqrt=True)
                for off, w in NC:
                    mean_rstd_bc(bcPf, off, w, bc3)
                    build_bcast(bcPf, bc3, "mx", off, w)
                    dsl = dscr[:, off:off + w]
                    S.op("dve", lambda e, ww=w: e.tensor_tensor(
                        bc3[:, 0:ww], bc3[:, 0:ww], bc2[:, 0:ww], ALU.mult),
                        r=[bc3, bc2], w=[bc3])
                    S.op("dve", lambda e, ww=w: e.tensor_scalar(
                        bc3[:, 0:ww], bc3[:, 0:ww], 1e-5, 0.0, ALU.max, ALU.add),
                        r=[bc3], w=[bc3])
                    S.op("dve", lambda e, d=dsl: e.reciprocal(d, bc3[:, 0:w]),
                         r=[bc3], w=[dscr])
                    S.op("dve", lambda e, d=dsl, ww=w: e.tensor_tensor(
                        d, d, bc2[:, 0:ww], ALU.mult), r=[dscr, bc2], w=[dscr])
                    S.op("dve", lambda e, d=dsl: e.tensor_scalar(
                        d, d, 127.0, 0.0, ALU.mult, ALU.add), r=[dscr], w=[dscr])
                    S.op("dve", lambda e, d=dsl, ww=w: e.tensor_tensor(
                        bc2[:, 0:ww], bc1[:, 0:ww], d, ALU.mult),
                        r=[bc1, dscr], w=[bc2])
                    S.op("dve", lambda e, ww=w: e.tensor_scalar(
                        bc3[:, 0:ww], bc3[:, 0:ww], mwb[:, 0:1], 1.0 / 127.0,
                        ALU.mult, ALU.mult), r=[bc3, mwb], w=[bc3])
                    S.op("dve", lambda e, d=dsl, ww=w: e.tensor_copy(
                        bc1[:, 0:ww], d), r=[dscr], w=[bc1])
                    for fc in range(NF):
                        S.op("dve", lambda e, f=fc, o=off, ww=w: e.tensor_tensor(
                            dscr[:, o:o + ww], x8[f][:, o:o + ww], bc1[:, 0:ww], ALU.mult),
                            r=[x8[fc], bc1], w=[dscr])
                        S.op("dve", lambda e, o=off, ww=w: e.tensor_tensor(
                            dscr[:, o:o + ww], dscr[:, o:o + ww], bc2[:, 0:ww], ALU.subtract),
                            r=[dscr, bc2], w=[dscr])
                        S.op("dve", lambda e, o=off, ww=w: e.tensor_scalar(
                            dscr[:, o:o + ww], dscr[:, o:o + ww], RND, -RND, ALU.add, ALU.add),
                            r=[dscr], w=[dscr])
                        S.op("dve", lambda e, f=fc, o=off, ww=w: e.tensor_tensor(
                            s1[f][:, o:o + ww], dscr[:, o:o + ww], bc3[:, 0:ww], ALU.mult),
                            r=[dscr, bc3], w=[s1[fc]])
                S.barrier()
          if not os.environ.get("SKIP_QI"):
            _qi_scope()
          if True:
            if xqstage is not None:
                for fc in range(NF):
                    S.dma("sp", lambda e, f=fc: e.dma_start(
                        out=s1[f][:], in_=xqstage.ap()[128 * f:128 * (f + 1), :]),
                        r=[xqstage], w=[s1[fc]])
                S.barrier()
            with contextlib.ExitStack() as ph:
                hs = [ph.enter_context(nc.sbuf_tensor(f"hs{tag}{i}", [128, T], F32)) for i in range(2)]
                utb = [ph.enter_context(nc.sbuf_tensor(f"ut{tag}{i}", [128, T], F32)) for i in range(2)]
                s2invb = ph.enter_context(nc.sbuf_tensor(f"s2i{tag}", [128, T], F32))
                hP = [ph.enter_context(nc.psum_tensor(f"hP{tag}{i}", [128, 512], F32)) for i in range(4)]
                hmx = dscr
                wsrc = wqstage if wqstage is not None else wq1T
                for hc in range(NHC):
                    wb = wbuf[hc % 2]
                    hcur = hs[hc % 2]
                    ut = utb[hc % 2]
                    S.dma("sp", lambda e, h=hc, wb=wb, ws=wsrc: e.dma_start(
                        out=wb[:, 0:1024],
                        in_=ws.ap().rearrange("(f p) o -> p f o", p=128)[:, :, 128 * h:128 * (h + 1)]),
                        r=[wsrc], w=[wb])
                    for ni, (off, w) in enumerate(NC):
                        pb = hP[ni % 4]
                        S.pre("pe", [wb] + [s1[f] for f in range(NF)], [pb])
                        for fc in range(NF):
                            inst = mm(nc.tensor, pb[:, 0:w], wb[:, 128 * fc:128 * (fc + 1)],
                                      s1[fc][:, off:off + w], fc == 0, fc == NF - 1)
                        S.post("pe", inst, [wb] + [s1[f] for f in range(NF)], [pb])
                        S.op("act", lambda e, of=off, ww=w, p=pb, hc_=hcur: e.copy(
                            hc_[:, of:of + ww], p[:, 0:ww]), r=[pb], w=[hcur])
                    S.op("dve", lambda e, h=hc, hc_=hcur: e.tensor_scalar(
                        ut[:, :], hc_[:, :], convwt[:, 3 * h + 1:3 * h + 2],
                        convbt[:, h:h + 1], ALU.mult, ALU.add),
                        r=[hcur, convwt, convbt], w=[ut])
                    S.op("dve", lambda e, h=hc, hc_=hcur: e.scalar_tensor_tensor(
                        ut[:, 1:T], hc_[:, 0:T - 1], convwt[:, 3 * h:3 * h + 1],
                        ut[:, 1:T], ALU.mult, ALU.add),
                        r=[hcur, convwt, ut], w=[ut])
                    S.op("dve", lambda e, h=hc, hc_=hcur: e.scalar_tensor_tensor(
                        ut[:, 0:T - 1], hc_[:, 1:T], convwt[:, 3 * h + 2:3 * h + 3],
                        ut[:, 0:T - 1], ALU.mult, ALU.add),
                        r=[hcur, convwt, ut], w=[ut])
                    S.op("act", lambda e: e.activation(ut[:, :], ut[:, :], AF.Silu),
                         r=[ut], w=[ut])
                    if os.environ.get("DUMP_PRECONV"):
                        S.dma("sp", lambda e, h=hc, hc_=hcur: e.dma_start(
                            out=hd.ap()[128 * h:128 * (h + 1), :], in_=hc_[:, :]), r=[hcur], w=[hd])
                    else:
                        S.dma("sp", lambda e, h=hc: e.dma_start(
                            out=hd.ap()[128 * h:128 * (h + 1), :], in_=ut[:, :]), r=[ut], w=[hd])
                    S.op("pool", lambda e, hc_=hcur: e.tensor_tensor(
                        hc_[:, :], ut[:, :], ut[:, :], ALU.mult), r=[ut], w=[hcur])
                    if hc == 0:
                        S.op("dve", lambda e, hc_=hcur: e.tensor_copy(hmx[:, :], hc_[:, :]),
                             r=[hcur], w=[hmx])
                    else:
                        S.op("dve", lambda e, hc_=hcur: e.tensor_tensor(
                            hmx[:, :], hc_[:, :], hmx[:, :], ALU.max), r=[hcur, hmx], w=[hmx])
                tokmax_to_row(ph, hmx, "mx", sqrt=True)
                bcPg = ph.enter_context(nc.psum_tensor(f"bg{next(_uid)}", [128, 512], F32))
                for off, w in NC:
                    build_bcast(bcPg, bc1, "mx", off, w)
                    S.op("dve", lambda e, ww=w: e.tensor_scalar(
                        bc1[:, 0:ww], bc1[:, 0:ww], 1e-5, 0.0, ALU.max, ALU.add),
                        r=[bc1], w=[bc1])
                    S.op("dve", lambda e, o=off, ww=w: e.tensor_scalar(
                        s2invb[:, o:o + ww], bc1[:, 0:ww], mwb[:, 1:2], 1.0 / 127.0,
                        ALU.mult, ALU.mult), r=[bc1, mwb], w=[s2invb])
                    S.op("dve", lambda e, ww=w: e.reciprocal(bc1[:, 0:ww], bc1[:, 0:ww]),
                         r=[bc1], w=[bc1])
                    S.op("dve", lambda e, ww=w: e.tensor_scalar(
                        bc1[:, 0:ww], bc1[:, 0:ww], 127.0, 0.0, ALU.mult, ALU.add),
                        r=[bc1], w=[bc1])
                    for hc in range(NHC):
                        hcur = hs[hc % 2]
                        qr = s2[hc % 2]
                        S.dma("sp", lambda e, h=hc, o=off, ww=w, hc_=hcur: e.dma_start(
                            out=hc_[:, o:o + ww], in_=hd.ap()[128 * h:128 * (h + 1), o:o + ww]),
                            r=[hd], w=[hcur])
                        S.op("dve", lambda e, o=off, ww=w, hc_=hcur: e.tensor_tensor(
                            bc2[:, 0:ww], hc_[:, o:o + ww], bc1[:, 0:ww], ALU.mult),
                            r=[hcur, bc1], w=[bc2])
                        S.op("dve", lambda e, o=off, ww=w, q=qr: e.tensor_scalar(
                            q[:, o:o + ww], bc2[:, 0:ww], RND, -RND, ALU.add, ALU.add),
                            r=[bc2], w=[qr])
                        S.dma("sp", lambda e, h=hc, o=off, ww=w, q=qr: e.dma_start(
                            out=xq2d.ap()[128 * h:128 * (h + 1), o:o + ww], in_=q[:, o:o + ww]),
                            r=[qr], w=[xq2d])
                S.barrier()

            with contextlib.ExitStack() as ph:
                oP = [ph.enter_context(nc.psum_tensor(f"oP{tag}{i}", [128, 512], F32)) for i in range(NF)]
                xq2r = [ph.enter_context(nc.sbuf_tensor(f"xr{tag}{i}", [128, 512], BF16)) for i in range(2)]
                for ni, (off, w) in enumerate(NC):
                    for hc in range(NHC):
                        rb, wb = xq2r[hc % 2], wbuf[hc % 2]
                        S.dma("sp", lambda e, h=hc, o=off, ww=w, r_=rb: e.dma_start(
                            out=r_[:, 0:ww], in_=xq2d.ap()[128 * h:128 * (h + 1), o:o + ww]),
                            r=[xq2d], w=[rb])
                        S.dma("pool", lambda e, h=hc, wb=wb: e.dma_start(
                            out=wb[:, 0:1024], in_=wq2T.ap()[128 * h:128 * (h + 1), :]),
                            r=[wq2T], w=[wb])
                        for oc in range(NF):
                            S.pre("pe", [wb, rb], [oP[oc]])
                            S.post("pe", mm(nc.tensor, oP[oc][:, 0:w],
                                            wb[:, 128 * oc:128 * (oc + 1)], rb[:, 0:w],
                                            hc == 0, hc == NHC - 1), [wb, rb], [oP[oc]])
                    for oc in range(NF):
                        S.op("dve", lambda e, o=oc, of=off, ww=w: e.tensor_tensor(
                            bc2[:, 0:ww], oP[o][:, 0:ww], s2invb[:, of:of + ww], ALU.mult),
                            r=[oP[oc], s2invb], w=[bc2])
                        S.op("dve", lambda e, o=oc, of=off, ww=w: e.tensor_tensor(
                            x8[o][:, of:of + ww], bc2[:, 0:ww], x8[o][:, of:of + ww], ALU.add),
                            r=[bc2, x8[oc]], w=[x8[oc]])
                S.barrier()

        if n_phases >= 2:
            ffn_phase("A")
            rezero_halo()

        if n_phases >= 3:
            with contextlib.ExitStack() as ph:
                ln_stats(ph)
                ln_to(ph, s1)
                S.barrier()
            with contextlib.ExitStack() as ph:
                kvt = ph.enter_context(nc.sbuf_tensor("kvt", [128, T], BF16))
                qmax8 = ph.enter_context(nc.sbuf_tensor("qmax8", [128, NF], F32))
                qmaxr = ph.enter_context(nc.sbuf_tensor("qmaxr", [1, NF], F32))
                vres = [ph.enter_context(nc.sbuf_tensor(f"vres{i}", [128, 1024], BF16)) for i in range(NF)]
                knacc = ph.enter_context(nc.sbuf_tensor("knacc", [2, NF], F32))
                kn2s = ph.enter_context(nc.sbuf_tensor("kn2s", [2, 512], F32))
                ksq = ph.enter_context(nc.sbuf_tensor("ksq", [128, 2048], BF16))
                kchunk = ph.enter_context(nc.sbuf_tensor("kchunk", [128, 2048], BF16))
                qkP = [ph.enter_context(nc.psum_tensor(f"qkP{i}", [128, 512], F32)) for i in range(2)]
                vP3 = ph.enter_context(nc.psum_tensor("vP3", [128, 1024], F32))
                trPq = ph.enter_context(nc.psum_tensor("trPq", [128, 512], F32))
                knP = ph.enter_context(nc.psum_tensor("knP", [2, 512], F32))

                for oc in range(16):
                    wb = wbuf[oc % 2]
                    S.dma("sp", lambda e, o=oc, wb=wb: e.dma_start(
                        out=wb[:, 0:1024],
                        in_=wbT.ap().rearrange("(f p) o -> p f o", p=128)[:, :, 128 * o:128 * (o + 1)]),
                        r=[wbT], w=[wb])
                    dst = s2[oc] if oc < 8 else kvt
                    for ni, (off, w) in enumerate(NC):
                        pb = qkP[ni % 2]
                        S.pre("pe", [wb] + [s1[f] for f in range(NF)], [pb])
                        for fc in range(NF):
                            inst = mm(nc.tensor, pb[:, 0:w], wb[:, 128 * fc:128 * (fc + 1)],
                                      s1[fc][:, off:off + w], fc == 0, fc == NF - 1)
                        S.post("pe", inst, [wb] + [s1[f] for f in range(NF)], [pb])
                        S.op("dve", lambda e, d=dst, of=off, ww=w, p=pb: e.tensor_copy(
                            d[:, of:of + ww], p[:, 0:ww]), r=[pb], w=[dst])
                    if oc < 8:
                        S.op("dve", lambda e, o=oc: e.reduce_max(
                            qmax8[:, o:o + 1], s2[o][:, :], axis=AX.X,
                            apply_absolute_value=True), r=[s2[oc]], w=[qmax8])
                    else:
                        S.dma("sp", lambda e, o=oc: e.dma_start(
                            out=kx.ap()[128 * (o - 8):128 * (o - 7), :],
                            in_=kvt[:, OWN0:OWN1]), r=[kvt], w=[kx])
                S.pre("pe", [qmax8, ident], [trPq])
                S.post("pe", mm(nc.tensor, trPq[0:8, 0:128], qmax8[:, 0:8], ident[:],
                                True, True), [qmax8, ident], [trPq])
                S.op("dve", lambda e: e.reduce_max(redc[0:8, 0:1], trPq[0:8, 0:128], axis=AX.X),
                     r=[trPq], w=[redc])
                S.pre("pe", [redc, ident], [trPq])
                S.post("pe", mm(nc.tensor, trPq[0:1, 128:136], redc[0:8, 0:1],
                                ident[0:8, 0:8], True, True), [redc, ident], [trPq])
                S.op("dve", lambda e: e.tensor_copy(qmaxr[0:1, 0:8], trPq[0:1, 128:136]),
                     r=[trPq], w=[qmaxr])

                for fc in range(NF):
                    S.dma("sp", lambda e, f=fc: e.dma_start(
                        out=vres[f][:], in_=wbT.ap()[128 * f:128 * (f + 1), 2048:3072]),
                        r=[wbT], w=[vres[fc]])
                for ti, (toff, tw) in enumerate(TT):
                    rd = [vres[f] for f in range(NF)] + [s1[f] for f in range(NF)]
                    S.pre("pe", rd, [vP3])
                    for fc in range(NF):
                        for oc in range(2):
                            inst = mm(nc.tensor, vP3[0:tw, 512 * oc:512 * (oc + 1)],
                                      s1[fc][:, toff:toff + tw],
                                      vres[fc][:, 512 * oc:512 * (oc + 1)],
                                      fc == 0, fc == NF - 1)
                    S.post("pe", inst, rd, [vP3])
                    S.op("dve", lambda e, ww=tw: e.tensor_copy(
                        kvt[0:ww, 0:1024], vP3[0:ww, :]), r=[vP3], w=[kvt])
                    lo, hi = max(toff, OWN0), min(toff + tw, OWN1)
                    if hi > lo:
                        S.dma("sp", lambda e, lo=lo, hi=hi, to=toff: e.dma_start(
                            out=vx.ap()[lo - OWN0:hi - OWN0, :],
                            in_=kvt[lo - to:hi - to, 0:1024]), r=[kvt], w=[vx])

                S.cc(lambda e: e.collective_compute(
                    "AllGather", ALU.bypass, replica_groups=REPL,
                    ins=[kx.ap().opt()], outs=[kg.ap().opt()]), r=[kx], w=[kg])
                S.cc(lambda e: e.collective_compute(
                    "AllGather", ALU.bypass, replica_groups=REPL,
                    ins=[vx.ap().opt()], outs=[vg.ap().opt()]), r=[vx], w=[vg])

                for r_ in range(2):
                    for c in range(NF):
                        S.dma("sp", lambda e, r2=r_, c=c: e.dma_start(
                            out=kchunk[:], in_=kg.ap()[1024 * r2 + 128 * c:1024 * r2 + 128 * (c + 1), :]),
                            r=[kg], w=[kchunk])
                        S.op("act", lambda e: e.activation(ksq[:], kchunk[:], AF.Square),
                             r=[kchunk], w=[ksq])
                        for n4 in range(4):
                            S.pre("pe", [sel2, ksq], [knP])
                            S.post("pe", mm(nc.tensor, knP[0:2, :], sel2[:],
                                            ksq[:, 512 * n4:512 * (n4 + 1)], True, True),
                                   [sel2, ksq], [knP])
                            S.op("dve", lambda e: e.tensor_copy(
                                kn2s[0:2, :], knP[0:2, :]), r=[knP], w=[kn2s])
                            S.op("dve", lambda e: e.reduce_max(
                                redc[0:2, 0:1], kn2s[0:2, :], axis=AX.X), r=[kn2s], w=[redc])
                            if r_ == 0 and n4 == 0:
                                S.op("dve", lambda e, c=c: e.tensor_copy(
                                    knacc[0:2, c:c + 1], redc[0:2, 0:1]), r=[redc], w=[knacc])
                            else:
                                S.op("dve", lambda e, c=c: e.tensor_tensor(
                                    knacc[0:2, c:c + 1], redc[0:2, 0:1],
                                    knacc[0:2, c:c + 1], ALU.max), r=[redc, knacc], w=[knacc])
                S.pre("pe", [knacc, ident], [trPq])
                S.post("pe", mm(nc.tensor, trPq[0:8, 136:138], knacc[0:2, 0:8],
                                ident[0:2, 0:2], True, True), [knacc, ident], [trPq])
                S.op("dve", lambda e: e.reduce_max(redc[0:8, 0:1], trPq[0:8, 136:138], axis=AX.X),
                     r=[trPq], w=[redc])
                S.pre("pe", [redc, ident], [trPq])
                S.post("pe", mm(nc.tensor, trPq[0:1, 138:146], redc[0:8, 0:1],
                                ident[0:8, 0:8], True, True), [redc, ident], [trPq])
                S.op("act", lambda e: e.activation(scal[0:1, 8:16], trPq[0:1, 138:146], AF.Sqrt),
                     r=[trPq], w=[scal])
                S.op("dve", lambda e: e.tensor_tensor(scal[0:1, 8:16], scal[0:1, 8:16],
                                                      qmaxr[0:1, 0:8], ALU.mult),
                     r=[scal, qmaxr], w=[scal])
                S.op("dve", lambda e: e.tensor_scalar(scal[0:1, 8:16], scal[0:1, 8:16],
                                                      -0.125, 0.0, ALU.mult, ALU.add),
                     r=[scal], w=[scal])
                S.barrier()
            with contextlib.ExitStack() as ph:
                ktile = [ph.enter_context(nc.sbuf_tensor("ktile0", [128, 4096], BF16))] * 2
                vstat = ph.enter_context(nc.sbuf_tensor("vstat", [128, 2 * 32 * 65], BF16))
                bbt = ph.enter_context(nc.sbuf_tensor("bbt", [128, 8], F32))
                zbs2 = ph.enter_context(nc.sbuf_tensor("zbs2", [64, 1024], F32))
                ptile = [ph.enter_context(nc.sbuf_tensor(f"pt{i}", [128, 2048], BF16)) for i in range(2)]
                zrowr = ph.enter_context(nc.sbuf_tensor(f"zrw{next(_uid)}", [1, 1024], F32))
                zrt = zrowr
                with nc.psum_tensor("bP3", [128, 512], F32) as bP3:
                    S.pre("pe", [onesrow, scal], [bP3])
                    S.post("pe", mm(nc.tensor, bP3[0:128, 0:8], onesrow[0:1, :],
                                    scal[0:1, 8:16], True, True), [onesrow, scal], [bP3])
                    S.op("dve", lambda e: e.tensor_copy(bbt[:, 0:8], bP3[0:128, 0:8]),
                         r=[bP3], w=[bbt])
                    S.barrier()
                STp = ph.enter_context(nc.psum_tensor("STp", [128, 2048], F32))
                pvP = [ph.enter_context(nc.psum_tensor(f"pvP{i}", [65, 512], F32)) for i in range(2)]
                zbP = ph.enter_context(nc.psum_tensor("zbP", [64, 1024], F32))
                S.op("pool", lambda e: e.memset(
                    vstat[:].rearrange("p (k c) -> p k c", c=65)[:, :, 64:65], 1.0), w=[vstat])

                for pr in range(8):
                    kt = ktile[pr % 2]
                    for r_ in range(2):
                        for hl in range(2):
                            S.dma("sp", lambda e, p=pr, r2=r_, hl=hl, kt=kt: e.dma_start(
                                out=kt[64 * hl:64 * (hl + 1), 2048 * r2:2048 * (r2 + 1)],
                                in_=kg.ap()[1024 * r2 + 64 * (2 * p + hl):1024 * r2 + 64 * (2 * p + hl) + 64, :]),
                                r=[kg], w=[kt])
                    for hl in range(2):
                        S.dma("sp", lambda e, p=pr, hl=hl: e.dma_start(
                            out=vstat[:].rearrange("p (k c) -> p k c", c=65)[:, 32 * hl:32 * (hl + 1), 0:64],
                            in_=vg.ap().rearrange("(k p) c -> p k c", p=128)[:, :, 64 * (2 * p + hl):64 * (2 * p + hl) + 64]),
                            r=[vg], w=[vstat])
                    for qi, (qoff, qw) in enumerate(NC):
                        def pv_emit(kb2, pt):
                            kb = 2 * kb2
                            S.pre("pe", [vstat, pt], [pvP[0]])
                            mm(nc.tensor, pvP[0][0:65, 0:qw],
                               vstat[:, 65 * kb:65 * (kb + 1)], pt[:, 0:qw],
                               kb == 0, False)
                            ia = mm(nc.tensor, pvP[0][0:65, 0:qw],
                                    vstat[:, 65 * (kb + 1):65 * (kb + 2)],
                                    pt[:, 1024:1024 + qw], False, kb == 30)
                            S.post("pe", ia, [vstat, pt], [pvP[0]])
                            S.pre("pe", [vstat, pt], [pvP[1]])
                            mm(nc.tensor, pvP[1][0:65, 0:qw],
                               vstat[:, 65 * (32 + kb):65 * (32 + kb) + 65],
                               pt[:, 512:512 + qw], kb == 0, False)
                            ib = mm(nc.tensor, pvP[1][0:65, 0:qw],
                                    vstat[:, 65 * (33 + kb):65 * (33 + kb) + 65],
                                    pt[:, 1536:1536 + qw], False, kb == 30)
                            S.post("pe", ib, [vstat, pt], [pvP[1]])
                        for kb2 in range(16):
                            kb = 2 * kb2
                            st = STp
                            S.pre("pe", [kt, s2[pr]], [st])
                            mm(nc.tensor, st[:, 0:qw], kt[0:64, 128 * kb:128 * (kb + 1)],
                               s2[pr][0:64, qoff:qoff + qw], True, True)
                            nc.tensor.matmul(
                                st[:, 512:512 + qw], kt[64:128, 128 * kb:128 * (kb + 1)],
                                s2[pr][64:128, qoff:qoff + qw], start=True, stop=True,
                                tile_position=(64, 0), skip_group_check=True)
                            mm(nc.tensor, st[:, 1024:1024 + qw],
                               kt[0:64, 128 * (kb + 1):128 * (kb + 2)],
                               s2[pr][0:64, qoff:qoff + qw], True, True)
                            i2 = nc.tensor.matmul(
                                st[:, 1536:1536 + qw], kt[64:128, 128 * (kb + 1):128 * (kb + 2)],
                                s2[pr][64:128, qoff:qoff + qw], start=True, stop=True,
                                tile_position=(64, 0), skip_group_check=True)
                            S.post("pe", i2, [kt, s2[pr]], [st])
                            pt = ptile[kb2 % 2]
                            if qw == 512:
                                S.op("act", lambda e, st=st, pt=pt, p=pr: e.activation(
                                    pt[:, :], st[:, :], AF.Exp, bias=bbt[:, p:p + 1],
                                    scale=0.125), r=[st, bbt], w=[pt])
                            else:
                                S.op("act", lambda e, st=st, pt=pt, p=pr, qw=qw: e.activation(
                                    pt[:, 0:2048].rearrange("p (s c) -> p s c", c=512)[:, :, 0:qw],
                                    st[:, 0:2048].rearrange("p (s c) -> p s c", c=512)[:, :, 0:qw],
                                    AF.Exp, bias=bbt[:, p:p + 1], scale=0.125),
                                    r=[st, bbt], w=[pt])
                            if kb2 > 0:
                                pv_emit(kb2 - 1, ptile[(kb2 - 1) % 2])
                        pv_emit(15, ptile[15 % 2])
                        S.op("dve", lambda e, qw=qw: e.reciprocal(
                            zrowr[0:1, 0:qw], pvP[0][64:65, 0:qw]), r=[pvP[0]], w=[zrt])
                        S.op("dve", lambda e, qw=qw: e.reciprocal(
                            zrowr[0:1, 512:512 + qw], pvP[1][64:65, 0:qw]), r=[pvP[1]], w=[zrt])
                        for hl in range(2):
                            S.pre("pe", [onesrow, zrt], [zbP])
                            S.post("pe", mm(nc.tensor, zbP[0:64, 512 * hl:512 * hl + qw],
                                            onesrow[0:1, 0:64],
                                            zrowr[0:1, 512 * hl:512 * hl + qw], True, True),
                                   [onesrow, zrt], [zbP])
                        S.op("act", lambda e, qw=qw: e.copy(
                            zbs2[0:64, 0:qw], zbP[0:64, 0:qw]), r=[zbP], w=[zbs2])
                        S.op("act", lambda e, qw=qw: e.copy(
                            zbs2[0:64, 512:512 + qw], zbP[0:64, 512:512 + qw]), r=[zbP], w=[zbs2])
                        S.op("dve", lambda e, p=pr, qo=qoff, qw=qw: e.tensor_tensor(
                            s2[p][0:64, qo:qo + qw], pvP[0][0:64, 0:qw],
                            zbs2[0:64, 0:qw], ALU.mult), r=[pvP[0], zbs2], w=[s2[pr]])
                        S.op("dve", lambda e, p=pr, qo=qoff, qw=qw: e.tensor_tensor(
                            s2[p][64:128, qo:qo + qw], pvP[1][0:64, 0:qw],
                            zbs2[0:64, 512:512 + qw], ALU.mult), r=[pvP[1], zbs2], w=[s2[pr]])
                S.barrier()
            with contextlib.ExitStack() as ph:
                woP = [ph.enter_context(nc.psum_tensor(f"woB{i}", [128, 512], F32)) for i in range(2)]
                for oc in range(NF):
                    wb = wbuf[oc % 2]
                    S.dma("sp", lambda e, o=oc, wb=wb: e.dma_start(
                        out=wb[:, 0:1024],
                        in_=wobT.ap().rearrange("(f p) o -> p f o", p=128)[:, :, 128 * o:128 * (o + 1)]),
                        r=[wobT], w=[wb])
                    for ni, (off, w) in enumerate(NC):
                        pb = woP[ni % 2]
                        S.pre("pe", [wb] + [s2[f] for f in range(NF)], [pb])
                        for fc in range(NF):
                            inst = mm(nc.tensor, pb[:, 0:w], wb[:, 128 * fc:128 * (fc + 1)],
                                      s2[fc][:, off:off + w], fc == 0, fc == NF - 1)
                        S.post("pe", inst, [wb] + [s2[f] for f in range(NF)], [pb])
                        S.op("dve", lambda e, o=oc, of=off, ww=w, p=pb: e.tensor_tensor(
                            x8[o][:, of:of + ww], p[:, 0:ww], x8[o][:, of:of + ww], ALU.add),
                            r=[pb, x8[oc]], w=[x8[oc]])
                S.barrier()
            rezero_halo()

        if n_phases >= 4:
            ffn_phase("B")

        if dump:
            if os.environ.get("DUMP_XQ1"):
                for fc in range(0, NF):
                    S.dma("sp", lambda e, f=fc: e.dma_start(
                        out=xq2d.ap()[128 * f:128 * (f + 1), :], in_=s1[f][:, :]),
                        r=[s1[fc]], w=[xq2d])
            dhd = nc.dram_tensor("dhd", [DFF, T], F32, kind="ExternalOutput")
            dxq = nc.dram_tensor("dxq", [DFF, T], BF16, kind="ExternalOutput")
            dwq = nc.dram_tensor("dwq", [D, DFF], BF16, kind="ExternalOutput")
            with nc.sbuf_tensor("dmpw", [128, DFF], BF16) as dmpw:
                for f in range(NF):
                    S.dma("sp", lambda e, f=f: e.dma_start(
                        out=dmpw[:], in_=wq1T.ap()[128 * f:128 * (f + 1), :]), r=[wq1T], w=[dmpw])
                    S.dma("sp", lambda e, f=f: e.dma_start(
                        out=dwq.ap()[128 * f:128 * (f + 1), :], in_=dmpw[:]), r=[dmpw], w=[dwq])
            with nc.sbuf_tensor("dmp", [128, T], F32) as dmp, \
                 nc.sbuf_tensor("dmpb", [128, T], BF16) as dmpb:
                for h in range(NHC):
                    S.dma("sp", lambda e, h=h: e.dma_start(
                        out=dmp[:], in_=hd.ap()[128 * h:128 * (h + 1), :]), r=[hd], w=[dmp])
                    S.dma("sp", lambda e, h=h: e.dma_start(
                        out=dhd.ap()[128 * h:128 * (h + 1), :], in_=dmp[:]), r=[dmp], w=[dhd])
                    S.dma("sp", lambda e, h=h: e.dma_start(
                        out=dmpb[:], in_=xq2d.ap()[128 * h:128 * (h + 1), :]), r=[xq2d], w=[dmpb])
                    S.dma("sp", lambda e, h=h: e.dma_start(
                        out=dxq.ap()[128 * h:128 * (h + 1), :], in_=dmpb[:]), r=[dmpb], w=[dxq])
                S.barrier()
        for fc in range(NF):
            S.dma("sp", lambda e, f=fc: e.dma_start(
                out=yout.ap()[128 * f:128 * (f + 1), :], in_=x8[f][:, OWN0:OWN1]),
                r=[x8[fc]], w=[yout])
        S.barrier()

    return nc


_CACHE = {}


def _stage(inputs):
    x = np.asarray(inputs["L0_attn_a_x"], dtype=np.float32)
    base = {
        "wqkv_a": np.ascontiguousarray(inputs["L0_attn_a_w_qkv"], dtype=np.float32),
        "wout_a": np.ascontiguousarray(inputs["L0_attn_a_w_out"], dtype=np.float32),
        "bout_a": np.ascontiguousarray(inputs["L0_attn_a_b_out"], dtype=np.float32),
        "wqkv_b": np.ascontiguousarray(inputs["L0_attn_b_w_qkv"], dtype=np.float32),
        "wout_b": np.ascontiguousarray(inputs["L0_attn_b_w_out"], dtype=np.float32),
        "w1": np.ascontiguousarray(inputs["L0_ffn_w1"], dtype=np.float32),
        "convw": np.ascontiguousarray(
            np.asarray(inputs["L0_ffn_conv_w"], dtype=np.float32).reshape(DFF, 3).T),
        "convb": np.ascontiguousarray(inputs["L0_ffn_conv_b"], dtype=np.float32),
        "w2": np.ascontiguousarray(inputs["L0_ffn_w2"], dtype=np.float32),
        "ident": np.eye(128, dtype=np.float32),
        "sel2": np.concatenate([
            np.repeat(np.float32([[1], [0]]), 64, axis=0),
            np.repeat(np.float32([[0], [1]]), 64, axis=0)], axis=1).astype(np.float32),
        "onescol_in": np.ones((128, 1), dtype=np.float32),
        "onesrow_in": np.ones((128, 128), dtype=np.float32),
    }
    import os as _os
    if _os.environ.get("BYPASS_XQ"):
        import ml_dtypes as _md
        base["xqstage"] = np.load("/root/problem/_xqstage.npy").view(_md.bfloat16)
    if _os.environ.get("BYPASS_WQ"):
        import ml_dtypes as _md
        w1f = np.asarray(inputs.get("L0_ffn_w1", base["w1"]), np.float64)
        tern = np.clip(np.rint(w1f / np.abs(w1f).mean()), -1, 1)
        base["wqstage"] = np.ascontiguousarray(tern.T).astype(_md.bfloat16)
    in_maps = []
    for c in range(8):
        b, hh = c // 2, c % 2
        s, e = 2048 * hh, 2048 * hh + 2048
        pad = np.zeros((T, D), dtype=np.float32)
        lo, hi = s - 2, e + 2
        clo, chi = max(lo, 0), min(hi, 4096)
        pad[clo - lo:chi - lo, :] = x[b, clo:chi, :]
        hm = np.ones((128, 4), dtype=np.float32)
        if hh == 0:
            hm[:, 0:2] = 0.0
        else:
            hm[:, 2:4] = 0.0
        m = dict(base)
        m["xin"] = np.ascontiguousarray(pad.T)
        m["hmask"] = hm
        in_maps.append(m)
    return in_maps


def kernel(**inputs):
    n_phases = int(inputs.pop("_n_phases", 4))
    dump = bool(inputs.pop("_dump", False))
    key = (n_phases, dump)
    if key not in _CACHE:
        _CACHE[key] = build(n_phases, dump)
    nc = _CACHE[key]
    in_maps = _stage(inputs)
    res = run_bass_kernel_spmd(nc, in_maps, core_ids=list(range(8)))
    out = np.empty((4, 4096, 1024), dtype=np.float32)
    for c in range(8):
        b, hh = c // 2, c % 2
        out[b, 2048 * hh:2048 * (hh + 1), :] = res.results[c]["y"].T
    if dump:
        kernel._dbg = res.results
    return out


# version 07
# speedup vs baseline: 1.2744x; 1.0046x over previous
import contextlib
import itertools

import numpy as np
import concourse.bass as bass
import concourse.mybir as mybir
from concourse.bass_utils import run_bass_kernel_spmd

F32 = mybir.dt.float32
BF16 = mybir.dt.bfloat16
AF = mybir.ActivationFunctionType
ALU = mybir.AluOpType
AX = mybir.AxisListType

T = 2052
OWN0, OWN1 = 2, 2050
D = 1024
H = 16
DH = 64
DFF = 4096
NF = 8
NHC = 32
RND = float(1.5 * 2.0 ** 23)
TT = [(128 * i, 128) for i in range(16)] + [(2048, 4)]
NC = [(0, 512), (512, 512), (1024, 512), (1536, 512), (2048, 4)]
REPL = [[0, 1], [2, 3], [4, 5], [6, 7]]
_uid = itertools.count()


NSLOT = 12


class Sched:

    def __init__(self, nc, sems):
        self.nc = nc
        self.sems = sems
        self.cnt = {s: 0 for s in sems}
        self.hwm = {}
        self.writers = {}
        self.readers = {}
        self.eng = {"pe": nc.tensor, "act": nc.scalar, "dve": nc.vector,
                    "pool": nc.gpsimd, "sp": nc.sync}
        self.slot = {"sp": 0, "pool": 0}

    def _wait(self, eng, sem, val):
        if val <= 0 or self.hwm.get((eng, sem), 0) >= val:
            return
        self.hwm[(eng, sem)] = val
        self.eng[eng].wait_ge(self.sems[sem], val)

    def pre(self, eng, reads, writes):
        for b in reads:
            for t in self.writers.get(id(b), ()):
                self._wait(eng, t[0], t[1])
        for b in writes:
            for t in self.readers.get(id(b), ()):
                self._wait(eng, t[0], t[1])

    def post(self, sem, inst, reads, writes, amt=1):
        inst.then_inc(self.sems[sem], amt)
        self.cnt[sem] += amt
        tok = (sem, self.cnt[sem])
        for b in writes:
            self.writers.setdefault(id(b), []).append(tok)
            self.readers[id(b)] = []
        for b in reads:
            self.readers.setdefault(id(b), []).append(tok)

    def op(self, eng, f, r=(), w=()):
        self.pre(eng, r, w)
        self.post(eng, f(self.eng[eng]), r, w)

    def dma(self, q, f, r=(), w=()):
        s = self.slot[q]
        self.slot[q] = (s + 1) % NSLOT
        sem = f"{q}d{s}"
        self._wait(q, sem, self.cnt[sem])
        self.pre(q, r, w)
        self.post(sem, f(self.eng[q]), r, w, 16)

    def cc(self, f, r=(), w=()):
        self.pre("pool", r, w)
        self._wait("pool", "cc", self.cnt["cc"])
        self.post("cc", f(self.eng["pool"]), r, w)

    def barrier(self):
        for eng in self.eng:
            for sem, v in self.cnt.items():
                self._wait(eng, sem, v)
        self.writers.clear()
        self.readers.clear()


def build(n_phases=4, dump=False):
    import os
    n_pro = int(os.environ.get('N_PRO', '9'))
    nc = bass.Bass("TRN2", target_bir_lowering=False)

    xin = nc.dram_tensor("xin", [D, T], F32, kind="ExternalInput")
    wqkv_a = nc.dram_tensor("wqkv_a", [D, 3 * D], F32, kind="ExternalInput")
    wout_a = nc.dram_tensor("wout_a", [D, D], F32, kind="ExternalInput")
    bout_a = nc.dram_tensor("bout_a", [D], F32, kind="ExternalInput")
    wqkv_b = nc.dram_tensor("wqkv_b", [3 * D, D], F32, kind="ExternalInput")
    wout_b = nc.dram_tensor("wout_b", [D, D], F32, kind="ExternalInput")
    w1 = nc.dram_tensor("w1", [DFF, D], F32, kind="ExternalInput")
    convw = nc.dram_tensor("convw", [3, DFF], F32, kind="ExternalInput")
    convb = nc.dram_tensor("convb", [DFF], F32, kind="ExternalInput")
    w2 = nc.dram_tensor("w2", [D, DFF], F32, kind="ExternalInput")
    ident_in = nc.dram_tensor("ident", [128, 128], F32, kind="ExternalInput")
    hmask_in = nc.dram_tensor("hmask", [128, 4], F32, kind="ExternalInput")
    sel2_in = nc.dram_tensor("sel2", [128, 2], F32, kind="ExternalInput")
    onescol_in = nc.dram_tensor("onescol_in", [128, 1], F32, kind="ExternalInput")
    onesrow_in = nc.dram_tensor("onesrow_in", [128, 128], F32, kind="ExternalInput")
    xqstage = (nc.dram_tensor("xqstage", [D, T], BF16, kind="ExternalInput")
               if os.environ.get("BYPASS_XQ") else None)
    wqstage = (nc.dram_tensor("wqstage", [D, DFF], BF16, kind="ExternalInput")
               if os.environ.get("BYPASS_WQ") else None)
    yout = nc.dram_tensor("y", [D, 2048], F32, kind="ExternalOutput")

    wa_bf = nc.dram_tensor("wa_bf", [D, 3 * D], BF16)
    woa_bf = nc.dram_tensor("woa_bf", [D, D], BF16)
    wb_nat = nc.dram_tensor("wb_nat", [3 * D, D], BF16)
    wbT = nc.dram_tensor("wbT", [D, 3 * D], BF16)
    wob_nat = nc.dram_tensor("wob_nat", [D, D], BF16)
    wobT = nc.dram_tensor("wobT", [D, D], BF16)
    w1q_nat = nc.dram_tensor("w1q_nat", [DFF, D], BF16)
    wq1T = nc.dram_tensor("wq1T", [D, DFF], BF16)
    w2q_nat = nc.dram_tensor("w2q_nat", [D, DFF], BF16)
    wq2T = nc.dram_tensor("wq2T", [DFF, D], BF16)
    kx = nc.dram_tensor("kx", [D, 2048], BF16)
    vx = nc.dram_tensor("vx", [2048, D], BF16)
    kg = nc.dram_tensor("kg", [2 * D, 2048], BF16)
    vg = nc.dram_tensor("vg", [4096, D], BF16)
    cxl = nc.dram_tensor("cxl", [65, D], F32)
    cxg = nc.dram_tensor("cxg", [65, D], F32)
    hd = nc.dram_tensor("hd", [DFF, T], F32)
    xq2d = nc.dram_tensor("xq2d", [DFF, T], BF16)

    stk = contextlib.ExitStack()
    sb = lambda n, s, d: stk.enter_context(nc.sbuf_tensor(n, s, d))
    sems = {n: stk.enter_context(nc.semaphore(n + "_s"))
            for n in ("pe", "act", "dve", "pool", "sp", "cc")}
    for q in ("sp", "pool"):
        for s in range(NSLOT):
            sems[f"{q}d{s}"] = stk.enter_context(nc.semaphore(f"{q}d{s}_s"))
    S = Sched(nc, sems)

    x8 = [sb(f"x{i}", [128, T], F32) for i in range(NF)]
    s1 = [sb(f"s1_{i}", [128, T], BF16) for i in range(NF)]
    s2 = [sb(f"s2_{i}", [128, T], BF16) for i in range(NF)]
    dscr = sb("dscr", [128, T], F32)
    xbf = [sb(f"xbf{i}", [128, 512], BF16) for i in range(2)]
    sqbf = [sb(f"sqbf{i}", [128, 512], BF16) for i in range(2)]
    wbuf = [sb(f"wbuf{i}", [128, 2048], BF16) for i in range(2)]
    ident = sb("identt", [128, 128], F32)
    hmask = sb("hmaskt", [128, 4], F32)
    sel2 = sb("sel2t", [128, 2], BF16)
    onescol = sb("onescolt", [128, 1], BF16)
    onesrow = sb("onesrowt", [128, 128], F32)
    b_out = sb("b_outt", [128, NF], F32)
    convwt = sb("convwt", [128, 3 * NHC], F32)
    convbt = sb("convbt", [128, NHC], F32)
    mwrow = sb("mwrow", [1, 4], F32)
    scal = sb("scal", [1, 16], F32)
    redc = sb("redc", [128, 1], F32)
    bc1 = sb("bc1", [128, 512], F32)
    bc2 = sb("bc2", [128, 512], F32)
    bc3 = sb("bc3", [128, 512], F32)
    rowt = sb("rowt", [65, T], F32)
    mwb = sb("mwb", [128, 4], F32)
    _ROWP = {"sum": 0, "sumsq": 32, "mx": 64, "bk": 64}

    def R(name, c0=0, c1=T):
        p = _ROWP[name]
        return rowt[p:p + 1, c0:c1], rowt

    def RP(name):
        return _ROWP[name]

    def mm(e, out, lhsT, rhs, start, stop):
        return e.matmul(out, lhsT, rhs, start=start, stop=stop,
                        skip_group_check=True)

    with stk:
        S.dma("sp", lambda e: e.dma_start(out=ident[:], in_=ident_in.ap()), w=[ident])
        S.dma("sp", lambda e: e.dma_start(out=hmask[:], in_=hmask_in.ap()), w=[hmask])
        S.dma("pool", lambda e: e.dma_start(out=sel2[:], in_=sel2_in.ap()), w=[sel2])
        S.dma("pool", lambda e: e.dma_start(out=onescol[:], in_=onescol_in.ap()), w=[onescol])
        S.dma("sp", lambda e: e.dma_start(out=onesrow[:], in_=onesrow_in.ap()), w=[onesrow])
        for c in range(NF):
            S.dma("sp", lambda e, c=c: e.dma_start(out=x8[c][:], in_=xin.ap()[128 * c:128 * (c + 1), :]), w=[x8[c]])
        S.dma("pool", lambda e: e.dma_start(out=wa_bf.ap(), in_=wqkv_a.ap()), w=[wa_bf])
        S.dma("pool", lambda e: e.dma_start(out=woa_bf.ap(), in_=wout_a.ap()), w=[woa_bf])
        S.dma("pool", lambda e: e.dma_start(out=wb_nat.ap(), in_=wqkv_b.ap()), w=[wb_nat])
        S.dma("pool", lambda e: e.dma_start(out=wob_nat.ap(), in_=wout_b.ap()), w=[wob_nat])

        with nc.psum_tensor("prT", [128, 512], F32) as prT, \
             nc.sbuf_tensor("rload", [1, 512], F32) as rload:
            def load_cols(dram_ap_1d, n, dst, colmap):
                for j in range(n):
                    S.dma("sp", lambda e, j=j: e.dma_start(
                        out=rload[0:1, 0:128],
                        in_=dram_ap_1d[128 * j:128 * (j + 1)].rearrange("(o p) -> o p", o=1)),
                        r=[], w=[rload])
                    S.pre("pe", [rload, ident], [prT])
                    S.post("pe", mm(nc.tensor, prT[0:128, 0:1], rload[0:1, 0:128],
                                    ident[0:1, 0:1], True, True), [rload, ident], [prT])
                    S.op("dve", lambda e, j=j: e.tensor_copy(
                        dst[:, colmap(j):colmap(j) + 1], prT[0:128, 0:1]),
                        r=[prT], w=[dst])
            if n_pro >= 2:
                load_cols(bout_a.ap(), NF, b_out, lambda j: j)
                load_cols(convb.ap(), NHC, convbt, lambda j: j)
                for k in range(3):
                    cwk = convw.ap()[k:k + 1, :].rearrange("one n -> (one n)")
                    load_cols(cwk, NHC, convwt, lambda j, k=k: 3 * j + k)

        slabs = [(wbuf[i // 4], 512 * (i % 4)) for i in range(8)]

        def transpose_copy(jobs):
            for b_ in range(0, len(jobs), 8):
                grp = jobs[b_:b_ + 8]
                for i, (sa, da, dt_) in enumerate(grp):
                    t_, o_ = slabs[i]
                    S.dma("sp", lambda e, sa=sa, t_=t_, o_=o_: e.dma_start_transpose(
                        t_[:, o_:o_ + 512], sa), r=[], w=[t_])
                for i, (sa, da, dt_) in enumerate(grp):
                    t_, o_ = slabs[i]
                    S.dma("pool", lambda e, da=da, t_=t_, o_=o_: e.dma_start(
                        out=da, in_=t_[:, o_:o_ + 512]), r=[t_], w=[dt_])
        if n_pro >= 3:
            jobs = []
            for fc in range(NF):
                for oc in range(6):
                    jobs.append((wb_nat.ap()[512 * oc:512 * (oc + 1), 128 * fc:128 * (fc + 1)],
                                 wbT.ap()[128 * fc:128 * (fc + 1), 512 * oc:512 * (oc + 1)], wbT))
                for oc in range(2):
                    jobs.append((wob_nat.ap()[512 * oc:512 * (oc + 1), 128 * fc:128 * (fc + 1)],
                                 wobT.ap()[128 * fc:128 * (fc + 1), 512 * oc:512 * (oc + 1)], wobT))
            S.pre("sp", [wb_nat, wob_nat], [])
            transpose_copy(jobs)

        with nc.sbuf_tensor("absacc", [128, 48], F32) as absacc, \
             nc.sbuf_tensor("qbf0", [128, 2048], BF16) as qbf0, \
             nc.sbuf_tensor("qbf1", [128, 2048], BF16) as qbf1, \
             nc.sbuf_tensor("wfa", [128, 2048], F32) as wfa, \
             nc.sbuf_tensor("wfb", [128, 2048], F32) as wfb, \
             nc.psum_tensor("pT0", [128, 512], F32) as pT0:
            wfs = [wfa, wfb]
            qbfs = [qbf0, qbf1]
            w2v = w2.ap().rearrange("a (b c) -> (a b) c", b=2)
            w2qv = w2q_nat.ap().rearrange("a (b c) -> (a b) c", b=2)
            for wi, (wap, nch, wid) in (enumerate(((w1.ap(), NHC, D), (w2v, 16, 2048))) if n_pro >= 4 else []):
                for c in range(nch):
                    wf = wfs[c % 2]
                    S.dma("sp", lambda e, w_=wap, c=c, ww=wid, wf=wf: e.dma_start(
                        out=wf[:, 0:ww], in_=w_[128 * c:128 * (c + 1), :]), w=[wf])
                    col = c if wi == 0 else NHC + c
                    S.op("dve", lambda e, col=col, ww=wid, wf=wf: e.reduce_sum(
                        absacc[:, col:col + 1], wf[:, 0:ww], axis=AX.X,
                        apply_absolute_value=True), r=[wf], w=[absacc])
            for wi, (lo, n, total) in (enumerate(((0, NHC, DFF * D), (NHC, 16, D * DFF))) if n_pro >= 4 else []):
                S.pre("pe", [absacc, ident], [pT0])
                S.post("pe", mm(nc.tensor, pT0[0:n, 0:128], absacc[:, lo:lo + n],
                                ident[:], True, True), [absacc, ident], [pT0])
                S.op("dve", lambda e, n=n: e.reduce_sum(
                    redc[0:n, 0:1], pT0[0:n, 0:128], axis=AX.X), r=[pT0], w=[redc])
                S.pre("pe", [redc, ident], [pT0])
                S.post("pe", mm(nc.tensor, pT0[0:1, 128:128 + n], redc[0:n, 0:1],
                                ident[0:n, 0:n], True, True), [redc, ident], [pT0])
                S.op("dve", lambda e, wi=wi, n=n: e.reduce_sum(
                    scal[0:1, wi:wi + 1], pT0[0:1, 128:128 + n], axis=AX.X), r=[pT0], w=[scal])
                S.op("dve", lambda e, wi=wi, tot=total: e.tensor_scalar(
                    mwrow[0:1, wi:wi + 1], scal[0:1, wi:wi + 1], 1.0 / tot, 1e-5,
                    ALU.mult, ALU.max), r=[scal], w=[mwrow])
            if n_pro < 4:
                S.op("pool", lambda e: e.memset(mwrow[0:1, :], 1.0), w=[mwrow])
            S.op("dve", lambda e: e.reciprocal(mwrow[0:1, 2:4], mwrow[0:1, 0:2]),
                 r=[mwrow], w=[mwrow])
            S.pre("pe", [onesrow, mwrow], [pT0])
            S.post("pe", mm(nc.tensor, pT0[0:128, 256:260], onesrow[0:1, :],
                            mwrow[0:1, 0:4], True, True), [onesrow, mwrow], [pT0])
            S.op("dve", lambda e: e.tensor_copy(mwb[:, 0:4], pT0[0:128, 256:260]),
                 r=[pT0], w=[mwb])
            for wi, (wap, wq_ap, nch, wid) in (enumerate(
                    ((w1.ap(), w1q_nat.ap(), NHC, D), (w2v, w2qv, 16, 2048))) if n_pro >= 4 else []):
                for c in range(nch):
                    wf = wfs[c % 2]
                    qb = qbfs[c % 2]
                    S.dma("sp", lambda e, w_=wap, c=c, ww=wid, wf=wf: e.dma_start(
                        out=wf[:, 0:ww], in_=w_[128 * c:128 * (c + 1), :]), w=[wf])
                    S.op("dve", lambda e, ww=wid, wi=wi, wf=wf: e.tensor_scalar(
                        wf[:, 0:ww], wf[:, 0:ww], mwb[:, 2 + wi:3 + wi], RND,
                        ALU.mult, ALU.add), r=[wf, mwb], w=[wf])
                    S.op("dve", lambda e, ww=wid, wf=wf: e.tensor_scalar(
                        wf[:, 0:ww], wf[:, 0:ww], -RND, -1.0, ALU.add, ALU.max),
                        r=[wf], w=[wf])
                    S.op("dve", lambda e, ww=wid, wf=wf, qb=qb: e.tensor_scalar(
                        qb[:, 0:ww], wf[:, 0:ww], 1.0, 0.0, ALU.min, ALU.add),
                        r=[wf], w=[qb])
                    wqt = w1q_nat if wi == 0 else w2q_nat
                    S.dma("sp", lambda e, q_=wq_ap, c=c, ww=wid, qb=qb: e.dma_start(
                        out=q_[128 * c:128 * (c + 1), :], in_=qb[:, 0:ww]),
                        r=[qb], w=[wqt])
        if n_pro >= 5:
            jobs = []
            for fc in range(NF):
                for oc in range(8):
                    jobs.append((w1q_nat.ap()[512 * oc:512 * (oc + 1), 128 * fc:128 * (fc + 1)],
                                 wq1T.ap()[128 * fc:128 * (fc + 1), 512 * oc:512 * (oc + 1)], wq1T))
            for hc in range(NHC):
                for oc in range(2):
                    jobs.append((w2q_nat.ap()[512 * oc:512 * (oc + 1), 128 * hc:128 * (hc + 1)],
                                 wq2T.ap()[128 * hc:128 * (hc + 1), 512 * oc:512 * (oc + 1)], wq2T))
            S.pre("sp", [w1q_nat, w2q_nat], [])
            transpose_copy(jobs)

        with nc.psum_tensor("bkP", [1, 512], F32) as bkP:
            for half in (range(2) if n_pro >= 6 else []):
                for fc in range(NF):
                    S.dma("sp", lambda e, f=fc, h=half: e.dma_start(
                        out=wbuf[1][:, 0:512],
                        in_=wa_bf.ap()[128 * f:128 * (f + 1), D + 512 * h:D + 512 * (h + 1)]),
                        r=[wa_bf], w=[wbuf[1]])
                    S.op("act", lambda e: e.activation(sqbf[0][:, 0:512], wbuf[1][:, 0:512], AF.Square),
                         r=[wbuf[1]], w=[sqbf[0]])
                    S.pre("pe", [onescol, sqbf[0]], [bkP])
                    S.post("pe", mm(nc.tensor, bkP[0:1, :], onescol[:], sqbf[0][:, 0:512],
                                    fc == 0, fc == NF - 1), [onescol, sqbf[0]], [bkP])
                ap, tt = R("bk", 512 * half, 512 * (half + 1))
                S.op("act", lambda e, ap=ap: e.activation(ap, bkP[0:1, :], AF.Sqrt),
                     r=[bkP], w=[tt])
        ap, tt = R("bk", 0, 1024)
        S.op("dve", lambda e, ap=ap: e.tensor_scalar(
            ap, ap, 32.5, 0.0, ALU.mult, ALU.add), r=[tt], w=[tt])

        def ln_stats(ph):
            statP = [ph.enter_context(nc.psum_tensor(f"stP{next(_uid)}", [1, 512], F32)) for _ in range(2)]
            for off, w in NC:
                for fc in range(NF):
                    xb, sq = xbf[fc % 2], sqbf[fc % 2]
                    S.op("dve", lambda e, f=fc, o=off, ww=w, xb=xb: e.tensor_copy(
                        xb[:, 0:ww], x8[f][:, o:o + ww]), r=[x8[fc]], w=[xb])
                    S.op("act", lambda e, ww=w, xb=xb, sq=sq: e.activation(
                        sq[:, 0:ww], xb[:, 0:ww], AF.Square), r=[xb], w=[sq])
                    S.pre("pe", [onescol, xb], [statP[0]])
                    S.post("pe", mm(nc.tensor, statP[0][0:1, 0:w], onescol[:], xb[:, 0:w],
                                    fc == 0, fc == NF - 1), [onescol, xb], [statP[0]])
                    S.pre("pe", [onescol, sq], [statP[1]])
                    S.post("pe", mm(nc.tensor, statP[1][0:1, 0:w], onescol[:], sq[:, 0:w],
                                    fc == 0, fc == NF - 1), [onescol, sq], [statP[1]])
                ap, tt = R("sum", off, off + w)
                S.op("dve", lambda e, ap=ap, ww=w: e.tensor_copy(ap, statP[0][0:1, 0:ww]),
                     r=[statP[0]], w=[tt])
                ap, tt = R("sumsq", off, off + w)
                S.op("dve", lambda e, ap=ap, ww=w: e.tensor_copy(ap, statP[1][0:1, 0:ww]),
                     r=[statP[1]], w=[tt])


        def mean_rstd_bc(bcP, off, w, scratch):
            build_bcast(bcP, bc1, "sum", off, w)
            build_bcast(bcP, bc2, "sumsq", off, w)
            S.op("dve", lambda e, ww=w: e.tensor_scalar(
                bc1[:, 0:ww], bc1[:, 0:ww], 1.0 / D, 0.0, ALU.mult, ALU.add),
                r=[bc1], w=[bc1])
            S.op("dve", lambda e, ww=w: e.tensor_scalar(
                bc2[:, 0:ww], bc2[:, 0:ww], 1.0 / D, 1e-5, ALU.mult, ALU.add),
                r=[bc2], w=[bc2])
            S.op("act", lambda e, ww=w: e.activation(
                scratch[:, 0:ww], bc1[:, 0:ww], AF.Square), r=[bc1], w=[scratch])
            S.op("dve", lambda e, ww=w: e.tensor_tensor(
                bc2[:, 0:ww], bc2[:, 0:ww], scratch[:, 0:ww], ALU.subtract),
                r=[bc2, scratch], w=[bc2])
            S.op("act", lambda e, ww=w: e.activation(
                bc2[:, 0:ww], bc2[:, 0:ww], AF.Sqrt), r=[bc2], w=[bc2])
            S.op("dve", lambda e, ww=w: e.reciprocal(bc2[:, 0:ww], bc2[:, 0:ww]),
                 r=[bc2], w=[bc2])

        def build_bcast(bcP, bc, rowname, off, w):
            ap, tt = R(rowname, off, off + w)
            p = RP(rowname)
            S.pre("pe", [onesrow, tt], [bcP])
            S.post("pe", mm(nc.tensor, bcP[0:128, 0:w], onesrow[p:p + 1, :], ap, True, True),
                   [onesrow, tt], [bcP])
            S.op("dve", lambda e, ww=w: e.tensor_copy(bc[:, 0:ww], bcP[0:128, 0:ww]),
                 r=[bcP], w=[bc])

        def ln_to(ph, dest):
            bcP = ph.enter_context(nc.psum_tensor(f"lnb{next(_uid)}", [128, 512], F32))
            for off, w in NC:
                mean_rstd_bc(bcP, off, w, bc3)
                for fc in range(NF):
                    S.op("dve", lambda e, f=fc, o=off, ww=w: e.tensor_tensor(
                        dscr[:, o:o + ww], x8[f][:, o:o + ww], bc1[:, 0:ww], ALU.subtract),
                        r=[x8[fc], bc1], w=[dscr])
                    S.op("dve", lambda e, f=fc, o=off, ww=w: e.tensor_tensor(
                        dest[f][:, o:o + ww], dscr[:, o:o + ww], bc2[:, 0:ww], ALU.mult),
                        r=[dscr, bc2], w=[dest[fc]])

        def rezero_halo():
            for fc in range(NF):
                S.op("dve", lambda e, f=fc: e.tensor_tensor(
                    x8[f][:, 0:2], x8[f][:, 0:2], hmask[:, 0:2], ALU.mult),
                    r=[x8[fc], hmask], w=[x8[fc]])
                S.op("dve", lambda e, f=fc: e.tensor_tensor(
                    x8[f][:, 2050:2052], x8[f][:, 2050:2052], hmask[:, 2:4], ALU.mult),
                    r=[x8[fc], hmask], w=[x8[fc]])

        def tokmax_to_row(ph, src, dstrow, sqrt=False):
            trP = ph.enter_context(nc.psum_tensor(f"tk{next(_uid)}", [128, 512], F32))
            for toff, tw in TT:
                S.pre("pe", [src, ident], [trP])
                S.post("pe", mm(nc.tensor, trP[0:tw, 0:128], src[:, toff:toff + tw],
                                ident[:], True, True), [src, ident], [trP])
                S.op("dve", lambda e, ww=tw: e.reduce_max(
                    redc[0:ww, 0:1], trP[0:ww, 0:128], axis=AX.X), r=[trP], w=[redc])
                S.pre("pe", [redc, ident], [trP])
                S.post("pe", mm(nc.tensor, trP[0:1, 128:128 + tw], redc[0:tw, 0:1],
                                ident[0:tw, 0:tw], True, True), [redc, ident], [trP])
                ap, tt = R(dstrow, toff, toff + tw)
                if sqrt:
                    S.op("act", lambda e, ap=ap, ww=tw: e.activation(
                        ap, trP[0:1, 128:128 + ww], AF.Sqrt), r=[trP], w=[tt])
                else:
                    S.op("dve", lambda e, ap=ap, ww=tw: e.tensor_copy(
                        ap, trP[0:1, 128:128 + ww]), r=[trP], w=[tt])

        if n_phases >= 1 and not os.environ.get("SKIP_P1"):
            with contextlib.ExitStack() as ph:
                ln_stats(ph)
                ln_to(ph, s1)
                S.barrier()
            with contextlib.ExitStack() as ph:
                qsm = ph.enter_context(nc.sbuf_tensor("qsm", [128, H * 65], BF16))
                expk = ph.enter_context(nc.sbuf_tensor("expk", [128, D], BF16))
                qmx = ph.enter_context(nc.sbuf_tensor("qmx", [128, H], F32))
                qsum = ph.enter_context(nc.sbuf_tensor("qsum", [128, H], F32))
                qrec = ph.enter_context(nc.sbuf_tensor("qrec", [128, H], F32))
                dq = ph.enter_context(nc.sbuf_tensor("dq", [128, 1024], F32))
                dqk = ph.enter_context(nc.sbuf_tensor("dqk", [128, 1024], F32))
                bkb = ph.enter_context(nc.sbuf_tensor("bkb", [128, 1024], F32))
                ctxs = ph.enter_context(nc.sbuf_tensor("ctxs", [65, 1024], F32))
                vP = [ph.enter_context(nc.psum_tensor(f"vP{i}", [128, 512], F32)) for i in range(2)]
                psA = ph.enter_context(nc.psum_tensor("psA", [128, 2048], F32))
                ctxP = ph.enter_context(nc.psum_tensor("ctxP", [65, 1024], F32))

                wres = [s2[f][:, 0:2048] for f in range(NF)]
                for fc in range(NF):
                    S.dma("sp", lambda e, f=fc: e.dma_start(
                        out=wres[f], in_=wa_bf.ap()[128 * f:128 * (f + 1), 0:2048]),
                        r=[wa_bf], w=[s2[fc]])
                S.op("pool", lambda e: e.memset(
                    qsm[:].rearrange("p (h c) -> p h c", c=65)[:, :, 64:65], 1.0), w=[qsm])
                for half in range(2):
                    ap, tt = R("bk", 512 * half, 512 * (half + 1))
                    S.pre("pe", [onesrow, tt], [vP[0]])
                    S.post("pe", mm(nc.tensor, vP[0][0:128, 0:512], onesrow[64:65, :], ap,
                                    True, True), [onesrow, tt], [vP[0]])
                    S.op("dve", lambda e, h=half: e.tensor_copy(
                        bkb[:, 512 * h:512 * (h + 1)], vP[0][0:128, 0:512]), r=[vP[0]], w=[bkb])

                for ti, (toff, tw) in enumerate(TT):
                    rd = [s2[f] for f in range(NF)] + [s1[f] for f in range(NF)]
                    S.pre("pe", rd, [psA])
                    for fc in range(NF):
                        for oc in range(4):
                            inst = mm(nc.tensor, psA[0:tw, 512 * oc:512 * (oc + 1)],
                                      s1[fc][:, toff:toff + tw],
                                      wres[fc][:, 512 * oc:512 * (oc + 1)],
                                      fc == 0, fc == NF - 1)
                    S.post("pe", inst, rd, [psA])
                    S.op("dve", lambda e, ww=tw: e.reduce_max(
                        qmx[0:ww, :], psA[0:ww, 0:1024].rearrange("p (h c) -> p h c", c=DH),
                        axis=AX.X), r=[psA], w=[qmx])
                    S.op("dve", lambda e, ww=tw: e.tensor_tensor(
                        dq[0:ww, :].rearrange("p (h c) -> p h c", c=DH),
                        psA[0:ww, 0:1024].rearrange("p (h c) -> p h c", c=DH),
                        qmx[0:ww, :].unsqueeze(2).broadcast_to((ww, H, DH)),
                        ALU.subtract), r=[psA, qmx], w=[dq])
                    S.op("dve", lambda e, ww=tw: e.tensor_tensor(
                        dqk[0:ww, :], psA[0:ww, 1024:2048], bkb[0:ww, :], ALU.subtract),
                        r=[psA, bkb], w=[dqk])
                    S.op("act", lambda e, ww=tw: e.activation(
                        dq[0:ww, :], dq[0:ww, :], AF.Exp, scale=0.125), r=[dq], w=[dq])
                    S.op("act", lambda e, ww=tw: e.activation(
                        expk[0:ww, :], dqk[0:ww, :], AF.Exp), r=[dqk], w=[expk])
                    S.op("dve", lambda e, ww=tw: e.reduce_sum(
                        qsum[0:ww, :], dq[0:ww, :].rearrange("p (h c) -> p h c", c=DH),
                        axis=AX.X), r=[dq], w=[qsum])
                    S.op("dve", lambda e, ww=tw: e.reciprocal(
                        qrec[0:ww, :], qsum[0:ww, :]), r=[qsum], w=[qrec])
                    S.op("dve", lambda e, ww=tw: e.tensor_tensor(
                        qsm[0:ww, :].rearrange("p (h c) -> p h c", c=65)[:, :, 0:64],
                        dq[0:ww, :].rearrange("p (h c) -> p h c", c=DH),
                        qrec[0:ww, :].unsqueeze(2).broadcast_to((ww, H, DH)),
                        ALU.mult), r=[dq, qrec], w=[qsm])
                    if ti == 0:
                        S.op("pool", lambda e: e.memset(expk[0:2, :], 0.0), w=[expk])
                    kw = tw if ti < 16 else 2
                    S.pre("pe", [qsm, expk], [ctxP])
                    for hh in range(H):
                        inst = mm(nc.tensor, ctxP[0:65, 64 * hh:64 * (hh + 1)],
                                  qsm[0:kw, 65 * hh:65 * (hh + 1)],
                                  expk[0:kw, 64 * hh:64 * (hh + 1)],
                                  ti == 0 and hh in (0, 8), ti == 16)
                    S.post("pe", inst, [qsm, expk], [ctxP])

                S.op("dve", lambda e: e.tensor_copy(ctxs[:], ctxP[0:65, :]), r=[ctxP], w=[ctxs])
                S.dma("sp", lambda e: e.dma_start(out=cxl.ap(), in_=ctxs[:]), r=[ctxs], w=[cxl])
                S.cc(lambda e: e.collective_compute(
                    "AllReduce", ALU.add, replica_groups=REPL,
                    ins=[cxl.ap().opt()], outs=[cxg.ap().opt()]), r=[cxl], w=[cxg])

                for oc in range(NF):
                    wb = wbuf[oc % 2]
                    S.dma("sp", lambda e, o=oc, wb=wb: e.dma_start(
                        out=wb[:, 0:1024],
                        in_=wa_bf.ap().rearrange("(f p) o -> p f o", p=128)[:, :, 2048 + 128 * o:2048 + 128 * (o + 1)]),
                        r=[wa_bf], w=[wb])
                    for ni, (off, w) in enumerate(NC):
                        pb = vP[ni % 2]
                        S.pre("pe", [wb] + [s1[f] for f in range(NF)], [pb])
                        for fc in range(NF):
                            inst = mm(nc.tensor, pb[:, 0:w], wb[:, 128 * fc:128 * (fc + 1)],
                                      s1[fc][:, off:off + w], fc == 0, fc == NF - 1)
                        S.post("pe", inst, [wb] + [s1[f] for f in range(NF)], [pb])
                        S.op("dve", lambda e, o=oc, of=off, ww=w, p=pb: e.tensor_copy(
                            s2[o][:, of:of + ww], p[:, 0:ww]), r=[pb], w=[s2[oc]])
                S.barrier()
            with contextlib.ExitStack() as ph:
                zbs = ph.enter_context(nc.sbuf_tensor("zbs", [64, 1024], F32))
                ctxn = ph.enter_context(nc.sbuf_tensor("ctxn", [128, 512], BF16))
                ctxs2 = ph.enter_context(nc.sbuf_tensor("ctxs2", [65, 1024], F32))
                bcP2 = ph.enter_context(nc.psum_tensor("bcP2", [128, 512], F32))
                oPa = ph.enter_context(nc.psum_tensor("oPa", [64, 512], F32))
                oPb = ph.enter_context(nc.psum_tensor("oPb", [64, 512], F32))
                woP = [ph.enter_context(nc.psum_tensor(f"woP{i}", [128, 512], F32)) for i in range(2)]
                zrowr = ph.enter_context(nc.sbuf_tensor(f"zrw{next(_uid)}", [1, 1024], F32))
                zr, zrt = zrowr[0:1, 0:1024], zrowr
                S.dma("sp", lambda e: e.dma_start(out=ctxs2[:], in_=cxg.ap()), r=[cxg], w=[ctxs2])
                S.op("dve", lambda e: e.reciprocal(zr, ctxs2[64:65, :]), r=[ctxs2], w=[zrt])
                for half in range(2):
                    ap = zrowr[0:1, 512 * half:512 * (half + 1)]
                    S.pre("pe", [onesrow, zrt], [bcP2])
                    S.post("pe", mm(nc.tensor, bcP2[0:64, 0:512], onesrow[0:1, 0:64],
                                    ap, True, True), [onesrow, zrt], [bcP2])
                    S.op("dve", lambda e, h=half: e.tensor_copy(
                        zbs[0:64, 512 * h:512 * (h + 1)], bcP2[0:64, 0:512]), r=[bcP2], w=[zbs])
                for hh in range(H):
                    ro, co = 64 * (hh % 2), 64 * (hh // 2)
                    S.op("dve", lambda e, h=hh, ro=ro, co=co: e.tensor_tensor(
                        ctxn[ro:ro + 64, co:co + 64], ctxs2[0:64, 64 * h:64 * (h + 1)],
                        zbs[0:64, 64 * h:64 * (h + 1)], ALU.mult),
                        r=[ctxs2, zbs], w=[ctxn])
                for ch in range(NF):
                    for ni, (off, w) in enumerate(NC):
                        S.pre("pe", [ctxn, s2[ch]], [oPa])
                        S.post("pe", mm(nc.tensor, oPa[0:64, 0:w],
                                        ctxn[0:64, 64 * ch:64 * (ch + 1)],
                                        s2[ch][0:64, off:off + w], True, True),
                               [ctxn, s2[ch]], [oPa])
                        S.pre("pe", [ctxn, s2[ch]], [oPb])
                        S.post("pe", mm(nc.tensor, oPb[0:64, 0:w],
                                        ctxn[64:128, 64 * ch:64 * (ch + 1)],
                                        s2[ch][64:128, off:off + w], True, True),
                               [ctxn, s2[ch]], [oPb])
                        S.op("dve", lambda e, c=ch, of=off, ww=w: e.tensor_copy(
                            s2[c][0:64, of:of + ww], oPa[0:64, 0:ww]), r=[oPa], w=[s2[ch]])
                        S.op("dve", lambda e, c=ch, of=off, ww=w: e.tensor_copy(
                            s2[c][64:128, of:of + ww], oPb[0:64, 0:ww]), r=[oPb], w=[s2[ch]])
                for oc in range(NF):
                    wb = wbuf[oc % 2]
                    S.dma("sp", lambda e, o=oc, wb=wb: e.dma_start(
                        out=wb[:, 0:1024],
                        in_=woa_bf.ap().rearrange("(f p) o -> p f o", p=128)[:, :, 128 * o:128 * (o + 1)]),
                        r=[woa_bf], w=[wb])
                    for ni, (off, w) in enumerate(NC):
                        pb = woP[ni % 2]
                        S.pre("pe", [wb] + [s2[f] for f in range(NF)], [pb])
                        for fc in range(NF):
                            inst = mm(nc.tensor, pb[:, 0:w], wb[:, 128 * fc:128 * (fc + 1)],
                                      s2[fc][:, off:off + w], fc == 0, fc == NF - 1)
                        S.post("pe", inst, [wb] + [s2[f] for f in range(NF)], [pb])
                        S.op("dve", lambda e, o=oc, of=off, ww=w, p=pb: e.scalar_tensor_tensor(
                            x8[o][:, of:of + ww], p[:, 0:ww], b_out[:, o:o + 1],
                            x8[o][:, of:of + ww], ALU.add, ALU.add),
                            r=[pb, b_out, x8[oc]], w=[x8[oc]])
                S.barrier()
            rezero_halo()

        def ffn_phase(tag):
          def _qi_scope():
            with contextlib.ExitStack() as ph:
                ln_stats(ph)
                bcPf = ph.enter_context(nc.psum_tensor(f"bf{next(_uid)}", [128, 512], F32))
                mx = dscr
                for off, w in NC:
                    build_bcast(bcPf, bc1, "sum", off, w)
                    S.op("dve", lambda e, ww=w: e.tensor_scalar(
                        bc1[:, 0:ww], bc1[:, 0:ww], 1.0 / D, 0.0, ALU.mult, ALU.add),
                        r=[bc1], w=[bc1])
                    for fc in range(NF):
                        S.op("dve", lambda e, f=fc, o=off, ww=w: e.tensor_tensor(
                            bc2[:, 0:ww], x8[f][:, o:o + ww], bc1[:, 0:ww], ALU.subtract),
                            r=[x8[fc], bc1], w=[bc2])
                        S.op("pool", lambda e, ww=w: e.tensor_tensor(
                            bc2[:, 0:ww], bc2[:, 0:ww], bc2[:, 0:ww], ALU.mult),
                            r=[bc2], w=[bc2])
                        if fc == 0:
                            S.op("dve", lambda e, o=off, ww=w: e.tensor_copy(
                                mx[:, o:o + ww], bc2[:, 0:ww]), r=[bc2], w=[mx])
                        else:
                            S.op("dve", lambda e, o=off, ww=w: e.tensor_tensor(
                                mx[:, o:o + ww], bc2[:, 0:ww], mx[:, o:o + ww], ALU.max),
                                r=[bc2, mx], w=[mx])
                tokmax_to_row(ph, mx, "mx", sqrt=True)
                for off, w in NC:
                    mean_rstd_bc(bcPf, off, w, bc3)
                    build_bcast(bcPf, bc3, "mx", off, w)
                    dsl = dscr[:, off:off + w]
                    S.op("dve", lambda e, ww=w: e.tensor_tensor(
                        bc3[:, 0:ww], bc3[:, 0:ww], bc2[:, 0:ww], ALU.mult),
                        r=[bc3, bc2], w=[bc3])
                    S.op("dve", lambda e, ww=w: e.tensor_scalar(
                        bc3[:, 0:ww], bc3[:, 0:ww], 1e-5, 0.0, ALU.max, ALU.add),
                        r=[bc3], w=[bc3])
                    S.op("dve", lambda e, d=dsl: e.reciprocal(d, bc3[:, 0:w]),
                         r=[bc3], w=[dscr])
                    S.op("dve", lambda e, d=dsl, ww=w: e.tensor_tensor(
                        d, d, bc2[:, 0:ww], ALU.mult), r=[dscr, bc2], w=[dscr])
                    S.op("dve", lambda e, d=dsl: e.tensor_scalar(
                        d, d, 127.0, 0.0, ALU.mult, ALU.add), r=[dscr], w=[dscr])
                    S.op("dve", lambda e, d=dsl, ww=w: e.tensor_tensor(
                        bc2[:, 0:ww], bc1[:, 0:ww], d, ALU.mult),
                        r=[bc1, dscr], w=[bc2])
                    S.op("dve", lambda e, ww=w: e.tensor_scalar(
                        bc3[:, 0:ww], bc3[:, 0:ww], mwb[:, 0:1], 1.0 / 127.0,
                        ALU.mult, ALU.mult), r=[bc3, mwb], w=[bc3])
                    S.op("dve", lambda e, d=dsl, ww=w: e.tensor_copy(
                        bc1[:, 0:ww], d), r=[dscr], w=[bc1])
                    for fc in range(NF):
                        S.op("dve", lambda e, f=fc, o=off, ww=w: e.tensor_tensor(
                            dscr[:, o:o + ww], x8[f][:, o:o + ww], bc1[:, 0:ww], ALU.mult),
                            r=[x8[fc], bc1], w=[dscr])
                        S.op("dve", lambda e, o=off, ww=w: e.tensor_tensor(
                            dscr[:, o:o + ww], dscr[:, o:o + ww], bc2[:, 0:ww], ALU.subtract),
                            r=[dscr, bc2], w=[dscr])
                        S.op("dve", lambda e, o=off, ww=w: e.tensor_scalar(
                            dscr[:, o:o + ww], dscr[:, o:o + ww], RND, -RND, ALU.add, ALU.add),
                            r=[dscr], w=[dscr])
                        S.op("dve", lambda e, f=fc, o=off, ww=w: e.tensor_tensor(
                            s1[f][:, o:o + ww], dscr[:, o:o + ww], bc3[:, 0:ww], ALU.mult),
                            r=[dscr, bc3], w=[s1[fc]])
                S.barrier()
          if not os.environ.get("SKIP_QI"):
            _qi_scope()
          if True:
            if xqstage is not None:
                for fc in range(NF):
                    S.dma("sp", lambda e, f=fc: e.dma_start(
                        out=s1[f][:], in_=xqstage.ap()[128 * f:128 * (f + 1), :]),
                        r=[xqstage], w=[s1[fc]])
                S.barrier()
            with contextlib.ExitStack() as ph:
                hs = [ph.enter_context(nc.sbuf_tensor(f"hs{tag}{i}", [128, T], F32)) for i in range(2)]
                utb = [ph.enter_context(nc.sbuf_tensor(f"ut{tag}{i}", [128, T], F32)) for i in range(2)]
                s2invb = ph.enter_context(nc.sbuf_tensor(f"s2i{tag}", [128, T], F32))
                hP = [ph.enter_context(nc.psum_tensor(f"hP{tag}{i}", [128, 512], F32)) for i in range(4)]
                hmx = dscr
                wsrc = wqstage if wqstage is not None else wq1T
                for hc in range(NHC):
                    wb = wbuf[hc % 2]
                    hcur = hs[hc % 2]
                    ut = utb[hc % 2]
                    S.dma("sp", lambda e, h=hc, wb=wb, ws=wsrc: e.dma_start(
                        out=wb[:, 0:1024],
                        in_=ws.ap().rearrange("(f p) o -> p f o", p=128)[:, :, 128 * h:128 * (h + 1)]),
                        r=[wsrc], w=[wb])
                    for ni, (off, w) in enumerate(NC):
                        pb = hP[ni % 4]
                        S.pre("pe", [wb] + [s1[f] for f in range(NF)], [pb])
                        for fc in range(NF):
                            inst = mm(nc.tensor, pb[:, 0:w], wb[:, 128 * fc:128 * (fc + 1)],
                                      s1[fc][:, off:off + w], fc == 0, fc == NF - 1)
                        S.post("pe", inst, [wb] + [s1[f] for f in range(NF)], [pb])
                        S.op("act", lambda e, of=off, ww=w, p=pb, hc_=hcur: e.copy(
                            hc_[:, of:of + ww], p[:, 0:ww]), r=[pb], w=[hcur])
                    S.op("act", lambda e, h=hc, hc_=hcur: e.activation(
                        ut[:, :], hc_[:, :], AF.Identity,
                        bias=convbt[:, h:h + 1], scale=convwt[:, 3 * h + 1:3 * h + 2]),
                        r=[hcur, convwt, convbt], w=[ut])
                    S.op("dve", lambda e, h=hc, hc_=hcur: e.scalar_tensor_tensor(
                        ut[:, 1:T], hc_[:, 0:T - 1], convwt[:, 3 * h:3 * h + 1],
                        ut[:, 1:T], ALU.mult, ALU.add),
                        r=[hcur, convwt, ut], w=[ut])
                    S.op("dve", lambda e, h=hc, hc_=hcur: e.scalar_tensor_tensor(
                        ut[:, 0:T - 1], hc_[:, 1:T], convwt[:, 3 * h + 2:3 * h + 3],
                        ut[:, 0:T - 1], ALU.mult, ALU.add),
                        r=[hcur, convwt, ut], w=[ut])
                    S.op("act", lambda e: e.activation(ut[:, :], ut[:, :], AF.Silu),
                         r=[ut], w=[ut])
                    if os.environ.get("DUMP_PRECONV"):
                        S.dma("sp", lambda e, h=hc, hc_=hcur: e.dma_start(
                            out=hd.ap()[128 * h:128 * (h + 1), :], in_=hc_[:, :]), r=[hcur], w=[hd])
                    else:
                        S.dma("sp", lambda e, h=hc: e.dma_start(
                            out=hd.ap()[128 * h:128 * (h + 1), :], in_=ut[:, :]), r=[ut], w=[hd])
                    S.op("pool", lambda e, hc_=hcur: e.tensor_tensor(
                        hc_[:, :], ut[:, :], ut[:, :], ALU.mult), r=[ut], w=[hcur])
                    if hc == 0:
                        S.op("dve", lambda e, hc_=hcur: e.tensor_copy(hmx[:, :], hc_[:, :]),
                             r=[hcur], w=[hmx])
                    else:
                        S.op("dve", lambda e, hc_=hcur: e.tensor_tensor(
                            hmx[:, :], hc_[:, :], hmx[:, :], ALU.max), r=[hcur, hmx], w=[hmx])
                tokmax_to_row(ph, hmx, "mx", sqrt=True)
                bcPg = ph.enter_context(nc.psum_tensor(f"bg{next(_uid)}", [128, 512], F32))
                for off, w in NC:
                    build_bcast(bcPg, bc1, "mx", off, w)
                    S.op("dve", lambda e, ww=w: e.tensor_scalar(
                        bc1[:, 0:ww], bc1[:, 0:ww], 1e-5, 0.0, ALU.max, ALU.add),
                        r=[bc1], w=[bc1])
                    S.op("dve", lambda e, o=off, ww=w: e.tensor_scalar(
                        s2invb[:, o:o + ww], bc1[:, 0:ww], mwb[:, 1:2], 1.0 / 127.0,
                        ALU.mult, ALU.mult), r=[bc1, mwb], w=[s2invb])
                    S.op("dve", lambda e, ww=w: e.reciprocal(bc1[:, 0:ww], bc1[:, 0:ww]),
                         r=[bc1], w=[bc1])
                    S.op("dve", lambda e, ww=w: e.tensor_scalar(
                        bc1[:, 0:ww], bc1[:, 0:ww], 127.0, 0.0, ALU.mult, ALU.add),
                        r=[bc1], w=[bc1])
                    for hc in range(NHC):
                        hcur = hs[hc % 2]
                        qr = s2[hc % 2]
                        S.dma("sp", lambda e, h=hc, o=off, ww=w, hc_=hcur: e.dma_start(
                            out=hc_[:, o:o + ww], in_=hd.ap()[128 * h:128 * (h + 1), o:o + ww]),
                            r=[hd], w=[hcur])
                        S.op("dve", lambda e, o=off, ww=w, hc_=hcur: e.tensor_tensor(
                            bc2[:, 0:ww], hc_[:, o:o + ww], bc1[:, 0:ww], ALU.mult),
                            r=[hcur, bc1], w=[bc2])
                        S.op("dve", lambda e, o=off, ww=w, q=qr: e.tensor_scalar(
                            q[:, o:o + ww], bc2[:, 0:ww], RND, -RND, ALU.add, ALU.add),
                            r=[bc2], w=[qr])
                        S.dma("sp", lambda e, h=hc, o=off, ww=w, q=qr: e.dma_start(
                            out=xq2d.ap()[128 * h:128 * (h + 1), o:o + ww], in_=q[:, o:o + ww]),
                            r=[qr], w=[xq2d])
                S.barrier()

            with contextlib.ExitStack() as ph:
                oP = [ph.enter_context(nc.psum_tensor(f"oP{tag}{i}", [128, 512], F32)) for i in range(NF)]
                xq2r = [ph.enter_context(nc.sbuf_tensor(f"xr{tag}{i}", [128, 512], BF16)) for i in range(2)]
                for ni, (off, w) in enumerate(NC):
                    for hc in range(NHC):
                        rb, wb = xq2r[hc % 2], wbuf[hc % 2]
                        S.dma("sp", lambda e, h=hc, o=off, ww=w, r_=rb: e.dma_start(
                            out=r_[:, 0:ww], in_=xq2d.ap()[128 * h:128 * (h + 1), o:o + ww]),
                            r=[xq2d], w=[rb])
                        S.dma("pool", lambda e, h=hc, wb=wb: e.dma_start(
                            out=wb[:, 0:1024], in_=wq2T.ap()[128 * h:128 * (h + 1), :]),
                            r=[wq2T], w=[wb])
                        for oc in range(NF):
                            S.pre("pe", [wb, rb], [oP[oc]])
                            S.post("pe", mm(nc.tensor, oP[oc][:, 0:w],
                                            wb[:, 128 * oc:128 * (oc + 1)], rb[:, 0:w],
                                            hc == 0, hc == NHC - 1), [wb, rb], [oP[oc]])
                    for oc in range(NF):
                        S.op("dve", lambda e, o=oc, of=off, ww=w: e.tensor_tensor(
                            bc2[:, 0:ww], oP[o][:, 0:ww], s2invb[:, of:of + ww], ALU.mult),
                            r=[oP[oc], s2invb], w=[bc2])
                        S.op("dve", lambda e, o=oc, of=off, ww=w: e.tensor_tensor(
                            x8[o][:, of:of + ww], bc2[:, 0:ww], x8[o][:, of:of + ww], ALU.add),
                            r=[bc2, x8[oc]], w=[x8[oc]])
                S.barrier()

        if n_phases >= 2:
            ffn_phase("A")
            rezero_halo()

        if n_phases >= 3:
            with contextlib.ExitStack() as ph:
                ln_stats(ph)
                ln_to(ph, s1)
                S.barrier()
            with contextlib.ExitStack() as ph:
                kvt = ph.enter_context(nc.sbuf_tensor("kvt", [128, T], BF16))
                qmax8 = ph.enter_context(nc.sbuf_tensor("qmax8", [128, NF], F32))
                qmaxr = ph.enter_context(nc.sbuf_tensor("qmaxr", [1, NF], F32))
                vres = [ph.enter_context(nc.sbuf_tensor(f"vres{i}", [128, 1024], BF16)) for i in range(NF)]
                knacc = ph.enter_context(nc.sbuf_tensor("knacc", [2, NF], F32))
                kn2s = ph.enter_context(nc.sbuf_tensor("kn2s", [2, 512], F32))
                ksq = ph.enter_context(nc.sbuf_tensor("ksq", [128, 2048], BF16))
                kchunk = ph.enter_context(nc.sbuf_tensor("kchunk", [128, 2048], BF16))
                qkP = [ph.enter_context(nc.psum_tensor(f"qkP{i}", [128, 512], F32)) for i in range(2)]
                vP3 = ph.enter_context(nc.psum_tensor("vP3", [128, 1024], F32))
                trPq = ph.enter_context(nc.psum_tensor("trPq", [128, 512], F32))
                knP = ph.enter_context(nc.psum_tensor("knP", [2, 512], F32))

                for oc in range(16):
                    wb = wbuf[oc % 2]
                    S.dma("sp", lambda e, o=oc, wb=wb: e.dma_start(
                        out=wb[:, 0:1024],
                        in_=wbT.ap().rearrange("(f p) o -> p f o", p=128)[:, :, 128 * o:128 * (o + 1)]),
                        r=[wbT], w=[wb])
                    dst = s2[oc] if oc < 8 else kvt
                    for ni, (off, w) in enumerate(NC):
                        pb = qkP[ni % 2]
                        S.pre("pe", [wb] + [s1[f] for f in range(NF)], [pb])
                        for fc in range(NF):
                            inst = mm(nc.tensor, pb[:, 0:w], wb[:, 128 * fc:128 * (fc + 1)],
                                      s1[fc][:, off:off + w], fc == 0, fc == NF - 1)
                        S.post("pe", inst, [wb] + [s1[f] for f in range(NF)], [pb])
                        S.op("dve", lambda e, d=dst, of=off, ww=w, p=pb: e.tensor_copy(
                            d[:, of:of + ww], p[:, 0:ww]), r=[pb], w=[dst])
                    if oc < 8:
                        S.op("dve", lambda e, o=oc: e.reduce_max(
                            qmax8[:, o:o + 1], s2[o][:, :], axis=AX.X,
                            apply_absolute_value=True), r=[s2[oc]], w=[qmax8])
                    else:
                        S.dma("sp", lambda e, o=oc: e.dma_start(
                            out=kx.ap()[128 * (o - 8):128 * (o - 7), :],
                            in_=kvt[:, OWN0:OWN1]), r=[kvt], w=[kx])
                S.pre("pe", [qmax8, ident], [trPq])
                S.post("pe", mm(nc.tensor, trPq[0:8, 0:128], qmax8[:, 0:8], ident[:],
                                True, True), [qmax8, ident], [trPq])
                S.op("dve", lambda e: e.reduce_max(redc[0:8, 0:1], trPq[0:8, 0:128], axis=AX.X),
                     r=[trPq], w=[redc])
                S.pre("pe", [redc, ident], [trPq])
                S.post("pe", mm(nc.tensor, trPq[0:1, 128:136], redc[0:8, 0:1],
                                ident[0:8, 0:8], True, True), [redc, ident], [trPq])
                S.op("dve", lambda e: e.tensor_copy(qmaxr[0:1, 0:8], trPq[0:1, 128:136]),
                     r=[trPq], w=[qmaxr])

                for fc in range(NF):
                    S.dma("sp", lambda e, f=fc: e.dma_start(
                        out=vres[f][:], in_=wbT.ap()[128 * f:128 * (f + 1), 2048:3072]),
                        r=[wbT], w=[vres[fc]])
                for ti, (toff, tw) in enumerate(TT):
                    rd = [vres[f] for f in range(NF)] + [s1[f] for f in range(NF)]
                    S.pre("pe", rd, [vP3])
                    for fc in range(NF):
                        for oc in range(2):
                            inst = mm(nc.tensor, vP3[0:tw, 512 * oc:512 * (oc + 1)],
                                      s1[fc][:, toff:toff + tw],
                                      vres[fc][:, 512 * oc:512 * (oc + 1)],
                                      fc == 0, fc == NF - 1)
                    S.post("pe", inst, rd, [vP3])
                    S.op("dve", lambda e, ww=tw: e.tensor_copy(
                        kvt[0:ww, 0:1024], vP3[0:ww, :]), r=[vP3], w=[kvt])
                    lo, hi = max(toff, OWN0), min(toff + tw, OWN1)
                    if hi > lo:
                        S.dma("sp", lambda e, lo=lo, hi=hi, to=toff: e.dma_start(
                            out=vx.ap()[lo - OWN0:hi - OWN0, :],
                            in_=kvt[lo - to:hi - to, 0:1024]), r=[kvt], w=[vx])

                S.cc(lambda e: e.collective_compute(
                    "AllGather", ALU.bypass, replica_groups=REPL,
                    ins=[kx.ap().opt()], outs=[kg.ap().opt()]), r=[kx], w=[kg])
                S.cc(lambda e: e.collective_compute(
                    "AllGather", ALU.bypass, replica_groups=REPL,
                    ins=[vx.ap().opt()], outs=[vg.ap().opt()]), r=[vx], w=[vg])

                for r_ in range(2):
                    for c in range(NF):
                        S.dma("sp", lambda e, r2=r_, c=c: e.dma_start(
                            out=kchunk[:], in_=kg.ap()[1024 * r2 + 128 * c:1024 * r2 + 128 * (c + 1), :]),
                            r=[kg], w=[kchunk])
                        S.op("act", lambda e: e.activation(ksq[:], kchunk[:], AF.Square),
                             r=[kchunk], w=[ksq])
                        for n4 in range(4):
                            S.pre("pe", [sel2, ksq], [knP])
                            S.post("pe", mm(nc.tensor, knP[0:2, :], sel2[:],
                                            ksq[:, 512 * n4:512 * (n4 + 1)], True, True),
                                   [sel2, ksq], [knP])
                            S.op("dve", lambda e: e.tensor_copy(
                                kn2s[0:2, :], knP[0:2, :]), r=[knP], w=[kn2s])
                            S.op("dve", lambda e: e.reduce_max(
                                redc[0:2, 0:1], kn2s[0:2, :], axis=AX.X), r=[kn2s], w=[redc])
                            if r_ == 0 and n4 == 0:
                                S.op("dve", lambda e, c=c: e.tensor_copy(
                                    knacc[0:2, c:c + 1], redc[0:2, 0:1]), r=[redc], w=[knacc])
                            else:
                                S.op("dve", lambda e, c=c: e.tensor_tensor(
                                    knacc[0:2, c:c + 1], redc[0:2, 0:1],
                                    knacc[0:2, c:c + 1], ALU.max), r=[redc, knacc], w=[knacc])
                S.pre("pe", [knacc, ident], [trPq])
                S.post("pe", mm(nc.tensor, trPq[0:8, 136:138], knacc[0:2, 0:8],
                                ident[0:2, 0:2], True, True), [knacc, ident], [trPq])
                S.op("dve", lambda e: e.reduce_max(redc[0:8, 0:1], trPq[0:8, 136:138], axis=AX.X),
                     r=[trPq], w=[redc])
                S.pre("pe", [redc, ident], [trPq])
                S.post("pe", mm(nc.tensor, trPq[0:1, 138:146], redc[0:8, 0:1],
                                ident[0:8, 0:8], True, True), [redc, ident], [trPq])
                S.op("act", lambda e: e.activation(scal[0:1, 8:16], trPq[0:1, 138:146], AF.Sqrt),
                     r=[trPq], w=[scal])
                S.op("dve", lambda e: e.tensor_tensor(scal[0:1, 8:16], scal[0:1, 8:16],
                                                      qmaxr[0:1, 0:8], ALU.mult),
                     r=[scal, qmaxr], w=[scal])
                S.op("dve", lambda e: e.tensor_scalar(scal[0:1, 8:16], scal[0:1, 8:16],
                                                      -0.125, 0.0, ALU.mult, ALU.add),
                     r=[scal], w=[scal])
                S.barrier()
            with contextlib.ExitStack() as ph:
                ktile = [ph.enter_context(nc.sbuf_tensor("ktile0", [128, 4096], BF16))] * 2
                vstat = ph.enter_context(nc.sbuf_tensor("vstat", [128, 2 * 32 * 65], BF16))
                bbt = ph.enter_context(nc.sbuf_tensor("bbt", [128, 8], F32))
                zbs2 = ph.enter_context(nc.sbuf_tensor("zbs2", [64, 1024], F32))
                ptile = [ph.enter_context(nc.sbuf_tensor(f"pt{i}", [128, 2048], BF16)) for i in range(2)]
                zrowr = ph.enter_context(nc.sbuf_tensor(f"zrw{next(_uid)}", [1, 1024], F32))
                zrt = zrowr
                with nc.psum_tensor("bP3", [128, 512], F32) as bP3:
                    S.pre("pe", [onesrow, scal], [bP3])
                    S.post("pe", mm(nc.tensor, bP3[0:128, 0:8], onesrow[0:1, :],
                                    scal[0:1, 8:16], True, True), [onesrow, scal], [bP3])
                    S.op("dve", lambda e: e.tensor_copy(bbt[:, 0:8], bP3[0:128, 0:8]),
                         r=[bP3], w=[bbt])
                    S.barrier()
                STp = ph.enter_context(nc.psum_tensor("STp", [128, 2048], F32))
                pvP = [ph.enter_context(nc.psum_tensor(f"pvP{i}", [65, 512], F32)) for i in range(2)]
                zbP = ph.enter_context(nc.psum_tensor("zbP", [64, 1024], F32))
                S.op("pool", lambda e: e.memset(
                    vstat[:].rearrange("p (k c) -> p k c", c=65)[:, :, 64:65], 1.0), w=[vstat])

                for pr in range(8):
                    kt = ktile[pr % 2]
                    for r_ in range(2):
                        for hl in range(2):
                            S.dma("sp", lambda e, p=pr, r2=r_, hl=hl, kt=kt: e.dma_start(
                                out=kt[64 * hl:64 * (hl + 1), 2048 * r2:2048 * (r2 + 1)],
                                in_=kg.ap()[1024 * r2 + 64 * (2 * p + hl):1024 * r2 + 64 * (2 * p + hl) + 64, :]),
                                r=[kg], w=[kt])
                    for hl in range(2):
                        S.dma("sp", lambda e, p=pr, hl=hl: e.dma_start(
                            out=vstat[:].rearrange("p (k c) -> p k c", c=65)[:, 32 * hl:32 * (hl + 1), 0:64],
                            in_=vg.ap().rearrange("(k p) c -> p k c", p=128)[:, :, 64 * (2 * p + hl):64 * (2 * p + hl) + 64]),
                            r=[vg], w=[vstat])
                    for qi, (qoff, qw) in enumerate(NC):
                        def pv_emit(kb2, pt):
                            kb = 2 * kb2
                            S.pre("pe", [vstat, pt], [pvP[0]])
                            mm(nc.tensor, pvP[0][0:65, 0:qw],
                               vstat[:, 65 * kb:65 * (kb + 1)], pt[:, 0:qw],
                               kb == 0, False)
                            ia = mm(nc.tensor, pvP[0][0:65, 0:qw],
                                    vstat[:, 65 * (kb + 1):65 * (kb + 2)],
                                    pt[:, 1024:1024 + qw], False, kb == 30)
                            S.post("pe", ia, [vstat, pt], [pvP[0]])
                            S.pre("pe", [vstat, pt], [pvP[1]])
                            mm(nc.tensor, pvP[1][0:65, 0:qw],
                               vstat[:, 65 * (32 + kb):65 * (32 + kb) + 65],
                               pt[:, 512:512 + qw], kb == 0, False)
                            ib = mm(nc.tensor, pvP[1][0:65, 0:qw],
                                    vstat[:, 65 * (33 + kb):65 * (33 + kb) + 65],
                                    pt[:, 1536:1536 + qw], False, kb == 30)
                            S.post("pe", ib, [vstat, pt], [pvP[1]])
                        for kb2 in range(16):
                            kb = 2 * kb2
                            st = STp
                            S.pre("pe", [kt, s2[pr]], [st])
                            mm(nc.tensor, st[:, 0:qw], kt[0:64, 128 * kb:128 * (kb + 1)],
                               s2[pr][0:64, qoff:qoff + qw], True, True)
                            nc.tensor.matmul(
                                st[:, 512:512 + qw], kt[64:128, 128 * kb:128 * (kb + 1)],
                                s2[pr][64:128, qoff:qoff + qw], start=True, stop=True,
                                tile_position=(64, 0), skip_group_check=True)
                            mm(nc.tensor, st[:, 1024:1024 + qw],
                               kt[0:64, 128 * (kb + 1):128 * (kb + 2)],
                               s2[pr][0:64, qoff:qoff + qw], True, True)
                            i2 = nc.tensor.matmul(
                                st[:, 1536:1536 + qw], kt[64:128, 128 * (kb + 1):128 * (kb + 2)],
                                s2[pr][64:128, qoff:qoff + qw], start=True, stop=True,
                                tile_position=(64, 0), skip_group_check=True)
                            S.post("pe", i2, [kt, s2[pr]], [st])
                            pt = ptile[kb2 % 2]
                            if qw == 512:
                                S.op("act", lambda e, st=st, pt=pt, p=pr: e.activation(
                                    pt[:, :], st[:, :], AF.Exp, bias=bbt[:, p:p + 1],
                                    scale=0.125), r=[st, bbt], w=[pt])
                            else:
                                S.op("act", lambda e, st=st, pt=pt, p=pr, qw=qw: e.activation(
                                    pt[:, 0:2048].rearrange("p (s c) -> p s c", c=512)[:, :, 0:qw],
                                    st[:, 0:2048].rearrange("p (s c) -> p s c", c=512)[:, :, 0:qw],
                                    AF.Exp, bias=bbt[:, p:p + 1], scale=0.125),
                                    r=[st, bbt], w=[pt])
                            if kb2 > 0:
                                pv_emit(kb2 - 1, ptile[(kb2 - 1) % 2])
                        pv_emit(15, ptile[15 % 2])
                        S.op("dve", lambda e, qw=qw: e.reciprocal(
                            zrowr[0:1, 0:qw], pvP[0][64:65, 0:qw]), r=[pvP[0]], w=[zrt])
                        S.op("dve", lambda e, qw=qw: e.reciprocal(
                            zrowr[0:1, 512:512 + qw], pvP[1][64:65, 0:qw]), r=[pvP[1]], w=[zrt])
                        for hl in range(2):
                            S.pre("pe", [onesrow, zrt], [zbP])
                            S.post("pe", mm(nc.tensor, zbP[0:64, 512 * hl:512 * hl + qw],
                                            onesrow[0:1, 0:64],
                                            zrowr[0:1, 512 * hl:512 * hl + qw], True, True),
                                   [onesrow, zrt], [zbP])
                        S.op("act", lambda e, qw=qw: e.copy(
                            zbs2[0:64, 0:qw], zbP[0:64, 0:qw]), r=[zbP], w=[zbs2])
                        S.op("act", lambda e, qw=qw: e.copy(
                            zbs2[0:64, 512:512 + qw], zbP[0:64, 512:512 + qw]), r=[zbP], w=[zbs2])
                        S.op("dve", lambda e, p=pr, qo=qoff, qw=qw: e.tensor_tensor(
                            s2[p][0:64, qo:qo + qw], pvP[0][0:64, 0:qw],
                            zbs2[0:64, 0:qw], ALU.mult), r=[pvP[0], zbs2], w=[s2[pr]])
                        S.op("dve", lambda e, p=pr, qo=qoff, qw=qw: e.tensor_tensor(
                            s2[p][64:128, qo:qo + qw], pvP[1][0:64, 0:qw],
                            zbs2[0:64, 512:512 + qw], ALU.mult), r=[pvP[1], zbs2], w=[s2[pr]])
                S.barrier()
            with contextlib.ExitStack() as ph:
                woP = [ph.enter_context(nc.psum_tensor(f"woB{i}", [128, 512], F32)) for i in range(2)]
                for oc in range(NF):
                    wb = wbuf[oc % 2]
                    S.dma("sp", lambda e, o=oc, wb=wb: e.dma_start(
                        out=wb[:, 0:1024],
                        in_=wobT.ap().rearrange("(f p) o -> p f o", p=128)[:, :, 128 * o:128 * (o + 1)]),
                        r=[wobT], w=[wb])
                    for ni, (off, w) in enumerate(NC):
                        pb = woP[ni % 2]
                        S.pre("pe", [wb] + [s2[f] for f in range(NF)], [pb])
                        for fc in range(NF):
                            inst = mm(nc.tensor, pb[:, 0:w], wb[:, 128 * fc:128 * (fc + 1)],
                                      s2[fc][:, off:off + w], fc == 0, fc == NF - 1)
                        S.post("pe", inst, [wb] + [s2[f] for f in range(NF)], [pb])
                        S.op("dve", lambda e, o=oc, of=off, ww=w, p=pb: e.tensor_tensor(
                            x8[o][:, of:of + ww], p[:, 0:ww], x8[o][:, of:of + ww], ALU.add),
                            r=[pb, x8[oc]], w=[x8[oc]])
                S.barrier()
            rezero_halo()

        if n_phases >= 4:
            ffn_phase("B")

        if dump:
            if os.environ.get("DUMP_XQ1"):
                for fc in range(0, NF):
                    S.dma("sp", lambda e, f=fc: e.dma_start(
                        out=xq2d.ap()[128 * f:128 * (f + 1), :], in_=s1[f][:, :]),
                        r=[s1[fc]], w=[xq2d])
            dhd = nc.dram_tensor("dhd", [DFF, T], F32, kind="ExternalOutput")
            dxq = nc.dram_tensor("dxq", [DFF, T], BF16, kind="ExternalOutput")
            dwq = nc.dram_tensor("dwq", [D, DFF], BF16, kind="ExternalOutput")
            with nc.sbuf_tensor("dmpw", [128, DFF], BF16) as dmpw:
                for f in range(NF):
                    S.dma("sp", lambda e, f=f: e.dma_start(
                        out=dmpw[:], in_=wq1T.ap()[128 * f:128 * (f + 1), :]), r=[wq1T], w=[dmpw])
                    S.dma("sp", lambda e, f=f: e.dma_start(
                        out=dwq.ap()[128 * f:128 * (f + 1), :], in_=dmpw[:]), r=[dmpw], w=[dwq])
            with nc.sbuf_tensor("dmp", [128, T], F32) as dmp, \
                 nc.sbuf_tensor("dmpb", [128, T], BF16) as dmpb:
                for h in range(NHC):
                    S.dma("sp", lambda e, h=h: e.dma_start(
                        out=dmp[:], in_=hd.ap()[128 * h:128 * (h + 1), :]), r=[hd], w=[dmp])
                    S.dma("sp", lambda e, h=h: e.dma_start(
                        out=dhd.ap()[128 * h:128 * (h + 1), :], in_=dmp[:]), r=[dmp], w=[dhd])
                    S.dma("sp", lambda e, h=h: e.dma_start(
                        out=dmpb[:], in_=xq2d.ap()[128 * h:128 * (h + 1), :]), r=[xq2d], w=[dmpb])
                    S.dma("sp", lambda e, h=h: e.dma_start(
                        out=dxq.ap()[128 * h:128 * (h + 1), :], in_=dmpb[:]), r=[dmpb], w=[dxq])
                S.barrier()
        for fc in range(NF):
            S.dma("sp", lambda e, f=fc: e.dma_start(
                out=yout.ap()[128 * f:128 * (f + 1), :], in_=x8[f][:, OWN0:OWN1]),
                r=[x8[fc]], w=[yout])
        S.barrier()

    return nc


_CACHE = {}


def _stage(inputs):
    x = np.asarray(inputs["L0_attn_a_x"], dtype=np.float32)
    base = {
        "wqkv_a": np.ascontiguousarray(inputs["L0_attn_a_w_qkv"], dtype=np.float32),
        "wout_a": np.ascontiguousarray(inputs["L0_attn_a_w_out"], dtype=np.float32),
        "bout_a": np.ascontiguousarray(inputs["L0_attn_a_b_out"], dtype=np.float32),
        "wqkv_b": np.ascontiguousarray(inputs["L0_attn_b_w_qkv"], dtype=np.float32),
        "wout_b": np.ascontiguousarray(inputs["L0_attn_b_w_out"], dtype=np.float32),
        "w1": np.ascontiguousarray(inputs["L0_ffn_w1"], dtype=np.float32),
        "convw": np.ascontiguousarray(
            np.asarray(inputs["L0_ffn_conv_w"], dtype=np.float32).reshape(DFF, 3).T),
        "convb": np.ascontiguousarray(inputs["L0_ffn_conv_b"], dtype=np.float32),
        "w2": np.ascontiguousarray(inputs["L0_ffn_w2"], dtype=np.float32),
        "ident": np.eye(128, dtype=np.float32),
        "sel2": np.concatenate([
            np.repeat(np.float32([[1], [0]]), 64, axis=0),
            np.repeat(np.float32([[0], [1]]), 64, axis=0)], axis=1).astype(np.float32),
        "onescol_in": np.ones((128, 1), dtype=np.float32),
        "onesrow_in": np.ones((128, 128), dtype=np.float32),
    }
    import os as _os
    if _os.environ.get("BYPASS_XQ"):
        import ml_dtypes as _md
        base["xqstage"] = np.load("/root/problem/_xqstage.npy").view(_md.bfloat16)
    if _os.environ.get("BYPASS_WQ"):
        import ml_dtypes as _md
        w1f = np.asarray(inputs.get("L0_ffn_w1", base["w1"]), np.float64)
        tern = np.clip(np.rint(w1f / np.abs(w1f).mean()), -1, 1)
        base["wqstage"] = np.ascontiguousarray(tern.T).astype(_md.bfloat16)
    in_maps = []
    for c in range(8):
        b, hh = c // 2, c % 2
        s, e = 2048 * hh, 2048 * hh + 2048
        pad = np.zeros((T, D), dtype=np.float32)
        lo, hi = s - 2, e + 2
        clo, chi = max(lo, 0), min(hi, 4096)
        pad[clo - lo:chi - lo, :] = x[b, clo:chi, :]
        hm = np.ones((128, 4), dtype=np.float32)
        if hh == 0:
            hm[:, 0:2] = 0.0
        else:
            hm[:, 2:4] = 0.0
        m = dict(base)
        m["xin"] = np.ascontiguousarray(pad.T)
        m["hmask"] = hm
        in_maps.append(m)
    return in_maps


def kernel(**inputs):
    n_phases = int(inputs.pop("_n_phases", 4))
    dump = bool(inputs.pop("_dump", False))
    key = (n_phases, dump)
    if key not in _CACHE:
        _CACHE[key] = build(n_phases, dump)
    nc = _CACHE[key]
    in_maps = _stage(inputs)
    res = run_bass_kernel_spmd(nc, in_maps, core_ids=list(range(8)))
    out = np.empty((4, 4096, 1024), dtype=np.float32)
    for c in range(8):
        b, hh = c // 2, c % 2
        out[b, 2048 * hh:2048 * (hh + 1), :] = res.results[c]["y"].T
    if dump:
        kernel._dbg = res.results
    return out
